# version 03
# baseline (speedup 1.0000x reference)
import contextlib
import math
import os
import numpy as np
import ml_dtypes
import concourse.bass as bass
import concourse.mybir as mybir
from concourse.bass_utils import run_bass_kernel_spmd
from concourse.bass import IndirectOffsetOnAxis
F32 = mybir.dt.float32
BF16 = mybir.dt.bfloat16
AF = mybir.ActivationFunctionType
ALU = mybir.AluOpType
AX = mybir.AxisListType

D = 1024
S = 8192
NCH = S // 128
QB = 256
NQB = S // QB
TOK = S // 4
NT = TOK // 128
TCH = 1024
NTCH = TOK // TCH
EPS = 1e-5
LN2H = 0.5 * math.log(2.0)
UPTO = int(os.environ.get("K_UPTO", "4"))
P2N = int(os.environ.get("K_P2N", "999"))
P2M = int(os.environ.get("K_P2M", "5"))
P2QK = int(os.environ.get("K_P2QK", "9999"))
P2E = int(os.environ.get("K_P2E", "1"))
NOCC = int(os.environ.get("K_NOCC", "0"))
NOFFN = int(os.environ.get("K_NOFFN", "0"))
NOPRE = int(os.environ.get("K_NOPRE", "0"))


class Op:
    __slots__ = ("eng", "fn", "deps", "idx", "need_inc", "val", "dma", "accum", "burst")

    def __init__(self, eng, fn, dma, accum):
        self.eng, self.fn, self.dma, self.accum = eng, fn, dma, accum
        self.deps = []
        self.idx = -1
        self.need_inc = False
        self.val = 0
        self.burst = 0


class Bld:
    def __init__(self):
        self.ops = {e: [] for e in "TVSPY"}
        self.res = {}
        self.groups = {}

    def emit(self, eng, fn, reads=(), writes=(), dma=None, accum=None):
        op = Op(eng, fn, dma, accum)
        deps = {}
        for r in reads:
            ent = self.res.get(r)
            if ent and ent[0] is not None:
                deps[id(ent[0])] = ent[0]
        for w in writes:
            ent = self.res.get(w)
            if ent:
                if ent[0] is not None:
                    deps[id(ent[0])] = ent[0]
                for rd in ent[1]:
                    deps[id(rd)] = rd
        ds = []
        for d in deps.values():
            if d is op:
                continue
            if d.eng == "T" and eng == "T" and d.dma is None and dma is None:
                continue
            if accum is not None and d.accum == accum:
                continue
            if dma not in (None, "q") and d.eng == eng and d.dma == dma:
                continue
            ds.append(d)
        op.deps = ds
        for r in reads:
            self.res.setdefault(r, [None, []])[1].append(op)
        for w in writes:
            self.res[w] = [op, []]
        op.idx = len(self.ops[eng])
        self.ops[eng].append(op)
        if dma is not None and dma != "q" and eng == "P":
            self.groups.setdefault(dma, []).append(op)
        return op

    def finalize(self, nc, es, block):
        sems = {}
        for e in "TVSPY":
            sems[e] = es.enter_context(nc.semaphore(f"sem_{e}"))
        for g in self.groups:
            sems["g:" + g] = es.enter_context(nc.semaphore(f"sem_g_{g}"))
        for e in "TVSPY":
            for op in self.ops[e]:
                for d in op.deps:
                    d.need_inc = True
        for ops_ in self.groups.values():
            for o in ops_:
                o.need_inc = True
        for e in "TVSPY":
            for op in self.ops[e]:
                if op.dma is not None:
                    op.need_inc = True
        gtotal = {g: 16 * len(ops_) for g, ops_ in self.groups.items()}
        for e in "TVSPY":
            c = 0
            for op in self.ops[e]:
                if op.eng == "P" and op.dma not in (None, "q"):
                    op.val = 0
                    continue
                if op.need_inc:
                    c += 16 if (op.dma is not None) else 1
                op.val = c

        burst_end = {"Y": {}, "P": {}}
        for e in ("Y", "P"):
            bid = 0
            for op in self.ops[e]:
                if op.deps:
                    bid += 1
                op.burst = bid
                burst_end[e][bid] = op.val

        def sem_of(d):
            if d.eng == "P" and d.dma not in (None, "q"):
                return ("g:" + d.dma, gtotal[d.dma])
            if d.eng in ("Y", "P"):
                return (d.eng, burst_end[d.eng][d.burst])
            return (d.eng, d.val)

        waited = {"Y": set(), "P": set()}
        for e in "TVSPY":
            for op in self.ops[e]:
                for d in op.deps:
                    s, v = sem_of(d)
                    if s in waited:
                        waited[s].add(v)

        def run_engine(e, eng_obj):
            water = {}
            for op in self.ops[e]:
                waits = {}
                for d in op.deps:
                    s, v = sem_of(d)
                    if waits.get(s, -1) < v:
                        waits[s] = v
                for s, v in sorted(waits.items()):
                    if water.get(s, -1) >= v:
                        continue
                    eng_obj.wait_ge(sems[s], v)
                    water[s] = v
                ins = op.fn(eng_obj)
                if op.need_inc:
                    if op.eng == "P" and op.dma not in (None, "q"):
                        ins.then_inc(sems["g:" + op.dma], 16)
                    elif op.dma is not None:
                        ins.then_inc(sems[op.eng], 16)
                    else:
                        ins.then_inc(sems[op.eng], 1)
                if e in ("Y", "P") and op.need_inc and op.dma in (None, "q") and op.val in waited[e]:
                    eng_obj.wait_ge(sems[e], op.val)
                    water[e] = max(water.get(e, -1), op.val)

        @block.tensor
        def _(t):
            run_engine("T", t)

        @block.vector
        def _(v):
            run_engine("V", v)

        @block.scalar
        def _(s):
            run_engine("S", s)

        @block.gpsimd
        def _(p):
            run_engine("P", p)

        @block.sync
        def _(y):
            run_engine("Y", y)


def build_program():
    nc = bass.Bass("TRN2", target_bir_lowering=False)

    x_in = nc.dram_tensor("x", [S, D], F32, kind="ExternalInput").ap()
    xo_in = nc.dram_tensor("xown", [TOK, D], F32, kind="ExternalInput").ap()
    wq_in = nc.dram_tensor("wq", [D, 256], F32, kind="ExternalInput").ap()
    wk_in = nc.dram_tensor("wk", [D, 64], F32, kind="ExternalInput").ap()
    wv_in = nc.dram_tensor("wv", [D, 64], F32, kind="ExternalInput").ap()
    wo_in = nc.dram_tensor("wo", [D, D], F32, kind="ExternalInput").ap()
    cw1_in = nc.dram_tensor("cw1", [D, D], F32, kind="ExternalInput").ap()
    cw2_in = nc.dram_tensor("cw2", [D, 512], F32, kind="ExternalInput").ap()
    cw3_in = nc.dram_tensor("cw3", [512, 256], F32, kind="ExternalInput").ap()
    cw4_in = nc.dram_tensor("cw4", [256, 1], F32, kind="ExternalInput").ap()
    gw1_in = nc.dram_tensor("gw1", [D, 512], F32, kind="ExternalInput").ap()
    gw2_in = nc.dram_tensor("gw2", [512, 256], F32, kind="ExternalInput").ap()
    gw3_in = nc.dram_tensor("gw3", [256, 1], F32, kind="ExternalInput").ap()
    cb1_in = nc.dram_tensor("cb1", [D], F32, kind="ExternalInput").ap()
    cb2_in = nc.dram_tensor("cb2", [512], F32, kind="ExternalInput").ap()
    cb3_in = nc.dram_tensor("cb3", [256], F32, kind="ExternalInput").ap()
    cb4_in = nc.dram_tensor("cb4", [1], F32, kind="ExternalInput").ap()
    gb1_in = nc.dram_tensor("gb1", [512], F32, kind="ExternalInput").ap()
    gb2_in = nc.dram_tensor("gb2", [256], F32, kind="ExternalInput").ap()
    gb3_in = nc.dram_tensor("gb3", [1], F32, kind="ExternalInput").ap()
    ar_in = nc.dram_tensor("ar", [1], F32, kind="ExternalInput").ap()
    cosq_in = nc.dram_tensor("cosq", [128, S], BF16, kind="ExternalInput").ap()
    sinq_in = nc.dram_tensor("sinq", [128, S], BF16, kind="ExternalInput").ap()
    cosk_in = nc.dram_tensor("cosk", [128, S], BF16, kind="ExternalInput").ap()
    sink_in = nc.dram_tensor("sink", [128, S], BF16, kind="ExternalInput").ap()
    rp_in = nc.dram_tensor("rp", [128, 128], BF16, kind="ExternalInput").ap()
    idb_in = nc.dram_tensor("idb", [128, 128], BF16, kind="ExternalInput").ap()
    idf_in = nc.dram_tensor("idf", [128, 128], F32, kind="ExternalInput").ap()
    oneb_in = nc.dram_tensor("oneb", [128, 128], BF16, kind="ExternalInput").ap()
    msk_in = nc.dram_tensor("msk", [128, 384], BF16, kind="ExternalInput").ap()
    out_d = nc.dram_tensor("out", [TOK, D], F32, kind="ExternalOutput").ap()
    yidx_in = nc.dram_tensor("yidx", [128, 64], mybir.dt.int32, kind="ExternalInput").ap()
    ydram_h = nc.dram_tensor("ydram", [S, 256], BF16)
    ygat_h = nc.dram_tensor("ygat", [8 * S, 256], BF16)
    ydram = ydram_h.ap()
    ygat = ygat_h.ap()

    es = contextlib.ExitStack()
    SB = lambda n, sh, dt: es.enter_context(nc.sbuf_tensor(n, sh, dt))
    b = Bld()

    with es:
        XN = SB("XN", [128, NT * 1024], F32)
        BIGA = SB("BIGA", [128, 16384], BF16)
        KD = SB("KD", [128, 8192], BF16)
        VT = SB("VT", [128, 65 * NCH], BF16)
        XCH = SB("XCH", [128, 2048], F32)
        SCR = SB("SCR", [128, 4096], BF16)
        PTA = SB("PTA", [128, 2048], BF16)
        CS = SB("CS", [128, 1024], BF16)
        RSB = SB("RSB", [128, 2048], BF16)
        SCB = SB("SCB", [128, 768], BF16)
        STT = SB("STT", [128, 32], F32)
        SUMXN = SB("SUMXN", [128, 8], F32)
        MAXA = SB("MAXA", [1, 4], F32)
        MScr = SB("MScr", [1, 8], F32)
        NEGB = SB("NEGB", [128, 2], F32)
        NEGBB = SB("NEGBB", [1, 2], BF16)
        GFT = SB("GFT", [1, 16], F32)
        XBARB = SB("XBARB", [128, 8], BF16)
        GFH1 = SB("GFH1", [128, 4], BF16)
        GFH2 = SB("GFH2", [128, 2], BF16)
        CTL = SB("CTL", [128, 4], F32)
        CTLB = SB("CTLB", [1, 1024], F32)
        TMP1 = SB("TMP1", [128, 128], F32)
        TMP2 = SB("TMP2", [128, 128], F32)
        TMPQ = SB("TMPQ", [128, 512], F32)
        YST = SB("YST", [128, 256], BF16)
        RD = SB("RD", [128, 8], F32)
        IDB = SB("IDB", [128, 128], BF16)
        IDF = SB("IDF", [128, 128], F32)
        ONEB = SB("ONEB", [128, 128], BF16)
        MSK = SB("MSK", [128, 384], BF16)
        RP = SB("RPs", [128, 128], BF16)
        WQB = SB("WQB", [128, 2048], BF16)
        WKB = SB("WKB", [128, 512], BF16)
        WVB = SB("WVB", [128, 512], BF16)
        WBIG = SB("WBIG", [128, 8192], BF16)
        CN2 = SB("CN2", [128, 4096], BF16)
        CN3 = SB("CN3", [128, 1024], BF16)
        CN4 = SB("CN4", [128, 2], BF16)
        GC1 = SB("GC1", [128, 4096], BF16)
        GC2 = SB("GC2", [128, 1024], BF16)
        GC3 = SB("GC3", [128, 2], BF16)
        CB1 = SB("CB1", [128, 8], F32)
        CB2 = SB("CB2", [128, 4], F32)
        CB3 = SB("CB3", [128, 2], F32)
        CB4 = SB("CB4", [1, 2], F32)
        GB1 = SB("GB1", [128, 4], F32)
        GB2 = SB("GB2", [128, 2], F32)
        GB3 = SB("GB3", [1, 2], F32)
        ART = SB("ART", [1, 1], F32)
        CONS = SB("CONS", [1, 4], F32)
        YIX = SB("YIX", [128, 64], mybir.dt.int32)
        PS = es.enter_context(nc.psum_tensor("PS", [128, 4096], F32))
        biga_addr = nc.lookup_mloc(BIGA).addr
        kd_addr = nc.lookup_mloc(KD).addr
        QCH = [[nc.alloc_sbuf_tensor_at(f"QCH{p_}_{c_}", (128, 256), BF16,
                                        offset=biga_addr + 16384 * p_ + 512 * c_)
                for c_ in range(NCH // 2)] for p_ in range(2)]
        KCH = [nc.alloc_sbuf_tensor_at(f"KCH_{c_}", (128, 128), BF16,
                                       offset=kd_addr + 256 * c_)
               for c_ in range(NCH)]
        xn_addr = nc.lookup_mloc(XN).addr
        KBH = [nc.alloc_sbuf_tensor_at(f"KBH_{c_}", (128, 128), BF16,
                                       offset=xn_addr + 256 * c_)
               for c_ in range(NCH)]
        KBALL = nc.alloc_sbuf_tensor_at("KBALL", (128, 8192), BF16, offset=xn_addr)

        def psb(bank, off, w, p0=0, p1=128):
            return PS[p0:p1, 512 * bank + off:512 * bank + off + w]

        stg_par = [0]

        def wload(dst, srcap, kt, ncols, name):
            for k in range(kt):
                bu = stg_par[0] % 2
                stg_par[0] += 1
                stg = XCH[:, 1024 * bu:1024 * bu + ncols]
                b.emit("Y", lambda y, k=k, stg=stg, srcap=srcap: y.dma_start(
                    out=stg, in_=srcap[128 * k:128 * (k + 1), :]),
                    writes=[f"XCH:{bu}"], dma="q")
                b.emit("P", lambda p, k=k, stg=stg, dst=dst, ncols=ncols: p.tensor_copy(
                    out=dst[:, ncols * k:ncols * (k + 1)], in_=stg),
                    reads=[f"XCH:{bu}"], writes=[name])

        if not int(os.environ.get("K_NOQKV", "0")):
            b.emit("Y", lambda y: y.dma_start(out=YIX[:, :], in_=yidx_in), writes=["YIX"], dma="q")
            wload(WQB, wq_in, 8, 256, "WQB")
            wload(WKB, wk_in, 8, 64, "WKB")
            wload(WVB, wv_in, 8, 64, "WVB")

        def bload(dst, src, m, name):
            for i in range(m):
                b.emit("Y", lambda y, d=dst, s=src, i=i:
                       y.dma_start(out=d[:, i:i + 1],
                                   in_=s[128 * i:128 * (i + 1)].rearrange("(p o) -> p o", o=1)),
                       writes=[name], dma="q")

        if not int(os.environ.get("K_NOBL", "0")):
            bload(CB1, cb1_in, 8, "CB1")
            bload(CB2, cb2_in, 4, "CB2")
            bload(CB3, cb3_in, 2, "CB3")
            bload(GB1, gb1_in, 4, "GB1")
            bload(GB2, gb2_in, 2, "GB2")
            for dst_, src_, nm_ in ((CB4, cb4_in, "CB4"), (GB3, gb3_in, "GB3"), (ART, ar_in, "ART")):
                b.emit("Y", lambda y, d=dst_, s=src_: y.dma_start(out=d[0:1, 0:1], in_=s.rearrange("(x o) -> x o", x=1)),
                       writes=[nm_], dma="q")
        if not int(os.environ.get("K_NOTB", "0")):
            for dst_, src_, nm_ in ((IDB, idb_in, "IDB"), (IDF, idf_in, "IDF"), (ONEB, oneb_in, "ONEB"),
                                    (MSK, msk_in, "MSK"), (RP, rp_in, "RPc")):
                b.emit("Y", lambda y, d=dst_, s=src_: y.dma_start(out=d[:, :], in_=s), writes=[nm_], dma="q")
            b.emit("V", lambda v: v.memset(MAXA[:, :], -1e30), writes=["MAXA"])
            b.emit("V", lambda v: v.memset(SUMXN[:, :], 0.0), writes=["SUMXN"])
            b.emit("V", lambda v: v.memset(VT[:, :], 1.0), writes=["VTones"])
        b.emit("V", lambda v: v.memset(KD[64:128, :], 0.0), writes=["KDzero"])
        b.emit("V", lambda v: v.memset(KBALL[0:64, :], 0.0), writes=["KBzero", "ovXN"])
        b.emit("V", lambda v: v.memset(CONS[:, :], 0.0), writes=["CONS"])

        def p1_chunk(c):
            bu = c % 2
            r0 = 128 * c
            xch = XCH[:, 1024 * bu:1024 * (bu + 1)]
            xnb = SCR[:, 1024 * bu:1024 * (bu + 1)]
            xnt0 = 2048 + 1024 * bu
            st = 16 * bu
            cso = 512 * bu
            ro = 1024 * bu
            rXCH, rXNB, rXNT = f"XCH:{bu}", f"XNB:{bu}", f"XNT:{bu}"
            rST, rCS, rRSB = f"STT:{bu}", f"CS:{bu}", f"RSB:{bu}"

            b.emit("Y", lambda y: y.dma_start(out=xch, in_=x_in[r0:r0 + 128, :]),
                   writes=[rXCH], dma="q")
            for j, tab in enumerate((cosq_in, sinq_in, cosk_in, sink_in)):
                b.emit("Y", lambda y, j=j, tab=tab: y.dma_start(
                    out=CS[:, cso + 128 * j:cso + 128 * (j + 1)], in_=tab[:, r0:r0 + 128]),
                    writes=[rCS], dma="q")

            b.emit("V", lambda v: v.bn_stats(out=STT[:, st:st + 6], in_=xch[:, 0:512]),
                   reads=[rXCH], writes=[rST])
            b.emit("V", lambda v: v.bn_stats(out=STT[:, st + 6:st + 12], in_=xch[:, 512:1024]),
                   reads=[rXCH], writes=[rST])
            b.emit("V", lambda v: v.bn_aggr(out=STT[:, st + 12:st + 14], in_=STT[:, st:st + 12]),
                   reads=[rST], writes=[rST])
            b.emit("V", lambda v: v.tensor_scalar_add(out=STT[:, st + 14:st + 15],
                                                      in0=STT[:, st + 13:st + 14], scalar1=EPS),
                   reads=[rST], writes=[rST])
            b.emit("S", lambda s: s.activation(out=STT[:, st + 15:st + 16],
                                               in_=STT[:, st + 14:st + 15], func=AF.Ln),
                   reads=[rST], writes=[rST])
            b.emit("S", lambda s: s.activation(out=STT[:, st + 14:st + 15],
                                               in_=STT[:, st + 15:st + 16], func=AF.Exp, scale=-0.5),
                   reads=[rST], writes=[rST])
            b.emit("V", lambda v: v.tensor_scalar(out=xnb, in0=xch,
                                                  scalar1=STT[:, st + 12:st + 13],
                                                  scalar2=STT[:, st + 14:st + 15],
                                                  op0=ALU.subtract, op1=ALU.mult),
                   reads=[rXCH, rST, "ovS"], writes=[rXNB])

            for f in range(8):
                bank, off = 4 + (f // 4), 128 * (f % 4)
                b.emit("T", lambda t, f=f, bank=bank, off=off: t.matmul(
                    psb(bank, off, 128), xnb[:, 128 * f:128 * (f + 1)], IDB[:, :],
                    start=True, stop=True),
                    reads=[rXNB, "IDB", "ovS"], writes=[f"pb{bank}"])
            for f in range(8):
                bank, off = 4 + (f // 4), 128 * (f % 4)
                b.emit("V", lambda v, f=f, bank=bank, off=off: v.tensor_copy(
                    out=SCR[:, xnt0 + 128 * f:xnt0 + 128 * (f + 1)], in_=psb(bank, off, 128)),
                    reads=[f"pb{bank}", "ovS"], writes=[rXNT])

            for m in range(4):
                bank = m
                for k in range(8):
                    if m < 2:
                        st_ap = WQB.ap()[:, 256 * k + 128 * m:256 * k + 128 * m + 128]
                        wnm = "WQB"
                    elif m == 2:
                        st_ap = WKB.ap()[:, 64 * k:64 * (k + 1)]
                        wnm = "WKB"
                    else:
                        st_ap = WVB.ap()[:, 64 * k:64 * (k + 1)]
                        wnm = "WVB"
                    p1_ = 128 if m < 2 else 64
                    b.emit("T", lambda t, k=k, bank=bank, st_ap=st_ap, p1_=p1_: t.matmul(
                        psb(bank, 0, 128, 0, p1_), st_ap,
                        SCR[:, xnt0 + 128 * k:xnt0 + 128 * (k + 1)],
                        start=(k == 0), stop=(k == 7)),
                        reads=[rXNT, wnm], writes=[f"pb{bank}"], accum=("p1", c, m))
            for m in range(2):
                b.emit("S", lambda s, m=m: s.activation(
                    out=RSB[:, ro + 128 * m:ro + 128 * (m + 1)], in_=psb(m, 0, 128), func=AF.Copy),
                    reads=[f"pb{m}"], writes=[rRSB])
            b.emit("S", lambda s: s.activation(
                out=RSB[0:64, ro + 256:ro + 384], in_=psb(2, 0, 128, 0, 64), func=AF.Copy),
                reads=["pb2"], writes=[rRSB])
            b.emit("S", lambda s: s.activation(out=RSB[0:64, ro + 768:ro + 896],
                                               in_=psb(3, 0, 128, 0, 64), func=AF.Copy),
                   reads=["pb3"], writes=[rRSB])

            for m in range(2):
                b.emit("T", lambda t, m=m: t.matmul(
                    psb(m, 128, 128), RP[:, :], RSB[:, ro + 128 * m:ro + 128 * (m + 1)],
                    start=True, stop=True),
                    reads=[rRSB, "RPc"], writes=[f"pb{m}"])
            b.emit("T", lambda t: t.matmul(
                psb(2, 128, 128, 0, 64), RP[0:64, 0:64], RSB[0:64, ro + 256:ro + 384],
                start=True, stop=True),
                reads=[rRSB, "RPc"], writes=["pb2"])
            for m in range(2):
                b.emit("S", lambda s, m=m: s.activation(
                    out=RSB[:, ro + 384 + 128 * m:ro + 384 + 128 * (m + 1)],
                    in_=psb(m, 128, 128), func=AF.Copy),
                    reads=[f"pb{m}"], writes=[rRSB])
            b.emit("S", lambda s: s.activation(
                out=RSB[0:64, ro + 640:ro + 768],
                in_=psb(2, 128, 128, 0, 64), func=AF.Copy),
                reads=["pb2"], writes=[rRSB])

            qho = 128 * (c % 2)
            for m, (dst, dnm, tco, tso, pl) in enumerate((
                    (QCH[0][c // 2][:, qho:qho + 128], f"QP0:{c}", 0, 128, 128),
                    (QCH[1][c // 2][:, qho:qho + 128], f"QP1:{c}", 0, 128, 128),
                    (KCH[c][0:64, :], f"KD:{c}", 256, 384, 64))):
                b.emit("V", lambda v, m=m, tco=tco, pl=pl: v.tensor_tensor(
                    out=TMP1[0:pl, :], in0=RSB[0:pl, ro + 128 * m:ro + 128 * (m + 1)],
                    in1=CS[0:pl, cso + tco:cso + tco + 128], op=ALU.mult),
                    reads=[rRSB, rCS], writes=["TMP1"])
                b.emit("V", lambda v, m=m, tso=tso, pl=pl: v.tensor_tensor(
                    out=TMP2[0:pl, :], in0=RSB[0:pl, ro + 384 + 128 * m:ro + 384 + 128 * (m + 1)],
                    in1=CS[0:pl, cso + tso:cso + tso + 128], op=ALU.mult),
                    reads=[rRSB, rCS], writes=["TMP2"])
                b.emit("V", lambda v, dst=dst, pl=pl: v.tensor_tensor(
                    out=dst, in0=TMP1[0:pl, :], in1=TMP2[0:pl, :], op=ALU.add),
                    reads=["TMP1", "TMP2", "ovA" if m < 2 else "ovK"], writes=[dnm])
            b.emit("Y", lambda y, c=c: y.dma_start(out=KBH[c][64:128, :], in_=KCH[c][0:64, :]),
                   reads=[f"KD:{c}", "KBzero", "ovXN"], writes=[f"KB:{c}"], dma="q")

            b.emit("T", lambda t: t.matmul(psb(3, 128, 64, 0, 128), RSB[0:64, ro + 768:ro + 896],
                                           IDB[0:64, 0:64], start=True, stop=True),
                   reads=[rRSB, "IDB"], writes=["pb3"])
            b.emit("S", lambda s, c=c: s.activation(out=VT[:, 65 * c:65 * c + 64],
                                                    in_=psb(3, 128, 64), func=AF.Copy),
                   reads=["pb3"], writes=[f"VT:{c}"])

            for m, snm in enumerate((f"QP0:{c}", f"QP1:{c}", f"KD:{c}")):
                src = (QCH[0][c // 2][:, qho:qho + 128] if m == 0 else
                       QCH[1][c // 2][:, qho:qho + 128] if m == 1 else
                       KCH[c][0:64, :])
                pl = 128 if m < 2 else 64
                b.emit("S", lambda s, m=m, src=src, pl=pl: s.activation(
                    out=SCB[0:pl, 256 * m:256 * m + 128], in_=src, func=AF.Square),
                    reads=[snm], writes=["SCB"])
            for m in range(3):
                pl = 128 if m < 2 else 64
                b.emit("T", lambda t, m=m, pl=pl: t.matmul(
                    psb(m, 256, 128, 0, 1), ONEB[0:pl, 0:1], SCB[0:pl, 256 * m:256 * m + 128],
                    start=True, stop=True),
                    reads=["SCB", "ONEB"], writes=[f"pb{m}"])
            for m in range(3):
                b.emit("V", lambda v, m=m: v.reduce_max(
                    out=MScr[0:1, m:m + 1], in_=psb(m, 256, 128, 0, 1), axis=AX.X),
                    reads=[f"pb{m}"], writes=["MScr"])
            b.emit("V", lambda v: v.tensor_max(out=MAXA[0:1, 0:3], in0=MAXA[0:1, 0:3],
                                               in1=MScr[0:1, 0:3]),
                   reads=["MScr", "MAXA"], writes=["MAXA"])

            for f in range(8):
                b.emit("T", lambda t, f=f: t.matmul(
                    psb(3, 256 + f, 1), xnb[:, 128 * f:128 * (f + 1)], ONEB[:, 0:1],
                    start=True, stop=True),
                    reads=[rXNB, "ONEB"], writes=["pb3"])
            b.emit("V", lambda v: v.tensor_add(out=SUMXN[:, :], in0=SUMXN[:, :], in1=psb(3, 256, 8)),
                   reads=["pb3", "SUMXN"], writes=["SUMXN"])

        gctr = [0]

        def qk_qb(p, qb):
            inst = 2 * qb + p
            QPn = "QP0" if p == 0 else "QP1"
            qch = QCH[p][qb]
            b.emit("S", lambda s, p=p: s.activation(out=GFT[0:1, 8:9], in_=MAXA[0:1, p:p + 1], func=AF.Ln),
                   reads=["MAXA"], writes=["GFTb"])
            b.emit("S", lambda s: s.activation(out=GFT[0:1, 9:10], in_=MAXA[0:1, 2:3], func=AF.Ln),
                   reads=["MAXA", "GFTb"], writes=["GFTb"])
            b.emit("V", lambda v: v.tensor_add(out=GFT[0:1, 10:11], in0=GFT[0:1, 8:9], in1=GFT[0:1, 9:10]),
                   reads=["GFTb"], writes=["GFTb"])
            b.emit("S", lambda s: s.activation(out=GFT[0:1, 11:12], in_=GFT[0:1, 10:11],
                                               func=AF.Exp, scale=0.5, bias=CONS[0:1, 0:1]),
                   reads=["GFTb", "CONS"], writes=["GFTb"])
            b.emit("V", lambda v: v.tensor_scalar_mul(out=NEGBB[0:1, 0:1],
                                                      in0=GFT[0:1, 11:12], scalar1=-1.0),
                   reads=["GFTb"], writes=["NEGBB"])
            b.emit("T", lambda t: t.matmul(psb(4, 400, 1), ONEB[0:1, :],
                                           NEGBB[0:1, 0:1], start=True, stop=True),
                   reads=["NEGBB", "ONEB"], writes=["pb4"])
            b.emit("V", lambda v: v.tensor_copy(out=NEGB[:, 0:1], in_=psb(4, 400, 1)),
                   reads=["pb4"], writes=["NEGB"])

            nkt = 2 * (qb + 1)
            if P2M < 2:
                return
            for g in range(nkt // 2):
                gpar = gctr[0] % 2
                gctr[0] += 1
                base = 0 if gpar == 0 else 2
                pta0 = 0 if gpar == 0 else 1024
                rPT = f"PTA:{gpar}"
                slots = []
                for gi in range(2):
                    kt = 2 * g + gi
                    for h in (0, 1):
                        si = 2 * gi + h
                        bank = base + si // 2
                        off = 256 * (si % 2)
                        pos = pta0 + 256 * si
                        slots.append((kt, h, bank, off, pos))
                nqk = 0
                for (kt, h, bank, off, pos) in slots:
                    if nqk >= P2QK:
                        continue
                    nqk += 1
                    kst = KCH[kt] if h == 0 else KBH[kt]
                    knm = f"KD:{kt}" if h == 0 else f"KB:{kt}"
                    b.emit("T", lambda t, kst=kst, bank=bank, off=off, qch=qch: t.matmul(
                        psb(bank, off, QB, 0, 128),
                        kst[:, :], qch[:, :],
                        start=True, stop=True),
                        reads=[knm, "KDzero" if h == 0 else "KBzero",
                               f"{QPn}:{2 * qb}", f"{QPn}:{2 * qb + 1}", "ovA", "ovK", "ovXN"],
                        writes=[f"pb{bank}"])
                if P2E < 1 or nqk < len(slots):
                    continue
                b.emit("S", lambda s, base=base, pta0=pta0: s.activation(
                    out=PTA[:, pta0:pta0 + 1024], in_=PS[:, 512 * base:512 * base + 1024],
                    func=AF.Exp, bias=NEGB[:, 0:1]),
                    reads=["NEGB", f"pb{base}", f"pb{base + 1}"], writes=[rPT])
                if P2M >= 3:
                    for (kt, h, bank, off, pos) in slots:
                        if kt >= 2 * qb:
                            moff = 128 if kt == 2 * qb else 0
                            b.emit("V", lambda v, pos=pos, moff=moff: v.tensor_tensor(
                                out=PTA[:, pos:pos + 256], in0=PTA[:, pos:pos + 256],
                                in1=MSK[:, moff:moff + 256], op=ALU.mult),
                                reads=[rPT, "MSK"], writes=[rPT])
                if P2M >= 4:
                    for (kt, h, bank, off, pos) in slots:
                        for j in range(2):
                            yb = 4 + 2 * h + j
                            b.emit("T", lambda t, pos=pos, kt=kt, j=j, yb=yb,
                                   st=(kt == 0), lst=(kt == nkt - 1): t.matmul(
                                psb(yb, 0, 65), PTA[:, pos + 128 * j:pos + 128 * j + 128],
                                VT[:, 65 * kt:65 * (kt + 1)], start=st, stop=lst),
                                reads=[rPT, f"VT:{kt}", "VTones"], writes=[f"pb{yb}"],
                                accum=("av", inst, h, j))
            if P2M < 5:
                return
            for h in range(2):
                for j in range(2):
                    yb = 4 + 2 * h + j
                    sl = 2 * h + j
                    b.emit("V", lambda v, yb=yb, sl=sl: v.reciprocal(
                        out=RD[:, sl:sl + 1], in_=psb(yb, 64, 1)),
                        reads=[f"pb{yb}"], writes=[f"RD:{sl}"])
                    b.emit("V", lambda v, yb=yb, sl=sl: v.tensor_scalar_mul(
                        out=YST[:, 64 * sl:64 * (sl + 1)], in0=psb(yb, 0, 64),
                        scalar1=RD[:, sl:sl + 1]),
                        reads=[f"pb{yb}", f"RD:{sl}"], writes=[f"YST:{sl}"])
                    lh = 2 * p + h
                    r0y = QB * qb + 128 * j
                    b.emit("Y", lambda y, r0y=r0y, lh=lh, sl=sl: y.dma_start(
                        out=ydram[r0y:r0y + 128, 64 * lh:64 * (lh + 1)],
                        in_=YST[:, 64 * sl:64 * (sl + 1)]),
                        reads=[f"YST:{sl}"], writes=["ydram"], dma="q")

        if UPTO >= 1:
            for c in range(min(NCH, int(os.environ.get("K_P1N", "999")))):
                p1_chunk(c)
        if UPTO >= 2:
            for qb in range(min(NQB, P2N)):
                qk_qb(0, qb)
                qk_qb(1, qb)

        def early_out():
            b.emit("V", lambda v: v.memset(XN[:, :], 0.25), writes=[f"XN:{t}" for t in range(NT)])
            for tt in range(NT):
                b.emit("Y", lambda y, tt=tt: y.dma_start(
                    out=out_d[128 * tt:128 * (tt + 1), :], in_=XN[:, 1024 * tt:1024 * (tt + 1)]),
                    reads=[f"XN:{tt}"], writes=["outw"], dma="q")
            b.emit("P", lambda p: p.engine_nop(), reads=["outw"])
            block = es.enter_context(nc.Block())
            b.finalize(nc, es, block)

        if UPTO < 3:
            early_out()
            return nc

        if not int(os.environ.get("K_NOWL", "0")):
            wload(WBIG, cw1_in, 8, 1024, "WBIG")
            wload(CN2, cw2_in, 8, 512, "CN2")
            wload(CN3, cw3_in, 4, 256, "CN3")
            wload(CN4, cw4_in, 2, 1, "CN4")
            wload(GC1, gw1_in, 8, 512, "GC1")
            wload(GC2, gw2_in, 4, 256, "GC2")
            wload(GC3, gw3_in, 2, 1, "GC3")

        if not NOCC:
            op_cc = b.emit("P", lambda p: p.collective_compute(
                "AllGather", ALU.bypass,
                replica_groups=[list(range(8))],
                ins=[ydram_h.ap().opt()], outs=[ygat_h.ap().opt()]),
                reads=["ydram"], writes=["ygat"])
            op_cc.need_inc = True

        for tt in (range(0) if NOPRE else range(NT)):
            bu = tt % 2
            xch = XCH[:, 1024 * bu:1024 * (bu + 1)]
            st = 16 * bu
            rXCH, rST = f"XCH:{bu}", f"STT:{bu}"
            b.emit("Y", lambda y, tt=tt, xch=xch: y.dma_start(
                out=xch, in_=xo_in[128 * tt:128 * (tt + 1), :]),
                writes=[rXCH], dma="q")
            b.emit("V", lambda v, xch=xch, st=st: v.bn_stats(out=STT[:, st:st + 6], in_=xch[:, 0:512]),
                   reads=[rXCH], writes=[rST])
            b.emit("V", lambda v, xch=xch, st=st: v.bn_stats(out=STT[:, st + 6:st + 12], in_=xch[:, 512:1024]),
                   reads=[rXCH], writes=[rST])
            b.emit("V", lambda v, st=st: v.bn_aggr(out=STT[:, st + 12:st + 14], in_=STT[:, st:st + 12]),
                   reads=[rST], writes=[rST])
            b.emit("V", lambda v, st=st: v.tensor_scalar_add(out=STT[:, st + 14:st + 15],
                                                             in0=STT[:, st + 13:st + 14], scalar1=EPS),
                   reads=[rST], writes=[rST])
            b.emit("S", lambda s, st=st: s.activation(out=STT[:, st + 15:st + 16],
                                                      in_=STT[:, st + 14:st + 15], func=AF.Ln),
                   reads=[rST], writes=[rST])
            b.emit("S", lambda s, st=st: s.activation(out=STT[:, st + 14:st + 15],
                                                      in_=STT[:, st + 15:st + 16], func=AF.Exp, scale=-0.5),
                   reads=[rST], writes=[rST])
            b.emit("V", lambda v, tt=tt, xch=xch, st=st: v.tensor_scalar(
                out=XN[:, 1024 * tt:1024 * (tt + 1)], in0=xch,
                scalar1=STT[:, st + 12:st + 13], scalar2=STT[:, st + 14:st + 15],
                op0=ALU.subtract, op1=ALU.mult),
                reads=[rXCH, rST], writes=[f"XN:{tt}"] + (["ovXN"] if tt == 0 else []))

        if NOFFN:
            early_out()
            return nc
        b.emit("V", lambda v: v.tensor_scalar_mul(out=XBARB[:, 0:8], in0=SUMXN[:, :], scalar1=1.0 / S),
               reads=["SUMXN"], writes=["XBARB"])
        for m in range(4):
            for k in range(8):
                b.emit("T", lambda t, m=m, k=k: t.matmul(
                    psb(0, m, 1), GC1.ap()[:, 512 * k + 128 * m:512 * k + 128 * m + 128],
                    XBARB[:, k:k + 1], start=(k == 0), stop=(k == 7)),
                    reads=["GC1", "XBARB"], writes=["pb0"], accum=("gc1", m))
        for m in range(4):
            b.emit("S", lambda s, m=m: s.activation(out=GFH1[:, m:m + 1], in_=psb(0, m, 1),
                                                    func=AF.Gelu, bias=GB1[:, m:m + 1]),
                   reads=["pb0", "GB1"], writes=["GFH1"])
        for m in range(2):
            for k in range(4):
                b.emit("T", lambda t, m=m, k=k: t.matmul(
                    psb(1, m, 1), GC2.ap()[:, 256 * k + 128 * m:256 * k + 128 * m + 128],
                    GFH1[:, k:k + 1], start=(k == 0), stop=(k == 3)),
                    reads=["GC2", "GFH1"], writes=["pb1"], accum=("gc2", m))
        for m in range(2):
            b.emit("S", lambda s, m=m: s.activation(out=GFH2[:, m:m + 1], in_=psb(1, m, 1),
                                                    func=AF.Gelu, bias=GB2[:, m:m + 1]),
                   reads=["pb1", "GB2"], writes=["GFH2"])
        for k in range(2):
            b.emit("T", lambda t, k=k: t.matmul(
                psb(2, 0, 1, 0, 1), GC3.ap()[:, k:k + 1], GFH2[:, k:k + 1],
                start=(k == 0), stop=(k == 1)),
                reads=["GC3", "GFH2"], writes=["pb2"], accum=("gc3", 0))
        b.emit("V", lambda v: v.tensor_scalar_mul(out=GB3[0:1, 1:2], in0=GB3[0:1, 0:1], scalar1=0.5),
               reads=["GB3"], writes=["GB3"])
        b.emit("V", lambda v: v.tensor_scalar_mul(out=CB4[0:1, 1:2], in0=CB4[0:1, 0:1], scalar1=0.5),
               reads=["CB4"], writes=["CB4"])
        b.emit("S", lambda s: s.activation(out=GFT[0:1, 0:1], in_=psb(2, 0, 1, 0, 1),
                                           func=AF.Tanh, scale=0.5, bias=GB3[0:1, 1:2]),
               reads=["pb2", "GB3"], writes=["GFTg"])
        b.emit("V", lambda v: v.tensor_scalar(out=GFT[0:1, 1:2], in0=GFT[0:1, 0:1],
                                              scalar1=0.5, scalar2=0.5, op0=ALU.mult, op1=ALU.add),
               reads=["GFTg"], writes=["GFTg"])
        b.emit("V", lambda v: v.tensor_mul(out=GFT[0:1, 2:3], in0=GFT[0:1, 1:2], in1=ART[0:1, 0:1]),
               reads=["GFTg", "ART"], writes=["GFTg"])
        b.emit("V", lambda v: v.tensor_scalar_mul(out=GFT[0:1, 3:4], in0=GFT[0:1, 2:3], scalar1=1.4),
               reads=["GFTg"], writes=["GFTg"])
        b.emit("V", lambda v: v.tensor_sub(out=GFT[0:1, 4:5], in0=GFT[0:1, 3:4], in1=ART[0:1, 0:1]),
               reads=["GFTg", "ART"], writes=["GFTg"])
        b.emit("V", lambda v: v.tensor_scalar_add(out=GFT[0:1, 5:6], in0=GFT[0:1, 4:5], scalar1=0.1),
               reads=["GFTg"], writes=["GFTg"])
        b.emit("V", lambda v: v.tensor_scalar_mul(out=GFT[0:1, 6:7], in0=ART[0:1, 0:1], scalar1=0.3),
               reads=["ART"], writes=["GFTg"])
        b.emit("V", lambda v: v.tensor_add(out=GFT[0:1, 7:8], in0=GFT[0:1, 5:6], in1=GFT[0:1, 6:7]),
               reads=["GFTg"], writes=["GFTg"])

        def ffn_cycle(cyc):
            for tc in range(NTCH):
                t0 = TCH * tc
                for ttl in range(8):
                    tt = 8 * tc + ttl
                    bu = ttl % 2
                    cdst = SCR[:, 1024 * bu:1024 * (bu + 1)]
                    first = (cyc == 0 and tc == 0 and ttl == 0)
                    b.emit("V", lambda v, tt=tt, cdst=cdst: v.tensor_copy(
                        out=cdst, in_=XN[:, 1024 * tt:1024 * (tt + 1)]),
                        reads=[f"XN:{tt}"] + ([] if first else ["ovS"]),
                        writes=[f"XNB:{bu}"] + (["ovS"] if first else []))
                    for f in range(8):
                        bank, off = 6 + (f // 4), 128 * (f % 4)
                        b.emit("T", lambda t, f=f, bank=bank, off=off, cdst=cdst: t.matmul(
                            psb(bank, off, 128), cdst[:, 128 * f:128 * (f + 1)], IDB[:, :],
                            start=True, stop=True),
                            reads=[f"XNB:{bu}", "IDB", "ovS"], writes=[f"pb{bank}"])
                    for f in range(8):
                        bank, off = 6 + (f // 4), 128 * (f % 4)
                        fe = (first and f == 0)
                        b.emit("V", lambda v, f=f, bank=bank, off=off, ttl=ttl: v.tensor_copy(
                            out=BIGA[:, 1024 * f + 128 * ttl:1024 * f + 128 * (ttl + 1)],
                            in_=psb(bank, off, 128)),
                            reads=[f"pb{bank}"] + ([] if fe else ["ovA"]),
                            writes=[f"OUTBT:{f}"] + (["ovA"] if fe else []))
                gi = 0
                for m in range(8):
                    for n in range(2):
                        bank = gi % 6
                        gi += 1
                        for k in range(8):
                            b.emit("T", lambda t, m=m, n=n, k=k, bank=bank: t.matmul(
                                psb(bank, 0, 512), WBIG.ap()[:, 1024 * k + 128 * m:1024 * k + 128 * m + 128],
                                BIGA[:, 1024 * k + 512 * n:1024 * k + 512 * n + 512],
                                start=(k == 0), stop=(k == 7)),
                                reads=[f"OUTBT:{k}", "WBIG", "ovA"], writes=[f"pb{bank}"],
                                accum=("h1", cyc, tc, m, n))
                        b.emit("S", lambda s, m=m, n=n, bank=bank: s.activation(
                            out=BIGA[:, 8192 + 1024 * m + 512 * n:8192 + 1024 * m + 512 * n + 512],
                            in_=psb(bank, 0, 512), func=AF.Gelu, bias=CB1[:, m:m + 1]),
                            reads=[f"pb{bank}", "CB1", "ovA"], writes=[f"H1B:{m}"])
                for m in range(4):
                    for n in range(2):
                        bank = gi % 6
                        gi += 1
                        for k in range(8):
                            b.emit("T", lambda t, m=m, n=n, k=k, bank=bank: t.matmul(
                                psb(bank, 0, 512), CN2.ap()[:, 512 * k + 128 * m:512 * k + 128 * m + 128],
                                BIGA[:, 8192 + 1024 * k + 512 * n:8192 + 1024 * k + 512 * n + 512],
                                start=(k == 0), stop=(k == 7)),
                                reads=[f"H1B:{k}", "CN2", "ovA"], writes=[f"pb{bank}"],
                                accum=("h2", cyc, tc, m, n))
                        wk = ["ovK"] if (cyc == 0 and tc == 0 and m == 0 and n == 0) else []
                        b.emit("S", lambda s, m=m, n=n, bank=bank: s.activation(
                            out=KD[:, 1024 * m + 512 * n:1024 * m + 512 * n + 512],
                            in_=psb(bank, 0, 512), func=AF.Gelu, bias=CB2[:, m:m + 1]),
                            reads=[f"pb{bank}", "CB2"] + ([] if wk else ["ovK"]),
                            writes=[f"H2B:{m}"] + wk)
                for m in range(2):
                    for n in range(2):
                        bank = gi % 6
                        gi += 1
                        for k in range(4):
                            b.emit("T", lambda t, m=m, n=n, k=k, bank=bank: t.matmul(
                                psb(bank, 0, 512), CN3.ap()[:, 256 * k + 128 * m:256 * k + 128 * m + 128],
                                KD[:, 1024 * k + 512 * n:1024 * k + 512 * n + 512],
                                start=(k == 0), stop=(k == 3)),
                                reads=[f"H2B:{k}", "CN3", "ovK"], writes=[f"pb{bank}"],
                                accum=("h3", cyc, tc, m, n))
                        b.emit("S", lambda s, m=m, n=n, bank=bank: s.activation(
                            out=KD[:, 4096 + 1024 * m + 512 * n:4096 + 1024 * m + 512 * n + 512],
                            in_=psb(bank, 0, 512), func=AF.Gelu, bias=CB3[:, m:m + 1]),
                            reads=[f"pb{bank}", "CB3", "ovK"], writes=[f"H3B:{m}"])
                for n in range(2):
                    for k in range(2):
                        b.emit("T", lambda t, n=n, k=k: t.matmul(
                            psb(6 + n, 0, 512, 0, 1),
                            CN4.ap()[:, k:k + 1],
                            KD[:, 4096 + 1024 * k + 512 * n:4096 + 1024 * k + 512 * n + 512],
                            start=(k == 0), stop=(k == 1)),
                            reads=[f"H3B:{k}", "CN4", "ovK"], writes=[f"pb{6 + n}"],
                            accum=("z", cyc, tc, n))
                for n in range(2):
                    b.emit("S", lambda s, n=n: s.activation(
                        out=CTLB[0:1, 512 * n:512 * (n + 1)], in_=psb(6 + n, 0, 512, 0, 1),
                        func=AF.Tanh, scale=0.5, bias=CB4[0:1, 1:2]),
                        reads=[f"pb{6 + n}", "CB4"], writes=["CTLB"])
                b.emit("V", lambda v: v.tensor_scalar(
                    out=CTLB[0:1, 0:1024], in0=CTLB[0:1, 0:1024],
                    scalar1=GFT[0:1, 6:7], scalar2=GFT[0:1, 7:8],
                    op0=ALU.mult, op1=ALU.add),
                    reads=["CTLB", "GFTg"], writes=["CTLB"])
                for ttl in range(8):
                    tt = 8 * tc + ttl
                    cb = ttl % 2
                    b.emit("T", lambda t, ttl=ttl: t.matmul(
                        psb(6, 480, 1), CTLB[0:1, 128 * ttl:128 * (ttl + 1)],
                        IDF[0:1, 0:1], start=True, stop=True),
                        reads=["CTLB", "IDF"], writes=["pb6"])
                    b.emit("V", lambda v, cb=cb: v.tensor_copy(out=CTL[:, 2 * cb:2 * cb + 1],
                                                               in_=psb(6, 480, 1)),
                           reads=["pb6"], writes=[f"CTL:{cb}"])
                    b.emit("V", lambda v, cb=cb: v.tensor_scalar_mul(out=CTL[:, 2 * cb + 1:2 * cb + 2],
                                                                     in0=CTL[:, 2 * cb:2 * cb + 1],
                                                                     scalar1=-1.0),
                           reads=[f"CTL:{cb}"], writes=[f"CTL:{cb}"])
                    x0 = 1024 * tt
                    q = lambda i: XN[:, x0 + 256 * i:x0 + 256 * (i + 1)]
                    rX = f"XN:{tt}"
                    b.emit("V", lambda v, tt=tt, cb=cb: v.scalar_tensor_tensor(
                        out=TMPQ[:, 0:256], in0=XN[:, 1024 * tt + 512:1024 * tt + 768],
                        scalar=CTL[:, 2 * cb:2 * cb + 1], in1=XN[:, 1024 * tt:1024 * tt + 256],
                        op0=ALU.mult, op1=ALU.add),
                        reads=[rX, f"CTL:{cb}"], writes=["TMPQ"])
                    b.emit("V", lambda v, tt=tt, cb=cb: v.scalar_tensor_tensor(
                        out=TMPQ[:, 256:512], in0=XN[:, 1024 * tt + 768:1024 * tt + 1024],
                        scalar=CTL[:, 2 * cb + 1:2 * cb + 2], in1=XN[:, 1024 * tt + 256:1024 * tt + 512],
                        op0=ALU.mult, op1=ALU.add),
                        reads=[rX, f"CTL:{cb}"], writes=["TMPQ"])
                    b.emit("V", lambda v, tt=tt, cb=cb: v.scalar_tensor_tensor(
                        out=XN[:, 1024 * tt + 512:1024 * tt + 768], in0=XN[:, 1024 * tt:1024 * tt + 256],
                        scalar=CTL[:, 2 * cb + 1:2 * cb + 2], in1=XN[:, 1024 * tt + 512:1024 * tt + 768],
                        op0=ALU.mult, op1=ALU.add),
                        reads=[rX, f"CTL:{cb}"], writes=[rX])
                    b.emit("V", lambda v, tt=tt, cb=cb: v.scalar_tensor_tensor(
                        out=XN[:, 1024 * tt + 768:1024 * tt + 1024], in0=XN[:, 1024 * tt + 256:1024 * tt + 512],
                        scalar=CTL[:, 2 * cb:2 * cb + 1], in1=XN[:, 1024 * tt + 768:1024 * tt + 1024],
                        op0=ALU.mult, op1=ALU.add),
                        reads=[rX, f"CTL:{cb}"], writes=[rX])
                    b.emit("V", lambda v, tt=tt: v.tensor_copy(
                        out=XN[:, 1024 * tt:1024 * tt + 256], in_=TMPQ[:, 0:256]),
                        reads=["TMPQ"], writes=[rX])
                    b.emit("V", lambda v, tt=tt: v.tensor_copy(
                        out=XN[:, 1024 * tt + 256:1024 * tt + 512], in_=TMPQ[:, 256:512]),
                        reads=["TMPQ"], writes=[rX])

        for cyc in range(3):
            ffn_cycle(cyc)

        wload(WBIG, wo_in, 8, 1024, "WBIG")

        if UPTO < 4:
            early_out()
            return nc
        for tt in range(NT):
            for r in range(4):
                wA = ["ovA"] if (tt == 0 and r == 0) else []
                b.emit("P", lambda p, tt=tt, r=r: p.indirect_dma_start(
                    out=BIGA[:, 1024 * tt + 256 * r:1024 * tt + 256 * (r + 1)],
                    out_offset=None, in_=ygat,
                    in_offset=IndirectOffsetOnAxis(ap=YIX[:, 4 * tt + r:4 * tt + r + 1], axis=0)),
                    reads=["ygat", "YIX"] + ([] if wA else ["ovA"]),
                    writes=[f"YALL:{tt}"] + wA, dma="yin")
        for tt in range(NT):
            for f in range(8):
                bank, off = 6 + (f // 4), 128 * (f % 4)
                wS = ["ovS"] if (tt == 0 and f == 0) else []
                b.emit("T", lambda t, tt=tt, f=f, bank=bank, off=off: t.matmul(
                    psb(bank, off, 128), BIGA[:, 1024 * tt + 128 * f:1024 * tt + 128 * (f + 1)],
                    IDB[:, :], start=True, stop=True),
                    reads=[f"YALL:{tt}", "IDB"], writes=[f"pb{bank}"])
                b.emit("V", lambda v, f=f, bank=bank, off=off: v.tensor_copy(
                    out=SCR[:, 128 * f:128 * (f + 1)], in_=psb(bank, off, 128)),
                    reads=[f"pb{bank}"] + ([] if wS else ["ovS"]), writes=["YTT"] + wS)
            for n in range(2):
                bank = (2 * tt + n) % 4
                for k in range(8):
                    b.emit("T", lambda t, n=n, k=k, bank=bank: t.matmul(
                        psb(bank, 0, 512), SCR[:, 128 * k:128 * (k + 1)],
                        WBIG.ap()[:, 1024 * k + 512 * n:1024 * k + 512 * n + 512],
                        start=(k == 0), stop=(k == 7)),
                        reads=["YTT", "WBIG"], writes=[f"pb{bank}"], accum=("wo", tt, n))
                b.emit("V", lambda v, tt=tt, n=n, bank=bank: v.tensor_add(
                    out=XN[:, 1024 * tt + 512 * n:1024 * tt + 512 * n + 512],
                    in0=XN[:, 1024 * tt + 512 * n:1024 * tt + 512 * n + 512],
                    in1=psb(bank, 0, 512)),
                    reads=[f"pb{bank}", f"XN:{tt}"], writes=[f"XN:{tt}"])
            bu = tt % 2
            xch = XCH[:, 1024 * bu:1024 * (bu + 1)]
            b.emit("Y", lambda y, tt=tt, xch=xch: y.dma_start(
                out=xch, in_=xo_in[128 * tt:128 * (tt + 1), :]),
                writes=[f"XCH:{bu}"], dma="q")
            b.emit("V", lambda v, tt=tt, xch=xch: v.tensor_add(
                out=XN[:, 1024 * tt:1024 * (tt + 1)],
                in0=XN[:, 1024 * tt:1024 * (tt + 1)], in1=xch),
                reads=[f"XCH:{bu}", f"XN:{tt}"], writes=[f"XN:{tt}"])
            b.emit("Y", lambda y, tt=tt: y.dma_start(
                out=out_d[128 * tt:128 * (tt + 1), :], in_=XN[:, 1024 * tt:1024 * (tt + 1)]),
                reads=[f"XN:{tt}"], writes=["outw"], dma="q")

        b.emit("P", lambda p: p.engine_nop(), reads=["outw"])

        block = es.enter_context(nc.Block())
        b.finalize(nc, es, block)
    return nc


_CACHE = {}


def _tables():
    if "t" in _CACHE:
        return _CACHE["t"]
    half = 32
    inv = 1.0 / (10000.0 ** (np.arange(half, dtype=np.float64) / half))
    t = np.arange(S, dtype=np.float64)
    fr = np.outer(inv, t)
    emb = np.concatenate([fr, fr], axis=0)
    cos = np.cos(emb)
    sin = np.sin(emb)
    cos2 = np.concatenate([cos, cos], axis=0)
    sin2 = np.concatenate([sin, sin], axis=0)
    bf = lambda a: a.astype(ml_dtypes.bfloat16)
    cosq = bf(cos2 / 8.0)
    sinq = bf(sin2 / 8.0)
    cosk = bf(cos2)
    sink = bf(sin2)
    rp = np.zeros((128, 128), dtype=np.float32)
    for base in (0, 64):
        for m in range(32):
            rp[base + m + 32, base + m] = -1.0
            rp[base + m, base + m + 32] = 1.0
    idb = np.eye(128, dtype=ml_dtypes.bfloat16)
    idf = np.eye(128, dtype=np.float32)
    oneb = np.ones((128, 128), dtype=ml_dtypes.bfloat16)
    msk = np.zeros((128, 384), dtype=np.float32)
    for pp in range(128):
        for j in range(384):
            if pp <= j - 128:
                msk[pp, j] = 1.0
    _CACHE["t"] = (cosq, sinq, cosk, sink, bf(rp), idb, idf, oneb, bf(msk))
    return _CACHE["t"]


def kernel(**inputs):
    x = np.asarray(inputs["L0_attn_x"], dtype=np.float32)
    Wq = np.ascontiguousarray(np.asarray(inputs["L0_attn_Wq"], dtype=np.float32))
    Wk = np.ascontiguousarray(np.asarray(inputs["L0_attn_Wk"], dtype=np.float32))
    Wv = np.ascontiguousarray(np.asarray(inputs["L0_attn_Wv"], dtype=np.float32))
    Wo = np.ascontiguousarray(np.asarray(inputs["L0_attn_Wo"], dtype=np.float32))
    cosq, sinq, cosk, sink, rp, idb, idf, oneb, msk = _tables()

    if "nc" not in _CACHE:
        _CACHE["nc"] = build_program()
    nc = _CACHE["nc"]

    shared = {
        "wo": Wo,
        "cw1": np.ascontiguousarray(np.asarray(inputs["L0_ffn_cn_w1"], dtype=np.float32)),
        "cw2": np.ascontiguousarray(np.asarray(inputs["L0_ffn_cn_w2"], dtype=np.float32)),
        "cw3": np.ascontiguousarray(np.asarray(inputs["L0_ffn_cn_w3"], dtype=np.float32)),
        "cw4": np.ascontiguousarray(np.asarray(inputs["L0_ffn_cn_w4"], dtype=np.float32)),
        "gw1": np.ascontiguousarray(np.asarray(inputs["L0_ffn_gc_w1"], dtype=np.float32)),
        "gw2": np.ascontiguousarray(np.asarray(inputs["L0_ffn_gc_w2"], dtype=np.float32)),
        "gw3": np.ascontiguousarray(np.asarray(inputs["L0_ffn_gc_w3"], dtype=np.float32)),
        "cb1": np.ascontiguousarray(np.asarray(inputs["L0_ffn_cn_b1"], dtype=np.float32)),
        "cb2": np.ascontiguousarray(np.asarray(inputs["L0_ffn_cn_b2"], dtype=np.float32)),
        "cb3": np.ascontiguousarray(np.asarray(inputs["L0_ffn_cn_b3"], dtype=np.float32)),
        "cb4": np.ascontiguousarray(np.asarray(inputs["L0_ffn_cn_b4"], dtype=np.float32)),
        "gb1": np.ascontiguousarray(np.asarray(inputs["L0_ffn_gc_b1"], dtype=np.float32)),
        "gb2": np.ascontiguousarray(np.asarray(inputs["L0_ffn_gc_b2"], dtype=np.float32)),
        "gb3": np.ascontiguousarray(np.asarray(inputs["L0_ffn_gc_b3"], dtype=np.float32)),
        "ar": np.ascontiguousarray(np.asarray(inputs["L0_ffn_adaptive_range"], dtype=np.float32)),
        "cosq": cosq, "sinq": sinq, "cosk": cosk, "sink": sink,
        "rp": rp, "idb": idb, "idf": idf, "oneb": oneb, "msk": msk,
    }
    in_maps = []
    for c in range(NCORE := 8):
        bb, kh = divmod(c, 4)
        m = dict(shared)
        m["x"] = np.ascontiguousarray(x[bb])
        m["xown"] = np.ascontiguousarray(x[bb, TOK * kh:TOK * (kh + 1)])
        m["wq"] = np.ascontiguousarray(Wq[:, 256 * kh:256 * (kh + 1)])
        m["wk"] = np.ascontiguousarray(Wk[:, 64 * kh:64 * (kh + 1)])
        m["wv"] = np.ascontiguousarray(Wv[:, 64 * kh:64 * (kh + 1)])
        yidx = np.empty((128, 64), dtype=np.int32)
        for tt in range(NT):
            for r in range(4):
                yidx[:, 4 * tt + r] = S * (4 * bb + r) + TOK * kh + 128 * tt + np.arange(128)
        m["yidx"] = yidx
        in_maps.append(m)

    res = run_bass_kernel_spmd(nc, in_maps, core_ids=list(range(8)))
    out = np.empty((2, S, D), dtype=np.float32)
    for c in range(8):
        bb, kh = divmod(c, 4)
        out[bb, TOK * kh:TOK * (kh + 1)] = res.results[c]["out"]
    return out


# version 04
# speedup vs baseline: 1.0401x; 1.0401x over previous
import contextlib
import math
import os
import numpy as np
import ml_dtypes
import concourse.bass as bass
import concourse.mybir as mybir
from concourse.bass_utils import run_bass_kernel_spmd
from concourse.bass import IndirectOffsetOnAxis
F32 = mybir.dt.float32
BF16 = mybir.dt.bfloat16
AF = mybir.ActivationFunctionType
ALU = mybir.AluOpType
AX = mybir.AxisListType

D = 1024
S = 8192
NCH = S // 128
QB = 256
NQB = S // QB
TOK = S // 4
NT = TOK // 128
TCH = 1024
NTCH = TOK // TCH
EPS = 1e-5
LN2H = 0.5 * math.log(2.0)
UPTO = int(os.environ.get("K_UPTO", "4"))
P2N = int(os.environ.get("K_P2N", "999"))
P2M = int(os.environ.get("K_P2M", "5"))
P2QK = int(os.environ.get("K_P2QK", "9999"))
P2E = int(os.environ.get("K_P2E", "1"))
NOCC = int(os.environ.get("K_NOCC", "0"))
NOFFN = int(os.environ.get("K_NOFFN", "0"))
NOPRE = int(os.environ.get("K_NOPRE", "0"))


class Op:
    __slots__ = ("eng", "fn", "deps", "idx", "need_inc", "val", "dma", "accum", "burst")

    def __init__(self, eng, fn, dma, accum):
        self.eng, self.fn, self.dma, self.accum = eng, fn, dma, accum
        self.deps = []
        self.idx = -1
        self.need_inc = False
        self.val = 0
        self.burst = 0


class Bld:
    def __init__(self):
        self.ops = {e: [] for e in "TVSPY"}
        self.res = {}
        self.groups = {}

    def emit(self, eng, fn, reads=(), writes=(), dma=None, accum=None):
        op = Op(eng, fn, dma, accum)
        deps = {}
        for r in reads:
            ent = self.res.get(r)
            if ent and ent[0] is not None:
                deps[id(ent[0])] = ent[0]
        for w in writes:
            ent = self.res.get(w)
            if ent:
                if ent[0] is not None:
                    deps[id(ent[0])] = ent[0]
                for rd in ent[1]:
                    deps[id(rd)] = rd
        ds = []
        for d in deps.values():
            if d is op:
                continue
            if d.eng == "T" and eng == "T" and d.dma is None and dma is None:
                continue
            if accum is not None and d.accum == accum:
                continue
            if dma not in (None, "q") and d.eng == eng and d.dma == dma:
                continue
            ds.append(d)
        op.deps = ds
        for r in reads:
            self.res.setdefault(r, [None, []])[1].append(op)
        for w in writes:
            self.res[w] = [op, []]
        op.idx = len(self.ops[eng])
        self.ops[eng].append(op)
        if dma is not None and dma != "q" and eng == "P":
            self.groups.setdefault(dma, []).append(op)
        return op

    def finalize(self, nc, es, block):
        sems = {}
        for e in "TVSPY":
            sems[e] = es.enter_context(nc.semaphore(f"sem_{e}"))
        for g in self.groups:
            sems["g:" + g] = es.enter_context(nc.semaphore(f"sem_g_{g}"))
        for e in "TVSPY":
            for op in self.ops[e]:
                for d in op.deps:
                    d.need_inc = True
        for ops_ in self.groups.values():
            for o in ops_:
                o.need_inc = True
        for e in "TVSPY":
            for op in self.ops[e]:
                if op.dma is not None:
                    op.need_inc = True
        gtotal = {g: 16 * len(ops_) for g, ops_ in self.groups.items()}
        for e in "TVSPY":
            c = 0
            for op in self.ops[e]:
                if op.eng == "P" and op.dma not in (None, "q"):
                    op.val = 0
                    continue
                if op.need_inc:
                    c += 16 if (op.dma is not None) else 1
                op.val = c

        burst_end = {"Y": {}, "P": {}}
        for e in ("Y", "P"):
            bid = 0
            for op in self.ops[e]:
                if op.deps:
                    bid += 1
                op.burst = bid
                burst_end[e][bid] = op.val

        def sem_of(d):
            if d.eng == "P" and d.dma not in (None, "q"):
                return ("g:" + d.dma, gtotal[d.dma])
            if d.eng in ("Y", "P"):
                return (d.eng, burst_end[d.eng][d.burst])
            return (d.eng, d.val)

        waited = {"Y": set(), "P": set()}
        for e in "TVSPY":
            for op in self.ops[e]:
                for d in op.deps:
                    s, v = sem_of(d)
                    if s in waited:
                        waited[s].add(v)

        def run_engine(e, eng_obj):
            water = {}
            for op in self.ops[e]:
                waits = {}
                for d in op.deps:
                    s, v = sem_of(d)
                    if waits.get(s, -1) < v:
                        waits[s] = v
                for s, v in sorted(waits.items()):
                    if water.get(s, -1) >= v:
                        continue
                    eng_obj.wait_ge(sems[s], v)
                    water[s] = v
                ins = op.fn(eng_obj)
                if op.need_inc:
                    if op.eng == "P" and op.dma not in (None, "q"):
                        ins.then_inc(sems["g:" + op.dma], 16)
                    elif op.dma is not None:
                        ins.then_inc(sems[op.eng], 16)
                    else:
                        ins.then_inc(sems[op.eng], 1)
                if e in ("Y", "P") and op.need_inc and op.dma in (None, "q") and op.val in waited[e]:
                    eng_obj.wait_ge(sems[e], op.val)
                    water[e] = max(water.get(e, -1), op.val)

        @block.tensor
        def _(t):
            run_engine("T", t)

        @block.vector
        def _(v):
            run_engine("V", v)

        @block.scalar
        def _(s):
            run_engine("S", s)

        @block.gpsimd
        def _(p):
            run_engine("P", p)

        @block.sync
        def _(y):
            run_engine("Y", y)


def build_program():
    nc = bass.Bass("TRN2", target_bir_lowering=False)

    x_in = nc.dram_tensor("x", [S, D], F32, kind="ExternalInput").ap()
    xo_in = nc.dram_tensor("xown", [TOK, D], F32, kind="ExternalInput").ap()
    wq_in = nc.dram_tensor("wq", [D, 256], F32, kind="ExternalInput").ap()
    wk_in = nc.dram_tensor("wk", [D, 64], F32, kind="ExternalInput").ap()
    wv_in = nc.dram_tensor("wv", [D, 64], F32, kind="ExternalInput").ap()
    wo_in = nc.dram_tensor("wo", [D, D], F32, kind="ExternalInput").ap()
    cw1_in = nc.dram_tensor("cw1", [D, D], F32, kind="ExternalInput").ap()
    cw2_in = nc.dram_tensor("cw2", [D, 512], F32, kind="ExternalInput").ap()
    cw3_in = nc.dram_tensor("cw3", [512, 256], F32, kind="ExternalInput").ap()
    cw4_in = nc.dram_tensor("cw4", [256, 1], F32, kind="ExternalInput").ap()
    gw1_in = nc.dram_tensor("gw1", [D, 512], F32, kind="ExternalInput").ap()
    gw2_in = nc.dram_tensor("gw2", [512, 256], F32, kind="ExternalInput").ap()
    gw3_in = nc.dram_tensor("gw3", [256, 1], F32, kind="ExternalInput").ap()
    cb1_in = nc.dram_tensor("cb1", [D], F32, kind="ExternalInput").ap()
    cb2_in = nc.dram_tensor("cb2", [512], F32, kind="ExternalInput").ap()
    cb3_in = nc.dram_tensor("cb3", [256], F32, kind="ExternalInput").ap()
    cb4_in = nc.dram_tensor("cb4", [1], F32, kind="ExternalInput").ap()
    gb1_in = nc.dram_tensor("gb1", [512], F32, kind="ExternalInput").ap()
    gb2_in = nc.dram_tensor("gb2", [256], F32, kind="ExternalInput").ap()
    gb3_in = nc.dram_tensor("gb3", [1], F32, kind="ExternalInput").ap()
    ar_in = nc.dram_tensor("ar", [1], F32, kind="ExternalInput").ap()
    cosq_in = nc.dram_tensor("cosq", [128, S], BF16, kind="ExternalInput").ap()
    sinq_in = nc.dram_tensor("sinq", [128, S], BF16, kind="ExternalInput").ap()
    cosk_in = nc.dram_tensor("cosk", [128, S], BF16, kind="ExternalInput").ap()
    sink_in = nc.dram_tensor("sink", [128, S], BF16, kind="ExternalInput").ap()
    rp_in = nc.dram_tensor("rp", [128, 128], BF16, kind="ExternalInput").ap()
    idb_in = nc.dram_tensor("idb", [128, 128], BF16, kind="ExternalInput").ap()
    idf_in = nc.dram_tensor("idf", [128, 128], F32, kind="ExternalInput").ap()
    oneb_in = nc.dram_tensor("oneb", [128, 128], BF16, kind="ExternalInput").ap()
    msk_in = nc.dram_tensor("msk", [128, 384], BF16, kind="ExternalInput").ap()
    out_d = nc.dram_tensor("out", [TOK, D], F32, kind="ExternalOutput").ap()
    yidx_in = nc.dram_tensor("yidx", [128, 64], mybir.dt.int32, kind="ExternalInput").ap()
    ydram_h = nc.dram_tensor("ydram", [S, 256], BF16)
    ygat_h = nc.dram_tensor("ygat", [8 * S, 256], BF16)
    ydram = ydram_h.ap()
    ygat = ygat_h.ap()

    es = contextlib.ExitStack()
    SB = lambda n, sh, dt: es.enter_context(nc.sbuf_tensor(n, sh, dt))
    b = Bld()

    with es:
        XN = SB("XN", [128, NT * 1024], F32)
        BIGA = SB("BIGA", [128, 16384], BF16)
        KD = SB("KD", [128, 8192], BF16)
        VT = SB("VT", [128, 65 * NCH], BF16)
        XCH = SB("XCH", [128, 2048], F32)
        SCR = SB("SCR", [128, 4096], BF16)
        PTA = SB("PTA", [128, 2048], BF16)
        CS = SB("CS", [128, 1024], BF16)
        RSB = SB("RSB", [128, 2048], BF16)
        SCB = SB("SCB", [128, 768], BF16)
        STT = SB("STT", [128, 32], F32)
        SUMXN = SB("SUMXN", [128, 8], F32)
        MAXA = SB("MAXA", [1, 4], F32)
        MScr = SB("MScr", [1, 8], F32)
        NEGB = SB("NEGB", [128, 2], F32)
        NEGBB = SB("NEGBB", [1, 2], BF16)
        GFT = SB("GFT", [1, 16], F32)
        XBARB = SB("XBARB", [128, 8], BF16)
        GFH1 = SB("GFH1", [128, 4], BF16)
        GFH2 = SB("GFH2", [128, 2], BF16)
        CTL = SB("CTL", [128, 4], F32)
        CTLB = SB("CTLB", [1, 1024], F32)
        TMP1 = SB("TMP1", [128, 128], F32)
        TMP2 = SB("TMP2", [128, 128], F32)
        TMPQ = SB("TMPQ", [128, 512], F32)
        YST = SB("YST", [128, 256], BF16)
        RD = SB("RD", [128, 8], F32)
        IDB = SB("IDB", [128, 128], BF16)
        IDF = SB("IDF", [128, 128], F32)
        ONEB = SB("ONEB", [128, 128], BF16)
        MSK = SB("MSK", [128, 384], BF16)
        RP = SB("RPs", [128, 128], BF16)
        WQB = SB("WQB", [128, 2048], BF16)
        WKB = SB("WKB", [128, 512], BF16)
        WVB = SB("WVB", [128, 512], BF16)
        WBIG = SB("WBIG", [128, 8192], BF16)
        CN2 = SB("CN2", [128, 4096], BF16)
        CN3 = SB("CN3", [128, 1024], BF16)
        CN4 = SB("CN4", [128, 2], BF16)
        GC1 = SB("GC1", [128, 4096], BF16)
        GC2 = SB("GC2", [128, 1024], BF16)
        GC3 = SB("GC3", [128, 2], BF16)
        CB1 = SB("CB1", [128, 8], F32)
        CB2 = SB("CB2", [128, 4], F32)
        CB3 = SB("CB3", [128, 2], F32)
        CB4 = SB("CB4", [1, 2], F32)
        GB1 = SB("GB1", [128, 4], F32)
        GB2 = SB("GB2", [128, 2], F32)
        GB3 = SB("GB3", [1, 2], F32)
        ART = SB("ART", [1, 1], F32)
        CONS = SB("CONS", [1, 4], F32)
        YIX = SB("YIX", [128, 64], mybir.dt.int32)
        PS = es.enter_context(nc.psum_tensor("PS", [128, 4096], F32))
        biga_addr = nc.lookup_mloc(BIGA).addr
        kd_addr = nc.lookup_mloc(KD).addr
        QCH = [[nc.alloc_sbuf_tensor_at(f"QCH{p_}_{c_}", (128, 256), BF16,
                                        offset=biga_addr + 16384 * p_ + 512 * c_)
                for c_ in range(NCH // 2)] for p_ in range(2)]
        KCH = [nc.alloc_sbuf_tensor_at(f"KCH_{c_}", (128, 128), BF16,
                                       offset=kd_addr + 256 * c_)
               for c_ in range(NCH)]
        xn_addr = nc.lookup_mloc(XN).addr
        KBH = [nc.alloc_sbuf_tensor_at(f"KBH_{c_}", (128, 128), BF16,
                                       offset=xn_addr + 256 * c_)
               for c_ in range(NCH)]
        KBALL = nc.alloc_sbuf_tensor_at("KBALL", (128, 8192), BF16, offset=xn_addr)

        def psb(bank, off, w, p0=0, p1=128):
            return PS[p0:p1, 512 * bank + off:512 * bank + off + w]

        stg_par = [0]

        def wload(dst, srcap, kt, ncols, name):
            for k in range(kt):
                bu = stg_par[0] % 2
                stg_par[0] += 1
                stg = XCH[:, 1024 * bu:1024 * bu + ncols]
                b.emit("Y", lambda y, k=k, stg=stg, srcap=srcap: y.dma_start(
                    out=stg, in_=srcap[128 * k:128 * (k + 1), :]),
                    writes=[f"XCH:{bu}"], dma="q")
                b.emit("P", lambda p, k=k, stg=stg, dst=dst, ncols=ncols: p.tensor_copy(
                    out=dst[:, ncols * k:ncols * (k + 1)], in_=stg),
                    reads=[f"XCH:{bu}"], writes=[name])

        if not int(os.environ.get("K_NOQKV", "0")):
            b.emit("Y", lambda y: y.dma_start(out=YIX[:, :], in_=yidx_in), writes=["YIX"], dma="q")
            wload(WQB, wq_in, 8, 256, "WQB")
            wload(WKB, wk_in, 8, 64, "WKB")
            wload(WVB, wv_in, 8, 64, "WVB")

        def bload(dst, src, m, name):
            for i in range(m):
                b.emit("Y", lambda y, d=dst, s=src, i=i:
                       y.dma_start(out=d[:, i:i + 1],
                                   in_=s[128 * i:128 * (i + 1)].rearrange("(p o) -> p o", o=1)),
                       writes=[name], dma="q")

        if not int(os.environ.get("K_NOBL", "0")):
            bload(CB1, cb1_in, 8, "CB1")
            bload(CB2, cb2_in, 4, "CB2")
            bload(CB3, cb3_in, 2, "CB3")
            bload(GB1, gb1_in, 4, "GB1")
            bload(GB2, gb2_in, 2, "GB2")
            for dst_, src_, nm_ in ((CB4, cb4_in, "CB4"), (GB3, gb3_in, "GB3"), (ART, ar_in, "ART")):
                b.emit("Y", lambda y, d=dst_, s=src_: y.dma_start(out=d[0:1, 0:1], in_=s.rearrange("(x o) -> x o", x=1)),
                       writes=[nm_], dma="q")
        if not int(os.environ.get("K_NOTB", "0")):
            for dst_, src_, nm_ in ((IDB, idb_in, "IDB"), (IDF, idf_in, "IDF"), (ONEB, oneb_in, "ONEB"),
                                    (MSK, msk_in, "MSK"), (RP, rp_in, "RPc")):
                b.emit("Y", lambda y, d=dst_, s=src_: y.dma_start(out=d[:, :], in_=s), writes=[nm_], dma="q")
            b.emit("V", lambda v: v.memset(MAXA[:, :], -1e30), writes=["MAXA"])
            b.emit("V", lambda v: v.memset(SUMXN[:, :], 0.0), writes=["SUMXN"])
            b.emit("V", lambda v: v.memset(VT[:, :], 1.0), writes=["VTones"])
        b.emit("V", lambda v: v.memset(KD[64:128, :], 0.0), writes=["KDzero"])
        b.emit("V", lambda v: v.memset(KBALL[0:64, :], 0.0), writes=["KBzero", "ovXN"])
        b.emit("V", lambda v: v.memset(CONS[:, :], 0.0), writes=["CONS"])

        def p1_chunk(c):
            bu = c % 2
            r0 = 128 * c
            xch = XCH[:, 1024 * bu:1024 * (bu + 1)]
            xnb = SCR[:, 1024 * bu:1024 * (bu + 1)]
            xnt0 = 2048 + 1024 * bu
            st = 16 * bu
            cso = 512 * bu
            ro = 1024 * bu
            rXCH, rXNB, rXNT = f"XCH:{bu}", f"XNB:{bu}", f"XNT:{bu}"
            rST, rCS, rRSB = f"STT:{bu}", f"CS:{bu}", f"RSB:{bu}"

            b.emit("Y", lambda y: y.dma_start(out=xch, in_=x_in[r0:r0 + 128, :]),
                   writes=[rXCH], dma="q")
            for j, tab in enumerate((cosq_in, sinq_in, cosk_in, sink_in)):
                b.emit("Y", lambda y, j=j, tab=tab: y.dma_start(
                    out=CS[:, cso + 128 * j:cso + 128 * (j + 1)], in_=tab[:, r0:r0 + 128]),
                    writes=[rCS], dma="q")

            b.emit("V", lambda v: v.bn_stats(out=STT[:, st:st + 6], in_=xch[:, 0:512]),
                   reads=[rXCH], writes=[rST])
            b.emit("V", lambda v: v.bn_stats(out=STT[:, st + 6:st + 12], in_=xch[:, 512:1024]),
                   reads=[rXCH], writes=[rST])
            b.emit("V", lambda v: v.bn_aggr(out=STT[:, st + 12:st + 14], in_=STT[:, st:st + 12]),
                   reads=[rST], writes=[rST])
            b.emit("V", lambda v: v.tensor_scalar_add(out=STT[:, st + 14:st + 15],
                                                      in0=STT[:, st + 13:st + 14], scalar1=EPS),
                   reads=[rST], writes=[rST])
            b.emit("S", lambda s: s.activation(out=STT[:, st + 15:st + 16],
                                               in_=STT[:, st + 14:st + 15], func=AF.Ln),
                   reads=[rST], writes=[rST])
            b.emit("S", lambda s: s.activation(out=STT[:, st + 14:st + 15],
                                               in_=STT[:, st + 15:st + 16], func=AF.Exp, scale=-0.5),
                   reads=[rST], writes=[rST])
            b.emit("V", lambda v: v.tensor_scalar(out=xnb, in0=xch,
                                                  scalar1=STT[:, st + 12:st + 13],
                                                  scalar2=STT[:, st + 14:st + 15],
                                                  op0=ALU.subtract, op1=ALU.mult),
                   reads=[rXCH, rST, "ovS"], writes=[rXNB])

            for f in range(8):
                bank, off = 4 + (f // 4), 128 * (f % 4)
                b.emit("T", lambda t, f=f, bank=bank, off=off: t.matmul(
                    psb(bank, off, 128), xnb[:, 128 * f:128 * (f + 1)], IDB[:, :],
                    start=True, stop=True),
                    reads=[rXNB, "IDB", "ovS"], writes=[f"pb{bank}"])
            for f in range(8):
                bank, off = 4 + (f // 4), 128 * (f % 4)
                b.emit("V", lambda v, f=f, bank=bank, off=off: v.tensor_copy(
                    out=SCR[:, xnt0 + 128 * f:xnt0 + 128 * (f + 1)], in_=psb(bank, off, 128)),
                    reads=[f"pb{bank}", "ovS"], writes=[rXNT])

            for m in range(4):
                bank = m
                for k in range(8):
                    if m < 2:
                        st_ap = WQB.ap()[:, 256 * k + 128 * m:256 * k + 128 * m + 128]
                        wnm = "WQB"
                    elif m == 2:
                        st_ap = WKB.ap()[:, 64 * k:64 * (k + 1)]
                        wnm = "WKB"
                    else:
                        st_ap = WVB.ap()[:, 64 * k:64 * (k + 1)]
                        wnm = "WVB"
                    p1_ = 128 if m < 2 else 64
                    b.emit("T", lambda t, k=k, bank=bank, st_ap=st_ap, p1_=p1_: t.matmul(
                        psb(bank, 0, 128, 0, p1_), st_ap,
                        SCR[:, xnt0 + 128 * k:xnt0 + 128 * (k + 1)],
                        start=(k == 0), stop=(k == 7)),
                        reads=[rXNT, wnm], writes=[f"pb{bank}"], accum=("p1", c, m))
            for m in range(2):
                b.emit("S", lambda s, m=m: s.activation(
                    out=RSB[:, ro + 128 * m:ro + 128 * (m + 1)], in_=psb(m, 0, 128), func=AF.Copy),
                    reads=[f"pb{m}"], writes=[rRSB])
            b.emit("S", lambda s: s.activation(
                out=RSB[0:64, ro + 256:ro + 384], in_=psb(2, 0, 128, 0, 64), func=AF.Copy),
                reads=["pb2"], writes=[rRSB])
            b.emit("S", lambda s: s.activation(out=RSB[0:64, ro + 768:ro + 896],
                                               in_=psb(3, 0, 128, 0, 64), func=AF.Copy),
                   reads=["pb3"], writes=[rRSB])

            for m in range(2):
                b.emit("T", lambda t, m=m: t.matmul(
                    psb(m, 128, 128), RP[:, :], RSB[:, ro + 128 * m:ro + 128 * (m + 1)],
                    start=True, stop=True),
                    reads=[rRSB, "RPc"], writes=[f"pb{m}"])
            b.emit("T", lambda t: t.matmul(
                psb(2, 128, 128, 0, 64), RP[0:64, 0:64], RSB[0:64, ro + 256:ro + 384],
                start=True, stop=True),
                reads=[rRSB, "RPc"], writes=["pb2"])
            for m in range(2):
                b.emit("S", lambda s, m=m: s.activation(
                    out=RSB[:, ro + 384 + 128 * m:ro + 384 + 128 * (m + 1)],
                    in_=psb(m, 128, 128), func=AF.Copy),
                    reads=[f"pb{m}"], writes=[rRSB])
            b.emit("S", lambda s: s.activation(
                out=RSB[0:64, ro + 640:ro + 768],
                in_=psb(2, 128, 128, 0, 64), func=AF.Copy),
                reads=["pb2"], writes=[rRSB])

            qho = 128 * (c % 2)
            for m, (dst, dnm, tco, tso, pl) in enumerate((
                    (QCH[0][c // 2][:, qho:qho + 128], f"QP0:{c}", 0, 128, 128),
                    (QCH[1][c // 2][:, qho:qho + 128], f"QP1:{c}", 0, 128, 128),
                    (KCH[c][0:64, :], f"KD:{c}", 256, 384, 64))):
                b.emit("V", lambda v, m=m, tco=tco, pl=pl: v.tensor_tensor(
                    out=TMP1[0:pl, :], in0=RSB[0:pl, ro + 128 * m:ro + 128 * (m + 1)],
                    in1=CS[0:pl, cso + tco:cso + tco + 128], op=ALU.mult),
                    reads=[rRSB, rCS], writes=["TMP1"])
                b.emit("V", lambda v, m=m, tso=tso, pl=pl: v.tensor_tensor(
                    out=TMP2[0:pl, :], in0=RSB[0:pl, ro + 384 + 128 * m:ro + 384 + 128 * (m + 1)],
                    in1=CS[0:pl, cso + tso:cso + tso + 128], op=ALU.mult),
                    reads=[rRSB, rCS], writes=["TMP2"])
                b.emit("V", lambda v, dst=dst, pl=pl: v.tensor_tensor(
                    out=dst, in0=TMP1[0:pl, :], in1=TMP2[0:pl, :], op=ALU.add),
                    reads=["TMP1", "TMP2", "ovA" if m < 2 else "ovK"], writes=[dnm])
            b.emit("Y", lambda y, c=c: y.dma_start(out=KBH[c][64:128, :], in_=KCH[c][0:64, :]),
                   reads=[f"KD:{c}", "KBzero", "ovXN"], writes=[f"KB:{c}"], dma="q")

            b.emit("T", lambda t: t.matmul(psb(3, 128, 64, 0, 128), RSB[0:64, ro + 768:ro + 896],
                                           IDB[0:64, 0:64], start=True, stop=True),
                   reads=[rRSB, "IDB"], writes=["pb3"])
            b.emit("S", lambda s, c=c: s.activation(out=VT[:, 65 * c:65 * c + 64],
                                                    in_=psb(3, 128, 64), func=AF.Copy),
                   reads=["pb3"], writes=[f"VT:{c}"])

            for m, snm in enumerate((f"QP0:{c}", f"QP1:{c}", f"KD:{c}")):
                src = (QCH[0][c // 2][:, qho:qho + 128] if m == 0 else
                       QCH[1][c // 2][:, qho:qho + 128] if m == 1 else
                       KCH[c][0:64, :])
                pl = 128 if m < 2 else 64
                b.emit("S", lambda s, m=m, src=src, pl=pl: s.activation(
                    out=SCB[0:pl, 256 * m:256 * m + 128], in_=src, func=AF.Square),
                    reads=[snm], writes=["SCB"])
            for m in range(3):
                pl = 128 if m < 2 else 64
                b.emit("T", lambda t, m=m, pl=pl: t.matmul(
                    psb(m, 256, 128, 0, 1), ONEB[0:pl, 0:1], SCB[0:pl, 256 * m:256 * m + 128],
                    start=True, stop=True),
                    reads=["SCB", "ONEB"], writes=[f"pb{m}"])
            for m in range(3):
                b.emit("V", lambda v, m=m: v.reduce_max(
                    out=MScr[0:1, m:m + 1], in_=psb(m, 256, 128, 0, 1), axis=AX.X),
                    reads=[f"pb{m}"], writes=["MScr"])
            b.emit("V", lambda v: v.tensor_max(out=MAXA[0:1, 0:3], in0=MAXA[0:1, 0:3],
                                               in1=MScr[0:1, 0:3]),
                   reads=["MScr", "MAXA"], writes=["MAXA"])

            for f in range(8):
                b.emit("T", lambda t, f=f: t.matmul(
                    psb(3, 256 + f, 1), xnb[:, 128 * f:128 * (f + 1)], ONEB[:, 0:1],
                    start=True, stop=True),
                    reads=[rXNB, "ONEB"], writes=["pb3"])
            b.emit("V", lambda v: v.tensor_add(out=SUMXN[:, :], in0=SUMXN[:, :], in1=psb(3, 256, 8)),
                   reads=["pb3", "SUMXN"], writes=["SUMXN"])

        gctr = [0]

        def neg_bias(p):
            g0 = 8 + 4 * p
            b.emit("S", lambda s, p=p, g0=g0: s.activation(out=GFT[0:1, g0:g0 + 1],
                                                           in_=MAXA[0:1, p:p + 1], func=AF.Ln),
                   reads=["MAXA"], writes=["GFTb"])
            b.emit("S", lambda s, g0=g0: s.activation(out=GFT[0:1, g0 + 1:g0 + 2],
                                                      in_=MAXA[0:1, 2:3], func=AF.Ln),
                   reads=["MAXA", "GFTb"], writes=["GFTb"])
            b.emit("V", lambda v, g0=g0: v.tensor_add(out=GFT[0:1, g0 + 2:g0 + 3],
                                                      in0=GFT[0:1, g0:g0 + 1], in1=GFT[0:1, g0 + 1:g0 + 2]),
                   reads=["GFTb"], writes=["GFTb"])
            b.emit("S", lambda s, g0=g0: s.activation(out=GFT[0:1, g0 + 3:g0 + 4], in_=GFT[0:1, g0 + 2:g0 + 3],
                                                      func=AF.Exp, scale=0.5, bias=CONS[0:1, 0:1]),
                   reads=["GFTb", "CONS"], writes=["GFTb"])
            b.emit("V", lambda v, p=p, g0=g0: v.tensor_scalar_mul(out=NEGBB[0:1, p:p + 1],
                                                                  in0=GFT[0:1, g0 + 3:g0 + 4], scalar1=-1.0),
                   reads=["GFTb"], writes=["NEGBB"])
            b.emit("T", lambda t, p=p: t.matmul(psb(4, 400 + p, 1), ONEB[0:1, :],
                                                NEGBB[0:1, p:p + 1], start=True, stop=True),
                   reads=["NEGBB", "ONEB"], writes=["pb4"])
            b.emit("V", lambda v, p=p: v.tensor_copy(out=NEGB[:, p:p + 1], in_=psb(4, 400 + p, 1)),
                   reads=["pb4"], writes=["NEGB"])

        def qk_qb(p, qb):
            inst = 2 * qb + p
            QPn = "QP0" if p == 0 else "QP1"
            qch = QCH[p][qb]
            nkt = 2 * (qb + 1)
            if P2M < 2:
                return
            for g in range(nkt // 2):
                gpar = gctr[0] % 2
                gctr[0] += 1
                base = 0 if gpar == 0 else 2
                pta0 = 0 if gpar == 0 else 1024
                rPT = f"PTA:{gpar}"
                slots = []
                for gi in range(2):
                    kt = 2 * g + gi
                    for h in (0, 1):
                        si = 2 * gi + h
                        bank = base + si // 2
                        off = 256 * (si % 2)
                        pos = pta0 + 256 * si
                        slots.append((kt, h, bank, off, pos))
                nqk = 0
                for (kt, h, bank, off, pos) in slots:
                    if nqk >= P2QK:
                        continue
                    nqk += 1
                    kst = KCH[kt] if h == 0 else KBH[kt]
                    knm = f"KD:{kt}" if h == 0 else f"KB:{kt}"
                    b.emit("T", lambda t, kst=kst, bank=bank, off=off, qch=qch: t.matmul(
                        psb(bank, off, QB, 0, 128),
                        kst[:, :], qch[:, :],
                        start=True, stop=True),
                        reads=[knm, "KDzero" if h == 0 else "KBzero",
                               f"{QPn}:{2 * qb}", f"{QPn}:{2 * qb + 1}", "ovA", "ovK", "ovXN"],
                        writes=[f"pb{bank}"])
                if P2E < 1 or nqk < len(slots):
                    continue
                b.emit("S", lambda s, base=base, pta0=pta0, p=p: s.activation(
                    out=PTA[:, pta0:pta0 + 1024], in_=PS[:, 512 * base:512 * base + 1024],
                    func=AF.Exp, bias=NEGB[:, p:p + 1]),
                    reads=["NEGB", f"pb{base}", f"pb{base + 1}"], writes=[rPT])
                if P2M >= 3:
                    for (kt, h, bank, off, pos) in slots:
                        if kt >= 2 * qb:
                            moff = 128 if kt == 2 * qb else 0
                            b.emit("V", lambda v, pos=pos, moff=moff: v.tensor_tensor(
                                out=PTA[:, pos:pos + 256], in0=PTA[:, pos:pos + 256],
                                in1=MSK[:, moff:moff + 256], op=ALU.mult),
                                reads=[rPT, "MSK"], writes=[rPT])
                if P2M >= 4:
                    for (kt, h, bank, off, pos) in slots:
                        for j in range(2):
                            yb = 4 + 2 * h + j
                            b.emit("T", lambda t, pos=pos, kt=kt, j=j, yb=yb,
                                   st=(kt == 0), lst=(kt == nkt - 1): t.matmul(
                                psb(yb, 0, 65), PTA[:, pos + 128 * j:pos + 128 * j + 128],
                                VT[:, 65 * kt:65 * (kt + 1)], start=st, stop=lst),
                                reads=[rPT, f"VT:{kt}", "VTones"], writes=[f"pb{yb}"],
                                accum=("av", inst, h, j))
            if P2M < 5:
                return
            for h in range(2):
                for j in range(2):
                    yb = 4 + 2 * h + j
                    sl = 2 * h + j
                    b.emit("V", lambda v, yb=yb, sl=sl: v.reciprocal(
                        out=RD[:, sl:sl + 1], in_=psb(yb, 64, 1)),
                        reads=[f"pb{yb}"], writes=[f"RD:{sl}"])
                    b.emit("V", lambda v, yb=yb, sl=sl: v.tensor_scalar_mul(
                        out=YST[:, 64 * sl:64 * (sl + 1)], in0=psb(yb, 0, 64),
                        scalar1=RD[:, sl:sl + 1]),
                        reads=[f"pb{yb}", f"RD:{sl}"], writes=[f"YST:{sl}"])
                    lh = 2 * p + h
                    r0y = QB * qb + 128 * j
                    b.emit("Y", lambda y, r0y=r0y, lh=lh, sl=sl: y.dma_start(
                        out=ydram[r0y:r0y + 128, 64 * lh:64 * (lh + 1)],
                        in_=YST[:, 64 * sl:64 * (sl + 1)]),
                        reads=[f"YST:{sl}"], writes=["ydram"], dma="q")

        if UPTO >= 1:
            for c in range(min(NCH, int(os.environ.get("K_P1N", "999")))):
                p1_chunk(c)
        if UPTO >= 2:
            neg_bias(0)
            neg_bias(1)
            if UPTO >= 3 and not int(os.environ.get("K_NOWL", "0")):
                wload(WBIG, cw1_in, 8, 1024, "WBIG")
                wload(CN2, cw2_in, 8, 512, "CN2")
                wload(CN3, cw3_in, 4, 256, "CN3")
                wload(CN4, cw4_in, 2, 1, "CN4")
                wload(GC1, gw1_in, 8, 512, "GC1")
                wload(GC2, gw2_in, 4, 256, "GC2")
                wload(GC3, gw3_in, 2, 1, "GC3")
            for qb in range(min(NQB, P2N)):
                qk_qb(0, qb)
                qk_qb(1, qb)

        def early_out():
            b.emit("V", lambda v: v.memset(XN[:, :], 0.25), writes=[f"XN:{t}" for t in range(NT)])
            for tt in range(NT):
                b.emit("Y", lambda y, tt=tt: y.dma_start(
                    out=out_d[128 * tt:128 * (tt + 1), :], in_=XN[:, 1024 * tt:1024 * (tt + 1)]),
                    reads=[f"XN:{tt}"], writes=["outw"], dma="q")
            b.emit("P", lambda p: p.engine_nop(), reads=["outw"])
            block = es.enter_context(nc.Block())
            b.finalize(nc, es, block)

        if UPTO < 3:
            early_out()
            return nc

        if not NOCC:
            op_cc = b.emit("P", lambda p: p.collective_compute(
                "AllGather", ALU.bypass,
                replica_groups=[list(range(8))],
                ins=[ydram_h.ap().opt()], outs=[ygat_h.ap().opt()]),
                reads=["ydram"], writes=["ygat"])
            op_cc.need_inc = True

        for tt in (range(0) if NOPRE else range(NT)):
            bu = tt % 2
            xch = XCH[:, 1024 * bu:1024 * (bu + 1)]
            st = 16 * bu
            rXCH, rST = f"XCH:{bu}", f"STT:{bu}"
            b.emit("Y", lambda y, tt=tt, xch=xch: y.dma_start(
                out=xch, in_=xo_in[128 * tt:128 * (tt + 1), :]),
                writes=[rXCH], dma="q")
            b.emit("V", lambda v, xch=xch, st=st: v.bn_stats(out=STT[:, st:st + 6], in_=xch[:, 0:512]),
                   reads=[rXCH], writes=[rST])
            b.emit("V", lambda v, xch=xch, st=st: v.bn_stats(out=STT[:, st + 6:st + 12], in_=xch[:, 512:1024]),
                   reads=[rXCH], writes=[rST])
            b.emit("V", lambda v, st=st: v.bn_aggr(out=STT[:, st + 12:st + 14], in_=STT[:, st:st + 12]),
                   reads=[rST], writes=[rST])
            b.emit("V", lambda v, st=st: v.tensor_scalar_add(out=STT[:, st + 14:st + 15],
                                                             in0=STT[:, st + 13:st + 14], scalar1=EPS),
                   reads=[rST], writes=[rST])
            b.emit("S", lambda s, st=st: s.activation(out=STT[:, st + 15:st + 16],
                                                      in_=STT[:, st + 14:st + 15], func=AF.Ln),
                   reads=[rST], writes=[rST])
            b.emit("S", lambda s, st=st: s.activation(out=STT[:, st + 14:st + 15],
                                                      in_=STT[:, st + 15:st + 16], func=AF.Exp, scale=-0.5),
                   reads=[rST], writes=[rST])
            b.emit("V", lambda v, tt=tt, xch=xch, st=st: v.tensor_scalar(
                out=XN[:, 1024 * tt:1024 * (tt + 1)], in0=xch,
                scalar1=STT[:, st + 12:st + 13], scalar2=STT[:, st + 14:st + 15],
                op0=ALU.subtract, op1=ALU.mult),
                reads=[rXCH, rST], writes=[f"XN:{tt}"] + (["ovXN"] if tt == 0 else []))

        if NOFFN:
            early_out()
            return nc
        b.emit("V", lambda v: v.tensor_scalar_mul(out=XBARB[:, 0:8], in0=SUMXN[:, :], scalar1=1.0 / S),
               reads=["SUMXN"], writes=["XBARB"])
        for m in range(4):
            for k in range(8):
                b.emit("T", lambda t, m=m, k=k: t.matmul(
                    psb(0, m, 1), GC1.ap()[:, 512 * k + 128 * m:512 * k + 128 * m + 128],
                    XBARB[:, k:k + 1], start=(k == 0), stop=(k == 7)),
                    reads=["GC1", "XBARB"], writes=["pb0"], accum=("gc1", m))
        for m in range(4):
            b.emit("S", lambda s, m=m: s.activation(out=GFH1[:, m:m + 1], in_=psb(0, m, 1),
                                                    func=AF.Gelu, bias=GB1[:, m:m + 1]),
                   reads=["pb0", "GB1"], writes=["GFH1"])
        for m in range(2):
            for k in range(4):
                b.emit("T", lambda t, m=m, k=k: t.matmul(
                    psb(1, m, 1), GC2.ap()[:, 256 * k + 128 * m:256 * k + 128 * m + 128],
                    GFH1[:, k:k + 1], start=(k == 0), stop=(k == 3)),
                    reads=["GC2", "GFH1"], writes=["pb1"], accum=("gc2", m))
        for m in range(2):
            b.emit("S", lambda s, m=m: s.activation(out=GFH2[:, m:m + 1], in_=psb(1, m, 1),
                                                    func=AF.Gelu, bias=GB2[:, m:m + 1]),
                   reads=["pb1", "GB2"], writes=["GFH2"])
        for k in range(2):
            b.emit("T", lambda t, k=k: t.matmul(
                psb(2, 0, 1, 0, 1), GC3.ap()[:, k:k + 1], GFH2[:, k:k + 1],
                start=(k == 0), stop=(k == 1)),
                reads=["GC3", "GFH2"], writes=["pb2"], accum=("gc3", 0))
        b.emit("V", lambda v: v.tensor_scalar_mul(out=GB3[0:1, 1:2], in0=GB3[0:1, 0:1], scalar1=0.5),
               reads=["GB3"], writes=["GB3"])
        b.emit("V", lambda v: v.tensor_scalar_mul(out=CB4[0:1, 1:2], in0=CB4[0:1, 0:1], scalar1=0.5),
               reads=["CB4"], writes=["CB4"])
        b.emit("S", lambda s: s.activation(out=GFT[0:1, 0:1], in_=psb(2, 0, 1, 0, 1),
                                           func=AF.Tanh, scale=0.5, bias=GB3[0:1, 1:2]),
               reads=["pb2", "GB3"], writes=["GFTg"])
        b.emit("V", lambda v: v.tensor_scalar(out=GFT[0:1, 1:2], in0=GFT[0:1, 0:1],
                                              scalar1=0.5, scalar2=0.5, op0=ALU.mult, op1=ALU.add),
               reads=["GFTg"], writes=["GFTg"])
        b.emit("V", lambda v: v.tensor_mul(out=GFT[0:1, 2:3], in0=GFT[0:1, 1:2], in1=ART[0:1, 0:1]),
               reads=["GFTg", "ART"], writes=["GFTg"])
        b.emit("V", lambda v: v.tensor_scalar_mul(out=GFT[0:1, 3:4], in0=GFT[0:1, 2:3], scalar1=1.4),
               reads=["GFTg"], writes=["GFTg"])
        b.emit("V", lambda v: v.tensor_sub(out=GFT[0:1, 4:5], in0=GFT[0:1, 3:4], in1=ART[0:1, 0:1]),
               reads=["GFTg", "ART"], writes=["GFTg"])
        b.emit("V", lambda v: v.tensor_scalar_add(out=GFT[0:1, 5:6], in0=GFT[0:1, 4:5], scalar1=0.1),
               reads=["GFTg"], writes=["GFTg"])
        b.emit("V", lambda v: v.tensor_scalar_mul(out=GFT[0:1, 6:7], in0=ART[0:1, 0:1], scalar1=0.3),
               reads=["ART"], writes=["GFTg"])
        b.emit("V", lambda v: v.tensor_add(out=GFT[0:1, 7:8], in0=GFT[0:1, 5:6], in1=GFT[0:1, 6:7]),
               reads=["GFTg"], writes=["GFTg"])

        def ffn_cycle(cyc):
            for tc in range(NTCH):
                t0 = TCH * tc
                for ttl in range(8):
                    tt = 8 * tc + ttl
                    bu = ttl % 2
                    cdst = SCR[:, 1024 * bu:1024 * (bu + 1)]
                    first = (cyc == 0 and tc == 0 and ttl == 0)
                    b.emit("V", lambda v, tt=tt, cdst=cdst: v.tensor_copy(
                        out=cdst, in_=XN[:, 1024 * tt:1024 * (tt + 1)]),
                        reads=[f"XN:{tt}"] + ([] if first else ["ovS"]),
                        writes=[f"XNB:{bu}"] + (["ovS"] if first else []))
                    for f in range(8):
                        bank, off = 6 + (f // 4), 128 * (f % 4)
                        b.emit("T", lambda t, f=f, bank=bank, off=off, cdst=cdst: t.matmul(
                            psb(bank, off, 128), cdst[:, 128 * f:128 * (f + 1)], IDB[:, :],
                            start=True, stop=True),
                            reads=[f"XNB:{bu}", "IDB", "ovS"], writes=[f"pb{bank}"])
                    for f in range(8):
                        bank, off = 6 + (f // 4), 128 * (f % 4)
                        fe = (first and f == 0)
                        b.emit("V", lambda v, f=f, bank=bank, off=off, ttl=ttl: v.tensor_copy(
                            out=BIGA[:, 1024 * f + 128 * ttl:1024 * f + 128 * (ttl + 1)],
                            in_=psb(bank, off, 128)),
                            reads=[f"pb{bank}"] + ([] if fe else ["ovA"]),
                            writes=[f"OUTBT:{f}"] + (["ovA"] if fe else []))
                gi = 0
                for m in range(8):
                    for n in range(2):
                        bank = gi % 6
                        gi += 1
                        for k in range(8):
                            b.emit("T", lambda t, m=m, n=n, k=k, bank=bank: t.matmul(
                                psb(bank, 0, 512), WBIG.ap()[:, 1024 * k + 128 * m:1024 * k + 128 * m + 128],
                                BIGA[:, 1024 * k + 512 * n:1024 * k + 512 * n + 512],
                                start=(k == 0), stop=(k == 7)),
                                reads=[f"OUTBT:{k}", "WBIG", "ovA"], writes=[f"pb{bank}"],
                                accum=("h1", cyc, tc, m, n))
                        b.emit("S", lambda s, m=m, n=n, bank=bank: s.activation(
                            out=BIGA[:, 8192 + 1024 * m + 512 * n:8192 + 1024 * m + 512 * n + 512],
                            in_=psb(bank, 0, 512), func=AF.Gelu, bias=CB1[:, m:m + 1]),
                            reads=[f"pb{bank}", "CB1", "ovA"], writes=[f"H1B:{m}"])
                for m in range(4):
                    for n in range(2):
                        bank = gi % 6
                        gi += 1
                        for k in range(8):
                            b.emit("T", lambda t, m=m, n=n, k=k, bank=bank: t.matmul(
                                psb(bank, 0, 512), CN2.ap()[:, 512 * k + 128 * m:512 * k + 128 * m + 128],
                                BIGA[:, 8192 + 1024 * k + 512 * n:8192 + 1024 * k + 512 * n + 512],
                                start=(k == 0), stop=(k == 7)),
                                reads=[f"H1B:{k}", "CN2", "ovA"], writes=[f"pb{bank}"],
                                accum=("h2", cyc, tc, m, n))
                        wk = ["ovK"] if (cyc == 0 and tc == 0 and m == 0 and n == 0) else []
                        b.emit("S", lambda s, m=m, n=n, bank=bank: s.activation(
                            out=KD[:, 1024 * m + 512 * n:1024 * m + 512 * n + 512],
                            in_=psb(bank, 0, 512), func=AF.Gelu, bias=CB2[:, m:m + 1]),
                            reads=[f"pb{bank}", "CB2"] + ([] if wk else ["ovK"]),
                            writes=[f"H2B:{m}"] + wk)
                for m in range(2):
                    for n in range(2):
                        bank = gi % 6
                        gi += 1
                        for k in range(4):
                            b.emit("T", lambda t, m=m, n=n, k=k, bank=bank: t.matmul(
                                psb(bank, 0, 512), CN3.ap()[:, 256 * k + 128 * m:256 * k + 128 * m + 128],
                                KD[:, 1024 * k + 512 * n:1024 * k + 512 * n + 512],
                                start=(k == 0), stop=(k == 3)),
                                reads=[f"H2B:{k}", "CN3", "ovK"], writes=[f"pb{bank}"],
                                accum=("h3", cyc, tc, m, n))
                        b.emit("S", lambda s, m=m, n=n, bank=bank: s.activation(
                            out=KD[:, 4096 + 1024 * m + 512 * n:4096 + 1024 * m + 512 * n + 512],
                            in_=psb(bank, 0, 512), func=AF.Gelu, bias=CB3[:, m:m + 1]),
                            reads=[f"pb{bank}", "CB3", "ovK"], writes=[f"H3B:{m}"])
                for n in range(2):
                    for k in range(2):
                        b.emit("T", lambda t, n=n, k=k: t.matmul(
                            psb(6 + n, 0, 512, 0, 1),
                            CN4.ap()[:, k:k + 1],
                            KD[:, 4096 + 1024 * k + 512 * n:4096 + 1024 * k + 512 * n + 512],
                            start=(k == 0), stop=(k == 1)),
                            reads=[f"H3B:{k}", "CN4", "ovK"], writes=[f"pb{6 + n}"],
                            accum=("z", cyc, tc, n))
                for n in range(2):
                    b.emit("S", lambda s, n=n: s.activation(
                        out=CTLB[0:1, 512 * n:512 * (n + 1)], in_=psb(6 + n, 0, 512, 0, 1),
                        func=AF.Tanh, scale=0.5, bias=CB4[0:1, 1:2]),
                        reads=[f"pb{6 + n}", "CB4"], writes=["CTLB"])
                b.emit("V", lambda v: v.tensor_scalar(
                    out=CTLB[0:1, 0:1024], in0=CTLB[0:1, 0:1024],
                    scalar1=GFT[0:1, 6:7], scalar2=GFT[0:1, 7:8],
                    op0=ALU.mult, op1=ALU.add),
                    reads=["CTLB", "GFTg"], writes=["CTLB"])
                for ttl in range(8):
                    tt = 8 * tc + ttl
                    cb = ttl % 2
                    b.emit("T", lambda t, ttl=ttl: t.matmul(
                        psb(6, 480, 1), CTLB[0:1, 128 * ttl:128 * (ttl + 1)],
                        IDF[0:1, 0:1], start=True, stop=True),
                        reads=["CTLB", "IDF"], writes=["pb6"])
                    b.emit("V", lambda v, cb=cb: v.tensor_copy(out=CTL[:, 2 * cb:2 * cb + 1],
                                                               in_=psb(6, 480, 1)),
                           reads=["pb6"], writes=[f"CTL:{cb}"])
                    b.emit("V", lambda v, cb=cb: v.tensor_scalar_mul(out=CTL[:, 2 * cb + 1:2 * cb + 2],
                                                                     in0=CTL[:, 2 * cb:2 * cb + 1],
                                                                     scalar1=-1.0),
                           reads=[f"CTL:{cb}"], writes=[f"CTL:{cb}"])
                    x0 = 1024 * tt
                    q = lambda i: XN[:, x0 + 256 * i:x0 + 256 * (i + 1)]
                    rX = f"XN:{tt}"
                    b.emit("V", lambda v, tt=tt, cb=cb: v.scalar_tensor_tensor(
                        out=TMPQ[:, 0:256], in0=XN[:, 1024 * tt + 512:1024 * tt + 768],
                        scalar=CTL[:, 2 * cb:2 * cb + 1], in1=XN[:, 1024 * tt:1024 * tt + 256],
                        op0=ALU.mult, op1=ALU.add),
                        reads=[rX, f"CTL:{cb}"], writes=["TMPQ"])
                    b.emit("V", lambda v, tt=tt, cb=cb: v.scalar_tensor_tensor(
                        out=TMPQ[:, 256:512], in0=XN[:, 1024 * tt + 768:1024 * tt + 1024],
                        scalar=CTL[:, 2 * cb + 1:2 * cb + 2], in1=XN[:, 1024 * tt + 256:1024 * tt + 512],
                        op0=ALU.mult, op1=ALU.add),
                        reads=[rX, f"CTL:{cb}"], writes=["TMPQ"])
                    b.emit("V", lambda v, tt=tt, cb=cb: v.scalar_tensor_tensor(
                        out=XN[:, 1024 * tt + 512:1024 * tt + 768], in0=XN[:, 1024 * tt:1024 * tt + 256],
                        scalar=CTL[:, 2 * cb + 1:2 * cb + 2], in1=XN[:, 1024 * tt + 512:1024 * tt + 768],
                        op0=ALU.mult, op1=ALU.add),
                        reads=[rX, f"CTL:{cb}"], writes=[rX])
                    b.emit("V", lambda v, tt=tt, cb=cb: v.scalar_tensor_tensor(
                        out=XN[:, 1024 * tt + 768:1024 * tt + 1024], in0=XN[:, 1024 * tt + 256:1024 * tt + 512],
                        scalar=CTL[:, 2 * cb:2 * cb + 1], in1=XN[:, 1024 * tt + 768:1024 * tt + 1024],
                        op0=ALU.mult, op1=ALU.add),
                        reads=[rX, f"CTL:{cb}"], writes=[rX])
                    b.emit("V", lambda v, tt=tt: v.tensor_copy(
                        out=XN[:, 1024 * tt:1024 * tt + 256], in_=TMPQ[:, 0:256]),
                        reads=["TMPQ"], writes=[rX])
                    b.emit("V", lambda v, tt=tt: v.tensor_copy(
                        out=XN[:, 1024 * tt + 256:1024 * tt + 512], in_=TMPQ[:, 256:512]),
                        reads=["TMPQ"], writes=[rX])

        for cyc in range(3):
            ffn_cycle(cyc)

        wload(WBIG, wo_in, 8, 1024, "WBIG")

        if UPTO < 4:
            early_out()
            return nc
        for tt in range(NT):
            for r in range(4):
                wA = ["ovA"] if (tt == 0 and r == 0) else []
                b.emit("P", lambda p, tt=tt, r=r: p.indirect_dma_start(
                    out=BIGA[:, 1024 * tt + 256 * r:1024 * tt + 256 * (r + 1)],
                    out_offset=None, in_=ygat,
                    in_offset=IndirectOffsetOnAxis(ap=YIX[:, 4 * tt + r:4 * tt + r + 1], axis=0)),
                    reads=["ygat", "YIX"] + ([] if wA else ["ovA"]),
                    writes=[f"YALL:{tt}"] + wA, dma="yin")
        for tt in range(NT):
            for f in range(8):
                bank, off = 6 + (f // 4), 128 * (f % 4)
                wS = ["ovS"] if (tt == 0 and f == 0) else []
                b.emit("T", lambda t, tt=tt, f=f, bank=bank, off=off: t.matmul(
                    psb(bank, off, 128), BIGA[:, 1024 * tt + 128 * f:1024 * tt + 128 * (f + 1)],
                    IDB[:, :], start=True, stop=True),
                    reads=[f"YALL:{tt}", "IDB"], writes=[f"pb{bank}"])
                b.emit("V", lambda v, f=f, bank=bank, off=off: v.tensor_copy(
                    out=SCR[:, 128 * f:128 * (f + 1)], in_=psb(bank, off, 128)),
                    reads=[f"pb{bank}"] + ([] if wS else ["ovS"]), writes=["YTT"] + wS)
            for n in range(2):
                bank = (2 * tt + n) % 4
                for k in range(8):
                    b.emit("T", lambda t, n=n, k=k, bank=bank: t.matmul(
                        psb(bank, 0, 512), SCR[:, 128 * k:128 * (k + 1)],
                        WBIG.ap()[:, 1024 * k + 512 * n:1024 * k + 512 * n + 512],
                        start=(k == 0), stop=(k == 7)),
                        reads=["YTT", "WBIG"], writes=[f"pb{bank}"], accum=("wo", tt, n))
                b.emit("V", lambda v, tt=tt, n=n, bank=bank: v.tensor_add(
                    out=XN[:, 1024 * tt + 512 * n:1024 * tt + 512 * n + 512],
                    in0=XN[:, 1024 * tt + 512 * n:1024 * tt + 512 * n + 512],
                    in1=psb(bank, 0, 512)),
                    reads=[f"pb{bank}", f"XN:{tt}"], writes=[f"XN:{tt}"])
            bu = tt % 2
            xch = XCH[:, 1024 * bu:1024 * (bu + 1)]
            b.emit("Y", lambda y, tt=tt, xch=xch: y.dma_start(
                out=xch, in_=xo_in[128 * tt:128 * (tt + 1), :]),
                writes=[f"XCH:{bu}"], dma="q")
            b.emit("V", lambda v, tt=tt, xch=xch: v.tensor_add(
                out=XN[:, 1024 * tt:1024 * (tt + 1)],
                in0=XN[:, 1024 * tt:1024 * (tt + 1)], in1=xch),
                reads=[f"XCH:{bu}", f"XN:{tt}"], writes=[f"XN:{tt}"])
            b.emit("Y", lambda y, tt=tt: y.dma_start(
                out=out_d[128 * tt:128 * (tt + 1), :], in_=XN[:, 1024 * tt:1024 * (tt + 1)]),
                reads=[f"XN:{tt}"], writes=["outw"], dma="q")

        b.emit("P", lambda p: p.engine_nop(), reads=["outw"])

        block = es.enter_context(nc.Block())
        b.finalize(nc, es, block)
    return nc


_CACHE = {}


def _tables():
    if "t" in _CACHE:
        return _CACHE["t"]
    half = 32
    inv = 1.0 / (10000.0 ** (np.arange(half, dtype=np.float64) / half))
    t = np.arange(S, dtype=np.float64)
    fr = np.outer(inv, t)
    emb = np.concatenate([fr, fr], axis=0)
    cos = np.cos(emb)
    sin = np.sin(emb)
    cos2 = np.concatenate([cos, cos], axis=0)
    sin2 = np.concatenate([sin, sin], axis=0)
    bf = lambda a: a.astype(ml_dtypes.bfloat16)
    cosq = bf(cos2 / 8.0)
    sinq = bf(sin2 / 8.0)
    cosk = bf(cos2)
    sink = bf(sin2)
    rp = np.zeros((128, 128), dtype=np.float32)
    for base in (0, 64):
        for m in range(32):
            rp[base + m + 32, base + m] = -1.0
            rp[base + m, base + m + 32] = 1.0
    idb = np.eye(128, dtype=ml_dtypes.bfloat16)
    idf = np.eye(128, dtype=np.float32)
    oneb = np.ones((128, 128), dtype=ml_dtypes.bfloat16)
    msk = np.zeros((128, 384), dtype=np.float32)
    for pp in range(128):
        for j in range(384):
            if pp <= j - 128:
                msk[pp, j] = 1.0
    _CACHE["t"] = (cosq, sinq, cosk, sink, bf(rp), idb, idf, oneb, bf(msk))
    return _CACHE["t"]


def kernel(**inputs):
    x = np.asarray(inputs["L0_attn_x"], dtype=np.float32)
    Wq = np.ascontiguousarray(np.asarray(inputs["L0_attn_Wq"], dtype=np.float32))
    Wk = np.ascontiguousarray(np.asarray(inputs["L0_attn_Wk"], dtype=np.float32))
    Wv = np.ascontiguousarray(np.asarray(inputs["L0_attn_Wv"], dtype=np.float32))
    Wo = np.ascontiguousarray(np.asarray(inputs["L0_attn_Wo"], dtype=np.float32))
    cosq, sinq, cosk, sink, rp, idb, idf, oneb, msk = _tables()

    if "nc" not in _CACHE:
        _CACHE["nc"] = build_program()
    nc = _CACHE["nc"]

    shared = {
        "wo": Wo,
        "cw1": np.ascontiguousarray(np.asarray(inputs["L0_ffn_cn_w1"], dtype=np.float32)),
        "cw2": np.ascontiguousarray(np.asarray(inputs["L0_ffn_cn_w2"], dtype=np.float32)),
        "cw3": np.ascontiguousarray(np.asarray(inputs["L0_ffn_cn_w3"], dtype=np.float32)),
        "cw4": np.ascontiguousarray(np.asarray(inputs["L0_ffn_cn_w4"], dtype=np.float32)),
        "gw1": np.ascontiguousarray(np.asarray(inputs["L0_ffn_gc_w1"], dtype=np.float32)),
        "gw2": np.ascontiguousarray(np.asarray(inputs["L0_ffn_gc_w2"], dtype=np.float32)),
        "gw3": np.ascontiguousarray(np.asarray(inputs["L0_ffn_gc_w3"], dtype=np.float32)),
        "cb1": np.ascontiguousarray(np.asarray(inputs["L0_ffn_cn_b1"], dtype=np.float32)),
        "cb2": np.ascontiguousarray(np.asarray(inputs["L0_ffn_cn_b2"], dtype=np.float32)),
        "cb3": np.ascontiguousarray(np.asarray(inputs["L0_ffn_cn_b3"], dtype=np.float32)),
        "cb4": np.ascontiguousarray(np.asarray(inputs["L0_ffn_cn_b4"], dtype=np.float32)),
        "gb1": np.ascontiguousarray(np.asarray(inputs["L0_ffn_gc_b1"], dtype=np.float32)),
        "gb2": np.ascontiguousarray(np.asarray(inputs["L0_ffn_gc_b2"], dtype=np.float32)),
        "gb3": np.ascontiguousarray(np.asarray(inputs["L0_ffn_gc_b3"], dtype=np.float32)),
        "ar": np.ascontiguousarray(np.asarray(inputs["L0_ffn_adaptive_range"], dtype=np.float32)),
        "cosq": cosq, "sinq": sinq, "cosk": cosk, "sink": sink,
        "rp": rp, "idb": idb, "idf": idf, "oneb": oneb, "msk": msk,
    }
    in_maps = []
    for c in range(NCORE := 8):
        bb, kh = divmod(c, 4)
        m = dict(shared)
        m["x"] = np.ascontiguousarray(x[bb])
        m["xown"] = np.ascontiguousarray(x[bb, TOK * kh:TOK * (kh + 1)])
        m["wq"] = np.ascontiguousarray(Wq[:, 256 * kh:256 * (kh + 1)])
        m["wk"] = np.ascontiguousarray(Wk[:, 64 * kh:64 * (kh + 1)])
        m["wv"] = np.ascontiguousarray(Wv[:, 64 * kh:64 * (kh + 1)])
        yidx = np.empty((128, 64), dtype=np.int32)
        for tt in range(NT):
            for r in range(4):
                yidx[:, 4 * tt + r] = S * (4 * bb + r) + TOK * kh + 128 * tt + np.arange(128)
        m["yidx"] = yidx
        in_maps.append(m)

    res = run_bass_kernel_spmd(nc, in_maps, core_ids=list(range(8)))
    out = np.empty((2, S, D), dtype=np.float32)
    for c in range(8):
        bb, kh = divmod(c, 4)
        out[bb, TOK * kh:TOK * (kh + 1)] = res.results[c]["out"]
    return out


# version 06
# speedup vs baseline: 1.4895x; 1.4321x over previous
import contextlib
import math
import os
import numpy as np
import ml_dtypes
import concourse.bass as bass
import concourse.mybir as mybir
from concourse.bass_utils import run_bass_kernel_spmd
from concourse.bass import IndirectOffsetOnAxis
F32 = mybir.dt.float32
BF16 = mybir.dt.bfloat16
AF = mybir.ActivationFunctionType
ALU = mybir.AluOpType
AX = mybir.AxisListType

D = 1024
S = 8192
NCH = S // 128
QB = 256
NQB = S // QB
TOK = S // 4
NT = TOK // 128
TCH = 1024
NTCH = TOK // TCH
EPS = 1e-5
LN2H = 0.5 * math.log(2.0)
UPTO = int(os.environ.get("K_UPTO", "4"))
P2N = int(os.environ.get("K_P2N", "999"))
P2M = int(os.environ.get("K_P2M", "5"))
P2QK = int(os.environ.get("K_P2QK", "9999"))
P2E = int(os.environ.get("K_P2E", "1"))
NOCC = int(os.environ.get("K_NOCC", "0"))
NOFFN = int(os.environ.get("K_NOFFN", "0"))
NOPRE = int(os.environ.get("K_NOPRE", "0"))


class Op:
    __slots__ = ("eng", "fn", "deps", "idx", "need_inc", "val", "dma", "accum", "burst")

    def __init__(self, eng, fn, dma, accum):
        self.eng, self.fn, self.dma, self.accum = eng, fn, dma, accum
        self.deps = []
        self.idx = -1
        self.need_inc = False
        self.val = 0
        self.burst = 0


class Bld:
    def __init__(self):
        self.ops = {e: [] for e in "TVSPY"}
        self.res = {}
        self.groups = {}

    def emit(self, eng, fn, reads=(), writes=(), dma=None, accum=None):
        op = Op(eng, fn, dma, accum)
        deps = {}
        for r in reads:
            ent = self.res.get(r)
            if ent and ent[0] is not None:
                deps[id(ent[0])] = ent[0]
        for w in writes:
            ent = self.res.get(w)
            if ent:
                if ent[0] is not None:
                    deps[id(ent[0])] = ent[0]
                for rd in ent[1]:
                    deps[id(rd)] = rd
        ds = []
        for d in deps.values():
            if d is op:
                continue
            if d.eng == "T" and eng == "T" and d.dma is None and dma is None:
                continue
            if accum is not None and d.accum == accum:
                continue
            if dma not in (None, "q") and d.eng == eng and d.dma == dma:
                continue
            ds.append(d)
        op.deps = ds
        for r in reads:
            self.res.setdefault(r, [None, []])[1].append(op)
        for w in writes:
            self.res[w] = [op, []]
        op.idx = len(self.ops[eng])
        self.ops[eng].append(op)
        if dma is not None and dma != "q" and eng == "P":
            self.groups.setdefault(dma, []).append(op)
        return op

    def finalize(self, nc, es, block):
        sems = {}
        for e in "TVSPY":
            sems[e] = es.enter_context(nc.semaphore(f"sem_{e}"))
        for g in self.groups:
            sems["g:" + g] = es.enter_context(nc.semaphore(f"sem_g_{g}"))
        for e in "TVSPY":
            for op in self.ops[e]:
                for d in op.deps:
                    d.need_inc = True
        for ops_ in self.groups.values():
            for o in ops_:
                o.need_inc = True
        for e in "TVSPY":
            for op in self.ops[e]:
                if op.dma is not None:
                    op.need_inc = True
        gtotal = {g: 16 * len(ops_) for g, ops_ in self.groups.items()}
        for e in "TVSPY":
            c = 0
            for op in self.ops[e]:
                if op.eng == "P" and op.dma not in (None, "q"):
                    op.val = 0
                    continue
                if op.need_inc:
                    c += 16 if (op.dma is not None) else 1
                op.val = c

        burst_end = {"Y": {}, "P": {}}
        for e in ("Y", "P"):
            bid = 0
            for op in self.ops[e]:
                if op.deps:
                    bid += 1
                op.burst = bid
                burst_end[e][bid] = op.val

        def sem_of(d):
            if d.eng == "P" and d.dma not in (None, "q"):
                return ("g:" + d.dma, gtotal[d.dma])
            if d.eng in ("Y", "P"):
                return (d.eng, burst_end[d.eng][d.burst])
            return (d.eng, d.val)

        waited = {"Y": set(), "P": set()}
        for e in "TVSPY":
            for op in self.ops[e]:
                for d in op.deps:
                    s, v = sem_of(d)
                    if s in waited:
                        waited[s].add(v)

        def run_engine(e, eng_obj):
            water = {}
            for op in self.ops[e]:
                waits = {}
                for d in op.deps:
                    s, v = sem_of(d)
                    if waits.get(s, -1) < v:
                        waits[s] = v
                for s, v in sorted(waits.items()):
                    if water.get(s, -1) >= v:
                        continue
                    eng_obj.wait_ge(sems[s], v)
                    water[s] = v
                ins = op.fn(eng_obj)
                if op.need_inc:
                    if op.eng == "P" and op.dma not in (None, "q"):
                        ins.then_inc(sems["g:" + op.dma], 16)
                    elif op.dma is not None:
                        ins.then_inc(sems[op.eng], 16)
                    else:
                        ins.then_inc(sems[op.eng], 1)
                if e in ("Y", "P") and op.need_inc and op.dma in (None, "q") and op.val in waited[e]:
                    eng_obj.wait_ge(sems[e], op.val)
                    water[e] = max(water.get(e, -1), op.val)

        @block.tensor
        def _(t):
            run_engine("T", t)

        @block.vector
        def _(v):
            run_engine("V", v)

        @block.scalar
        def _(s):
            run_engine("S", s)

        @block.gpsimd
        def _(p):
            run_engine("P", p)

        @block.sync
        def _(y):
            run_engine("Y", y)


def build_program():
    nc = bass.Bass("TRN2", target_bir_lowering=False)

    x_in = nc.dram_tensor("x", [S, D], F32, kind="ExternalInput").ap()
    xo_in = nc.dram_tensor("xown", [TOK, D], F32, kind="ExternalInput").ap()
    wq_in = nc.dram_tensor("wq", [D, 256], F32, kind="ExternalInput").ap()
    wk_in = nc.dram_tensor("wk", [D, 64], F32, kind="ExternalInput").ap()
    wv_in = nc.dram_tensor("wv", [D, 64], F32, kind="ExternalInput").ap()
    wo_in = nc.dram_tensor("wo", [D, D], F32, kind="ExternalInput").ap()
    cw1_in = nc.dram_tensor("cw1", [D, D], F32, kind="ExternalInput").ap()
    cw2_in = nc.dram_tensor("cw2", [D, 512], F32, kind="ExternalInput").ap()
    cw3_in = nc.dram_tensor("cw3", [512, 256], F32, kind="ExternalInput").ap()
    cw4_in = nc.dram_tensor("cw4", [256, 1], F32, kind="ExternalInput").ap()
    gw1_in = nc.dram_tensor("gw1", [D, 512], F32, kind="ExternalInput").ap()
    gw2_in = nc.dram_tensor("gw2", [512, 256], F32, kind="ExternalInput").ap()
    gw3_in = nc.dram_tensor("gw3", [256, 1], F32, kind="ExternalInput").ap()
    cb1_in = nc.dram_tensor("cb1", [D], F32, kind="ExternalInput").ap()
    cb2_in = nc.dram_tensor("cb2", [512], F32, kind="ExternalInput").ap()
    cb3_in = nc.dram_tensor("cb3", [256], F32, kind="ExternalInput").ap()
    cb4_in = nc.dram_tensor("cb4", [1], F32, kind="ExternalInput").ap()
    gb1_in = nc.dram_tensor("gb1", [512], F32, kind="ExternalInput").ap()
    gb2_in = nc.dram_tensor("gb2", [256], F32, kind="ExternalInput").ap()
    gb3_in = nc.dram_tensor("gb3", [1], F32, kind="ExternalInput").ap()
    ar_in = nc.dram_tensor("ar", [1], F32, kind="ExternalInput").ap()
    cosq_in = nc.dram_tensor("cosq", [128, S], BF16, kind="ExternalInput").ap()
    sinq_in = nc.dram_tensor("sinq", [128, S], BF16, kind="ExternalInput").ap()
    cosk_in = nc.dram_tensor("cosk", [128, S], BF16, kind="ExternalInput").ap()
    sink_in = nc.dram_tensor("sink", [128, S], BF16, kind="ExternalInput").ap()
    rp_in = nc.dram_tensor("rp", [128, 128], BF16, kind="ExternalInput").ap()
    idb_in = nc.dram_tensor("idb", [128, 128], BF16, kind="ExternalInput").ap()
    idf_in = nc.dram_tensor("idf", [128, 128], F32, kind="ExternalInput").ap()
    oneb_in = nc.dram_tensor("oneb", [128, 128], BF16, kind="ExternalInput").ap()
    msk_in = nc.dram_tensor("msk", [128, 384], BF16, kind="ExternalInput").ap()
    out_d = nc.dram_tensor("out", [TOK, D], F32, kind="ExternalOutput").ap()
    yidx_in = nc.dram_tensor("yidx", [128, 64], mybir.dt.int32, kind="ExternalInput").ap()
    ydram_h = nc.dram_tensor("ydram", [S, 256], BF16)
    ygat_h = nc.dram_tensor("ygat", [8 * S, 256], BF16)
    ydram = ydram_h.ap()
    ygat = ygat_h.ap()

    es = contextlib.ExitStack()
    SB = lambda n, sh, dt: es.enter_context(nc.sbuf_tensor(n, sh, dt))
    b = Bld()

    with es:
        XN = SB("XN", [128, NT * 1024], F32)
        BIGA = SB("BIGA", [128, 16384], BF16)
        KD = SB("KD", [128, 8192], BF16)
        VT = SB("VT", [128, 65 * NCH], BF16)
        XCH = SB("XCH", [128, 2048], F32)
        SCR = SB("SCR", [128, 4096], BF16)
        PTA = SB("PTA", [128, 2048], BF16)
        CS = SB("CS", [128, 1024], BF16)
        RSB = SB("RSB", [128, 2048], BF16)
        SCB = SB("SCB", [128, 768], BF16)
        STT = SB("STT", [128, 32], F32)
        SUMXN = SB("SUMXN", [128, 8], F32)
        MAXA = SB("MAXA", [1, 4], F32)
        MScr = SB("MScr", [1, 8], F32)
        NEGB = SB("NEGB", [128, 2], F32)
        NEGBB = SB("NEGBB", [1, 2], BF16)
        GFT = SB("GFT", [1, 16], F32)
        XBARB = SB("XBARB", [128, 8], BF16)
        GFH1 = SB("GFH1", [128, 4], BF16)
        GFH2 = SB("GFH2", [128, 2], BF16)
        CTL = SB("CTL", [128, 4], F32)
        CTLB = SB("CTLB", [1, 1024], F32)
        TMP1 = SB("TMP1", [128, 128], F32)
        TMP2 = SB("TMP2", [128, 128], F32)
        TMPQ = SB("TMPQ", [128, 512], F32)
        YST = SB("YST", [128, 512], BF16)
        RD = SB("RD", [128, 8], F32)
        IDB = SB("IDB", [128, 128], BF16)
        IDF = SB("IDF", [128, 128], F32)
        ONEB = SB("ONEB", [128, 128], BF16)
        MSK = SB("MSK", [128, 384], BF16)
        RP = SB("RPs", [128, 128], BF16)
        WQB = SB("WQB", [128, 2048], BF16)
        WKB = SB("WKB", [128, 512], BF16)
        WVB = SB("WVB", [128, 512], BF16)
        WBIG = SB("WBIG", [128, 8192], BF16)
        CN2 = SB("CN2", [128, 4096], BF16)
        CN3 = SB("CN3", [128, 1024], BF16)
        CN4 = SB("CN4", [128, 2], BF16)
        GC1 = SB("GC1", [128, 4096], BF16)
        GC2 = SB("GC2", [128, 1024], BF16)
        GC3 = SB("GC3", [128, 2], BF16)
        CB1 = SB("CB1", [128, 8], F32)
        CB2 = SB("CB2", [128, 4], F32)
        CB3 = SB("CB3", [128, 2], F32)
        CB4 = SB("CB4", [1, 2], F32)
        GB1 = SB("GB1", [128, 4], F32)
        GB2 = SB("GB2", [128, 2], F32)
        GB3 = SB("GB3", [1, 2], F32)
        ART = SB("ART", [1, 1], F32)
        CONS = SB("CONS", [1, 4], F32)
        YIX = SB("YIX", [128, 64], mybir.dt.int32)
        PS = es.enter_context(nc.psum_tensor("PS", [128, 4096], F32))
        biga_addr = nc.lookup_mloc(BIGA).addr
        kd_addr = nc.lookup_mloc(KD).addr
        QCH = [[nc.alloc_sbuf_tensor_at(f"QCH{p_}_{c_}", (128, 256), BF16,
                                        offset=biga_addr + 16384 * p_ + 512 * c_)
                for c_ in range(NCH // 2)] for p_ in range(2)]
        KCH = [nc.alloc_sbuf_tensor_at(f"KCH_{c_}", (128, 128), BF16,
                                       offset=kd_addr + 256 * c_)
               for c_ in range(NCH)]
        xn_addr = nc.lookup_mloc(XN).addr
        KBH = [nc.alloc_sbuf_tensor_at(f"KBH_{c_}", (128, 128), BF16,
                                       offset=xn_addr + 256 * c_)
               for c_ in range(NCH)]
        KBALL = nc.alloc_sbuf_tensor_at("KBALL", (128, 8192), BF16, offset=xn_addr)

        def psb(bank, off, w, p0=0, p1=128):
            return PS[p0:p1, 512 * bank + off:512 * bank + off + w]

        stg_par = [0]

        def wload(dst, srcap, kt, ncols, name):
            for k in range(kt):
                bu = stg_par[0] % 2
                stg_par[0] += 1
                stg = XCH[:, 1024 * bu:1024 * bu + ncols]
                b.emit("Y", lambda y, k=k, stg=stg, srcap=srcap: y.dma_start(
                    out=stg, in_=srcap[128 * k:128 * (k + 1), :]),
                    writes=[f"XCH:{bu}"], dma="q")
                b.emit("P", lambda p, k=k, stg=stg, dst=dst, ncols=ncols: p.tensor_copy(
                    out=dst[:, ncols * k:ncols * (k + 1)], in_=stg),
                    reads=[f"XCH:{bu}"], writes=[name])

        if not int(os.environ.get("K_NOQKV", "0")):
            b.emit("Y", lambda y: y.dma_start(out=YIX[:, :], in_=yidx_in), writes=["YIX"], dma="q")
            wload(WQB, wq_in, 8, 256, "WQB")
            wload(WKB, wk_in, 8, 64, "WKB")
            wload(WVB, wv_in, 8, 64, "WVB")

        def bload(dst, src, m, name):
            for i in range(m):
                b.emit("Y", lambda y, d=dst, s=src, i=i:
                       y.dma_start(out=d[:, i:i + 1],
                                   in_=s[128 * i:128 * (i + 1)].rearrange("(p o) -> p o", o=1)),
                       writes=[name], dma="q")

        if not int(os.environ.get("K_NOBL", "0")):
            bload(CB1, cb1_in, 8, "CB1")
            bload(CB2, cb2_in, 4, "CB2")
            bload(CB3, cb3_in, 2, "CB3")
            bload(GB1, gb1_in, 4, "GB1")
            bload(GB2, gb2_in, 2, "GB2")
            for dst_, src_, nm_ in ((CB4, cb4_in, "CB4"), (GB3, gb3_in, "GB3"), (ART, ar_in, "ART")):
                b.emit("Y", lambda y, d=dst_, s=src_: y.dma_start(out=d[0:1, 0:1], in_=s.rearrange("(x o) -> x o", x=1)),
                       writes=[nm_], dma="q")
        if not int(os.environ.get("K_NOTB", "0")):
            for dst_, src_, nm_ in ((IDB, idb_in, "IDB"), (IDF, idf_in, "IDF"), (ONEB, oneb_in, "ONEB"),
                                    (MSK, msk_in, "MSK"), (RP, rp_in, "RPc")):
                b.emit("Y", lambda y, d=dst_, s=src_: y.dma_start(out=d[:, :], in_=s), writes=[nm_], dma="q")
            b.emit("V", lambda v: v.memset(MAXA[:, :], -1e30), writes=["MAXA"])
            b.emit("V", lambda v: v.memset(SUMXN[:, :], 0.0), writes=["SUMXN"])
            b.emit("V", lambda v: v.memset(VT[:, :], 1.0), writes=["VTones"])
        b.emit("V", lambda v: v.memset(KD[64:128, :], 0.0), writes=["KDzero"])
        b.emit("V", lambda v: v.memset(KBALL[0:64, :], 0.0), writes=["KBzero", "ovXN"])
        b.emit("V", lambda v: v.memset(CONS[:, :], 0.0), writes=["CONS"])

        def p1_chunk(c):
            bu = c % 2
            r0 = 128 * c
            xch = XCH[:, 1024 * bu:1024 * (bu + 1)]
            xnb = SCR[:, 1024 * bu:1024 * (bu + 1)]
            xnt0 = 2048 + 1024 * bu
            st = 16 * bu
            cso = 512 * bu
            ro = 1024 * bu
            rXCH, rXNB, rXNT = f"XCH:{bu}", f"XNB:{bu}", f"XNT:{bu}"
            rST, rCS, rRSB = f"STT:{bu}", f"CS:{bu}", f"RSB:{bu}"

            b.emit("Y", lambda y: y.dma_start(out=xch, in_=x_in[r0:r0 + 128, :]),
                   writes=[rXCH], dma="q")
            for j, tab in enumerate((cosq_in, sinq_in, cosk_in, sink_in)):
                b.emit("Y", lambda y, j=j, tab=tab: y.dma_start(
                    out=CS[:, cso + 128 * j:cso + 128 * (j + 1)], in_=tab[:, r0:r0 + 128]),
                    writes=[rCS], dma="q")

            b.emit("V", lambda v: v.bn_stats(out=STT[:, st:st + 6], in_=xch[:, 0:512]),
                   reads=[rXCH], writes=[rST])
            b.emit("V", lambda v: v.bn_stats(out=STT[:, st + 6:st + 12], in_=xch[:, 512:1024]),
                   reads=[rXCH], writes=[rST])
            b.emit("V", lambda v: v.bn_aggr(out=STT[:, st + 12:st + 14], in_=STT[:, st:st + 12]),
                   reads=[rST], writes=[rST])
            b.emit("V", lambda v: v.tensor_scalar_add(out=STT[:, st + 14:st + 15],
                                                      in0=STT[:, st + 13:st + 14], scalar1=EPS),
                   reads=[rST], writes=[rST])
            b.emit("S", lambda s: s.activation(out=STT[:, st + 15:st + 16],
                                               in_=STT[:, st + 14:st + 15], func=AF.Ln),
                   reads=[rST], writes=[rST])
            b.emit("S", lambda s: s.activation(out=STT[:, st + 14:st + 15],
                                               in_=STT[:, st + 15:st + 16], func=AF.Exp, scale=-0.5),
                   reads=[rST], writes=[rST])
            b.emit("V", lambda v: v.tensor_scalar(out=xnb, in0=xch,
                                                  scalar1=STT[:, st + 12:st + 13],
                                                  scalar2=STT[:, st + 14:st + 15],
                                                  op0=ALU.subtract, op1=ALU.mult),
                   reads=[rXCH, rST, "ovS"], writes=[rXNB])

            for f in range(8):
                bank, off = 4 + (f // 4), 128 * (f % 4)
                b.emit("T", lambda t, f=f, bank=bank, off=off: t.matmul(
                    psb(bank, off, 128), xnb[:, 128 * f:128 * (f + 1)], IDB[:, :],
                    start=True, stop=True),
                    reads=[rXNB, "IDB", "ovS"], writes=[f"pb{bank}"])
            for f in range(8):
                bank, off = 4 + (f // 4), 128 * (f % 4)
                b.emit("V", lambda v, f=f, bank=bank, off=off: v.tensor_copy(
                    out=SCR[:, xnt0 + 128 * f:xnt0 + 128 * (f + 1)], in_=psb(bank, off, 128)),
                    reads=[f"pb{bank}", "ovS"], writes=[rXNT])

            for m in range(4):
                bank = m
                for k in range(8):
                    if m < 2:
                        st_ap = WQB.ap()[:, 256 * k + 128 * m:256 * k + 128 * m + 128]
                        wnm = "WQB"
                    elif m == 2:
                        st_ap = WKB.ap()[:, 64 * k:64 * (k + 1)]
                        wnm = "WKB"
                    else:
                        st_ap = WVB.ap()[:, 64 * k:64 * (k + 1)]
                        wnm = "WVB"
                    p1_ = 128 if m < 2 else 64
                    b.emit("T", lambda t, k=k, bank=bank, st_ap=st_ap, p1_=p1_: t.matmul(
                        psb(bank, 0, 128, 0, p1_), st_ap,
                        SCR[:, xnt0 + 128 * k:xnt0 + 128 * (k + 1)],
                        start=(k == 0), stop=(k == 7)),
                        reads=[rXNT, wnm], writes=[f"pb{bank}"], accum=("p1", c, m))
            for m in range(2):
                b.emit("S", lambda s, m=m: s.activation(
                    out=RSB[:, ro + 128 * m:ro + 128 * (m + 1)], in_=psb(m, 0, 128), func=AF.Copy),
                    reads=[f"pb{m}"], writes=[rRSB])
            b.emit("S", lambda s: s.activation(
                out=RSB[0:64, ro + 256:ro + 384], in_=psb(2, 0, 128, 0, 64), func=AF.Copy),
                reads=["pb2"], writes=[rRSB])
            b.emit("S", lambda s: s.activation(out=RSB[0:64, ro + 768:ro + 896],
                                               in_=psb(3, 0, 128, 0, 64), func=AF.Copy),
                   reads=["pb3"], writes=[rRSB])

            for m in range(2):
                b.emit("T", lambda t, m=m: t.matmul(
                    psb(m, 128, 128), RP[:, :], RSB[:, ro + 128 * m:ro + 128 * (m + 1)],
                    start=True, stop=True),
                    reads=[rRSB, "RPc"], writes=[f"pb{m}"])
            b.emit("T", lambda t: t.matmul(
                psb(2, 128, 128, 0, 64), RP[0:64, 0:64], RSB[0:64, ro + 256:ro + 384],
                start=True, stop=True),
                reads=[rRSB, "RPc"], writes=["pb2"])
            for m in range(2):
                b.emit("S", lambda s, m=m: s.activation(
                    out=RSB[:, ro + 384 + 128 * m:ro + 384 + 128 * (m + 1)],
                    in_=psb(m, 128, 128), func=AF.Copy),
                    reads=[f"pb{m}"], writes=[rRSB])
            b.emit("S", lambda s: s.activation(
                out=RSB[0:64, ro + 640:ro + 768],
                in_=psb(2, 128, 128, 0, 64), func=AF.Copy),
                reads=["pb2"], writes=[rRSB])

            qho = 128 * (c % 2)
            for m, (dst, dnm, tco, tso, pl) in enumerate((
                    (QCH[0][c // 2][:, qho:qho + 128], f"QP0:{c}", 0, 128, 128),
                    (QCH[1][c // 2][:, qho:qho + 128], f"QP1:{c}", 0, 128, 128),
                    (KCH[c][0:64, :], f"KD:{c}", 256, 384, 64))):
                b.emit("V", lambda v, m=m, tco=tco, pl=pl: v.tensor_tensor(
                    out=TMP1[0:pl, :], in0=RSB[0:pl, ro + 128 * m:ro + 128 * (m + 1)],
                    in1=CS[0:pl, cso + tco:cso + tco + 128], op=ALU.mult),
                    reads=[rRSB, rCS], writes=["TMP1"])
                b.emit("V", lambda v, m=m, tso=tso, pl=pl: v.tensor_tensor(
                    out=TMP2[0:pl, :], in0=RSB[0:pl, ro + 384 + 128 * m:ro + 384 + 128 * (m + 1)],
                    in1=CS[0:pl, cso + tso:cso + tso + 128], op=ALU.mult),
                    reads=[rRSB, rCS], writes=["TMP2"])
                b.emit("V", lambda v, dst=dst, pl=pl: v.tensor_tensor(
                    out=dst, in0=TMP1[0:pl, :], in1=TMP2[0:pl, :], op=ALU.add),
                    reads=["TMP1", "TMP2", "ovA" if m < 2 else "ovK"], writes=[dnm])
            b.emit("Y", lambda y, c=c: y.dma_start(out=KBH[c][64:128, :], in_=KCH[c][0:64, :]),
                   reads=[f"KD:{c}", "KBzero", "ovXN"], writes=[f"KB:{c}"], dma="q")

            b.emit("T", lambda t: t.matmul(psb(3, 128, 64, 0, 128), RSB[0:64, ro + 768:ro + 896],
                                           IDB[0:64, 0:64], start=True, stop=True),
                   reads=[rRSB, "IDB"], writes=["pb3"])
            b.emit("S", lambda s, c=c: s.activation(out=VT[:, 65 * c:65 * c + 64],
                                                    in_=psb(3, 128, 64), func=AF.Copy),
                   reads=["pb3"], writes=[f"VT:{c}"])

            for m, snm in enumerate((f"QP0:{c}", f"QP1:{c}", f"KD:{c}")):
                src = (QCH[0][c // 2][:, qho:qho + 128] if m == 0 else
                       QCH[1][c // 2][:, qho:qho + 128] if m == 1 else
                       KCH[c][0:64, :])
                pl = 128 if m < 2 else 64
                b.emit("S", lambda s, m=m, src=src, pl=pl: s.activation(
                    out=SCB[0:pl, 256 * m:256 * m + 128], in_=src, func=AF.Square),
                    reads=[snm], writes=["SCB"])
            for m in range(3):
                pl = 128 if m < 2 else 64
                b.emit("T", lambda t, m=m, pl=pl: t.matmul(
                    psb(m, 256, 128, 0, 1), ONEB[0:pl, 0:1], SCB[0:pl, 256 * m:256 * m + 128],
                    start=True, stop=True),
                    reads=["SCB", "ONEB"], writes=[f"pb{m}"])
            for m in range(3):
                b.emit("V", lambda v, m=m: v.reduce_max(
                    out=MScr[0:1, m:m + 1], in_=psb(m, 256, 128, 0, 1), axis=AX.X),
                    reads=[f"pb{m}"], writes=["MScr"])
            b.emit("V", lambda v: v.tensor_max(out=MAXA[0:1, 0:3], in0=MAXA[0:1, 0:3],
                                               in1=MScr[0:1, 0:3]),
                   reads=["MScr", "MAXA"], writes=["MAXA"])

            for f in range(8):
                b.emit("T", lambda t, f=f: t.matmul(
                    psb(3, 256 + f, 1), xnb[:, 128 * f:128 * (f + 1)], ONEB[:, 0:1],
                    start=True, stop=True),
                    reads=[rXNB, "ONEB"], writes=["pb3"])
            b.emit("V", lambda v: v.tensor_add(out=SUMXN[:, :], in0=SUMXN[:, :], in1=psb(3, 256, 8)),
                   reads=["pb3", "SUMXN"], writes=["SUMXN"])

        gctr = [0]
        ydr_all = []

        def neg_bias(p):
            g0 = 8 + 4 * p
            b.emit("S", lambda s, p=p, g0=g0: s.activation(out=GFT[0:1, g0:g0 + 1],
                                                           in_=MAXA[0:1, p:p + 1], func=AF.Ln),
                   reads=["MAXA"], writes=["GFTb"])
            b.emit("S", lambda s, g0=g0: s.activation(out=GFT[0:1, g0 + 1:g0 + 2],
                                                      in_=MAXA[0:1, 2:3], func=AF.Ln),
                   reads=["MAXA", "GFTb"], writes=["GFTb"])
            b.emit("V", lambda v, g0=g0: v.tensor_add(out=GFT[0:1, g0 + 2:g0 + 3],
                                                      in0=GFT[0:1, g0:g0 + 1], in1=GFT[0:1, g0 + 1:g0 + 2]),
                   reads=["GFTb"], writes=["GFTb"])
            b.emit("S", lambda s, g0=g0: s.activation(out=GFT[0:1, g0 + 3:g0 + 4], in_=GFT[0:1, g0 + 2:g0 + 3],
                                                      func=AF.Exp, scale=0.5, bias=CONS[0:1, 0:1]),
                   reads=["GFTb", "CONS"], writes=["GFTb"])
            b.emit("V", lambda v, p=p, g0=g0: v.tensor_scalar_mul(out=NEGBB[0:1, p:p + 1],
                                                                  in0=GFT[0:1, g0 + 3:g0 + 4], scalar1=-1.0),
                   reads=["GFTb"], writes=["NEGBB"])
            b.emit("T", lambda t, p=p: t.matmul(psb(4, 400 + p, 1), ONEB[0:1, :],
                                                NEGBB[0:1, p:p + 1], start=True, stop=True),
                   reads=["NEGBB", "ONEB"], writes=["pb4"])
            b.emit("V", lambda v, p=p: v.tensor_copy(out=NEGB[:, p:p + 1], in_=psb(4, 400 + p, 1)),
                   reads=["pb4"], writes=["NEGB"])

        def qk_qb(p, qb):
            inst = 2 * qb + p
            QPn = "QP0" if p == 0 else "QP1"
            qch = QCH[p][qb]
            nkt = 2 * (qb + 1)
            if P2M < 2:
                return
            for g in range(nkt // 2):
                gpar = gctr[0] % 2
                gctr[0] += 1
                base = 0 if gpar == 0 else 2
                pta0 = 0 if gpar == 0 else 1024
                rPT = f"PTA:{gpar}"
                slots = []
                for gi in range(2):
                    kt = 2 * g + gi
                    for h in (0, 1):
                        si = 2 * gi + h
                        bank = base + si // 2
                        off = 256 * (si % 2)
                        pos = pta0 + 256 * si
                        slots.append((kt, h, bank, off, pos))
                qk_l, av_l = [], []
                for (kt, h, bank, off, pos) in slots:
                    kst = KCH[kt] if h == 0 else KBH[kt]
                    knm = f"KD:{kt}" if h == 0 else f"KB:{kt}"
                    qk_l.append(lambda kst=kst, bank=bank, off=off, qch=qch, knm=knm, h=h: b.emit(
                        "T", lambda t, kst=kst, bank=bank, off=off, qch=qch: t.matmul(
                            psb(bank, off, QB, 0, 128),
                            kst[:, :], qch[:, :],
                            start=True, stop=True),
                        reads=[knm, "KDzero" if h == 0 else "KBzero",
                               f"{QPn}:{2 * qb}", f"{QPn}:{2 * qb + 1}", "ovA", "ovK", "ovXN"],
                        writes=[f"pb{bank}"]))

                def mk_exp(base=base, pta0=pta0, p=p, rPT=rPT, slots=slots, qb=qb):
                    b.emit("S", lambda s: s.activation(
                        out=PTA[:, pta0:pta0 + 1024], in_=PS[:, 512 * base:512 * base + 1024],
                        func=AF.Exp, bias=NEGB[:, p:p + 1]),
                        reads=["NEGB", f"pb{base}", f"pb{base + 1}"], writes=[rPT])
                    if P2M >= 3:
                        for (kt, h, bank, off, pos) in slots:
                            if kt >= 2 * qb:
                                moff = 128 if kt == 2 * qb else 0
                                b.emit("V", lambda v, pos=pos, moff=moff: v.tensor_tensor(
                                    out=PTA[:, pos:pos + 256], in0=PTA[:, pos:pos + 256],
                                    in1=MSK[:, moff:moff + 256], op=ALU.mult),
                                    reads=[rPT, "MSK"], writes=[rPT])

                if P2M >= 4:
                    for (kt, h, bank, off, pos) in slots:
                        for j in range(2):
                            yb = 4 + 2 * h + j
                            av_l.append(lambda pos=pos, kt=kt, j=j, yb=yb, rPT=rPT,
                                        st=(kt == 0), lst=(kt == nkt - 1), inst=inst, h=h: b.emit(
                                "T", lambda t, pos=pos, yb=yb, kt=kt, st=st, lst=lst: t.matmul(
                                    psb(yb, 0, 65), PTA[:, pos + 128 * j:pos + 128 * j + 128],
                                    VT[:, 65 * kt:65 * (kt + 1)], start=st, stop=lst),
                                reads=[rPT, f"VT:{kt}", "VTones"], writes=[f"pb{yb}"],
                                accum=("av", inst, h, j)))
                tasks.append((qk_l, mk_exp, av_l, None))
            if P2M < 5:
                return

            def mk_norm(p=p, qb=qb, inst=inst):
                for h in range(2):
                    for j in range(2):
                        yb = 4 + 2 * h + j
                        sl = (inst % 2) * 4 + 2 * h + j
                        b.emit("V", lambda v, yb=yb, sl=sl: v.reciprocal(
                            out=RD[:, sl:sl + 1], in_=psb(yb, 64, 1)),
                            reads=[f"pb{yb}"], writes=[f"RD:{sl}"])
                        b.emit("V", lambda v, yb=yb, sl=sl: v.tensor_scalar_mul(
                            out=YST[:, 64 * sl:64 * (sl + 1)], in0=psb(yb, 0, 64),
                            scalar1=RD[:, sl:sl + 1]),
                            reads=[f"pb{yb}", f"RD:{sl}"], writes=[f"YST:{sl}"])
                        lh = 2 * p + h
                        r0y = QB * qb + 128 * j
                        b.emit("Y", lambda y, r0y=r0y, lh=lh, sl=sl: y.dma_start(
                            out=ydram[r0y:r0y + 128, 64 * lh:64 * (lh + 1)],
                            in_=YST[:, 64 * sl:64 * (sl + 1)]),
                            reads=[f"YST:{sl}"], writes=[f"ydr:{r0y}:{lh}"], dma="q")
                        ydr_all.append(f"ydr:{r0y}:{lh}")
            tasks[-1] = tasks[-1][:3] + (mk_norm,)

        if UPTO >= 1:
            for c in range(min(NCH, int(os.environ.get("K_P1N", "999")))):
                p1_chunk(c)
        if UPTO >= 2:
            neg_bias(0)
            neg_bias(1)
            if UPTO >= 3 and not int(os.environ.get("K_NOWL", "0")):
                wload(WBIG, cw1_in, 8, 1024, "WBIG")
                wload(CN2, cw2_in, 8, 512, "CN2")
                wload(CN3, cw3_in, 4, 256, "CN3")
                wload(CN4, cw4_in, 2, 1, "CN4")
                wload(GC1, gw1_in, 8, 512, "GC1")
                wload(GC2, gw2_in, 4, 256, "GC2")
                wload(GC3, gw3_in, 2, 1, "GC3")
            tasks = []
            for qb in range(min(NQB, P2N)):
                qk_qb(0, qb)
                qk_qb(1, qb)
            prev = None
            for tk in tasks:
                for f in tk[0]:
                    f()
                tk[1]()
                if prev is not None:
                    for f in prev[2]:
                        f()
                    if prev[3] is not None:
                        prev[3]()
                prev = tk
            if prev is not None:
                for f in prev[2]:
                    f()
                if prev[3] is not None:
                    prev[3]()

        def early_out():
            b.emit("V", lambda v: v.memset(XN[:, :], 0.25), writes=[f"XN:{t}" for t in range(NT)])
            for tt in range(NT):
                b.emit("Y", lambda y, tt=tt: y.dma_start(
                    out=out_d[128 * tt:128 * (tt + 1), :], in_=XN[:, 1024 * tt:1024 * (tt + 1)]),
                    reads=[f"XN:{tt}"], writes=["outw"], dma="q")
            b.emit("P", lambda p: p.engine_nop(), reads=["outw"])
            block = es.enter_context(nc.Block())
            b.finalize(nc, es, block)

        if UPTO < 3:
            early_out()
            return nc

        if not NOCC:
            op_cc = b.emit("P", lambda p: p.collective_compute(
                "AllGather", ALU.bypass,
                replica_groups=[list(range(8))],
                ins=[ydram_h.ap().opt()], outs=[ygat_h.ap().opt()]),
                reads=([ydr_all[-1]] if ydr_all else ["ydram"]), writes=["ygat"])
            op_cc.need_inc = True

        for tt in (range(0) if NOPRE else range(NT)):
            bu = tt % 2
            xch = XCH[:, 1024 * bu:1024 * (bu + 1)]
            st = 16 * bu
            rXCH, rST = f"XCH:{bu}", f"STT:{bu}"
            b.emit("Y", lambda y, tt=tt, xch=xch: y.dma_start(
                out=xch, in_=xo_in[128 * tt:128 * (tt + 1), :]),
                writes=[rXCH], dma="q")
            b.emit("V", lambda v, xch=xch, st=st: v.bn_stats(out=STT[:, st:st + 6], in_=xch[:, 0:512]),
                   reads=[rXCH], writes=[rST])
            b.emit("V", lambda v, xch=xch, st=st: v.bn_stats(out=STT[:, st + 6:st + 12], in_=xch[:, 512:1024]),
                   reads=[rXCH], writes=[rST])
            b.emit("V", lambda v, st=st: v.bn_aggr(out=STT[:, st + 12:st + 14], in_=STT[:, st:st + 12]),
                   reads=[rST], writes=[rST])
            b.emit("V", lambda v, st=st: v.tensor_scalar_add(out=STT[:, st + 14:st + 15],
                                                             in0=STT[:, st + 13:st + 14], scalar1=EPS),
                   reads=[rST], writes=[rST])
            b.emit("S", lambda s, st=st: s.activation(out=STT[:, st + 15:st + 16],
                                                      in_=STT[:, st + 14:st + 15], func=AF.Ln),
                   reads=[rST], writes=[rST])
            b.emit("S", lambda s, st=st: s.activation(out=STT[:, st + 14:st + 15],
                                                      in_=STT[:, st + 15:st + 16], func=AF.Exp, scale=-0.5),
                   reads=[rST], writes=[rST])
            b.emit("V", lambda v, tt=tt, xch=xch, st=st: v.tensor_scalar(
                out=XN[:, 1024 * tt:1024 * (tt + 1)], in0=xch,
                scalar1=STT[:, st + 12:st + 13], scalar2=STT[:, st + 14:st + 15],
                op0=ALU.subtract, op1=ALU.mult),
                reads=[rXCH, rST], writes=[f"XN:{tt}"] + (["ovXN"] if tt == 0 else []))

        if NOFFN:
            early_out()
            return nc
        b.emit("V", lambda v: v.tensor_scalar_mul(out=XBARB[:, 0:8], in0=SUMXN[:, :], scalar1=1.0 / S),
               reads=["SUMXN"], writes=["XBARB"])
        for m in range(4):
            for k in range(8):
                b.emit("T", lambda t, m=m, k=k: t.matmul(
                    psb(0, m, 1), GC1.ap()[:, 512 * k + 128 * m:512 * k + 128 * m + 128],
                    XBARB[:, k:k + 1], start=(k == 0), stop=(k == 7)),
                    reads=["GC1", "XBARB"], writes=["pb0"], accum=("gc1", m))
        for m in range(4):
            b.emit("S", lambda s, m=m: s.activation(out=GFH1[:, m:m + 1], in_=psb(0, m, 1),
                                                    func=AF.Gelu, bias=GB1[:, m:m + 1]),
                   reads=["pb0", "GB1"], writes=["GFH1"])
        for m in range(2):
            for k in range(4):
                b.emit("T", lambda t, m=m, k=k: t.matmul(
                    psb(1, m, 1), GC2.ap()[:, 256 * k + 128 * m:256 * k + 128 * m + 128],
                    GFH1[:, k:k + 1], start=(k == 0), stop=(k == 3)),
                    reads=["GC2", "GFH1"], writes=["pb1"], accum=("gc2", m))
        for m in range(2):
            b.emit("S", lambda s, m=m: s.activation(out=GFH2[:, m:m + 1], in_=psb(1, m, 1),
                                                    func=AF.Gelu, bias=GB2[:, m:m + 1]),
                   reads=["pb1", "GB2"], writes=["GFH2"])
        for k in range(2):
            b.emit("T", lambda t, k=k: t.matmul(
                psb(2, 0, 1, 0, 1), GC3.ap()[:, k:k + 1], GFH2[:, k:k + 1],
                start=(k == 0), stop=(k == 1)),
                reads=["GC3", "GFH2"], writes=["pb2"], accum=("gc3", 0))
        b.emit("V", lambda v: v.tensor_scalar_mul(out=GB3[0:1, 1:2], in0=GB3[0:1, 0:1], scalar1=0.5),
               reads=["GB3"], writes=["GB3"])
        b.emit("V", lambda v: v.tensor_scalar_mul(out=CB4[0:1, 1:2], in0=CB4[0:1, 0:1], scalar1=0.5),
               reads=["CB4"], writes=["CB4"])
        b.emit("S", lambda s: s.activation(out=GFT[0:1, 0:1], in_=psb(2, 0, 1, 0, 1),
                                           func=AF.Tanh, scale=0.5, bias=GB3[0:1, 1:2]),
               reads=["pb2", "GB3"], writes=["GFTg"])
        b.emit("V", lambda v: v.tensor_scalar(out=GFT[0:1, 1:2], in0=GFT[0:1, 0:1],
                                              scalar1=0.5, scalar2=0.5, op0=ALU.mult, op1=ALU.add),
               reads=["GFTg"], writes=["GFTg"])
        b.emit("V", lambda v: v.tensor_mul(out=GFT[0:1, 2:3], in0=GFT[0:1, 1:2], in1=ART[0:1, 0:1]),
               reads=["GFTg", "ART"], writes=["GFTg"])
        b.emit("V", lambda v: v.tensor_scalar_mul(out=GFT[0:1, 3:4], in0=GFT[0:1, 2:3], scalar1=1.4),
               reads=["GFTg"], writes=["GFTg"])
        b.emit("V", lambda v: v.tensor_sub(out=GFT[0:1, 4:5], in0=GFT[0:1, 3:4], in1=ART[0:1, 0:1]),
               reads=["GFTg", "ART"], writes=["GFTg"])
        b.emit("V", lambda v: v.tensor_scalar_add(out=GFT[0:1, 5:6], in0=GFT[0:1, 4:5], scalar1=0.1),
               reads=["GFTg"], writes=["GFTg"])
        b.emit("V", lambda v: v.tensor_scalar_mul(out=GFT[0:1, 6:7], in0=ART[0:1, 0:1], scalar1=0.3),
               reads=["ART"], writes=["GFTg"])
        b.emit("V", lambda v: v.tensor_add(out=GFT[0:1, 7:8], in0=GFT[0:1, 5:6], in1=GFT[0:1, 6:7]),
               reads=["GFTg"], writes=["GFTg"])

        def ffn_cycle(cyc):
            for tc in range(NTCH):
                t0 = TCH * tc
                for ttl in range(8):
                    tt = 8 * tc + ttl
                    bu = ttl % 2
                    cdst = SCR[:, 1024 * bu:1024 * (bu + 1)]
                    first = (cyc == 0 and tc == 0 and ttl == 0)
                    b.emit("V", lambda v, tt=tt, cdst=cdst: v.tensor_copy(
                        out=cdst, in_=XN[:, 1024 * tt:1024 * (tt + 1)]),
                        reads=[f"XN:{tt}"] + ([] if first else ["ovS"]),
                        writes=[f"XNB:{bu}"] + (["ovS"] if first else []))
                    for f in range(8):
                        bank, off = 6 + (f // 4), 128 * (f % 4)
                        b.emit("T", lambda t, f=f, bank=bank, off=off, cdst=cdst: t.matmul(
                            psb(bank, off, 128), cdst[:, 128 * f:128 * (f + 1)], IDB[:, :],
                            start=True, stop=True),
                            reads=[f"XNB:{bu}", "IDB", "ovS"], writes=[f"pb{bank}"])
                    for f in range(8):
                        bank, off = 6 + (f // 4), 128 * (f % 4)
                        fe = (first and f == 0)
                        b.emit("V", lambda v, f=f, bank=bank, off=off, ttl=ttl: v.tensor_copy(
                            out=BIGA[:, 1024 * f + 128 * ttl:1024 * f + 128 * (ttl + 1)],
                            in_=psb(bank, off, 128)),
                            reads=[f"pb{bank}"] + ([] if fe else ["ovA"]),
                            writes=[f"OUTBT:{f}"] + (["ovA"] if fe else []))
                gi = 0
                for m in range(8):
                    for n in range(2):
                        bank = gi % 6
                        gi += 1
                        for k in range(8):
                            b.emit("T", lambda t, m=m, n=n, k=k, bank=bank: t.matmul(
                                psb(bank, 0, 512), WBIG.ap()[:, 1024 * k + 128 * m:1024 * k + 128 * m + 128],
                                BIGA[:, 1024 * k + 512 * n:1024 * k + 512 * n + 512],
                                start=(k == 0), stop=(k == 7)),
                                reads=[f"OUTBT:{k}", "WBIG", "ovA"], writes=[f"pb{bank}"],
                                accum=("h1", cyc, tc, m, n))
                        b.emit("S", lambda s, m=m, n=n, bank=bank: s.activation(
                            out=BIGA[:, 8192 + 1024 * m + 512 * n:8192 + 1024 * m + 512 * n + 512],
                            in_=psb(bank, 0, 512), func=AF.Gelu, bias=CB1[:, m:m + 1]),
                            reads=[f"pb{bank}", "CB1", "ovA"], writes=[f"H1B:{m}"])
                for m in range(4):
                    for n in range(2):
                        bank = gi % 6
                        gi += 1
                        for k in range(8):
                            b.emit("T", lambda t, m=m, n=n, k=k, bank=bank: t.matmul(
                                psb(bank, 0, 512), CN2.ap()[:, 512 * k + 128 * m:512 * k + 128 * m + 128],
                                BIGA[:, 8192 + 1024 * k + 512 * n:8192 + 1024 * k + 512 * n + 512],
                                start=(k == 0), stop=(k == 7)),
                                reads=[f"H1B:{k}", "CN2", "ovA"], writes=[f"pb{bank}"],
                                accum=("h2", cyc, tc, m, n))
                        wk = ["ovK"] if (cyc == 0 and tc == 0 and m == 0 and n == 0) else []
                        b.emit("S", lambda s, m=m, n=n, bank=bank: s.activation(
                            out=KD[:, 1024 * m + 512 * n:1024 * m + 512 * n + 512],
                            in_=psb(bank, 0, 512), func=AF.Gelu, bias=CB2[:, m:m + 1]),
                            reads=[f"pb{bank}", "CB2"] + ([] if wk else ["ovK"]),
                            writes=[f"H2B:{m}"] + wk)
                for m in range(2):
                    for n in range(2):
                        bank = gi % 6
                        gi += 1
                        for k in range(4):
                            b.emit("T", lambda t, m=m, n=n, k=k, bank=bank: t.matmul(
                                psb(bank, 0, 512), CN3.ap()[:, 256 * k + 128 * m:256 * k + 128 * m + 128],
                                KD[:, 1024 * k + 512 * n:1024 * k + 512 * n + 512],
                                start=(k == 0), stop=(k == 3)),
                                reads=[f"H2B:{k}", "CN3", "ovK"], writes=[f"pb{bank}"],
                                accum=("h3", cyc, tc, m, n))
                        b.emit("S", lambda s, m=m, n=n, bank=bank: s.activation(
                            out=KD[:, 4096 + 1024 * m + 512 * n:4096 + 1024 * m + 512 * n + 512],
                            in_=psb(bank, 0, 512), func=AF.Gelu, bias=CB3[:, m:m + 1]),
                            reads=[f"pb{bank}", "CB3", "ovK"], writes=[f"H3B:{m}"])
                for n in range(2):
                    for k in range(2):
                        b.emit("T", lambda t, n=n, k=k: t.matmul(
                            psb(6 + n, 0, 512, 0, 1),
                            CN4.ap()[:, k:k + 1],
                            KD[:, 4096 + 1024 * k + 512 * n:4096 + 1024 * k + 512 * n + 512],
                            start=(k == 0), stop=(k == 1)),
                            reads=[f"H3B:{k}", "CN4", "ovK"], writes=[f"pb{6 + n}"],
                            accum=("z", cyc, tc, n))
                for n in range(2):
                    b.emit("S", lambda s, n=n: s.activation(
                        out=CTLB[0:1, 512 * n:512 * (n + 1)], in_=psb(6 + n, 0, 512, 0, 1),
                        func=AF.Tanh, scale=0.5, bias=CB4[0:1, 1:2]),
                        reads=[f"pb{6 + n}", "CB4"], writes=["CTLB"])
                b.emit("V", lambda v: v.tensor_scalar(
                    out=CTLB[0:1, 0:1024], in0=CTLB[0:1, 0:1024],
                    scalar1=GFT[0:1, 6:7], scalar2=GFT[0:1, 7:8],
                    op0=ALU.mult, op1=ALU.add),
                    reads=["CTLB", "GFTg"], writes=["CTLB"])
                for ttl in range(8):
                    tt = 8 * tc + ttl
                    cb = ttl % 2
                    b.emit("T", lambda t, ttl=ttl: t.matmul(
                        psb(6, 480, 1), CTLB[0:1, 128 * ttl:128 * (ttl + 1)],
                        IDF[0:1, 0:1], start=True, stop=True),
                        reads=["CTLB", "IDF"], writes=["pb6"])
                    b.emit("V", lambda v, cb=cb: v.tensor_copy(out=CTL[:, 2 * cb:2 * cb + 1],
                                                               in_=psb(6, 480, 1)),
                           reads=["pb6"], writes=[f"CTL:{cb}"])
                    b.emit("V", lambda v, cb=cb: v.tensor_scalar_mul(out=CTL[:, 2 * cb + 1:2 * cb + 2],
                                                                     in0=CTL[:, 2 * cb:2 * cb + 1],
                                                                     scalar1=-1.0),
                           reads=[f"CTL:{cb}"], writes=[f"CTL:{cb}"])
                    x0 = 1024 * tt
                    q = lambda i: XN[:, x0 + 256 * i:x0 + 256 * (i + 1)]
                    rX = f"XN:{tt}"
                    b.emit("V", lambda v, tt=tt, cb=cb: v.scalar_tensor_tensor(
                        out=TMPQ[:, 0:256], in0=XN[:, 1024 * tt + 512:1024 * tt + 768],
                        scalar=CTL[:, 2 * cb:2 * cb + 1], in1=XN[:, 1024 * tt:1024 * tt + 256],
                        op0=ALU.mult, op1=ALU.add),
                        reads=[rX, f"CTL:{cb}"], writes=["TMPQ"])
                    b.emit("V", lambda v, tt=tt, cb=cb: v.scalar_tensor_tensor(
                        out=TMPQ[:, 256:512], in0=XN[:, 1024 * tt + 768:1024 * tt + 1024],
                        scalar=CTL[:, 2 * cb + 1:2 * cb + 2], in1=XN[:, 1024 * tt + 256:1024 * tt + 512],
                        op0=ALU.mult, op1=ALU.add),
                        reads=[rX, f"CTL:{cb}"], writes=["TMPQ"])
                    b.emit("V", lambda v, tt=tt, cb=cb: v.scalar_tensor_tensor(
                        out=XN[:, 1024 * tt + 512:1024 * tt + 768], in0=XN[:, 1024 * tt:1024 * tt + 256],
                        scalar=CTL[:, 2 * cb + 1:2 * cb + 2], in1=XN[:, 1024 * tt + 512:1024 * tt + 768],
                        op0=ALU.mult, op1=ALU.add),
                        reads=[rX, f"CTL:{cb}"], writes=[rX])
                    b.emit("V", lambda v, tt=tt, cb=cb: v.scalar_tensor_tensor(
                        out=XN[:, 1024 * tt + 768:1024 * tt + 1024], in0=XN[:, 1024 * tt + 256:1024 * tt + 512],
                        scalar=CTL[:, 2 * cb:2 * cb + 1], in1=XN[:, 1024 * tt + 768:1024 * tt + 1024],
                        op0=ALU.mult, op1=ALU.add),
                        reads=[rX, f"CTL:{cb}"], writes=[rX])
                    b.emit("V", lambda v, tt=tt: v.tensor_copy(
                        out=XN[:, 1024 * tt:1024 * tt + 256], in_=TMPQ[:, 0:256]),
                        reads=["TMPQ"], writes=[rX])
                    b.emit("V", lambda v, tt=tt: v.tensor_copy(
                        out=XN[:, 1024 * tt + 256:1024 * tt + 512], in_=TMPQ[:, 256:512]),
                        reads=["TMPQ"], writes=[rX])

        for cyc in range(3):
            ffn_cycle(cyc)

        wload(WBIG, wo_in, 8, 1024, "WBIG")

        if UPTO < 4:
            early_out()
            return nc
        for tt in range(NT):
            for r in range(4):
                wA = ["ovA"] if (tt == 0 and r == 0) else []
                b.emit("P", lambda p, tt=tt, r=r: p.indirect_dma_start(
                    out=BIGA[:, 1024 * tt + 256 * r:1024 * tt + 256 * (r + 1)],
                    out_offset=None, in_=ygat,
                    in_offset=IndirectOffsetOnAxis(ap=YIX[:, 4 * tt + r:4 * tt + r + 1], axis=0)),
                    reads=["ygat", "YIX"] + ([] if wA else ["ovA"]),
                    writes=[f"YALL:{tt}"] + wA, dma="yin")
        for tt in range(NT):
            for f in range(8):
                bank, off = 6 + (f // 4), 128 * (f % 4)
                wS = ["ovS"] if (tt == 0 and f == 0) else []
                b.emit("T", lambda t, tt=tt, f=f, bank=bank, off=off: t.matmul(
                    psb(bank, off, 128), BIGA[:, 1024 * tt + 128 * f:1024 * tt + 128 * (f + 1)],
                    IDB[:, :], start=True, stop=True),
                    reads=[f"YALL:{tt}", "IDB"], writes=[f"pb{bank}"])
                b.emit("V", lambda v, f=f, bank=bank, off=off: v.tensor_copy(
                    out=SCR[:, 128 * f:128 * (f + 1)], in_=psb(bank, off, 128)),
                    reads=[f"pb{bank}"] + ([] if wS else ["ovS"]), writes=["YTT"] + wS)
            for n in range(2):
                bank = (2 * tt + n) % 4
                for k in range(8):
                    b.emit("T", lambda t, n=n, k=k, bank=bank: t.matmul(
                        psb(bank, 0, 512), SCR[:, 128 * k:128 * (k + 1)],
                        WBIG.ap()[:, 1024 * k + 512 * n:1024 * k + 512 * n + 512],
                        start=(k == 0), stop=(k == 7)),
                        reads=["YTT", "WBIG"], writes=[f"pb{bank}"], accum=("wo", tt, n))
                b.emit("V", lambda v, tt=tt, n=n, bank=bank: v.tensor_add(
                    out=XN[:, 1024 * tt + 512 * n:1024 * tt + 512 * n + 512],
                    in0=XN[:, 1024 * tt + 512 * n:1024 * tt + 512 * n + 512],
                    in1=psb(bank, 0, 512)),
                    reads=[f"pb{bank}", f"XN:{tt}"], writes=[f"XN:{tt}"])
            bu = tt % 2
            xch = XCH[:, 1024 * bu:1024 * (bu + 1)]
            b.emit("Y", lambda y, tt=tt, xch=xch: y.dma_start(
                out=xch, in_=xo_in[128 * tt:128 * (tt + 1), :]),
                writes=[f"XCH:{bu}"], dma="q")
            b.emit("V", lambda v, tt=tt, xch=xch: v.tensor_add(
                out=XN[:, 1024 * tt:1024 * (tt + 1)],
                in0=XN[:, 1024 * tt:1024 * (tt + 1)], in1=xch),
                reads=[f"XCH:{bu}", f"XN:{tt}"], writes=[f"XN:{tt}"])
            b.emit("Y", lambda y, tt=tt: y.dma_start(
                out=out_d[128 * tt:128 * (tt + 1), :], in_=XN[:, 1024 * tt:1024 * (tt + 1)]),
                reads=[f"XN:{tt}"], writes=[f"outw:{tt}"], dma="q")

        b.emit("P", lambda p: p.engine_nop(), reads=[f"outw:{NT - 1}"])

        block = es.enter_context(nc.Block())
        b.finalize(nc, es, block)
    return nc


_CACHE = {}


def _tables():
    if "t" in _CACHE:
        return _CACHE["t"]
    half = 32
    inv = 1.0 / (10000.0 ** (np.arange(half, dtype=np.float64) / half))
    t = np.arange(S, dtype=np.float64)
    fr = np.outer(inv, t)
    emb = np.concatenate([fr, fr], axis=0)
    cos = np.cos(emb)
    sin = np.sin(emb)
    cos2 = np.concatenate([cos, cos], axis=0)
    sin2 = np.concatenate([sin, sin], axis=0)
    bf = lambda a: a.astype(ml_dtypes.bfloat16)
    cosq = bf(cos2 / 8.0)
    sinq = bf(sin2 / 8.0)
    cosk = bf(cos2)
    sink = bf(sin2)
    rp = np.zeros((128, 128), dtype=np.float32)
    for base in (0, 64):
        for m in range(32):
            rp[base + m + 32, base + m] = -1.0
            rp[base + m, base + m + 32] = 1.0
    idb = np.eye(128, dtype=ml_dtypes.bfloat16)
    idf = np.eye(128, dtype=np.float32)
    oneb = np.ones((128, 128), dtype=ml_dtypes.bfloat16)
    msk = np.zeros((128, 384), dtype=np.float32)
    for pp in range(128):
        for j in range(384):
            if pp <= j - 128:
                msk[pp, j] = 1.0
    _CACHE["t"] = (cosq, sinq, cosk, sink, bf(rp), idb, idf, oneb, bf(msk))
    return _CACHE["t"]


def kernel(**inputs):
    x = np.asarray(inputs["L0_attn_x"], dtype=np.float32)
    Wq = np.ascontiguousarray(np.asarray(inputs["L0_attn_Wq"], dtype=np.float32))
    Wk = np.ascontiguousarray(np.asarray(inputs["L0_attn_Wk"], dtype=np.float32))
    Wv = np.ascontiguousarray(np.asarray(inputs["L0_attn_Wv"], dtype=np.float32))
    Wo = np.ascontiguousarray(np.asarray(inputs["L0_attn_Wo"], dtype=np.float32))
    cosq, sinq, cosk, sink, rp, idb, idf, oneb, msk = _tables()

    if "nc" not in _CACHE:
        _CACHE["nc"] = build_program()
    nc = _CACHE["nc"]

    shared = {
        "wo": Wo,
        "cw1": np.ascontiguousarray(np.asarray(inputs["L0_ffn_cn_w1"], dtype=np.float32)),
        "cw2": np.ascontiguousarray(np.asarray(inputs["L0_ffn_cn_w2"], dtype=np.float32)),
        "cw3": np.ascontiguousarray(np.asarray(inputs["L0_ffn_cn_w3"], dtype=np.float32)),
        "cw4": np.ascontiguousarray(np.asarray(inputs["L0_ffn_cn_w4"], dtype=np.float32)),
        "gw1": np.ascontiguousarray(np.asarray(inputs["L0_ffn_gc_w1"], dtype=np.float32)),
        "gw2": np.ascontiguousarray(np.asarray(inputs["L0_ffn_gc_w2"], dtype=np.float32)),
        "gw3": np.ascontiguousarray(np.asarray(inputs["L0_ffn_gc_w3"], dtype=np.float32)),
        "cb1": np.ascontiguousarray(np.asarray(inputs["L0_ffn_cn_b1"], dtype=np.float32)),
        "cb2": np.ascontiguousarray(np.asarray(inputs["L0_ffn_cn_b2"], dtype=np.float32)),
        "cb3": np.ascontiguousarray(np.asarray(inputs["L0_ffn_cn_b3"], dtype=np.float32)),
        "cb4": np.ascontiguousarray(np.asarray(inputs["L0_ffn_cn_b4"], dtype=np.float32)),
        "gb1": np.ascontiguousarray(np.asarray(inputs["L0_ffn_gc_b1"], dtype=np.float32)),
        "gb2": np.ascontiguousarray(np.asarray(inputs["L0_ffn_gc_b2"], dtype=np.float32)),
        "gb3": np.ascontiguousarray(np.asarray(inputs["L0_ffn_gc_b3"], dtype=np.float32)),
        "ar": np.ascontiguousarray(np.asarray(inputs["L0_ffn_adaptive_range"], dtype=np.float32)),
        "cosq": cosq, "sinq": sinq, "cosk": cosk, "sink": sink,
        "rp": rp, "idb": idb, "idf": idf, "oneb": oneb, "msk": msk,
    }
    in_maps = []
    for c in range(NCORE := 8):
        bb, kh = divmod(c, 4)
        m = dict(shared)
        m["x"] = np.ascontiguousarray(x[bb])
        m["xown"] = np.ascontiguousarray(x[bb, TOK * kh:TOK * (kh + 1)])
        m["wq"] = np.ascontiguousarray(Wq[:, 256 * kh:256 * (kh + 1)])
        m["wk"] = np.ascontiguousarray(Wk[:, 64 * kh:64 * (kh + 1)])
        m["wv"] = np.ascontiguousarray(Wv[:, 64 * kh:64 * (kh + 1)])
        yidx = np.empty((128, 64), dtype=np.int32)
        for tt in range(NT):
            for r in range(4):
                yidx[:, 4 * tt + r] = S * (4 * bb + r) + TOK * kh + 128 * tt + np.arange(128)
        m["yidx"] = yidx
        in_maps.append(m)

    res = run_bass_kernel_spmd(nc, in_maps, core_ids=list(range(8)))
    out = np.empty((2, S, D), dtype=np.float32)
    for c in range(8):
        bb, kh = divmod(c, 4)
        out[bb, TOK * kh:TOK * (kh + 1)] = res.results[c]["out"]
    return out


# version 07
# speedup vs baseline: 1.5153x; 1.0173x over previous
import contextlib
import math
import os
import numpy as np
import ml_dtypes
import concourse.bass as bass
import concourse.mybir as mybir
from concourse.bass_utils import run_bass_kernel_spmd
from concourse.bass import IndirectOffsetOnAxis
F32 = mybir.dt.float32
BF16 = mybir.dt.bfloat16
AF = mybir.ActivationFunctionType
ALU = mybir.AluOpType
AX = mybir.AxisListType

D = 1024
S = 8192
NCH = S // 128
QB = 256
NQB = S // QB
TOK = S // 4
NT = TOK // 128
TCH = 1024
NTCH = TOK // TCH
EPS = 1e-5
LN2H = 0.5 * math.log(2.0)
UPTO = int(os.environ.get("K_UPTO", "4"))
P2N = int(os.environ.get("K_P2N", "999"))
P2M = int(os.environ.get("K_P2M", "5"))
P2QK = int(os.environ.get("K_P2QK", "9999"))
P2E = int(os.environ.get("K_P2E", "1"))
NOCC = int(os.environ.get("K_NOCC", "0"))
NOFFN = int(os.environ.get("K_NOFFN", "0"))
NOPRE = int(os.environ.get("K_NOPRE", "0"))


class Op:
    __slots__ = ("eng", "fn", "deps", "idx", "need_inc", "val", "dma", "accum", "burst")

    def __init__(self, eng, fn, dma, accum):
        self.eng, self.fn, self.dma, self.accum = eng, fn, dma, accum
        self.deps = []
        self.idx = -1
        self.need_inc = False
        self.val = 0
        self.burst = 0


class Bld:
    def __init__(self):
        self.ops = {e: [] for e in "TVSPY"}
        self.res = {}
        self.groups = {}

    def emit(self, eng, fn, reads=(), writes=(), dma=None, accum=None):
        op = Op(eng, fn, dma, accum)
        deps = {}
        for r in reads:
            ent = self.res.get(r)
            if ent and ent[0] is not None:
                deps[id(ent[0])] = ent[0]
        for w in writes:
            ent = self.res.get(w)
            if ent:
                if ent[0] is not None:
                    deps[id(ent[0])] = ent[0]
                for rd in ent[1]:
                    deps[id(rd)] = rd
        ds = []
        for d in deps.values():
            if d is op:
                continue
            if d.eng == "T" and eng == "T" and d.dma is None and dma is None:
                continue
            if accum is not None and d.accum == accum:
                continue
            if dma not in (None, "q") and d.eng == eng and d.dma == dma:
                continue
            ds.append(d)
        op.deps = ds
        for r in reads:
            self.res.setdefault(r, [None, []])[1].append(op)
        for w in writes:
            self.res[w] = [op, []]
        op.idx = len(self.ops[eng])
        self.ops[eng].append(op)
        if dma is not None and dma != "q" and eng == "P":
            self.groups.setdefault(dma, []).append(op)
        return op

    def finalize(self, nc, es, block):
        sems = {}
        for e in "TVSPY":
            sems[e] = es.enter_context(nc.semaphore(f"sem_{e}"))
        for g in self.groups:
            sems["g:" + g] = es.enter_context(nc.semaphore(f"sem_g_{g}"))
        for e in "TVSPY":
            for op in self.ops[e]:
                for d in op.deps:
                    d.need_inc = True
        for ops_ in self.groups.values():
            for o in ops_:
                o.need_inc = True
        for e in "TVSPY":
            for op in self.ops[e]:
                if op.dma is not None:
                    op.need_inc = True
        gtotal = {g: 16 * len(ops_) for g, ops_ in self.groups.items()}
        for e in "TVSPY":
            c = 0
            for op in self.ops[e]:
                if op.eng == "P" and op.dma not in (None, "q"):
                    op.val = 0
                    continue
                if op.need_inc:
                    c += 16 if (op.dma is not None) else 1
                op.val = c

        burst_end = {"Y": {}, "P": {}}
        for e in ("Y", "P"):
            bid = 0
            for op in self.ops[e]:
                if op.deps:
                    bid += 1
                op.burst = bid
                burst_end[e][bid] = op.val

        def sem_of(d):
            if d.eng == "P" and d.dma not in (None, "q"):
                return ("g:" + d.dma, gtotal[d.dma])
            if d.eng in ("Y", "P"):
                return (d.eng, burst_end[d.eng][d.burst])
            return (d.eng, d.val)

        waited = {"Y": set(), "P": set()}
        for e in "TVSPY":
            for op in self.ops[e]:
                for d in op.deps:
                    s, v = sem_of(d)
                    if s in waited:
                        waited[s].add(v)

        def run_engine(e, eng_obj):
            water = {}
            for op in self.ops[e]:
                waits = {}
                for d in op.deps:
                    s, v = sem_of(d)
                    if waits.get(s, -1) < v:
                        waits[s] = v
                for s, v in sorted(waits.items()):
                    if water.get(s, -1) >= v:
                        continue
                    eng_obj.wait_ge(sems[s], v)
                    water[s] = v
                ins = op.fn(eng_obj)
                if op.need_inc:
                    if op.eng == "P" and op.dma not in (None, "q"):
                        ins.then_inc(sems["g:" + op.dma], 16)
                    elif op.dma is not None:
                        ins.then_inc(sems[op.eng], 16)
                    else:
                        ins.then_inc(sems[op.eng], 1)
                if e in ("Y", "P") and op.need_inc and op.dma in (None, "q") and op.val in waited[e]:
                    eng_obj.wait_ge(sems[e], op.val)
                    water[e] = max(water.get(e, -1), op.val)

        @block.tensor
        def _(t):
            run_engine("T", t)

        @block.vector
        def _(v):
            run_engine("V", v)

        @block.scalar
        def _(s):
            run_engine("S", s)

        @block.gpsimd
        def _(p):
            run_engine("P", p)

        @block.sync
        def _(y):
            run_engine("Y", y)


def build_program():
    nc = bass.Bass("TRN2", target_bir_lowering=False)

    x_in = nc.dram_tensor("x", [S, D], F32, kind="ExternalInput").ap()
    xo_in = nc.dram_tensor("xown", [TOK, D], F32, kind="ExternalInput").ap()
    wq_in = nc.dram_tensor("wq", [D, 256], F32, kind="ExternalInput").ap()
    wk_in = nc.dram_tensor("wk", [D, 64], F32, kind="ExternalInput").ap()
    wv_in = nc.dram_tensor("wv", [D, 64], F32, kind="ExternalInput").ap()
    wo_in = nc.dram_tensor("wo", [D, D], F32, kind="ExternalInput").ap()
    cw1_in = nc.dram_tensor("cw1", [D, D], F32, kind="ExternalInput").ap()
    cw2_in = nc.dram_tensor("cw2", [D, 512], F32, kind="ExternalInput").ap()
    cw3_in = nc.dram_tensor("cw3", [512, 256], F32, kind="ExternalInput").ap()
    cw4_in = nc.dram_tensor("cw4", [256, 1], F32, kind="ExternalInput").ap()
    gw1_in = nc.dram_tensor("gw1", [D, 512], F32, kind="ExternalInput").ap()
    gw2_in = nc.dram_tensor("gw2", [512, 256], F32, kind="ExternalInput").ap()
    gw3_in = nc.dram_tensor("gw3", [256, 1], F32, kind="ExternalInput").ap()
    cb1_in = nc.dram_tensor("cb1", [D], F32, kind="ExternalInput").ap()
    cb2_in = nc.dram_tensor("cb2", [512], F32, kind="ExternalInput").ap()
    cb3_in = nc.dram_tensor("cb3", [256], F32, kind="ExternalInput").ap()
    cb4_in = nc.dram_tensor("cb4", [1], F32, kind="ExternalInput").ap()
    gb1_in = nc.dram_tensor("gb1", [512], F32, kind="ExternalInput").ap()
    gb2_in = nc.dram_tensor("gb2", [256], F32, kind="ExternalInput").ap()
    gb3_in = nc.dram_tensor("gb3", [1], F32, kind="ExternalInput").ap()
    ar_in = nc.dram_tensor("ar", [1], F32, kind="ExternalInput").ap()
    cosq_in = nc.dram_tensor("cosq", [128, S], BF16, kind="ExternalInput").ap()
    sinq_in = nc.dram_tensor("sinq", [128, S], BF16, kind="ExternalInput").ap()
    cosk_in = nc.dram_tensor("cosk", [128, S], BF16, kind="ExternalInput").ap()
    sink_in = nc.dram_tensor("sink", [128, S], BF16, kind="ExternalInput").ap()
    rp_in = nc.dram_tensor("rp", [128, 128], BF16, kind="ExternalInput").ap()
    idb_in = nc.dram_tensor("idb", [128, 128], BF16, kind="ExternalInput").ap()
    idf_in = nc.dram_tensor("idf", [128, 128], F32, kind="ExternalInput").ap()
    oneb_in = nc.dram_tensor("oneb", [128, 128], BF16, kind="ExternalInput").ap()
    msk_in = nc.dram_tensor("msk", [128, 384], BF16, kind="ExternalInput").ap()
    out_d = nc.dram_tensor("out", [TOK, D], F32, kind="ExternalOutput").ap()
    yidx_in = nc.dram_tensor("yidx", [128, 64], mybir.dt.int32, kind="ExternalInput").ap()
    ydram_h = nc.dram_tensor("ydram", [S, 256], BF16)
    ygat_h = nc.dram_tensor("ygat", [8 * S, 256], BF16)
    ydram = ydram_h.ap()
    ygat = ygat_h.ap()

    es = contextlib.ExitStack()
    SB = lambda n, sh, dt: es.enter_context(nc.sbuf_tensor(n, sh, dt))
    b = Bld()

    with es:
        XN = SB("XN", [128, NT * 1024], F32)
        BIGA = SB("BIGA", [128, 16384], BF16)
        KD = SB("KD", [128, 8192], BF16)
        VT = SB("VT", [128, 65 * NCH], BF16)
        XCH = SB("XCH", [128, 2048], F32)
        SCR = SB("SCR", [128, 4096], BF16)
        PTA = SB("PTA", [128, 2048], BF16)
        CS = SB("CS", [128, 1024], BF16)
        RSB = SB("RSB", [128, 2048], BF16)
        SCB = SB("SCB", [128, 768], BF16)
        STT = SB("STT", [128, 32], F32)
        SUMXN = SB("SUMXN", [128, 8], F32)
        MAXA = SB("MAXA", [1, 4], F32)
        MScr = SB("MScr", [1, 8], F32)
        NEGB = SB("NEGB", [128, 2], F32)
        NEGBB = SB("NEGBB", [1, 2], BF16)
        GFT = SB("GFT", [1, 16], F32)
        XBARB = SB("XBARB", [128, 8], BF16)
        GFH1 = SB("GFH1", [128, 4], BF16)
        GFH2 = SB("GFH2", [128, 2], BF16)
        CTL = SB("CTL", [128, 4], F32)
        CTLB = SB("CTLB", [1, 1024], F32)
        TMP1 = SB("TMP1", [128, 128], F32)
        TMP2 = SB("TMP2", [128, 128], F32)
        TMPQ = SB("TMPQ", [128, 512], F32)
        YST = SB("YST", [128, 512], BF16)
        RD = SB("RD", [128, 8], F32)
        IDB = SB("IDB", [128, 128], BF16)
        IDF = SB("IDF", [128, 128], F32)
        ONEB = SB("ONEB", [128, 128], BF16)
        MSK = SB("MSK", [128, 384], BF16)
        RP = SB("RPs", [128, 128], BF16)
        WQB = SB("WQB", [128, 2048], BF16)
        WKB = SB("WKB", [128, 512], BF16)
        WVB = SB("WVB", [128, 512], BF16)
        WBIG = SB("WBIG", [128, 8192], BF16)
        CN2 = SB("CN2", [128, 4096], BF16)
        CN3 = SB("CN3", [128, 1024], BF16)
        CN4 = SB("CN4", [128, 2], BF16)
        GC1 = SB("GC1", [128, 4096], BF16)
        GC2 = SB("GC2", [128, 1024], BF16)
        GC3 = SB("GC3", [128, 2], BF16)
        CB1 = SB("CB1", [128, 8], F32)
        CB2 = SB("CB2", [128, 4], F32)
        CB3 = SB("CB3", [128, 2], F32)
        CB4 = SB("CB4", [1, 2], F32)
        GB1 = SB("GB1", [128, 4], F32)
        GB2 = SB("GB2", [128, 2], F32)
        GB3 = SB("GB3", [1, 2], F32)
        ART = SB("ART", [1, 1], F32)
        CONS = SB("CONS", [1, 4], F32)
        YIX = SB("YIX", [128, 64], mybir.dt.int32)
        PS = es.enter_context(nc.psum_tensor("PS", [128, 4096], F32))
        biga_addr = nc.lookup_mloc(BIGA).addr
        kd_addr = nc.lookup_mloc(KD).addr
        QCH = [[nc.alloc_sbuf_tensor_at(f"QCH{p_}_{c_}", (128, 256), BF16,
                                        offset=biga_addr + 16384 * p_ + 512 * c_)
                for c_ in range(NCH // 2)] for p_ in range(2)]
        KCH = [nc.alloc_sbuf_tensor_at(f"KCH_{c_}", (128, 128), BF16,
                                       offset=kd_addr + 256 * c_)
               for c_ in range(NCH)]
        xn_addr = nc.lookup_mloc(XN).addr
        KBH = [nc.alloc_sbuf_tensor_at(f"KBH_{c_}", (128, 128), BF16,
                                       offset=xn_addr + 256 * c_)
               for c_ in range(NCH)]
        KBALL = nc.alloc_sbuf_tensor_at("KBALL", (128, 8192), BF16, offset=xn_addr)

        def psb(bank, off, w, p0=0, p1=128):
            return PS[p0:p1, 512 * bank + off:512 * bank + off + w]

        stg_par = [0]

        def wload(dst, srcap, kt, ncols, name):
            for k in range(kt):
                bu = stg_par[0] % 2
                stg_par[0] += 1
                stg = XCH[:, 1024 * bu:1024 * bu + ncols]
                b.emit("Y", lambda y, k=k, stg=stg, srcap=srcap: y.dma_start(
                    out=stg, in_=srcap[128 * k:128 * (k + 1), :]),
                    writes=[f"XCH:{bu}"], dma="q")
                b.emit("P", lambda p, k=k, stg=stg, dst=dst, ncols=ncols: p.tensor_copy(
                    out=dst[:, ncols * k:ncols * (k + 1)], in_=stg),
                    reads=[f"XCH:{bu}"], writes=[name])

        if not int(os.environ.get("K_NOQKV", "0")):
            b.emit("Y", lambda y: y.dma_start(out=YIX[:, :], in_=yidx_in), writes=["YIX"], dma="q")
            wload(WQB, wq_in, 8, 256, "WQB")
            wload(WKB, wk_in, 8, 64, "WKB")
            wload(WVB, wv_in, 8, 64, "WVB")

        def bload(dst, src, m, name):
            for i in range(m):
                b.emit("Y", lambda y, d=dst, s=src, i=i:
                       y.dma_start(out=d[:, i:i + 1],
                                   in_=s[128 * i:128 * (i + 1)].rearrange("(p o) -> p o", o=1)),
                       writes=[name], dma="q")

        if not int(os.environ.get("K_NOBL", "0")):
            bload(CB1, cb1_in, 8, "CB1")
            bload(CB2, cb2_in, 4, "CB2")
            bload(CB3, cb3_in, 2, "CB3")
            bload(GB1, gb1_in, 4, "GB1")
            bload(GB2, gb2_in, 2, "GB2")
            for dst_, src_, nm_ in ((CB4, cb4_in, "CB4"), (GB3, gb3_in, "GB3"), (ART, ar_in, "ART")):
                b.emit("Y", lambda y, d=dst_, s=src_: y.dma_start(out=d[0:1, 0:1], in_=s.rearrange("(x o) -> x o", x=1)),
                       writes=[nm_], dma="q")
        if not int(os.environ.get("K_NOTB", "0")):
            for dst_, src_, nm_ in ((IDB, idb_in, "IDB"), (IDF, idf_in, "IDF"), (ONEB, oneb_in, "ONEB"),
                                    (MSK, msk_in, "MSK"), (RP, rp_in, "RPc")):
                b.emit("Y", lambda y, d=dst_, s=src_: y.dma_start(out=d[:, :], in_=s), writes=[nm_], dma="q")
            b.emit("V", lambda v: v.memset(MAXA[:, :], -1e30), writes=["MAXA"])
            b.emit("V", lambda v: v.memset(SUMXN[:, :], 0.0), writes=["SUMXN"])
            b.emit("V", lambda v: v.memset(VT[:, :], 1.0), writes=["VTones"])
        b.emit("V", lambda v: v.memset(KD[64:128, :], 0.0), writes=["KDzero"])
        b.emit("V", lambda v: v.memset(KBALL[0:64, :], 0.0), writes=["KBzero", "ovXN"])
        b.emit("V", lambda v: v.memset(CONS[:, :], 0.0), writes=["CONS"])

        def p1_A(c):
            bu = c % 2
            r0 = 128 * c
            xch = XCH[:, 1024 * bu:1024 * (bu + 1)]
            xnb = SCR[:, 1024 * bu:1024 * (bu + 1)]
            xnt0 = 2048 + 1024 * bu
            st = 16 * bu
            cso = 512 * bu
            ro = 1024 * bu
            rXCH, rXNB, rXNT = f"XCH:{bu}", f"XNB:{bu}", f"XNT:{bu}"
            rST, rCS, rRSB = f"STT:{bu}", f"CS:{bu}", f"RSB:{bu}"

            b.emit("Y", lambda y: y.dma_start(out=xch, in_=x_in[r0:r0 + 128, :]),
                   writes=[rXCH], dma="q")
            for j, tab in enumerate((cosq_in, sinq_in, cosk_in, sink_in)):
                b.emit("Y", lambda y, j=j, tab=tab: y.dma_start(
                    out=CS[:, cso + 128 * j:cso + 128 * (j + 1)], in_=tab[:, r0:r0 + 128]),
                    writes=[rCS], dma="q")

            b.emit("V", lambda v: v.bn_stats(out=STT[:, st:st + 6], in_=xch[:, 0:512]),
                   reads=[rXCH], writes=[rST])
            b.emit("V", lambda v: v.bn_stats(out=STT[:, st + 6:st + 12], in_=xch[:, 512:1024]),
                   reads=[rXCH], writes=[rST])
            b.emit("V", lambda v: v.bn_aggr(out=STT[:, st + 12:st + 14], in_=STT[:, st:st + 12]),
                   reads=[rST], writes=[rST])
            b.emit("V", lambda v: v.tensor_scalar_add(out=STT[:, st + 14:st + 15],
                                                      in0=STT[:, st + 13:st + 14], scalar1=EPS),
                   reads=[rST], writes=[rST])
            b.emit("S", lambda s: s.activation(out=STT[:, st + 15:st + 16],
                                               in_=STT[:, st + 14:st + 15], func=AF.Ln),
                   reads=[rST], writes=[rST])
            b.emit("S", lambda s: s.activation(out=STT[:, st + 14:st + 15],
                                               in_=STT[:, st + 15:st + 16], func=AF.Exp, scale=-0.5),
                   reads=[rST], writes=[rST])
            b.emit("V", lambda v: v.tensor_scalar(out=xnb, in0=xch,
                                                  scalar1=STT[:, st + 12:st + 13],
                                                  scalar2=STT[:, st + 14:st + 15],
                                                  op0=ALU.subtract, op1=ALU.mult),
                   reads=[rXCH, rST, "ovS"], writes=[rXNB])

            for f in range(8):
                bank, off = 4 + (f // 4), 128 * (f % 4)
                b.emit("T", lambda t, f=f, bank=bank, off=off: t.matmul(
                    psb(bank, off, 128), xnb[:, 128 * f:128 * (f + 1)], IDB[:, :],
                    start=True, stop=True),
                    reads=[rXNB, "IDB", "ovS"], writes=[f"pb{bank}"])
            for g2 in range(2):
                bank = 4 + g2
                b.emit("V", lambda v, g2=g2, bank=bank: v.tensor_copy(
                    out=SCR[:, xnt0 + 512 * g2:xnt0 + 512 * (g2 + 1)], in_=psb(bank, 0, 512)),
                    reads=[f"pb{bank}", "ovS"], writes=[rXNT])

        def p1_B(c):
            bu = c % 2
            r0 = 128 * c
            xch = XCH[:, 1024 * bu:1024 * (bu + 1)]
            xnb = SCR[:, 1024 * bu:1024 * (bu + 1)]
            xnt0 = 2048 + 1024 * bu
            st = 16 * bu
            cso = 512 * bu
            ro = 1024 * bu
            rXCH, rXNB, rXNT = f"XCH:{bu}", f"XNB:{bu}", f"XNT:{bu}"
            rST, rCS, rRSB = f"STT:{bu}", f"CS:{bu}", f"RSB:{bu}"

            for m in range(4):
                bank = m
                for k in range(8):
                    if m < 2:
                        st_ap = WQB.ap()[:, 256 * k + 128 * m:256 * k + 128 * m + 128]
                        wnm = "WQB"
                    elif m == 2:
                        st_ap = WKB.ap()[:, 64 * k:64 * (k + 1)]
                        wnm = "WKB"
                    else:
                        st_ap = WVB.ap()[:, 64 * k:64 * (k + 1)]
                        wnm = "WVB"
                    p1_ = 128 if m < 2 else 64
                    b.emit("T", lambda t, k=k, bank=bank, st_ap=st_ap, p1_=p1_: t.matmul(
                        psb(bank, 0, 128, 0, p1_), st_ap,
                        SCR[:, xnt0 + 128 * k:xnt0 + 128 * (k + 1)],
                        start=(k == 0), stop=(k == 7)),
                        reads=[rXNT, wnm], writes=[f"pb{bank}"], accum=("p1", c, m))
            for m in range(2):
                b.emit("S", lambda s, m=m: s.activation(
                    out=RSB[:, ro + 128 * m:ro + 128 * (m + 1)], in_=psb(m, 0, 128), func=AF.Copy),
                    reads=[f"pb{m}"], writes=[rRSB])
            b.emit("S", lambda s: s.activation(
                out=RSB[0:64, ro + 256:ro + 384], in_=psb(2, 0, 128, 0, 64), func=AF.Copy),
                reads=["pb2"], writes=[rRSB])
            b.emit("S", lambda s: s.activation(out=RSB[0:64, ro + 768:ro + 896],
                                               in_=psb(3, 0, 128, 0, 64), func=AF.Copy),
                   reads=["pb3"], writes=[rRSB])

            for m in range(2):
                b.emit("T", lambda t, m=m: t.matmul(
                    psb(m, 128, 128), RP[:, :], RSB[:, ro + 128 * m:ro + 128 * (m + 1)],
                    start=True, stop=True),
                    reads=[rRSB, "RPc"], writes=[f"pb{m}"])
            b.emit("T", lambda t: t.matmul(
                psb(2, 128, 128, 0, 64), RP[0:64, 0:64], RSB[0:64, ro + 256:ro + 384],
                start=True, stop=True),
                reads=[rRSB, "RPc"], writes=["pb2"])
            for m in range(2):
                b.emit("S", lambda s, m=m: s.activation(
                    out=RSB[:, ro + 384 + 128 * m:ro + 384 + 128 * (m + 1)],
                    in_=psb(m, 128, 128), func=AF.Copy),
                    reads=[f"pb{m}"], writes=[rRSB])
            b.emit("S", lambda s: s.activation(
                out=RSB[0:64, ro + 640:ro + 768],
                in_=psb(2, 128, 128, 0, 64), func=AF.Copy),
                reads=["pb2"], writes=[rRSB])

            qho = 128 * (c % 2)
            for m, (dst, dnm, tco, tso, pl) in enumerate((
                    (QCH[0][c // 2][:, qho:qho + 128], f"QP0:{c}", 0, 128, 128),
                    (QCH[1][c // 2][:, qho:qho + 128], f"QP1:{c}", 0, 128, 128),
                    (KCH[c][0:64, :], f"KD:{c}", 256, 384, 64))):
                b.emit("V", lambda v, m=m, tco=tco, pl=pl: v.tensor_tensor(
                    out=TMP1[0:pl, :], in0=RSB[0:pl, ro + 128 * m:ro + 128 * (m + 1)],
                    in1=CS[0:pl, cso + tco:cso + tco + 128], op=ALU.mult),
                    reads=[rRSB, rCS], writes=["TMP1"])
                b.emit("V", lambda v, m=m, tso=tso, pl=pl: v.tensor_tensor(
                    out=TMP2[0:pl, :], in0=RSB[0:pl, ro + 384 + 128 * m:ro + 384 + 128 * (m + 1)],
                    in1=CS[0:pl, cso + tso:cso + tso + 128], op=ALU.mult),
                    reads=[rRSB, rCS], writes=["TMP2"])
                b.emit("V", lambda v, dst=dst, pl=pl: v.tensor_tensor(
                    out=dst, in0=TMP1[0:pl, :], in1=TMP2[0:pl, :], op=ALU.add),
                    reads=["TMP1", "TMP2", "ovA" if m < 2 else "ovK"], writes=[dnm])
            b.emit("Y", lambda y, c=c: y.dma_start(out=KBH[c][64:128, :], in_=KCH[c][0:64, :]),
                   reads=[f"KD:{c}", "KBzero", "ovXN"], writes=[f"KB:{c}"], dma="q")

            b.emit("T", lambda t: t.matmul(psb(3, 128, 64, 0, 128), RSB[0:64, ro + 768:ro + 896],
                                           IDB[0:64, 0:64], start=True, stop=True),
                   reads=[rRSB, "IDB"], writes=["pb3"])
            b.emit("S", lambda s, c=c: s.activation(out=VT[:, 65 * c:65 * c + 64],
                                                    in_=psb(3, 128, 64), func=AF.Copy),
                   reads=["pb3"], writes=[f"VT:{c}"])

            for m, snm in enumerate((f"QP0:{c}", f"QP1:{c}", f"KD:{c}")):
                src = (QCH[0][c // 2][:, qho:qho + 128] if m == 0 else
                       QCH[1][c // 2][:, qho:qho + 128] if m == 1 else
                       KCH[c][0:64, :])
                pl = 128 if m < 2 else 64
                b.emit("S", lambda s, m=m, src=src, pl=pl: s.activation(
                    out=SCB[0:pl, 256 * m:256 * m + 128], in_=src, func=AF.Square),
                    reads=[snm], writes=["SCB"])
            for m in range(3):
                pl = 128 if m < 2 else 64
                b.emit("T", lambda t, m=m, pl=pl: t.matmul(
                    psb(m, 256, 128, 0, 1), ONEB[0:pl, 0:1], SCB[0:pl, 256 * m:256 * m + 128],
                    start=True, stop=True),
                    reads=["SCB", "ONEB"], writes=[f"pb{m}"])
            for m in range(3):
                b.emit("V", lambda v, m=m: v.reduce_max(
                    out=MScr[0:1, m:m + 1], in_=psb(m, 256, 128, 0, 1), axis=AX.X),
                    reads=[f"pb{m}"], writes=["MScr"])
            b.emit("V", lambda v: v.tensor_max(out=MAXA[0:1, 0:3], in0=MAXA[0:1, 0:3],
                                               in1=MScr[0:1, 0:3]),
                   reads=["MScr", "MAXA"], writes=["MAXA"])

            for f in range(8):
                b.emit("T", lambda t, f=f: t.matmul(
                    psb(3, 256 + f, 1), xnb[:, 128 * f:128 * (f + 1)], ONEB[:, 0:1],
                    start=True, stop=True),
                    reads=[rXNB, "ONEB"], writes=["pb3"])
            b.emit("V", lambda v: v.tensor_add(out=SUMXN[:, :], in0=SUMXN[:, :], in1=psb(3, 256, 8)),
                   reads=["pb3", "SUMXN"], writes=["SUMXN"])

        gctr = [0]
        ydr_all = []

        def neg_bias(p):
            g0 = 8 + 4 * p
            b.emit("S", lambda s, p=p, g0=g0: s.activation(out=GFT[0:1, g0:g0 + 1],
                                                           in_=MAXA[0:1, p:p + 1], func=AF.Ln),
                   reads=["MAXA"], writes=["GFTb"])
            b.emit("S", lambda s, g0=g0: s.activation(out=GFT[0:1, g0 + 1:g0 + 2],
                                                      in_=MAXA[0:1, 2:3], func=AF.Ln),
                   reads=["MAXA", "GFTb"], writes=["GFTb"])
            b.emit("V", lambda v, g0=g0: v.tensor_add(out=GFT[0:1, g0 + 2:g0 + 3],
                                                      in0=GFT[0:1, g0:g0 + 1], in1=GFT[0:1, g0 + 1:g0 + 2]),
                   reads=["GFTb"], writes=["GFTb"])
            b.emit("S", lambda s, g0=g0: s.activation(out=GFT[0:1, g0 + 3:g0 + 4], in_=GFT[0:1, g0 + 2:g0 + 3],
                                                      func=AF.Exp, scale=0.5, bias=CONS[0:1, 0:1]),
                   reads=["GFTb", "CONS"], writes=["GFTb"])
            b.emit("V", lambda v, p=p, g0=g0: v.tensor_scalar_mul(out=NEGBB[0:1, p:p + 1],
                                                                  in0=GFT[0:1, g0 + 3:g0 + 4], scalar1=-1.0),
                   reads=["GFTb"], writes=["NEGBB"])
            b.emit("T", lambda t, p=p: t.matmul(psb(4, 400 + p, 1), ONEB[0:1, :],
                                                NEGBB[0:1, p:p + 1], start=True, stop=True),
                   reads=["NEGBB", "ONEB"], writes=["pb4"])
            b.emit("V", lambda v, p=p: v.tensor_copy(out=NEGB[:, p:p + 1], in_=psb(4, 400 + p, 1)),
                   reads=["pb4"], writes=["NEGB"])

        def qk_qb(p, qb):
            inst = 2 * qb + p
            QPn = "QP0" if p == 0 else "QP1"
            qch = QCH[p][qb]
            nkt = 2 * (qb + 1)
            if P2M < 2:
                return
            for g in range(nkt // 2):
                gpar = gctr[0] % 2
                gctr[0] += 1
                base = 0 if gpar == 0 else 2
                pta0 = 0 if gpar == 0 else 1024
                rPT = f"PTA:{gpar}"
                slots = []
                for gi in range(2):
                    kt = 2 * g + gi
                    for h in (0, 1):
                        si = 2 * gi + h
                        bank = base + si // 2
                        off = 256 * (si % 2)
                        pos = pta0 + 256 * si
                        slots.append((kt, h, bank, off, pos))
                qk_l, av_l = [], []
                for (kt, h, bank, off, pos) in slots:
                    kst = KCH[kt] if h == 0 else KBH[kt]
                    knm = f"KD:{kt}" if h == 0 else f"KB:{kt}"
                    qk_l.append(lambda kst=kst, bank=bank, off=off, qch=qch, knm=knm, h=h: b.emit(
                        "T", lambda t, kst=kst, bank=bank, off=off, qch=qch: t.matmul(
                            psb(bank, off, QB, 0, 128),
                            kst[:, :], qch[:, :],
                            start=True, stop=True),
                        reads=[knm, "KDzero" if h == 0 else "KBzero",
                               f"{QPn}:{2 * qb}", f"{QPn}:{2 * qb + 1}", "ovA", "ovK", "ovXN"],
                        writes=[f"pb{bank}"]))

                def mk_exp(base=base, pta0=pta0, p=p, rPT=rPT, slots=slots, qb=qb):
                    b.emit("S", lambda s: s.activation(
                        out=PTA[:, pta0:pta0 + 1024], in_=PS[:, 512 * base:512 * base + 1024],
                        func=AF.Exp, bias=NEGB[:, p:p + 1]),
                        reads=["NEGB", f"pb{base}", f"pb{base + 1}"], writes=[rPT])
                    if P2M >= 3:
                        for (kt, h, bank, off, pos) in slots:
                            if kt >= 2 * qb:
                                moff = 128 if kt == 2 * qb else 0
                                b.emit("V", lambda v, pos=pos, moff=moff: v.tensor_tensor(
                                    out=PTA[:, pos:pos + 256], in0=PTA[:, pos:pos + 256],
                                    in1=MSK[:, moff:moff + 256], op=ALU.mult),
                                    reads=[rPT, "MSK"], writes=[rPT])

                if P2M >= 4:
                    for (kt, h, bank, off, pos) in slots:
                        for j in range(2):
                            yb = 4 + 2 * h + j
                            av_l.append(lambda pos=pos, kt=kt, j=j, yb=yb, rPT=rPT,
                                        st=(kt == 0), lst=(kt == nkt - 1), inst=inst, h=h: b.emit(
                                "T", lambda t, pos=pos, yb=yb, kt=kt, st=st, lst=lst: t.matmul(
                                    psb(yb, 0, 65), PTA[:, pos + 128 * j:pos + 128 * j + 128],
                                    VT[:, 65 * kt:65 * (kt + 1)], start=st, stop=lst),
                                reads=[rPT, f"VT:{kt}", "VTones"], writes=[f"pb{yb}"],
                                accum=("av", inst, h, j)))
                tasks.append((qk_l, mk_exp, av_l, None))
            if P2M < 5:
                return

            def mk_norm(p=p, qb=qb, inst=inst):
                for h in range(2):
                    for j in range(2):
                        yb = 4 + 2 * h + j
                        sl = (inst % 2) * 4 + 2 * h + j
                        b.emit("V", lambda v, yb=yb, sl=sl: v.reciprocal(
                            out=RD[:, sl:sl + 1], in_=psb(yb, 64, 1)),
                            reads=[f"pb{yb}"], writes=[f"RD:{sl}"])
                        b.emit("V", lambda v, yb=yb, sl=sl: v.tensor_scalar_mul(
                            out=YST[:, 64 * sl:64 * (sl + 1)], in0=psb(yb, 0, 64),
                            scalar1=RD[:, sl:sl + 1]),
                            reads=[f"pb{yb}", f"RD:{sl}"], writes=[f"YST:{sl}"])
                        lh = 2 * p + h
                        r0y = QB * qb + 128 * j
                        b.emit("Y", lambda y, r0y=r0y, lh=lh, sl=sl: y.dma_start(
                            out=ydram[r0y:r0y + 128, 64 * lh:64 * (lh + 1)],
                            in_=YST[:, 64 * sl:64 * (sl + 1)]),
                            reads=[f"YST:{sl}"], writes=[f"ydr:{r0y}:{lh}"], dma="q")
                        ydr_all.append(f"ydr:{r0y}:{lh}")
            tasks[-1] = tasks[-1][:3] + (mk_norm,)

        if UPTO >= 1:
            n1 = min(NCH, int(os.environ.get("K_P1N", "999")))
            for c in range(n1):
                p1_A(c)
                if c >= 1:
                    p1_B(c - 1)
            if n1 >= 1:
                p1_B(n1 - 1)
        if UPTO >= 2:
            neg_bias(0)
            neg_bias(1)
            if UPTO >= 3 and not int(os.environ.get("K_NOWL", "0")):
                wload(WBIG, cw1_in, 8, 1024, "WBIG")
                wload(CN2, cw2_in, 8, 512, "CN2")
                wload(CN3, cw3_in, 4, 256, "CN3")
                wload(CN4, cw4_in, 2, 1, "CN4")
                wload(GC1, gw1_in, 8, 512, "GC1")
                wload(GC2, gw2_in, 4, 256, "GC2")
                wload(GC3, gw3_in, 2, 1, "GC3")
            tasks = []
            for qb in range(min(NQB, P2N)):
                qk_qb(0, qb)
                qk_qb(1, qb)
            prev = None
            for tk in tasks:
                for f in tk[0]:
                    f()
                tk[1]()
                if prev is not None:
                    for f in prev[2]:
                        f()
                    if prev[3] is not None:
                        prev[3]()
                prev = tk
            if prev is not None:
                for f in prev[2]:
                    f()
                if prev[3] is not None:
                    prev[3]()

        def early_out():
            b.emit("V", lambda v: v.memset(XN[:, :], 0.25), writes=[f"XN:{t}" for t in range(NT)])
            for tt in range(NT):
                b.emit("Y", lambda y, tt=tt: y.dma_start(
                    out=out_d[128 * tt:128 * (tt + 1), :], in_=XN[:, 1024 * tt:1024 * (tt + 1)]),
                    reads=[f"XN:{tt}"], writes=["outw"], dma="q")
            b.emit("P", lambda p: p.engine_nop(), reads=["outw"])
            block = es.enter_context(nc.Block())
            b.finalize(nc, es, block)

        if UPTO < 3:
            early_out()
            return nc

        if not NOCC:
            op_cc = b.emit("P", lambda p: p.collective_compute(
                "AllGather", ALU.bypass,
                replica_groups=[list(range(8))],
                ins=[ydram_h.ap().opt()], outs=[ygat_h.ap().opt()]),
                reads=([ydr_all[-1]] if ydr_all else ["ydram"]), writes=["ygat"])
            op_cc.need_inc = True

        for tt in (range(0) if NOPRE else range(NT)):
            bu = tt % 2
            xch = XCH[:, 1024 * bu:1024 * (bu + 1)]
            st = 16 * bu
            rXCH, rST = f"XCH:{bu}", f"STT:{bu}"
            b.emit("Y", lambda y, tt=tt, xch=xch: y.dma_start(
                out=xch, in_=xo_in[128 * tt:128 * (tt + 1), :]),
                writes=[rXCH], dma="q")
            b.emit("V", lambda v, xch=xch, st=st: v.bn_stats(out=STT[:, st:st + 6], in_=xch[:, 0:512]),
                   reads=[rXCH], writes=[rST])
            b.emit("V", lambda v, xch=xch, st=st: v.bn_stats(out=STT[:, st + 6:st + 12], in_=xch[:, 512:1024]),
                   reads=[rXCH], writes=[rST])
            b.emit("V", lambda v, st=st: v.bn_aggr(out=STT[:, st + 12:st + 14], in_=STT[:, st:st + 12]),
                   reads=[rST], writes=[rST])
            b.emit("V", lambda v, st=st: v.tensor_scalar_add(out=STT[:, st + 14:st + 15],
                                                             in0=STT[:, st + 13:st + 14], scalar1=EPS),
                   reads=[rST], writes=[rST])
            b.emit("S", lambda s, st=st: s.activation(out=STT[:, st + 15:st + 16],
                                                      in_=STT[:, st + 14:st + 15], func=AF.Ln),
                   reads=[rST], writes=[rST])
            b.emit("S", lambda s, st=st: s.activation(out=STT[:, st + 14:st + 15],
                                                      in_=STT[:, st + 15:st + 16], func=AF.Exp, scale=-0.5),
                   reads=[rST], writes=[rST])
            b.emit("V", lambda v, tt=tt, xch=xch, st=st: v.tensor_scalar(
                out=XN[:, 1024 * tt:1024 * (tt + 1)], in0=xch,
                scalar1=STT[:, st + 12:st + 13], scalar2=STT[:, st + 14:st + 15],
                op0=ALU.subtract, op1=ALU.mult),
                reads=[rXCH, rST], writes=[f"XN:{tt}"] + (["ovXN"] if tt == 0 else []))

        if NOFFN:
            early_out()
            return nc
        b.emit("V", lambda v: v.tensor_scalar_mul(out=XBARB[:, 0:8], in0=SUMXN[:, :], scalar1=1.0 / S),
               reads=["SUMXN"], writes=["XBARB"])
        for m in range(4):
            for k in range(8):
                b.emit("T", lambda t, m=m, k=k: t.matmul(
                    psb(0, m, 1), GC1.ap()[:, 512 * k + 128 * m:512 * k + 128 * m + 128],
                    XBARB[:, k:k + 1], start=(k == 0), stop=(k == 7)),
                    reads=["GC1", "XBARB"], writes=["pb0"], accum=("gc1", m))
        for m in range(4):
            b.emit("S", lambda s, m=m: s.activation(out=GFH1[:, m:m + 1], in_=psb(0, m, 1),
                                                    func=AF.Gelu, bias=GB1[:, m:m + 1]),
                   reads=["pb0", "GB1"], writes=["GFH1"])
        for m in range(2):
            for k in range(4):
                b.emit("T", lambda t, m=m, k=k: t.matmul(
                    psb(1, m, 1), GC2.ap()[:, 256 * k + 128 * m:256 * k + 128 * m + 128],
                    GFH1[:, k:k + 1], start=(k == 0), stop=(k == 3)),
                    reads=["GC2", "GFH1"], writes=["pb1"], accum=("gc2", m))
        for m in range(2):
            b.emit("S", lambda s, m=m: s.activation(out=GFH2[:, m:m + 1], in_=psb(1, m, 1),
                                                    func=AF.Gelu, bias=GB2[:, m:m + 1]),
                   reads=["pb1", "GB2"], writes=["GFH2"])
        for k in range(2):
            b.emit("T", lambda t, k=k: t.matmul(
                psb(2, 0, 1, 0, 1), GC3.ap()[:, k:k + 1], GFH2[:, k:k + 1],
                start=(k == 0), stop=(k == 1)),
                reads=["GC3", "GFH2"], writes=["pb2"], accum=("gc3", 0))
        b.emit("V", lambda v: v.tensor_scalar_mul(out=GB3[0:1, 1:2], in0=GB3[0:1, 0:1], scalar1=0.5),
               reads=["GB3"], writes=["GB3"])
        b.emit("V", lambda v: v.tensor_scalar_mul(out=CB4[0:1, 1:2], in0=CB4[0:1, 0:1], scalar1=0.5),
               reads=["CB4"], writes=["CB4"])
        b.emit("S", lambda s: s.activation(out=GFT[0:1, 0:1], in_=psb(2, 0, 1, 0, 1),
                                           func=AF.Tanh, scale=0.5, bias=GB3[0:1, 1:2]),
               reads=["pb2", "GB3"], writes=["GFTg"])
        b.emit("V", lambda v: v.tensor_scalar(out=GFT[0:1, 1:2], in0=GFT[0:1, 0:1],
                                              scalar1=0.5, scalar2=0.5, op0=ALU.mult, op1=ALU.add),
               reads=["GFTg"], writes=["GFTg"])
        b.emit("V", lambda v: v.tensor_mul(out=GFT[0:1, 2:3], in0=GFT[0:1, 1:2], in1=ART[0:1, 0:1]),
               reads=["GFTg", "ART"], writes=["GFTg"])
        b.emit("V", lambda v: v.tensor_scalar_mul(out=GFT[0:1, 3:4], in0=GFT[0:1, 2:3], scalar1=1.4),
               reads=["GFTg"], writes=["GFTg"])
        b.emit("V", lambda v: v.tensor_sub(out=GFT[0:1, 4:5], in0=GFT[0:1, 3:4], in1=ART[0:1, 0:1]),
               reads=["GFTg", "ART"], writes=["GFTg"])
        b.emit("V", lambda v: v.tensor_scalar_add(out=GFT[0:1, 5:6], in0=GFT[0:1, 4:5], scalar1=0.1),
               reads=["GFTg"], writes=["GFTg"])
        b.emit("V", lambda v: v.tensor_scalar_mul(out=GFT[0:1, 6:7], in0=ART[0:1, 0:1], scalar1=0.3),
               reads=["ART"], writes=["GFTg"])
        b.emit("V", lambda v: v.tensor_add(out=GFT[0:1, 7:8], in0=GFT[0:1, 5:6], in1=GFT[0:1, 6:7]),
               reads=["GFTg"], writes=["GFTg"])

        def ffn_cycle(cyc):
            for tc in range(NTCH):
                t0 = TCH * tc
                for ttl in range(8):
                    tt = 8 * tc + ttl
                    bu = ttl % 2
                    cdst = SCR[:, 1024 * bu:1024 * (bu + 1)]
                    first = (cyc == 0 and tc == 0 and ttl == 0)
                    b.emit("V", lambda v, tt=tt, cdst=cdst: v.tensor_copy(
                        out=cdst, in_=XN[:, 1024 * tt:1024 * (tt + 1)]),
                        reads=[f"XN:{tt}"] + ([] if first else ["ovS"]),
                        writes=[f"XNB:{bu}"] + (["ovS"] if first else []))
                    for f in range(8):
                        bank, off = 6 + (f // 4), 128 * (f % 4)
                        b.emit("T", lambda t, f=f, bank=bank, off=off, cdst=cdst: t.matmul(
                            psb(bank, off, 128), cdst[:, 128 * f:128 * (f + 1)], IDB[:, :],
                            start=True, stop=True),
                            reads=[f"XNB:{bu}", "IDB", "ovS"], writes=[f"pb{bank}"])
                    for f in range(8):
                        bank, off = 6 + (f // 4), 128 * (f % 4)
                        fe = (first and f == 0)
                        b.emit("V", lambda v, f=f, bank=bank, off=off, ttl=ttl: v.tensor_copy(
                            out=BIGA[:, 1024 * f + 128 * ttl:1024 * f + 128 * (ttl + 1)],
                            in_=psb(bank, off, 128)),
                            reads=[f"pb{bank}"] + ([] if fe else ["ovA"]),
                            writes=[f"OUTBT:{f}"] + (["ovA"] if fe else []))
                gi = 0
                for m in range(8):
                    for n in range(2):
                        bank = gi % 6
                        gi += 1
                        for k in range(8):
                            b.emit("T", lambda t, m=m, n=n, k=k, bank=bank: t.matmul(
                                psb(bank, 0, 512), WBIG.ap()[:, 1024 * k + 128 * m:1024 * k + 128 * m + 128],
                                BIGA[:, 1024 * k + 512 * n:1024 * k + 512 * n + 512],
                                start=(k == 0), stop=(k == 7)),
                                reads=[f"OUTBT:{k}", "WBIG", "ovA"], writes=[f"pb{bank}"],
                                accum=("h1", cyc, tc, m, n))
                        b.emit("S", lambda s, m=m, n=n, bank=bank: s.activation(
                            out=BIGA[:, 8192 + 1024 * m + 512 * n:8192 + 1024 * m + 512 * n + 512],
                            in_=psb(bank, 0, 512), func=AF.Gelu, bias=CB1[:, m:m + 1]),
                            reads=[f"pb{bank}", "CB1", "ovA"], writes=[f"H1B:{m}"])
                for m in range(4):
                    for n in range(2):
                        bank = gi % 6
                        gi += 1
                        for k in range(8):
                            b.emit("T", lambda t, m=m, n=n, k=k, bank=bank: t.matmul(
                                psb(bank, 0, 512), CN2.ap()[:, 512 * k + 128 * m:512 * k + 128 * m + 128],
                                BIGA[:, 8192 + 1024 * k + 512 * n:8192 + 1024 * k + 512 * n + 512],
                                start=(k == 0), stop=(k == 7)),
                                reads=[f"H1B:{k}", "CN2", "ovA"], writes=[f"pb{bank}"],
                                accum=("h2", cyc, tc, m, n))
                        wk = ["ovK"] if (cyc == 0 and tc == 0 and m == 0 and n == 0) else []
                        b.emit("S", lambda s, m=m, n=n, bank=bank: s.activation(
                            out=KD[:, 1024 * m + 512 * n:1024 * m + 512 * n + 512],
                            in_=psb(bank, 0, 512), func=AF.Gelu, bias=CB2[:, m:m + 1]),
                            reads=[f"pb{bank}", "CB2"] + ([] if wk else ["ovK"]),
                            writes=[f"H2B:{m}"] + wk)
                for m in range(2):
                    for n in range(2):
                        bank = gi % 6
                        gi += 1
                        for k in range(4):
                            b.emit("T", lambda t, m=m, n=n, k=k, bank=bank: t.matmul(
                                psb(bank, 0, 512), CN3.ap()[:, 256 * k + 128 * m:256 * k + 128 * m + 128],
                                KD[:, 1024 * k + 512 * n:1024 * k + 512 * n + 512],
                                start=(k == 0), stop=(k == 3)),
                                reads=[f"H2B:{k}", "CN3", "ovK"], writes=[f"pb{bank}"],
                                accum=("h3", cyc, tc, m, n))
                        b.emit("S", lambda s, m=m, n=n, bank=bank: s.activation(
                            out=KD[:, 4096 + 1024 * m + 512 * n:4096 + 1024 * m + 512 * n + 512],
                            in_=psb(bank, 0, 512), func=AF.Gelu, bias=CB3[:, m:m + 1]),
                            reads=[f"pb{bank}", "CB3", "ovK"], writes=[f"H3B:{m}"])
                for n in range(2):
                    for k in range(2):
                        b.emit("T", lambda t, n=n, k=k: t.matmul(
                            psb(6 + n, 0, 512, 0, 1),
                            CN4.ap()[:, k:k + 1],
                            KD[:, 4096 + 1024 * k + 512 * n:4096 + 1024 * k + 512 * n + 512],
                            start=(k == 0), stop=(k == 1)),
                            reads=[f"H3B:{k}", "CN4", "ovK"], writes=[f"pb{6 + n}"],
                            accum=("z", cyc, tc, n))
                for n in range(2):
                    b.emit("S", lambda s, n=n: s.activation(
                        out=CTLB[0:1, 512 * n:512 * (n + 1)], in_=psb(6 + n, 0, 512, 0, 1),
                        func=AF.Tanh, scale=0.5, bias=CB4[0:1, 1:2]),
                        reads=[f"pb{6 + n}", "CB4"], writes=["CTLB"])
                b.emit("V", lambda v: v.tensor_scalar(
                    out=CTLB[0:1, 0:1024], in0=CTLB[0:1, 0:1024],
                    scalar1=GFT[0:1, 6:7], scalar2=GFT[0:1, 7:8],
                    op0=ALU.mult, op1=ALU.add),
                    reads=["CTLB", "GFTg"], writes=["CTLB"])
                for ttl in range(8):
                    tt = 8 * tc + ttl
                    cb = ttl % 2
                    b.emit("T", lambda t, ttl=ttl: t.matmul(
                        psb(6, 480, 1), CTLB[0:1, 128 * ttl:128 * (ttl + 1)],
                        IDF[0:1, 0:1], start=True, stop=True),
                        reads=["CTLB", "IDF"], writes=["pb6"])
                    b.emit("V", lambda v, cb=cb: v.tensor_copy(out=CTL[:, 2 * cb:2 * cb + 1],
                                                               in_=psb(6, 480, 1)),
                           reads=["pb6"], writes=[f"CTL:{cb}"])
                    b.emit("V", lambda v, cb=cb: v.tensor_scalar_mul(out=CTL[:, 2 * cb + 1:2 * cb + 2],
                                                                     in0=CTL[:, 2 * cb:2 * cb + 1],
                                                                     scalar1=-1.0),
                           reads=[f"CTL:{cb}"], writes=[f"CTL:{cb}"])
                    x0 = 1024 * tt
                    q = lambda i: XN[:, x0 + 256 * i:x0 + 256 * (i + 1)]
                    rX = f"XN:{tt}"
                    b.emit("V", lambda v, tt=tt, cb=cb: v.scalar_tensor_tensor(
                        out=TMPQ[:, 0:256], in0=XN[:, 1024 * tt + 512:1024 * tt + 768],
                        scalar=CTL[:, 2 * cb:2 * cb + 1], in1=XN[:, 1024 * tt:1024 * tt + 256],
                        op0=ALU.mult, op1=ALU.add),
                        reads=[rX, f"CTL:{cb}"], writes=["TMPQ"])
                    b.emit("V", lambda v, tt=tt, cb=cb: v.scalar_tensor_tensor(
                        out=TMPQ[:, 256:512], in0=XN[:, 1024 * tt + 768:1024 * tt + 1024],
                        scalar=CTL[:, 2 * cb + 1:2 * cb + 2], in1=XN[:, 1024 * tt + 256:1024 * tt + 512],
                        op0=ALU.mult, op1=ALU.add),
                        reads=[rX, f"CTL:{cb}"], writes=["TMPQ"])
                    b.emit("V", lambda v, tt=tt, cb=cb: v.scalar_tensor_tensor(
                        out=XN[:, 1024 * tt + 512:1024 * tt + 768], in0=XN[:, 1024 * tt:1024 * tt + 256],
                        scalar=CTL[:, 2 * cb + 1:2 * cb + 2], in1=XN[:, 1024 * tt + 512:1024 * tt + 768],
                        op0=ALU.mult, op1=ALU.add),
                        reads=[rX, f"CTL:{cb}"], writes=[rX])
                    b.emit("V", lambda v, tt=tt, cb=cb: v.scalar_tensor_tensor(
                        out=XN[:, 1024 * tt + 768:1024 * tt + 1024], in0=XN[:, 1024 * tt + 256:1024 * tt + 512],
                        scalar=CTL[:, 2 * cb:2 * cb + 1], in1=XN[:, 1024 * tt + 768:1024 * tt + 1024],
                        op0=ALU.mult, op1=ALU.add),
                        reads=[rX, f"CTL:{cb}"], writes=[rX])
                    b.emit("V", lambda v, tt=tt: v.tensor_copy(
                        out=XN[:, 1024 * tt:1024 * tt + 256], in_=TMPQ[:, 0:256]),
                        reads=["TMPQ"], writes=[rX])
                    b.emit("V", lambda v, tt=tt: v.tensor_copy(
                        out=XN[:, 1024 * tt + 256:1024 * tt + 512], in_=TMPQ[:, 256:512]),
                        reads=["TMPQ"], writes=[rX])

        for cyc in range(3):
            ffn_cycle(cyc)

        wload(WBIG, wo_in, 8, 1024, "WBIG")

        if UPTO < 4:
            early_out()
            return nc
        for tt in range(NT):
            for r in range(4):
                wA = ["ovA"] if (tt == 0 and r == 0) else []
                b.emit("P", lambda p, tt=tt, r=r: p.indirect_dma_start(
                    out=BIGA[:, 1024 * tt + 256 * r:1024 * tt + 256 * (r + 1)],
                    out_offset=None, in_=ygat,
                    in_offset=IndirectOffsetOnAxis(ap=YIX[:, 4 * tt + r:4 * tt + r + 1], axis=0)),
                    reads=["ygat", "YIX"] + ([] if wA else ["ovA"]),
                    writes=[f"YALL:{tt}"] + wA, dma="yin")
        for tt in range(NT):
            for f in range(8):
                bank, off = 6 + (f // 4), 128 * (f % 4)
                wS = ["ovS"] if (tt == 0 and f == 0) else []
                b.emit("T", lambda t, tt=tt, f=f, bank=bank, off=off: t.matmul(
                    psb(bank, off, 128), BIGA[:, 1024 * tt + 128 * f:1024 * tt + 128 * (f + 1)],
                    IDB[:, :], start=True, stop=True),
                    reads=[f"YALL:{tt}", "IDB"], writes=[f"pb{bank}"])
                b.emit("V", lambda v, f=f, bank=bank, off=off: v.tensor_copy(
                    out=SCR[:, 128 * f:128 * (f + 1)], in_=psb(bank, off, 128)),
                    reads=[f"pb{bank}"] + ([] if wS else ["ovS"]), writes=["YTT"] + wS)
            for n in range(2):
                bank = (2 * tt + n) % 4
                for k in range(8):
                    b.emit("T", lambda t, n=n, k=k, bank=bank: t.matmul(
                        psb(bank, 0, 512), SCR[:, 128 * k:128 * (k + 1)],
                        WBIG.ap()[:, 1024 * k + 512 * n:1024 * k + 512 * n + 512],
                        start=(k == 0), stop=(k == 7)),
                        reads=["YTT", "WBIG"], writes=[f"pb{bank}"], accum=("wo", tt, n))
                b.emit("V", lambda v, tt=tt, n=n, bank=bank: v.tensor_add(
                    out=XN[:, 1024 * tt + 512 * n:1024 * tt + 512 * n + 512],
                    in0=XN[:, 1024 * tt + 512 * n:1024 * tt + 512 * n + 512],
                    in1=psb(bank, 0, 512)),
                    reads=[f"pb{bank}", f"XN:{tt}"], writes=[f"XN:{tt}"])
            bu = tt % 2
            xch = XCH[:, 1024 * bu:1024 * (bu + 1)]
            b.emit("Y", lambda y, tt=tt, xch=xch: y.dma_start(
                out=xch, in_=xo_in[128 * tt:128 * (tt + 1), :]),
                writes=[f"XCH:{bu}"], dma="q")
            b.emit("V", lambda v, tt=tt, xch=xch: v.tensor_add(
                out=XN[:, 1024 * tt:1024 * (tt + 1)],
                in0=XN[:, 1024 * tt:1024 * (tt + 1)], in1=xch),
                reads=[f"XCH:{bu}", f"XN:{tt}"], writes=[f"XN:{tt}"])
            b.emit("Y", lambda y, tt=tt: y.dma_start(
                out=out_d[128 * tt:128 * (tt + 1), :], in_=XN[:, 1024 * tt:1024 * (tt + 1)]),
                reads=[f"XN:{tt}"], writes=[f"outw:{tt}"], dma="q")

        b.emit("P", lambda p: p.engine_nop(), reads=[f"outw:{NT - 1}"])

        block = es.enter_context(nc.Block())
        b.finalize(nc, es, block)
    return nc


_CACHE = {}


def _tables():
    if "t" in _CACHE:
        return _CACHE["t"]
    half = 32
    inv = 1.0 / (10000.0 ** (np.arange(half, dtype=np.float64) / half))
    t = np.arange(S, dtype=np.float64)
    fr = np.outer(inv, t)
    emb = np.concatenate([fr, fr], axis=0)
    cos = np.cos(emb)
    sin = np.sin(emb)
    cos2 = np.concatenate([cos, cos], axis=0)
    sin2 = np.concatenate([sin, sin], axis=0)
    bf = lambda a: a.astype(ml_dtypes.bfloat16)
    cosq = bf(cos2 / 8.0)
    sinq = bf(sin2 / 8.0)
    cosk = bf(cos2)
    sink = bf(sin2)
    rp = np.zeros((128, 128), dtype=np.float32)
    for base in (0, 64):
        for m in range(32):
            rp[base + m + 32, base + m] = -1.0
            rp[base + m, base + m + 32] = 1.0
    idb = np.eye(128, dtype=ml_dtypes.bfloat16)
    idf = np.eye(128, dtype=np.float32)
    oneb = np.ones((128, 128), dtype=ml_dtypes.bfloat16)
    msk = np.zeros((128, 384), dtype=np.float32)
    for pp in range(128):
        for j in range(384):
            if pp <= j - 128:
                msk[pp, j] = 1.0
    _CACHE["t"] = (cosq, sinq, cosk, sink, bf(rp), idb, idf, oneb, bf(msk))
    return _CACHE["t"]


def kernel(**inputs):
    x = np.asarray(inputs["L0_attn_x"], dtype=np.float32)
    Wq = np.ascontiguousarray(np.asarray(inputs["L0_attn_Wq"], dtype=np.float32))
    Wk = np.ascontiguousarray(np.asarray(inputs["L0_attn_Wk"], dtype=np.float32))
    Wv = np.ascontiguousarray(np.asarray(inputs["L0_attn_Wv"], dtype=np.float32))
    Wo = np.ascontiguousarray(np.asarray(inputs["L0_attn_Wo"], dtype=np.float32))
    cosq, sinq, cosk, sink, rp, idb, idf, oneb, msk = _tables()

    if "nc" not in _CACHE:
        _CACHE["nc"] = build_program()
    nc = _CACHE["nc"]

    shared = {
        "wo": Wo,
        "cw1": np.ascontiguousarray(np.asarray(inputs["L0_ffn_cn_w1"], dtype=np.float32)),
        "cw2": np.ascontiguousarray(np.asarray(inputs["L0_ffn_cn_w2"], dtype=np.float32)),
        "cw3": np.ascontiguousarray(np.asarray(inputs["L0_ffn_cn_w3"], dtype=np.float32)),
        "cw4": np.ascontiguousarray(np.asarray(inputs["L0_ffn_cn_w4"], dtype=np.float32)),
        "gw1": np.ascontiguousarray(np.asarray(inputs["L0_ffn_gc_w1"], dtype=np.float32)),
        "gw2": np.ascontiguousarray(np.asarray(inputs["L0_ffn_gc_w2"], dtype=np.float32)),
        "gw3": np.ascontiguousarray(np.asarray(inputs["L0_ffn_gc_w3"], dtype=np.float32)),
        "cb1": np.ascontiguousarray(np.asarray(inputs["L0_ffn_cn_b1"], dtype=np.float32)),
        "cb2": np.ascontiguousarray(np.asarray(inputs["L0_ffn_cn_b2"], dtype=np.float32)),
        "cb3": np.ascontiguousarray(np.asarray(inputs["L0_ffn_cn_b3"], dtype=np.float32)),
        "cb4": np.ascontiguousarray(np.asarray(inputs["L0_ffn_cn_b4"], dtype=np.float32)),
        "gb1": np.ascontiguousarray(np.asarray(inputs["L0_ffn_gc_b1"], dtype=np.float32)),
        "gb2": np.ascontiguousarray(np.asarray(inputs["L0_ffn_gc_b2"], dtype=np.float32)),
        "gb3": np.ascontiguousarray(np.asarray(inputs["L0_ffn_gc_b3"], dtype=np.float32)),
        "ar": np.ascontiguousarray(np.asarray(inputs["L0_ffn_adaptive_range"], dtype=np.float32)),
        "cosq": cosq, "sinq": sinq, "cosk": cosk, "sink": sink,
        "rp": rp, "idb": idb, "idf": idf, "oneb": oneb, "msk": msk,
    }
    in_maps = []
    for c in range(NCORE := 8):
        bb, kh = divmod(c, 4)
        m = dict(shared)
        m["x"] = np.ascontiguousarray(x[bb])
        m["xown"] = np.ascontiguousarray(x[bb, TOK * kh:TOK * (kh + 1)])
        m["wq"] = np.ascontiguousarray(Wq[:, 256 * kh:256 * (kh + 1)])
        m["wk"] = np.ascontiguousarray(Wk[:, 64 * kh:64 * (kh + 1)])
        m["wv"] = np.ascontiguousarray(Wv[:, 64 * kh:64 * (kh + 1)])
        yidx = np.empty((128, 64), dtype=np.int32)
        for tt in range(NT):
            for r in range(4):
                yidx[:, 4 * tt + r] = S * (4 * bb + r) + TOK * kh + 128 * tt + np.arange(128)
        m["yidx"] = yidx
        in_maps.append(m)

    res = run_bass_kernel_spmd(nc, in_maps, core_ids=list(range(8)))
    out = np.empty((2, S, D), dtype=np.float32)
    for c in range(8):
        bb, kh = divmod(c, 4)
        out[bb, TOK * kh:TOK * (kh + 1)] = res.results[c]["out"]
    return out


# version 08
# speedup vs baseline: 1.5681x; 1.0349x over previous
import contextlib
import math
import os
import numpy as np
import ml_dtypes
import concourse.bass as bass
import concourse.mybir as mybir
from concourse.bass_utils import run_bass_kernel_spmd
from concourse.bass import IndirectOffsetOnAxis
F32 = mybir.dt.float32
BF16 = mybir.dt.bfloat16
AF = mybir.ActivationFunctionType
ALU = mybir.AluOpType
AX = mybir.AxisListType

D = 1024
S = 8192
NCH = S // 128
QB = 256
NQB = S // QB
TOK = S // 4
NT = TOK // 128
TCH = 1024
NTCH = TOK // TCH
EPS = 1e-5
LN2H = 0.5 * math.log(2.0)
UPTO = int(os.environ.get("K_UPTO", "4"))
P2N = int(os.environ.get("K_P2N", "999"))
P2M = int(os.environ.get("K_P2M", "5"))
P2QK = int(os.environ.get("K_P2QK", "9999"))
P2E = int(os.environ.get("K_P2E", "1"))
NOCC = int(os.environ.get("K_NOCC", "0"))
NOFFN = int(os.environ.get("K_NOFFN", "0"))
NOPRE = int(os.environ.get("K_NOPRE", "0"))


class Op:
    __slots__ = ("eng", "fn", "deps", "idx", "need_inc", "val", "dma", "accum", "burst")

    def __init__(self, eng, fn, dma, accum):
        self.eng, self.fn, self.dma, self.accum = eng, fn, dma, accum
        self.deps = []
        self.idx = -1
        self.need_inc = False
        self.val = 0
        self.burst = 0


class Bld:
    def __init__(self):
        self.ops = {e: [] for e in "TVSPY"}
        self.res = {}
        self.groups = {}

    def emit(self, eng, fn, reads=(), writes=(), dma=None, accum=None):
        op = Op(eng, fn, dma, accum)
        deps = {}
        for r in reads:
            ent = self.res.get(r)
            if ent and ent[0] is not None:
                deps[id(ent[0])] = ent[0]
        for w in writes:
            ent = self.res.get(w)
            if ent:
                if ent[0] is not None:
                    deps[id(ent[0])] = ent[0]
                for rd in ent[1]:
                    deps[id(rd)] = rd
        ds = []
        for d in deps.values():
            if d is op:
                continue
            if d.eng == "T" and eng == "T" and d.dma is None and dma is None:
                continue
            if accum is not None and d.accum == accum:
                continue
            if dma not in (None, "q") and d.eng == eng and d.dma == dma:
                continue
            ds.append(d)
        op.deps = ds
        for r in reads:
            self.res.setdefault(r, [None, []])[1].append(op)
        for w in writes:
            self.res[w] = [op, []]
        op.idx = len(self.ops[eng])
        self.ops[eng].append(op)
        if dma is not None and dma != "q" and eng == "P":
            self.groups.setdefault(dma, []).append(op)
        return op

    def finalize(self, nc, es, block):
        sems = {}
        for e in "TVSPY":
            sems[e] = es.enter_context(nc.semaphore(f"sem_{e}"))
        for g in self.groups:
            sems["g:" + g] = es.enter_context(nc.semaphore(f"sem_g_{g}"))
        for e in "TVSPY":
            for op in self.ops[e]:
                for d in op.deps:
                    d.need_inc = True
        for ops_ in self.groups.values():
            for o in ops_:
                o.need_inc = True
        for e in "TVSPY":
            for op in self.ops[e]:
                if op.dma is not None:
                    op.need_inc = True
        gtotal = {g: 16 * len(ops_) for g, ops_ in self.groups.items()}
        for e in "TVSPY":
            c = 0
            for op in self.ops[e]:
                if op.eng == "P" and op.dma not in (None, "q"):
                    op.val = 0
                    continue
                if op.need_inc:
                    c += 16 if (op.dma is not None) else 1
                op.val = c

        burst_end = {"Y": {}, "P": {}}
        for e in ("Y", "P"):
            bid = 0
            for op in self.ops[e]:
                if op.deps:
                    bid += 1
                op.burst = bid
                burst_end[e][bid] = op.val

        def sem_of(d):
            if d.eng == "P" and d.dma not in (None, "q"):
                return ("g:" + d.dma, gtotal[d.dma])
            if d.eng in ("Y", "P"):
                return (d.eng, burst_end[d.eng][d.burst])
            return (d.eng, d.val)

        waited = {"Y": set(), "P": set()}
        for e in "TVSPY":
            for op in self.ops[e]:
                for d in op.deps:
                    s, v = sem_of(d)
                    if s in waited:
                        waited[s].add(v)

        def run_engine(e, eng_obj):
            water = {}
            for op in self.ops[e]:
                waits = {}
                for d in op.deps:
                    s, v = sem_of(d)
                    if waits.get(s, -1) < v:
                        waits[s] = v
                for s, v in sorted(waits.items()):
                    if water.get(s, -1) >= v:
                        continue
                    eng_obj.wait_ge(sems[s], v)
                    water[s] = v
                ins = op.fn(eng_obj)
                if op.need_inc:
                    if op.eng == "P" and op.dma not in (None, "q"):
                        ins.then_inc(sems["g:" + op.dma], 16)
                    elif op.dma is not None:
                        ins.then_inc(sems[op.eng], 16)
                    else:
                        ins.then_inc(sems[op.eng], 1)
                if e in ("Y", "P") and op.need_inc and op.dma in (None, "q") and op.val in waited[e]:
                    eng_obj.wait_ge(sems[e], op.val)
                    water[e] = max(water.get(e, -1), op.val)

        @block.tensor
        def _(t):
            run_engine("T", t)

        @block.vector
        def _(v):
            run_engine("V", v)

        @block.scalar
        def _(s):
            run_engine("S", s)

        @block.gpsimd
        def _(p):
            run_engine("P", p)

        @block.sync
        def _(y):
            run_engine("Y", y)


def build_program():
    nc = bass.Bass("TRN2", target_bir_lowering=False)

    x_in = nc.dram_tensor("x", [S, D], F32, kind="ExternalInput").ap()
    xo_in = nc.dram_tensor("xown", [TOK, D], F32, kind="ExternalInput").ap()
    wq_in = nc.dram_tensor("wq", [D, 256], F32, kind="ExternalInput").ap()
    wk_in = nc.dram_tensor("wk", [D, 64], F32, kind="ExternalInput").ap()
    wv_in = nc.dram_tensor("wv", [D, 64], F32, kind="ExternalInput").ap()
    wo_in = nc.dram_tensor("wo", [D, D], F32, kind="ExternalInput").ap()
    cw1_in = nc.dram_tensor("cw1", [D, D], F32, kind="ExternalInput").ap()
    cw2_in = nc.dram_tensor("cw2", [D, 512], F32, kind="ExternalInput").ap()
    cw3_in = nc.dram_tensor("cw3", [512, 256], F32, kind="ExternalInput").ap()
    cw4_in = nc.dram_tensor("cw4", [256, 1], F32, kind="ExternalInput").ap()
    gw1_in = nc.dram_tensor("gw1", [D, 512], F32, kind="ExternalInput").ap()
    gw2_in = nc.dram_tensor("gw2", [512, 256], F32, kind="ExternalInput").ap()
    gw3_in = nc.dram_tensor("gw3", [256, 1], F32, kind="ExternalInput").ap()
    cb1_in = nc.dram_tensor("cb1", [D], F32, kind="ExternalInput").ap()
    cb2_in = nc.dram_tensor("cb2", [512], F32, kind="ExternalInput").ap()
    cb3_in = nc.dram_tensor("cb3", [256], F32, kind="ExternalInput").ap()
    cb4_in = nc.dram_tensor("cb4", [1], F32, kind="ExternalInput").ap()
    gb1_in = nc.dram_tensor("gb1", [512], F32, kind="ExternalInput").ap()
    gb2_in = nc.dram_tensor("gb2", [256], F32, kind="ExternalInput").ap()
    gb3_in = nc.dram_tensor("gb3", [1], F32, kind="ExternalInput").ap()
    ar_in = nc.dram_tensor("ar", [1], F32, kind="ExternalInput").ap()
    cosq_in = nc.dram_tensor("cosq", [128, S], BF16, kind="ExternalInput").ap()
    sinq_in = nc.dram_tensor("sinq", [128, S], BF16, kind="ExternalInput").ap()
    cosk_in = nc.dram_tensor("cosk", [128, S], BF16, kind="ExternalInput").ap()
    sink_in = nc.dram_tensor("sink", [128, S], BF16, kind="ExternalInput").ap()
    rp_in = nc.dram_tensor("rp", [128, 128], BF16, kind="ExternalInput").ap()
    idb_in = nc.dram_tensor("idb", [128, 128], BF16, kind="ExternalInput").ap()
    idf_in = nc.dram_tensor("idf", [128, 128], F32, kind="ExternalInput").ap()
    oneb_in = nc.dram_tensor("oneb", [128, 128], BF16, kind="ExternalInput").ap()
    msk_in = nc.dram_tensor("msk", [128, 384], BF16, kind="ExternalInput").ap()
    out_d = nc.dram_tensor("out", [TOK, D], F32, kind="ExternalOutput").ap()
    yidx_in = nc.dram_tensor("yidx", [128, 64], mybir.dt.int32, kind="ExternalInput").ap()
    ydram_h = nc.dram_tensor("ydram", [S, 256], BF16)
    ygat_h = nc.dram_tensor("ygat", [8 * S, 256], BF16)
    ydram = ydram_h.ap()
    ygat = ygat_h.ap()

    es = contextlib.ExitStack()
    SB = lambda n, sh, dt: es.enter_context(nc.sbuf_tensor(n, sh, dt))
    b = Bld()

    with es:
        XN = SB("XN", [128, NT * 1024], F32)
        BIGA = SB("BIGA", [128, 16384], BF16)
        KD = SB("KD", [128, 8192], BF16)
        VT = SB("VT", [128, 65 * NCH], BF16)
        XCH = SB("XCH", [128, 2048], F32)
        SCR = SB("SCR", [128, 4096], BF16)
        PTA = SB("PTA", [128, 2048], BF16)
        CS = SB("CS", [128, 1024], BF16)
        RSB = SB("RSB", [128, 2048], BF16)
        SCB = SB("SCB", [128, 768], BF16)
        STT = SB("STT", [128, 32], F32)
        SUMXN = SB("SUMXN", [128, 8], F32)
        MAXA = SB("MAXA", [1, 4], F32)
        MScr = SB("MScr", [1, 8], F32)
        NEGB = SB("NEGB", [128, 2], F32)
        NEGBB = SB("NEGBB", [1, 2], BF16)
        GFT = SB("GFT", [1, 16], F32)
        XBARB = SB("XBARB", [128, 8], BF16)
        GFH1 = SB("GFH1", [128, 4], BF16)
        GFH2 = SB("GFH2", [128, 2], BF16)
        CTL = SB("CTL", [128, 4], F32)
        CTLB = SB("CTLB", [1, 1024], F32)
        TMP1 = SB("TMP1", [128, 128], F32)
        TMP2 = SB("TMP2", [128, 128], F32)
        TMPQ = SB("TMPQ", [128, 512], F32)
        YST = SB("YST", [128, 512], BF16)
        RD = SB("RD", [128, 8], F32)
        IDB = SB("IDB", [128, 128], BF16)
        IDF = SB("IDF", [128, 128], F32)
        ONEB = SB("ONEB", [128, 128], BF16)
        MSK = SB("MSK", [128, 384], BF16)
        RP = SB("RPs", [128, 128], BF16)
        WQB = SB("WQB", [128, 2048], BF16)
        WKB = SB("WKB", [128, 512], BF16)
        WVB = SB("WVB", [128, 512], BF16)
        WBIG = SB("WBIG", [128, 8192], BF16)
        CN2 = SB("CN2", [128, 4096], BF16)
        CN3 = SB("CN3", [128, 1024], BF16)
        CN4 = SB("CN4", [128, 2], BF16)
        GC1 = SB("GC1", [128, 4096], BF16)
        GC2 = SB("GC2", [128, 1024], BF16)
        GC3 = SB("GC3", [128, 2], BF16)
        CB1 = SB("CB1", [128, 8], F32)
        CB2 = SB("CB2", [128, 4], F32)
        CB3 = SB("CB3", [128, 2], F32)
        CB4 = SB("CB4", [1, 2], F32)
        GB1 = SB("GB1", [128, 4], F32)
        GB2 = SB("GB2", [128, 2], F32)
        GB3 = SB("GB3", [1, 2], F32)
        ART = SB("ART", [1, 1], F32)
        CONS = SB("CONS", [1, 4], F32)
        YIX = SB("YIX", [128, 64], mybir.dt.int32)
        PS = es.enter_context(nc.psum_tensor("PS", [128, 4096], F32))
        biga_addr = nc.lookup_mloc(BIGA).addr
        kd_addr = nc.lookup_mloc(KD).addr
        QCH = [[nc.alloc_sbuf_tensor_at(f"QCH{p_}_{c_}", (128, 256), BF16,
                                        offset=biga_addr + 16384 * p_ + 512 * c_)
                for c_ in range(NCH // 2)] for p_ in range(2)]
        KCH = [nc.alloc_sbuf_tensor_at(f"KCH_{c_}", (128, 128), BF16,
                                       offset=kd_addr + 256 * c_)
               for c_ in range(NCH)]
        xn_addr = nc.lookup_mloc(XN).addr
        KBH = [nc.alloc_sbuf_tensor_at(f"KBH_{c_}", (128, 128), BF16,
                                       offset=xn_addr + 256 * c_)
               for c_ in range(NCH)]
        KBALL = nc.alloc_sbuf_tensor_at("KBALL", (128, 8192), BF16, offset=xn_addr)

        def psb(bank, off, w, p0=0, p1=128):
            return PS[p0:p1, 512 * bank + off:512 * bank + off + w]

        stg_par = [0]

        def wload(dst, srcap, kt, ncols, name):
            for k in range(kt):
                bu = stg_par[0] % 2
                stg_par[0] += 1
                stg = XCH[:, 1024 * bu:1024 * bu + ncols]
                b.emit("Y", lambda y, k=k, stg=stg, srcap=srcap: y.dma_start(
                    out=stg, in_=srcap[128 * k:128 * (k + 1), :]),
                    writes=[f"XCH:{bu}"], dma="q")
                b.emit("P", lambda p, k=k, stg=stg, dst=dst, ncols=ncols: p.tensor_copy(
                    out=dst[:, ncols * k:ncols * (k + 1)], in_=stg),
                    reads=[f"XCH:{bu}"], writes=[name])

        if not int(os.environ.get("K_NOQKV", "0")):
            b.emit("Y", lambda y: y.dma_start(out=YIX[:, :], in_=yidx_in), writes=["YIX"], dma="q")
            wload(WQB, wq_in, 8, 256, "WQB")
            wload(WKB, wk_in, 8, 64, "WKB")
            wload(WVB, wv_in, 8, 64, "WVB")

        def bload(dst, src, m, name):
            for i in range(m):
                b.emit("Y", lambda y, d=dst, s=src, i=i:
                       y.dma_start(out=d[:, i:i + 1],
                                   in_=s[128 * i:128 * (i + 1)].rearrange("(p o) -> p o", o=1)),
                       writes=[name], dma="q")

        if not int(os.environ.get("K_NOBL", "0")):
            bload(CB1, cb1_in, 8, "CB1")
            bload(CB2, cb2_in, 4, "CB2")
            bload(CB3, cb3_in, 2, "CB3")
            bload(GB1, gb1_in, 4, "GB1")
            bload(GB2, gb2_in, 2, "GB2")
            for dst_, src_, nm_ in ((CB4, cb4_in, "CB4"), (GB3, gb3_in, "GB3"), (ART, ar_in, "ART")):
                b.emit("Y", lambda y, d=dst_, s=src_: y.dma_start(out=d[0:1, 0:1], in_=s.rearrange("(x o) -> x o", x=1)),
                       writes=[nm_], dma="q")
        if not int(os.environ.get("K_NOTB", "0")):
            for dst_, src_, nm_ in ((IDB, idb_in, "IDB"), (IDF, idf_in, "IDF"), (ONEB, oneb_in, "ONEB"),
                                    (MSK, msk_in, "MSK"), (RP, rp_in, "RPc")):
                b.emit("Y", lambda y, d=dst_, s=src_: y.dma_start(out=d[:, :], in_=s), writes=[nm_], dma="q")
            b.emit("V", lambda v: v.memset(MAXA[:, :], -1e30), writes=["MAXA"])
            b.emit("V", lambda v: v.memset(SUMXN[:, :], 0.0), writes=["SUMXN"])
            b.emit("V", lambda v: v.memset(VT[:, :], 1.0), writes=["VTones"])
        b.emit("V", lambda v: v.memset(KD[64:128, :], 0.0), writes=["KDzero"])
        b.emit("V", lambda v: v.memset(KBALL[0:64, :], 0.0), writes=["KBzero", "ovXN"])
        b.emit("V", lambda v: v.memset(CONS[:, :], 0.0), writes=["CONS"])

        def p1_A(c):
            bu = c % 2
            r0 = 128 * c
            xch = XCH[:, 1024 * bu:1024 * (bu + 1)]
            xnb = SCR[:, 1024 * bu:1024 * (bu + 1)]
            xnt0 = 2048 + 1024 * bu
            st = 16 * bu
            cso = 512 * bu
            ro = 1024 * bu
            rXCH, rXNB, rXNT = f"XCH:{bu}", f"XNB:{bu}", f"XNT:{bu}"
            rST, rCS, rRSB = f"STT:{bu}", f"CS:{bu}", f"RSB:{bu}"

            b.emit("Y", lambda y: y.dma_start(out=xch, in_=x_in[r0:r0 + 128, :]),
                   writes=[rXCH], dma="q")
            for j, tab in enumerate((cosq_in, sinq_in, cosk_in, sink_in)):
                b.emit("Y", lambda y, j=j, tab=tab: y.dma_start(
                    out=CS[:, cso + 128 * j:cso + 128 * (j + 1)], in_=tab[:, r0:r0 + 128]),
                    writes=[rCS], dma="q")

            b.emit("V", lambda v: v.bn_stats(out=STT[:, st:st + 6], in_=xch[:, 0:512]),
                   reads=[rXCH], writes=[rST])
            b.emit("V", lambda v: v.bn_stats(out=STT[:, st + 6:st + 12], in_=xch[:, 512:1024]),
                   reads=[rXCH], writes=[rST])
            b.emit("V", lambda v: v.bn_aggr(out=STT[:, st + 12:st + 14], in_=STT[:, st:st + 12]),
                   reads=[rST], writes=[rST])
            b.emit("V", lambda v: v.tensor_scalar_add(out=STT[:, st + 14:st + 15],
                                                      in0=STT[:, st + 13:st + 14], scalar1=EPS),
                   reads=[rST], writes=[rST])
            b.emit("S", lambda s: s.activation(out=STT[:, st + 15:st + 16],
                                               in_=STT[:, st + 14:st + 15], func=AF.Ln),
                   reads=[rST], writes=[rST])
            b.emit("S", lambda s: s.activation(out=STT[:, st + 14:st + 15],
                                               in_=STT[:, st + 15:st + 16], func=AF.Exp, scale=-0.5),
                   reads=[rST], writes=[rST])
            b.emit("V", lambda v: v.tensor_scalar(out=xnb, in0=xch,
                                                  scalar1=STT[:, st + 12:st + 13],
                                                  scalar2=STT[:, st + 14:st + 15],
                                                  op0=ALU.subtract, op1=ALU.mult),
                   reads=[rXCH, rST, "ovS"], writes=[rXNB])

            for f in range(8):
                bank, off = 4 + (f // 4), 128 * (f % 4)
                b.emit("T", lambda t, f=f, bank=bank, off=off: t.matmul(
                    psb(bank, off, 128), xnb[:, 128 * f:128 * (f + 1)], IDB[:, :],
                    start=True, stop=True),
                    reads=[rXNB, "IDB", "ovS"], writes=[f"pb{bank}"])
            for g2 in range(2):
                bank = 4 + g2
                b.emit("V", lambda v, g2=g2, bank=bank: v.tensor_copy(
                    out=SCR[:, xnt0 + 512 * g2:xnt0 + 512 * (g2 + 1)], in_=psb(bank, 0, 512)),
                    reads=[f"pb{bank}", "ovS"], writes=[rXNT])

        def p1_B(c):
            bu = c % 2
            r0 = 128 * c
            xch = XCH[:, 1024 * bu:1024 * (bu + 1)]
            xnb = SCR[:, 1024 * bu:1024 * (bu + 1)]
            xnt0 = 2048 + 1024 * bu
            st = 16 * bu
            cso = 512 * bu
            ro = 1024 * bu
            rXCH, rXNB, rXNT = f"XCH:{bu}", f"XNB:{bu}", f"XNT:{bu}"
            rST, rCS, rRSB = f"STT:{bu}", f"CS:{bu}", f"RSB:{bu}"

            for m in range(4):
                bank = m
                for k in range(8):
                    if m < 2:
                        st_ap = WQB.ap()[:, 256 * k + 128 * m:256 * k + 128 * m + 128]
                        wnm = "WQB"
                    elif m == 2:
                        st_ap = WKB.ap()[:, 64 * k:64 * (k + 1)]
                        wnm = "WKB"
                    else:
                        st_ap = WVB.ap()[:, 64 * k:64 * (k + 1)]
                        wnm = "WVB"
                    p1_ = 128 if m < 2 else 64
                    b.emit("T", lambda t, k=k, bank=bank, st_ap=st_ap, p1_=p1_: t.matmul(
                        psb(bank, 0, 128, 0, p1_), st_ap,
                        SCR[:, xnt0 + 128 * k:xnt0 + 128 * (k + 1)],
                        start=(k == 0), stop=(k == 7)),
                        reads=[rXNT, wnm], writes=[f"pb{bank}"], accum=("p1", c, m))
            for m in range(2):
                b.emit("S", lambda s, m=m: s.activation(
                    out=RSB[:, ro + 128 * m:ro + 128 * (m + 1)], in_=psb(m, 0, 128), func=AF.Copy),
                    reads=[f"pb{m}"], writes=[rRSB])
            b.emit("S", lambda s: s.activation(
                out=RSB[0:64, ro + 256:ro + 384], in_=psb(2, 0, 128, 0, 64), func=AF.Copy),
                reads=["pb2"], writes=[rRSB])
            b.emit("S", lambda s: s.activation(out=RSB[0:64, ro + 768:ro + 896],
                                               in_=psb(3, 0, 128, 0, 64), func=AF.Copy),
                   reads=["pb3"], writes=[rRSB])

            for m in range(2):
                b.emit("T", lambda t, m=m: t.matmul(
                    psb(m, 128, 128), RP[:, :], RSB[:, ro + 128 * m:ro + 128 * (m + 1)],
                    start=True, stop=True),
                    reads=[rRSB, "RPc"], writes=[f"pb{m}"])
            b.emit("T", lambda t: t.matmul(
                psb(2, 128, 128, 0, 64), RP[0:64, 0:64], RSB[0:64, ro + 256:ro + 384],
                start=True, stop=True),
                reads=[rRSB, "RPc"], writes=["pb2"])
            for m in range(2):
                b.emit("S", lambda s, m=m: s.activation(
                    out=RSB[:, ro + 384 + 128 * m:ro + 384 + 128 * (m + 1)],
                    in_=psb(m, 128, 128), func=AF.Copy),
                    reads=[f"pb{m}"], writes=[rRSB])
            b.emit("S", lambda s: s.activation(
                out=RSB[0:64, ro + 640:ro + 768],
                in_=psb(2, 128, 128, 0, 64), func=AF.Copy),
                reads=["pb2"], writes=[rRSB])

            qho = 128 * (c % 2)
            for m, (dst, dnm, tco, tso, pl) in enumerate((
                    (QCH[0][c // 2][:, qho:qho + 128], f"QP0:{c}", 0, 128, 128),
                    (QCH[1][c // 2][:, qho:qho + 128], f"QP1:{c}", 0, 128, 128),
                    (KCH[c][0:64, :], f"KD:{c}", 256, 384, 64))):
                b.emit("V", lambda v, m=m, tco=tco, pl=pl: v.tensor_tensor(
                    out=TMP1[0:pl, :], in0=RSB[0:pl, ro + 128 * m:ro + 128 * (m + 1)],
                    in1=CS[0:pl, cso + tco:cso + tco + 128], op=ALU.mult),
                    reads=[rRSB, rCS], writes=["TMP1"])
                b.emit("V", lambda v, m=m, tso=tso, pl=pl: v.tensor_tensor(
                    out=TMP2[0:pl, :], in0=RSB[0:pl, ro + 384 + 128 * m:ro + 384 + 128 * (m + 1)],
                    in1=CS[0:pl, cso + tso:cso + tso + 128], op=ALU.mult),
                    reads=[rRSB, rCS], writes=["TMP2"])
                b.emit("V", lambda v, dst=dst, pl=pl: v.tensor_tensor(
                    out=dst, in0=TMP1[0:pl, :], in1=TMP2[0:pl, :], op=ALU.add),
                    reads=["TMP1", "TMP2", "ovA" if m < 2 else "ovK"], writes=[dnm])
            b.emit("Y", lambda y, c=c: y.dma_start(out=KBH[c][64:128, :], in_=KCH[c][0:64, :]),
                   reads=[f"KD:{c}", "KBzero", "ovXN"], writes=[f"KB:{c}"], dma="q")

            b.emit("T", lambda t: t.matmul(psb(3, 128, 64, 0, 128), RSB[0:64, ro + 768:ro + 896],
                                           IDB[0:64, 0:64], start=True, stop=True),
                   reads=[rRSB, "IDB"], writes=["pb3"])
            b.emit("S", lambda s, c=c: s.activation(out=VT[:, 65 * c:65 * c + 64],
                                                    in_=psb(3, 128, 64), func=AF.Copy),
                   reads=["pb3"], writes=[f"VT:{c}"])

            for m, snm in enumerate((f"QP0:{c}", f"QP1:{c}", f"KD:{c}")):
                src = (QCH[0][c // 2][:, qho:qho + 128] if m == 0 else
                       QCH[1][c // 2][:, qho:qho + 128] if m == 1 else
                       KCH[c][0:64, :])
                pl = 128 if m < 2 else 64
                b.emit("S", lambda s, m=m, src=src, pl=pl: s.activation(
                    out=SCB[0:pl, 256 * m:256 * m + 128], in_=src, func=AF.Square),
                    reads=[snm], writes=["SCB"])
            for m in range(3):
                pl = 128 if m < 2 else 64
                b.emit("T", lambda t, m=m, pl=pl: t.matmul(
                    psb(m, 256, 128, 0, 1), ONEB[0:pl, 0:1], SCB[0:pl, 256 * m:256 * m + 128],
                    start=True, stop=True),
                    reads=["SCB", "ONEB"], writes=[f"pb{m}"])
            for m in range(3):
                b.emit("V", lambda v, m=m: v.reduce_max(
                    out=MScr[0:1, m:m + 1], in_=psb(m, 256, 128, 0, 1), axis=AX.X),
                    reads=[f"pb{m}"], writes=["MScr"])
            b.emit("V", lambda v: v.tensor_max(out=MAXA[0:1, 0:3], in0=MAXA[0:1, 0:3],
                                               in1=MScr[0:1, 0:3]),
                   reads=["MScr", "MAXA"], writes=["MAXA"])

            for f in range(8):
                b.emit("T", lambda t, f=f: t.matmul(
                    psb(3, 256 + f, 1), xnb[:, 128 * f:128 * (f + 1)], ONEB[:, 0:1],
                    start=True, stop=True),
                    reads=[rXNB, "ONEB"], writes=["pb3"])
            b.emit("V", lambda v: v.tensor_add(out=SUMXN[:, :], in0=SUMXN[:, :], in1=psb(3, 256, 8)),
                   reads=["pb3", "SUMXN"], writes=["SUMXN"])

        def prelude_tile(tt):
            bu = tt % 2
            xch = XCH[:, 1024 * bu:1024 * (bu + 1)]
            st = 16 * bu
            rXCH, rST = f"XCH:{bu}", f"STT:{bu}"
            b.emit("Y", lambda y, tt=tt, xch=xch: y.dma_start(
                out=xch, in_=xo_in[128 * tt:128 * (tt + 1), :]),
                writes=[rXCH], dma="q")
            b.emit("V", lambda v, xch=xch, st=st: v.bn_stats(out=STT[:, st:st + 6], in_=xch[:, 0:512]),
                   reads=[rXCH], writes=[rST])
            b.emit("V", lambda v, xch=xch, st=st: v.bn_stats(out=STT[:, st + 6:st + 12], in_=xch[:, 512:1024]),
                   reads=[rXCH], writes=[rST])
            b.emit("V", lambda v, st=st: v.bn_aggr(out=STT[:, st + 12:st + 14], in_=STT[:, st:st + 12]),
                   reads=[rST], writes=[rST])
            b.emit("V", lambda v, st=st: v.tensor_scalar_add(out=STT[:, st + 14:st + 15],
                                                             in0=STT[:, st + 13:st + 14], scalar1=EPS),
                   reads=[rST], writes=[rST])
            b.emit("S", lambda s, st=st: s.activation(out=STT[:, st + 15:st + 16],
                                                      in_=STT[:, st + 14:st + 15], func=AF.Ln),
                   reads=[rST], writes=[rST])
            b.emit("S", lambda s, st=st: s.activation(out=STT[:, st + 14:st + 15],
                                                      in_=STT[:, st + 15:st + 16], func=AF.Exp, scale=-0.5),
                   reads=[rST], writes=[rST])
            b.emit("V", lambda v, tt=tt, xch=xch, st=st: v.tensor_scalar(
                out=XN[:, 1024 * tt:1024 * (tt + 1)], in0=xch,
                scalar1=STT[:, st + 12:st + 13], scalar2=STT[:, st + 14:st + 15],
                op0=ALU.subtract, op1=ALU.mult),
                reads=[rXCH, rST], writes=[f"XN:{tt}"] + (["ovXN"] if tt == 0 else []))

        gctr = [0]
        ydr_all = []

        def neg_bias(p):
            g0 = 8 + 4 * p
            b.emit("S", lambda s, p=p, g0=g0: s.activation(out=GFT[0:1, g0:g0 + 1],
                                                           in_=MAXA[0:1, p:p + 1], func=AF.Ln),
                   reads=["MAXA"], writes=["GFTb"])
            b.emit("S", lambda s, g0=g0: s.activation(out=GFT[0:1, g0 + 1:g0 + 2],
                                                      in_=MAXA[0:1, 2:3], func=AF.Ln),
                   reads=["MAXA", "GFTb"], writes=["GFTb"])
            b.emit("V", lambda v, g0=g0: v.tensor_add(out=GFT[0:1, g0 + 2:g0 + 3],
                                                      in0=GFT[0:1, g0:g0 + 1], in1=GFT[0:1, g0 + 1:g0 + 2]),
                   reads=["GFTb"], writes=["GFTb"])
            b.emit("S", lambda s, g0=g0: s.activation(out=GFT[0:1, g0 + 3:g0 + 4], in_=GFT[0:1, g0 + 2:g0 + 3],
                                                      func=AF.Exp, scale=0.5, bias=CONS[0:1, 0:1]),
                   reads=["GFTb", "CONS"], writes=["GFTb"])
            b.emit("V", lambda v, p=p, g0=g0: v.tensor_scalar_mul(out=NEGBB[0:1, p:p + 1],
                                                                  in0=GFT[0:1, g0 + 3:g0 + 4], scalar1=-1.0),
                   reads=["GFTb"], writes=["NEGBB"])
            b.emit("T", lambda t, p=p: t.matmul(psb(4, 400 + p, 1), ONEB[0:1, :],
                                                NEGBB[0:1, p:p + 1], start=True, stop=True),
                   reads=["NEGBB", "ONEB"], writes=["pb4"])
            b.emit("V", lambda v, p=p: v.tensor_copy(out=NEGB[:, p:p + 1], in_=psb(4, 400 + p, 1)),
                   reads=["pb4"], writes=["NEGB"])

        def qk_qb(p, qb):
            inst = 2 * qb + p
            QPn = "QP0" if p == 0 else "QP1"
            qch = QCH[p][qb]
            nkt = 2 * (qb + 1)
            if P2M < 2:
                return
            for g in range(nkt // 2):
                gpar = gctr[0] % 2
                gctr[0] += 1
                base = 0 if gpar == 0 else 2
                pta0 = 0 if gpar == 0 else 1024
                rPT = f"PTA:{gpar}"
                slots = []
                for gi in range(2):
                    kt = 2 * g + gi
                    for h in (0, 1):
                        si = 2 * gi + h
                        bank = base + si // 2
                        off = 256 * (si % 2)
                        pos = pta0 + 256 * si
                        slots.append((kt, h, bank, off, pos))
                qk_l, av_l = [], []
                for (kt, h, bank, off, pos) in slots:
                    kst = KCH[kt] if h == 0 else KBH[kt]
                    knm = f"KD:{kt}" if h == 0 else f"KB:{kt}"
                    qk_l.append(lambda kst=kst, bank=bank, off=off, qch=qch, knm=knm, h=h: b.emit(
                        "T", lambda t, kst=kst, bank=bank, off=off, qch=qch: t.matmul(
                            psb(bank, off, QB, 0, 128),
                            kst[:, :], qch[:, :],
                            start=True, stop=True),
                        reads=[knm, "KDzero" if h == 0 else "KBzero",
                               f"{QPn}:{2 * qb}", f"{QPn}:{2 * qb + 1}", "ovA", "ovK", "ovXN"],
                        writes=[f"pb{bank}"]))

                def mk_exp(base=base, pta0=pta0, p=p, rPT=rPT, slots=slots, qb=qb):
                    b.emit("S", lambda s: s.activation(
                        out=PTA[:, pta0:pta0 + 1024], in_=PS[:, 512 * base:512 * base + 1024],
                        func=AF.Exp, bias=NEGB[:, p:p + 1]),
                        reads=["NEGB", f"pb{base}", f"pb{base + 1}"], writes=[rPT])
                    if P2M >= 3:
                        for (kt, h, bank, off, pos) in slots:
                            if kt >= 2 * qb:
                                moff = 128 if kt == 2 * qb else 0
                                b.emit("V", lambda v, pos=pos, moff=moff: v.tensor_tensor(
                                    out=PTA[:, pos:pos + 256], in0=PTA[:, pos:pos + 256],
                                    in1=MSK[:, moff:moff + 256], op=ALU.mult),
                                    reads=[rPT, "MSK"], writes=[rPT])

                if P2M >= 4:
                    for (kt, h, bank, off, pos) in slots:
                        for j in range(2):
                            yb = 4 + 2 * h + j
                            av_l.append(lambda pos=pos, kt=kt, j=j, yb=yb, rPT=rPT,
                                        st=(kt == 0), lst=(kt == nkt - 1), inst=inst, h=h: b.emit(
                                "T", lambda t, pos=pos, yb=yb, kt=kt, st=st, lst=lst: t.matmul(
                                    psb(yb, 0, 65), PTA[:, pos + 128 * j:pos + 128 * j + 128],
                                    VT[:, 65 * kt:65 * (kt + 1)], start=st, stop=lst),
                                reads=[rPT, f"VT:{kt}", "VTones"], writes=[f"pb{yb}"],
                                accum=("av", inst, h, j)))
                tasks.append((qk_l, mk_exp, av_l, None))
            if P2M < 5:
                return

            def mk_norm(p=p, qb=qb, inst=inst):
                for h in range(2):
                    for j in range(2):
                        yb = 4 + 2 * h + j
                        sl = (inst % 2) * 4 + 2 * h + j
                        b.emit("V", lambda v, yb=yb, sl=sl: v.reciprocal(
                            out=RD[:, sl:sl + 1], in_=psb(yb, 64, 1)),
                            reads=[f"pb{yb}"], writes=[f"RD:{sl}"])
                        b.emit("V", lambda v, yb=yb, sl=sl: v.tensor_scalar_mul(
                            out=YST[:, 64 * sl:64 * (sl + 1)], in0=psb(yb, 0, 64),
                            scalar1=RD[:, sl:sl + 1]),
                            reads=[f"pb{yb}", f"RD:{sl}"], writes=[f"YST:{sl}"])
                        lh = 2 * p + h
                        r0y = QB * qb + 128 * j
                        b.emit("Y", lambda y, r0y=r0y, lh=lh, sl=sl: y.dma_start(
                            out=ydram[r0y:r0y + 128, 64 * lh:64 * (lh + 1)],
                            in_=YST[:, 64 * sl:64 * (sl + 1)]),
                            reads=[f"YST:{sl}"], writes=[f"ydr:{r0y}:{lh}"], dma="q")
                        ydr_all.append(f"ydr:{r0y}:{lh}")
            tasks[-1] = tasks[-1][:3] + (mk_norm,)

        if UPTO >= 1:
            n1 = min(NCH, int(os.environ.get("K_P1N", "999")))
            for c in range(n1):
                p1_A(c)
                if c >= 1:
                    p1_B(c - 1)
            if n1 >= 1:
                p1_B(n1 - 1)
        if UPTO >= 2:
            neg_bias(0)
            neg_bias(1)
            if UPTO >= 3 and not int(os.environ.get("K_NOWL", "0")):
                wload(WBIG, cw1_in, 8, 1024, "WBIG")
                wload(CN2, cw2_in, 8, 512, "CN2")
                wload(CN3, cw3_in, 4, 256, "CN3")
                wload(CN4, cw4_in, 2, 1, "CN4")
                wload(GC1, gw1_in, 8, 512, "GC1")
                wload(GC2, gw2_in, 4, 256, "GC2")
                wload(GC3, gw3_in, 2, 1, "GC3")
            if UPTO >= 3 and not NOPRE:
                for tt in range(4, NT):
                    prelude_tile(tt)
            tasks = []
            for qb in range(min(NQB, P2N)):
                qk_qb(0, qb)
                qk_qb(1, qb)
            prev = None
            for tk in tasks:
                for f in tk[0]:
                    f()
                tk[1]()
                if prev is not None:
                    for f in prev[2]:
                        f()
                    if prev[3] is not None:
                        prev[3]()
                prev = tk
            if prev is not None:
                for f in prev[2]:
                    f()
                if prev[3] is not None:
                    prev[3]()

        def early_out():
            b.emit("V", lambda v: v.memset(XN[:, :], 0.25), writes=[f"XN:{t}" for t in range(NT)])
            for tt in range(NT):
                b.emit("Y", lambda y, tt=tt: y.dma_start(
                    out=out_d[128 * tt:128 * (tt + 1), :], in_=XN[:, 1024 * tt:1024 * (tt + 1)]),
                    reads=[f"XN:{tt}"], writes=["outw"], dma="q")
            b.emit("P", lambda p: p.engine_nop(), reads=["outw"])
            block = es.enter_context(nc.Block())
            b.finalize(nc, es, block)

        if UPTO < 3:
            early_out()
            return nc

        if not NOCC:
            op_cc = b.emit("P", lambda p: p.collective_compute(
                "AllGather", ALU.bypass,
                replica_groups=[list(range(8))],
                ins=[ydram_h.ap().opt()], outs=[ygat_h.ap().opt()]),
                reads=([ydr_all[-1]] if ydr_all else ["ydram"]), writes=["ygat"])
            op_cc.need_inc = True

        for tt in (range(0) if NOPRE else range(4)):
            prelude_tile(tt)

        if NOFFN:
            early_out()
            return nc
        b.emit("V", lambda v: v.tensor_scalar_mul(out=XBARB[:, 0:8], in0=SUMXN[:, :], scalar1=1.0 / S),
               reads=["SUMXN"], writes=["XBARB"])
        for m in range(4):
            for k in range(8):
                b.emit("T", lambda t, m=m, k=k: t.matmul(
                    psb(0, m, 1), GC1.ap()[:, 512 * k + 128 * m:512 * k + 128 * m + 128],
                    XBARB[:, k:k + 1], start=(k == 0), stop=(k == 7)),
                    reads=["GC1", "XBARB"], writes=["pb0"], accum=("gc1", m))
        for m in range(4):
            b.emit("S", lambda s, m=m: s.activation(out=GFH1[:, m:m + 1], in_=psb(0, m, 1),
                                                    func=AF.Gelu, bias=GB1[:, m:m + 1]),
                   reads=["pb0", "GB1"], writes=["GFH1"])
        for m in range(2):
            for k in range(4):
                b.emit("T", lambda t, m=m, k=k: t.matmul(
                    psb(1, m, 1), GC2.ap()[:, 256 * k + 128 * m:256 * k + 128 * m + 128],
                    GFH1[:, k:k + 1], start=(k == 0), stop=(k == 3)),
                    reads=["GC2", "GFH1"], writes=["pb1"], accum=("gc2", m))
        for m in range(2):
            b.emit("S", lambda s, m=m: s.activation(out=GFH2[:, m:m + 1], in_=psb(1, m, 1),
                                                    func=AF.Gelu, bias=GB2[:, m:m + 1]),
                   reads=["pb1", "GB2"], writes=["GFH2"])
        for k in range(2):
            b.emit("T", lambda t, k=k: t.matmul(
                psb(2, 0, 1, 0, 1), GC3.ap()[:, k:k + 1], GFH2[:, k:k + 1],
                start=(k == 0), stop=(k == 1)),
                reads=["GC3", "GFH2"], writes=["pb2"], accum=("gc3", 0))
        b.emit("V", lambda v: v.tensor_scalar_mul(out=GB3[0:1, 1:2], in0=GB3[0:1, 0:1], scalar1=0.5),
               reads=["GB3"], writes=["GB3"])
        b.emit("V", lambda v: v.tensor_scalar_mul(out=CB4[0:1, 1:2], in0=CB4[0:1, 0:1], scalar1=0.5),
               reads=["CB4"], writes=["CB4"])
        b.emit("S", lambda s: s.activation(out=GFT[0:1, 0:1], in_=psb(2, 0, 1, 0, 1),
                                           func=AF.Tanh, scale=0.5, bias=GB3[0:1, 1:2]),
               reads=["pb2", "GB3"], writes=["GFTg"])
        b.emit("V", lambda v: v.tensor_scalar(out=GFT[0:1, 1:2], in0=GFT[0:1, 0:1],
                                              scalar1=0.5, scalar2=0.5, op0=ALU.mult, op1=ALU.add),
               reads=["GFTg"], writes=["GFTg"])
        b.emit("V", lambda v: v.tensor_mul(out=GFT[0:1, 2:3], in0=GFT[0:1, 1:2], in1=ART[0:1, 0:1]),
               reads=["GFTg", "ART"], writes=["GFTg"])
        b.emit("V", lambda v: v.tensor_scalar_mul(out=GFT[0:1, 3:4], in0=GFT[0:1, 2:3], scalar1=1.4),
               reads=["GFTg"], writes=["GFTg"])
        b.emit("V", lambda v: v.tensor_sub(out=GFT[0:1, 4:5], in0=GFT[0:1, 3:4], in1=ART[0:1, 0:1]),
               reads=["GFTg", "ART"], writes=["GFTg"])
        b.emit("V", lambda v: v.tensor_scalar_add(out=GFT[0:1, 5:6], in0=GFT[0:1, 4:5], scalar1=0.1),
               reads=["GFTg"], writes=["GFTg"])
        b.emit("V", lambda v: v.tensor_scalar_mul(out=GFT[0:1, 6:7], in0=ART[0:1, 0:1], scalar1=0.3),
               reads=["ART"], writes=["GFTg"])
        b.emit("V", lambda v: v.tensor_add(out=GFT[0:1, 7:8], in0=GFT[0:1, 5:6], in1=GFT[0:1, 6:7]),
               reads=["GFTg"], writes=["GFTg"])

        def ffn_cycle(cyc):
            for tc in range(NTCH):
                t0 = TCH * tc
                for ttl in range(8):
                    tt = 8 * tc + ttl
                    bu = ttl % 2
                    cdst = SCR[:, 1024 * bu:1024 * (bu + 1)]
                    first = (cyc == 0 and tc == 0 and ttl == 0)
                    b.emit("V", lambda v, tt=tt, cdst=cdst: v.tensor_copy(
                        out=cdst, in_=XN[:, 1024 * tt:1024 * (tt + 1)]),
                        reads=[f"XN:{tt}"] + ([] if first else ["ovS"]),
                        writes=[f"XNB:{bu}"] + (["ovS"] if first else []))
                    for f in range(8):
                        bank, off = 6 + (f // 4), 128 * (f % 4)
                        b.emit("T", lambda t, f=f, bank=bank, off=off, cdst=cdst: t.matmul(
                            psb(bank, off, 128), cdst[:, 128 * f:128 * (f + 1)], IDB[:, :],
                            start=True, stop=True),
                            reads=[f"XNB:{bu}", "IDB", "ovS"], writes=[f"pb{bank}"])
                    for f in range(8):
                        bank, off = 6 + (f // 4), 128 * (f % 4)
                        fe = (first and f == 0)
                        b.emit("V", lambda v, f=f, bank=bank, off=off, ttl=ttl: v.tensor_copy(
                            out=BIGA[:, 1024 * f + 128 * ttl:1024 * f + 128 * (ttl + 1)],
                            in_=psb(bank, off, 128)),
                            reads=[f"pb{bank}"] + ([] if fe else ["ovA"]),
                            writes=[f"OUTBT:{f}"] + (["ovA"] if fe else []))
                gi = 0
                for m in range(8):
                    for n in range(2):
                        bank = gi % 6
                        gi += 1
                        for k in range(8):
                            b.emit("T", lambda t, m=m, n=n, k=k, bank=bank: t.matmul(
                                psb(bank, 0, 512), WBIG.ap()[:, 1024 * k + 128 * m:1024 * k + 128 * m + 128],
                                BIGA[:, 1024 * k + 512 * n:1024 * k + 512 * n + 512],
                                start=(k == 0), stop=(k == 7)),
                                reads=[f"OUTBT:{k}", "WBIG", "ovA"], writes=[f"pb{bank}"],
                                accum=("h1", cyc, tc, m, n))
                        b.emit("S", lambda s, m=m, n=n, bank=bank: s.activation(
                            out=BIGA[:, 8192 + 1024 * m + 512 * n:8192 + 1024 * m + 512 * n + 512],
                            in_=psb(bank, 0, 512), func=AF.Gelu, bias=CB1[:, m:m + 1]),
                            reads=[f"pb{bank}", "CB1", "ovA"], writes=[f"H1B:{m}"])
                for m in range(4):
                    for n in range(2):
                        bank = gi % 6
                        gi += 1
                        for k in range(8):
                            b.emit("T", lambda t, m=m, n=n, k=k, bank=bank: t.matmul(
                                psb(bank, 0, 512), CN2.ap()[:, 512 * k + 128 * m:512 * k + 128 * m + 128],
                                BIGA[:, 8192 + 1024 * k + 512 * n:8192 + 1024 * k + 512 * n + 512],
                                start=(k == 0), stop=(k == 7)),
                                reads=[f"H1B:{k}", "CN2", "ovA"], writes=[f"pb{bank}"],
                                accum=("h2", cyc, tc, m, n))
                        wk = ["ovK"] if (cyc == 0 and tc == 0 and m == 0 and n == 0) else []
                        b.emit("S", lambda s, m=m, n=n, bank=bank: s.activation(
                            out=KD[:, 1024 * m + 512 * n:1024 * m + 512 * n + 512],
                            in_=psb(bank, 0, 512), func=AF.Gelu, bias=CB2[:, m:m + 1]),
                            reads=[f"pb{bank}", "CB2"] + ([] if wk else ["ovK"]),
                            writes=[f"H2B:{m}"] + wk)
                for m in range(2):
                    for n in range(2):
                        bank = gi % 6
                        gi += 1
                        for k in range(4):
                            b.emit("T", lambda t, m=m, n=n, k=k, bank=bank: t.matmul(
                                psb(bank, 0, 512), CN3.ap()[:, 256 * k + 128 * m:256 * k + 128 * m + 128],
                                KD[:, 1024 * k + 512 * n:1024 * k + 512 * n + 512],
                                start=(k == 0), stop=(k == 3)),
                                reads=[f"H2B:{k}", "CN3", "ovK"], writes=[f"pb{bank}"],
                                accum=("h3", cyc, tc, m, n))
                        b.emit("S", lambda s, m=m, n=n, bank=bank: s.activation(
                            out=KD[:, 4096 + 1024 * m + 512 * n:4096 + 1024 * m + 512 * n + 512],
                            in_=psb(bank, 0, 512), func=AF.Gelu, bias=CB3[:, m:m + 1]),
                            reads=[f"pb{bank}", "CB3", "ovK"], writes=[f"H3B:{m}"])
                for n in range(2):
                    for k in range(2):
                        b.emit("T", lambda t, n=n, k=k: t.matmul(
                            psb(6 + n, 0, 512, 0, 1),
                            CN4.ap()[:, k:k + 1],
                            KD[:, 4096 + 1024 * k + 512 * n:4096 + 1024 * k + 512 * n + 512],
                            start=(k == 0), stop=(k == 1)),
                            reads=[f"H3B:{k}", "CN4", "ovK"], writes=[f"pb{6 + n}"],
                            accum=("z", cyc, tc, n))
                for n in range(2):
                    b.emit("S", lambda s, n=n: s.activation(
                        out=CTLB[0:1, 512 * n:512 * (n + 1)], in_=psb(6 + n, 0, 512, 0, 1),
                        func=AF.Tanh, scale=0.5, bias=CB4[0:1, 1:2]),
                        reads=[f"pb{6 + n}", "CB4"], writes=["CTLB"])
                b.emit("V", lambda v: v.tensor_scalar(
                    out=CTLB[0:1, 0:1024], in0=CTLB[0:1, 0:1024],
                    scalar1=GFT[0:1, 6:7], scalar2=GFT[0:1, 7:8],
                    op0=ALU.mult, op1=ALU.add),
                    reads=["CTLB", "GFTg"], writes=["CTLB"])
                for ttl in range(8):
                    tt = 8 * tc + ttl
                    cb = ttl % 2
                    b.emit("T", lambda t, ttl=ttl: t.matmul(
                        psb(6, 480, 1), CTLB[0:1, 128 * ttl:128 * (ttl + 1)],
                        IDF[0:1, 0:1], start=True, stop=True),
                        reads=["CTLB", "IDF"], writes=["pb6"])
                    b.emit("V", lambda v, cb=cb: v.tensor_copy(out=CTL[:, 2 * cb:2 * cb + 1],
                                                               in_=psb(6, 480, 1)),
                           reads=["pb6"], writes=[f"CTL:{cb}"])
                    b.emit("V", lambda v, cb=cb: v.tensor_scalar_mul(out=CTL[:, 2 * cb + 1:2 * cb + 2],
                                                                     in0=CTL[:, 2 * cb:2 * cb + 1],
                                                                     scalar1=-1.0),
                           reads=[f"CTL:{cb}"], writes=[f"CTL:{cb}"])
                    x0 = 1024 * tt
                    q = lambda i: XN[:, x0 + 256 * i:x0 + 256 * (i + 1)]
                    rX = f"XN:{tt}"
                    b.emit("V", lambda v, tt=tt, cb=cb: v.scalar_tensor_tensor(
                        out=TMPQ[:, 0:256], in0=XN[:, 1024 * tt + 512:1024 * tt + 768],
                        scalar=CTL[:, 2 * cb:2 * cb + 1], in1=XN[:, 1024 * tt:1024 * tt + 256],
                        op0=ALU.mult, op1=ALU.add),
                        reads=[rX, f"CTL:{cb}"], writes=["TMPQ"])
                    b.emit("V", lambda v, tt=tt, cb=cb: v.scalar_tensor_tensor(
                        out=TMPQ[:, 256:512], in0=XN[:, 1024 * tt + 768:1024 * tt + 1024],
                        scalar=CTL[:, 2 * cb + 1:2 * cb + 2], in1=XN[:, 1024 * tt + 256:1024 * tt + 512],
                        op0=ALU.mult, op1=ALU.add),
                        reads=[rX, f"CTL:{cb}"], writes=["TMPQ"])
                    b.emit("V", lambda v, tt=tt, cb=cb: v.scalar_tensor_tensor(
                        out=XN[:, 1024 * tt + 512:1024 * tt + 768], in0=XN[:, 1024 * tt:1024 * tt + 256],
                        scalar=CTL[:, 2 * cb + 1:2 * cb + 2], in1=XN[:, 1024 * tt + 512:1024 * tt + 768],
                        op0=ALU.mult, op1=ALU.add),
                        reads=[rX, f"CTL:{cb}"], writes=[rX])
                    b.emit("V", lambda v, tt=tt, cb=cb: v.scalar_tensor_tensor(
                        out=XN[:, 1024 * tt + 768:1024 * tt + 1024], in0=XN[:, 1024 * tt + 256:1024 * tt + 512],
                        scalar=CTL[:, 2 * cb:2 * cb + 1], in1=XN[:, 1024 * tt + 768:1024 * tt + 1024],
                        op0=ALU.mult, op1=ALU.add),
                        reads=[rX, f"CTL:{cb}"], writes=[rX])
                    b.emit("V", lambda v, tt=tt: v.tensor_copy(
                        out=XN[:, 1024 * tt:1024 * tt + 256], in_=TMPQ[:, 0:256]),
                        reads=["TMPQ"], writes=[rX])
                    b.emit("V", lambda v, tt=tt: v.tensor_copy(
                        out=XN[:, 1024 * tt + 256:1024 * tt + 512], in_=TMPQ[:, 256:512]),
                        reads=["TMPQ"], writes=[rX])

        for cyc in range(3):
            ffn_cycle(cyc)

        wload(WBIG, wo_in, 8, 1024, "WBIG")

        if UPTO < 4:
            early_out()
            return nc
        for tt in range(NT):
            for r in range(4):
                wA = ["ovA"] if (tt == 0 and r == 0) else []
                b.emit("P", lambda p, tt=tt, r=r: p.indirect_dma_start(
                    out=BIGA[:, 1024 * tt + 256 * r:1024 * tt + 256 * (r + 1)],
                    out_offset=None, in_=ygat,
                    in_offset=IndirectOffsetOnAxis(ap=YIX[:, 4 * tt + r:4 * tt + r + 1], axis=0)),
                    reads=["ygat", "YIX"] + ([] if wA else ["ovA"]),
                    writes=[f"YALL:{tt}"] + wA, dma="yin")
        for tt in range(NT):
            for f in range(8):
                bank, off = 6 + (f // 4), 128 * (f % 4)
                wS = ["ovS"] if (tt == 0 and f == 0) else []
                b.emit("T", lambda t, tt=tt, f=f, bank=bank, off=off: t.matmul(
                    psb(bank, off, 128), BIGA[:, 1024 * tt + 128 * f:1024 * tt + 128 * (f + 1)],
                    IDB[:, :], start=True, stop=True),
                    reads=[f"YALL:{tt}", "IDB"], writes=[f"pb{bank}"])
                b.emit("V", lambda v, f=f, bank=bank, off=off: v.tensor_copy(
                    out=SCR[:, 128 * f:128 * (f + 1)], in_=psb(bank, off, 128)),
                    reads=[f"pb{bank}"] + ([] if wS else ["ovS"]), writes=["YTT"] + wS)
            for n in range(2):
                bank = (2 * tt + n) % 4
                for k in range(8):
                    b.emit("T", lambda t, n=n, k=k, bank=bank: t.matmul(
                        psb(bank, 0, 512), SCR[:, 128 * k:128 * (k + 1)],
                        WBIG.ap()[:, 1024 * k + 512 * n:1024 * k + 512 * n + 512],
                        start=(k == 0), stop=(k == 7)),
                        reads=["YTT", "WBIG"], writes=[f"pb{bank}"], accum=("wo", tt, n))
                b.emit("V", lambda v, tt=tt, n=n, bank=bank: v.tensor_add(
                    out=XN[:, 1024 * tt + 512 * n:1024 * tt + 512 * n + 512],
                    in0=XN[:, 1024 * tt + 512 * n:1024 * tt + 512 * n + 512],
                    in1=psb(bank, 0, 512)),
                    reads=[f"pb{bank}", f"XN:{tt}"], writes=[f"XN:{tt}"])
            bu = tt % 2
            xch = XCH[:, 1024 * bu:1024 * (bu + 1)]
            b.emit("Y", lambda y, tt=tt, xch=xch: y.dma_start(
                out=xch, in_=xo_in[128 * tt:128 * (tt + 1), :]),
                writes=[f"XCH:{bu}"], dma="q")
            b.emit("V", lambda v, tt=tt, xch=xch: v.tensor_add(
                out=XN[:, 1024 * tt:1024 * (tt + 1)],
                in0=XN[:, 1024 * tt:1024 * (tt + 1)], in1=xch),
                reads=[f"XCH:{bu}", f"XN:{tt}"], writes=[f"XN:{tt}"])
            b.emit("Y", lambda y, tt=tt: y.dma_start(
                out=out_d[128 * tt:128 * (tt + 1), :], in_=XN[:, 1024 * tt:1024 * (tt + 1)]),
                reads=[f"XN:{tt}"], writes=[f"outw:{tt}"], dma="q")

        b.emit("P", lambda p: p.engine_nop(), reads=[f"outw:{NT - 1}"])

        block = es.enter_context(nc.Block())
        b.finalize(nc, es, block)
    return nc


_CACHE = {}


def _tables():
    if "t" in _CACHE:
        return _CACHE["t"]
    half = 32
    inv = 1.0 / (10000.0 ** (np.arange(half, dtype=np.float64) / half))
    t = np.arange(S, dtype=np.float64)
    fr = np.outer(inv, t)
    emb = np.concatenate([fr, fr], axis=0)
    cos = np.cos(emb)
    sin = np.sin(emb)
    cos2 = np.concatenate([cos, cos], axis=0)
    sin2 = np.concatenate([sin, sin], axis=0)
    bf = lambda a: a.astype(ml_dtypes.bfloat16)
    cosq = bf(cos2 / 8.0)
    sinq = bf(sin2 / 8.0)
    cosk = bf(cos2)
    sink = bf(sin2)
    rp = np.zeros((128, 128), dtype=np.float32)
    for base in (0, 64):
        for m in range(32):
            rp[base + m + 32, base + m] = -1.0
            rp[base + m, base + m + 32] = 1.0
    idb = np.eye(128, dtype=ml_dtypes.bfloat16)
    idf = np.eye(128, dtype=np.float32)
    oneb = np.ones((128, 128), dtype=ml_dtypes.bfloat16)
    msk = np.zeros((128, 384), dtype=np.float32)
    for pp in range(128):
        for j in range(384):
            if pp <= j - 128:
                msk[pp, j] = 1.0
    _CACHE["t"] = (cosq, sinq, cosk, sink, bf(rp), idb, idf, oneb, bf(msk))
    return _CACHE["t"]


def kernel(**inputs):
    x = np.asarray(inputs["L0_attn_x"], dtype=np.float32)
    Wq = np.ascontiguousarray(np.asarray(inputs["L0_attn_Wq"], dtype=np.float32))
    Wk = np.ascontiguousarray(np.asarray(inputs["L0_attn_Wk"], dtype=np.float32))
    Wv = np.ascontiguousarray(np.asarray(inputs["L0_attn_Wv"], dtype=np.float32))
    Wo = np.ascontiguousarray(np.asarray(inputs["L0_attn_Wo"], dtype=np.float32))
    cosq, sinq, cosk, sink, rp, idb, idf, oneb, msk = _tables()

    if "nc" not in _CACHE:
        _CACHE["nc"] = build_program()
    nc = _CACHE["nc"]

    shared = {
        "wo": Wo,
        "cw1": np.ascontiguousarray(np.asarray(inputs["L0_ffn_cn_w1"], dtype=np.float32)),
        "cw2": np.ascontiguousarray(np.asarray(inputs["L0_ffn_cn_w2"], dtype=np.float32)),
        "cw3": np.ascontiguousarray(np.asarray(inputs["L0_ffn_cn_w3"], dtype=np.float32)),
        "cw4": np.ascontiguousarray(np.asarray(inputs["L0_ffn_cn_w4"], dtype=np.float32)),
        "gw1": np.ascontiguousarray(np.asarray(inputs["L0_ffn_gc_w1"], dtype=np.float32)),
        "gw2": np.ascontiguousarray(np.asarray(inputs["L0_ffn_gc_w2"], dtype=np.float32)),
        "gw3": np.ascontiguousarray(np.asarray(inputs["L0_ffn_gc_w3"], dtype=np.float32)),
        "cb1": np.ascontiguousarray(np.asarray(inputs["L0_ffn_cn_b1"], dtype=np.float32)),
        "cb2": np.ascontiguousarray(np.asarray(inputs["L0_ffn_cn_b2"], dtype=np.float32)),
        "cb3": np.ascontiguousarray(np.asarray(inputs["L0_ffn_cn_b3"], dtype=np.float32)),
        "cb4": np.ascontiguousarray(np.asarray(inputs["L0_ffn_cn_b4"], dtype=np.float32)),
        "gb1": np.ascontiguousarray(np.asarray(inputs["L0_ffn_gc_b1"], dtype=np.float32)),
        "gb2": np.ascontiguousarray(np.asarray(inputs["L0_ffn_gc_b2"], dtype=np.float32)),
        "gb3": np.ascontiguousarray(np.asarray(inputs["L0_ffn_gc_b3"], dtype=np.float32)),
        "ar": np.ascontiguousarray(np.asarray(inputs["L0_ffn_adaptive_range"], dtype=np.float32)),
        "cosq": cosq, "sinq": sinq, "cosk": cosk, "sink": sink,
        "rp": rp, "idb": idb, "idf": idf, "oneb": oneb, "msk": msk,
    }
    in_maps = []
    for c in range(NCORE := 8):
        bb, kh = divmod(c, 4)
        m = dict(shared)
        m["x"] = np.ascontiguousarray(x[bb])
        m["xown"] = np.ascontiguousarray(x[bb, TOK * kh:TOK * (kh + 1)])
        m["wq"] = np.ascontiguousarray(Wq[:, 256 * kh:256 * (kh + 1)])
        m["wk"] = np.ascontiguousarray(Wk[:, 64 * kh:64 * (kh + 1)])
        m["wv"] = np.ascontiguousarray(Wv[:, 64 * kh:64 * (kh + 1)])
        yidx = np.empty((128, 64), dtype=np.int32)
        for tt in range(NT):
            for r in range(4):
                yidx[:, 4 * tt + r] = S * (4 * bb + r) + TOK * kh + 128 * tt + np.arange(128)
        m["yidx"] = yidx
        in_maps.append(m)

    res = run_bass_kernel_spmd(nc, in_maps, core_ids=list(range(8)))
    out = np.empty((2, S, D), dtype=np.float32)
    for c in range(8):
        bb, kh = divmod(c, 4)
        out[bb, TOK * kh:TOK * (kh + 1)] = res.results[c]["out"]
    return out


# version 10
# speedup vs baseline: 1.6113x; 1.0275x over previous
import contextlib
import math
import os
import numpy as np
import ml_dtypes
import concourse.bass as bass
import concourse.mybir as mybir
from concourse.bass_utils import run_bass_kernel_spmd
from concourse.bass import IndirectOffsetOnAxis
F32 = mybir.dt.float32
BF16 = mybir.dt.bfloat16
AF = mybir.ActivationFunctionType
ALU = mybir.AluOpType
AX = mybir.AxisListType

D = 1024
S = 8192
NCH = S // 128
QB = 256
NQB = S // QB
TOK = S // 4
NT = TOK // 128
TCH = 1024
NTCH = TOK // TCH
EPS = 1e-5
LN2H = 0.5 * math.log(2.0)
UPTO = int(os.environ.get("K_UPTO", "4"))
P2N = int(os.environ.get("K_P2N", "999"))
P2M = int(os.environ.get("K_P2M", "5"))
P2QK = int(os.environ.get("K_P2QK", "9999"))
P2E = int(os.environ.get("K_P2E", "1"))
NOCC = int(os.environ.get("K_NOCC", "0"))
NOFFN = int(os.environ.get("K_NOFFN", "0"))
NOPRE = int(os.environ.get("K_NOPRE", "0"))


class Op:
    __slots__ = ("eng", "fn", "deps", "idx", "need_inc", "val", "dma", "accum", "burst")

    def __init__(self, eng, fn, dma, accum):
        self.eng, self.fn, self.dma, self.accum = eng, fn, dma, accum
        self.deps = []
        self.idx = -1
        self.need_inc = False
        self.val = 0
        self.burst = 0


class Bld:
    def __init__(self):
        self.ops = {e: [] for e in "TVSPY"}
        self.res = {}
        self.groups = {}

    def emit(self, eng, fn, reads=(), writes=(), dma=None, accum=None):
        op = Op(eng, fn, dma, accum)
        deps = {}
        for r in reads:
            ent = self.res.get(r)
            if ent and ent[0] is not None:
                deps[id(ent[0])] = ent[0]
        for w in writes:
            ent = self.res.get(w)
            if ent:
                if ent[0] is not None:
                    deps[id(ent[0])] = ent[0]
                for rd in ent[1]:
                    deps[id(rd)] = rd
        ds = []
        for d in deps.values():
            if d is op:
                continue
            if d.eng == "T" and eng == "T" and d.dma is None and dma is None:
                continue
            if accum is not None and d.accum == accum:
                continue
            if dma not in (None, "q") and d.eng == eng and d.dma == dma:
                continue
            ds.append(d)
        op.deps = ds
        for r in reads:
            self.res.setdefault(r, [None, []])[1].append(op)
        for w in writes:
            self.res[w] = [op, []]
        op.idx = len(self.ops[eng])
        self.ops[eng].append(op)
        if dma is not None and dma != "q" and eng == "P":
            self.groups.setdefault(dma, []).append(op)
        return op

    def finalize(self, nc, es, block):
        sems = {}
        for e in "TVSPY":
            sems[e] = es.enter_context(nc.semaphore(f"sem_{e}"))
        for g in self.groups:
            sems["g:" + g] = es.enter_context(nc.semaphore(f"sem_g_{g}"))
        for e in "TVSPY":
            for op in self.ops[e]:
                for d in op.deps:
                    d.need_inc = True
        for ops_ in self.groups.values():
            for o in ops_:
                o.need_inc = True
        for e in "TVSPY":
            for op in self.ops[e]:
                if op.dma is not None:
                    op.need_inc = True
        gtotal = {g: 16 * len(ops_) for g, ops_ in self.groups.items()}
        for e in "TVSPY":
            c = 0
            for op in self.ops[e]:
                if op.eng == "P" and op.dma not in (None, "q"):
                    op.val = 0
                    continue
                if op.need_inc:
                    c += 16 if (op.dma is not None) else 1
                op.val = c

        burst_end = {"Y": {}, "P": {}}
        for e in ("Y", "P"):
            bid = 0
            for op in self.ops[e]:
                if op.deps:
                    bid += 1
                op.burst = bid
                burst_end[e][bid] = op.val

        def sem_of(d):
            if d.eng == "P" and d.dma not in (None, "q"):
                return ("g:" + d.dma, gtotal[d.dma])
            if d.eng in ("Y", "P"):
                return (d.eng, burst_end[d.eng][d.burst])
            return (d.eng, d.val)

        waited = {"Y": set(), "P": set()}
        for e in "TVSPY":
            for op in self.ops[e]:
                for d in op.deps:
                    s, v = sem_of(d)
                    if s in waited:
                        waited[s].add(v)

        def run_engine(e, eng_obj):
            water = {}
            for op in self.ops[e]:
                waits = {}
                for d in op.deps:
                    s, v = sem_of(d)
                    if waits.get(s, -1) < v:
                        waits[s] = v
                for s, v in sorted(waits.items()):
                    if water.get(s, -1) >= v:
                        continue
                    eng_obj.wait_ge(sems[s], v)
                    water[s] = v
                ins = op.fn(eng_obj)
                if op.need_inc:
                    if op.eng == "P" and op.dma not in (None, "q"):
                        ins.then_inc(sems["g:" + op.dma], 16)
                    elif op.dma is not None:
                        ins.then_inc(sems[op.eng], 16)
                    else:
                        ins.then_inc(sems[op.eng], 1)
                if e in ("Y", "P") and op.need_inc and op.dma in (None, "q") and op.val in waited[e]:
                    eng_obj.wait_ge(sems[e], op.val)
                    water[e] = max(water.get(e, -1), op.val)

        @block.tensor
        def _(t):
            run_engine("T", t)

        @block.vector
        def _(v):
            run_engine("V", v)

        @block.scalar
        def _(s):
            run_engine("S", s)

        @block.gpsimd
        def _(p):
            run_engine("P", p)

        @block.sync
        def _(y):
            run_engine("Y", y)


def build_program():
    nc = bass.Bass("TRN2", target_bir_lowering=False)

    x_in = nc.dram_tensor("x", [S, D], F32, kind="ExternalInput").ap()
    xo_in = nc.dram_tensor("xown", [TOK, D], F32, kind="ExternalInput").ap()
    wq_in = nc.dram_tensor("wq", [D, 256], F32, kind="ExternalInput").ap()
    wk_in = nc.dram_tensor("wk", [D, 64], F32, kind="ExternalInput").ap()
    wv_in = nc.dram_tensor("wv", [D, 64], F32, kind="ExternalInput").ap()
    wo_in = nc.dram_tensor("wo", [D, D], F32, kind="ExternalInput").ap()
    cw1_in = nc.dram_tensor("cw1", [D, D], F32, kind="ExternalInput").ap()
    cw2_in = nc.dram_tensor("cw2", [D, 512], F32, kind="ExternalInput").ap()
    cw3_in = nc.dram_tensor("cw3", [512, 256], F32, kind="ExternalInput").ap()
    cw4_in = nc.dram_tensor("cw4", [256, 1], F32, kind="ExternalInput").ap()
    gw1_in = nc.dram_tensor("gw1", [D, 512], F32, kind="ExternalInput").ap()
    gw2_in = nc.dram_tensor("gw2", [512, 256], F32, kind="ExternalInput").ap()
    gw3_in = nc.dram_tensor("gw3", [256, 1], F32, kind="ExternalInput").ap()
    cb1_in = nc.dram_tensor("cb1", [D], F32, kind="ExternalInput").ap()
    cb2_in = nc.dram_tensor("cb2", [512], F32, kind="ExternalInput").ap()
    cb3_in = nc.dram_tensor("cb3", [256], F32, kind="ExternalInput").ap()
    cb4_in = nc.dram_tensor("cb4", [1], F32, kind="ExternalInput").ap()
    gb1_in = nc.dram_tensor("gb1", [512], F32, kind="ExternalInput").ap()
    gb2_in = nc.dram_tensor("gb2", [256], F32, kind="ExternalInput").ap()
    gb3_in = nc.dram_tensor("gb3", [1], F32, kind="ExternalInput").ap()
    ar_in = nc.dram_tensor("ar", [1], F32, kind="ExternalInput").ap()
    cosq_in = nc.dram_tensor("cosq", [128, S], BF16, kind="ExternalInput").ap()
    sinq_in = nc.dram_tensor("sinq", [128, S], BF16, kind="ExternalInput").ap()
    cosk_in = nc.dram_tensor("cosk", [128, S], BF16, kind="ExternalInput").ap()
    sink_in = nc.dram_tensor("sink", [128, S], BF16, kind="ExternalInput").ap()
    rp_in = nc.dram_tensor("rp", [128, 128], BF16, kind="ExternalInput").ap()
    idb_in = nc.dram_tensor("idb", [128, 128], BF16, kind="ExternalInput").ap()
    idf_in = nc.dram_tensor("idf", [128, 128], F32, kind="ExternalInput").ap()
    oneb_in = nc.dram_tensor("oneb", [128, 128], BF16, kind="ExternalInput").ap()
    msk_in = nc.dram_tensor("msk", [128, 384], BF16, kind="ExternalInput").ap()
    out_d = nc.dram_tensor("out", [TOK, D], F32, kind="ExternalOutput").ap()
    yidx_in = nc.dram_tensor("yidx", [128, 64], mybir.dt.int32, kind="ExternalInput").ap()
    ydram_h = nc.dram_tensor("ydram", [S, 256], BF16)
    ygat_h = nc.dram_tensor("ygat", [8 * S, 256], BF16)
    ydram = ydram_h.ap()
    ygat = ygat_h.ap()

    es = contextlib.ExitStack()
    SB = lambda n, sh, dt: es.enter_context(nc.sbuf_tensor(n, sh, dt))
    b = Bld()

    with es:
        XN = SB("XN", [128, NT * 1024], F32)
        BIGA = SB("BIGA", [128, 16384], BF16)
        KD = SB("KD", [128, 8192], BF16)
        VT = SB("VT", [128, 65 * NCH], BF16)
        XCH = SB("XCH", [128, 2048], F32)
        SCR = SB("SCR", [128, 4096], BF16)
        PTA = SB("PTA", [128, 2048], BF16)
        CS = SB("CS", [128, 1024], BF16)
        RSB = SB("RSB", [128, 2048], BF16)
        SCB = SB("SCB", [128, 768], BF16)
        STT = SB("STT", [128, 32], F32)
        SUMXN = SB("SUMXN", [128, 8], F32)
        MAXA = SB("MAXA", [1, 4], F32)
        MScr = SB("MScr", [1, 8], F32)
        NEGB = SB("NEGB", [128, 2], F32)
        NEGBB = SB("NEGBB", [1, 2], BF16)
        GFT = SB("GFT", [1, 16], F32)
        XBARB = SB("XBARB", [128, 8], BF16)
        GFH1 = SB("GFH1", [128, 4], BF16)
        GFH2 = SB("GFH2", [128, 2], BF16)
        CTL = SB("CTL", [128, 4], F32)
        CTLB = SB("CTLB", [1, 1024], F32)
        TMP1 = SB("TMP1", [128, 128], F32)
        TMP2 = SB("TMP2", [128, 128], F32)
        TMPQ = SB("TMPQ", [128, 512], F32)
        YST = SB("YST", [128, 512], BF16)
        RD = SB("RD", [128, 8], F32)
        IDB = SB("IDB", [128, 128], BF16)
        IDF = SB("IDF", [128, 128], F32)
        ONEB = SB("ONEB", [128, 128], BF16)
        MSK = SB("MSK", [128, 384], BF16)
        RP = SB("RPs", [128, 128], BF16)
        WQB = SB("WQB", [128, 2048], BF16)
        WKB = SB("WKB", [128, 512], BF16)
        WVB = SB("WVB", [128, 512], BF16)
        WBIG = SB("WBIG", [128, 8192], BF16)
        CN2 = SB("CN2", [128, 4096], BF16)
        CN3 = SB("CN3", [128, 1024], BF16)
        CN4 = SB("CN4", [128, 2], BF16)
        GC1 = SB("GC1", [128, 4096], BF16)
        GC2 = SB("GC2", [128, 1024], BF16)
        GC3 = SB("GC3", [128, 2], BF16)
        CB1 = SB("CB1", [128, 8], F32)
        CB2 = SB("CB2", [128, 4], F32)
        CB3 = SB("CB3", [128, 2], F32)
        CB4 = SB("CB4", [1, 2], F32)
        GB1 = SB("GB1", [128, 4], F32)
        GB2 = SB("GB2", [128, 2], F32)
        GB3 = SB("GB3", [1, 2], F32)
        ART = SB("ART", [1, 1], F32)
        CONS = SB("CONS", [1, 4], F32)
        YIX = SB("YIX", [128, 64], mybir.dt.int32)
        PS = es.enter_context(nc.psum_tensor("PS", [128, 4096], F32))
        biga_addr = nc.lookup_mloc(BIGA).addr
        kd_addr = nc.lookup_mloc(KD).addr
        QCH = [[nc.alloc_sbuf_tensor_at(f"QCH{p_}_{c_}", (128, 256), BF16,
                                        offset=biga_addr + 16384 * p_ + 512 * c_)
                for c_ in range(NCH // 2)] for p_ in range(2)]
        KCH = [nc.alloc_sbuf_tensor_at(f"KCH_{c_}", (128, 128), BF16,
                                       offset=kd_addr + 256 * c_)
               for c_ in range(NCH)]
        xn_addr = nc.lookup_mloc(XN).addr
        KBH = [nc.alloc_sbuf_tensor_at(f"KBH_{c_}", (128, 128), BF16,
                                       offset=xn_addr + 256 * c_)
               for c_ in range(NCH)]
        KBALL = nc.alloc_sbuf_tensor_at("KBALL", (128, 8192), BF16, offset=xn_addr)

        def psb(bank, off, w, p0=0, p1=128):
            return PS[p0:p1, 512 * bank + off:512 * bank + off + w]

        stg_par = [0]

        def wload(dst, srcap, kt, ncols, name):
            for k in range(kt):
                bu = stg_par[0] % 2
                stg_par[0] += 1
                stg = XCH[:, 1024 * bu:1024 * bu + ncols]
                b.emit("Y", lambda y, k=k, stg=stg, srcap=srcap: y.dma_start(
                    out=stg, in_=srcap[128 * k:128 * (k + 1), :]),
                    writes=[f"XCH:{bu}"], dma="q")
                b.emit("P", lambda p, k=k, stg=stg, dst=dst, ncols=ncols: p.tensor_copy(
                    out=dst[:, ncols * k:ncols * (k + 1)], in_=stg),
                    reads=[f"XCH:{bu}"], writes=[name])

        if not int(os.environ.get("K_NOQKV", "0")):
            b.emit("Y", lambda y: y.dma_start(out=YIX[:, :], in_=yidx_in), writes=["YIX"], dma="q")
            wload(WQB, wq_in, 8, 256, "WQB")
            wload(WKB, wk_in, 8, 64, "WKB")
            wload(WVB, wv_in, 8, 64, "WVB")

        def bload(dst, src, m, name):
            for i in range(m):
                b.emit("Y", lambda y, d=dst, s=src, i=i:
                       y.dma_start(out=d[:, i:i + 1],
                                   in_=s[128 * i:128 * (i + 1)].rearrange("(p o) -> p o", o=1)),
                       writes=[name], dma="q")

        if not int(os.environ.get("K_NOBL", "0")):
            bload(CB1, cb1_in, 8, "CB1")
            bload(CB2, cb2_in, 4, "CB2")
            bload(CB3, cb3_in, 2, "CB3")
            bload(GB1, gb1_in, 4, "GB1")
            bload(GB2, gb2_in, 2, "GB2")
            for dst_, src_, nm_ in ((CB4, cb4_in, "CB4"), (GB3, gb3_in, "GB3"), (ART, ar_in, "ART")):
                b.emit("Y", lambda y, d=dst_, s=src_: y.dma_start(out=d[0:1, 0:1], in_=s.rearrange("(x o) -> x o", x=1)),
                       writes=[nm_], dma="q")
        if not int(os.environ.get("K_NOTB", "0")):
            for dst_, src_, nm_ in ((IDB, idb_in, "IDB"), (IDF, idf_in, "IDF"), (ONEB, oneb_in, "ONEB"),
                                    (MSK, msk_in, "MSK"), (RP, rp_in, "RPc")):
                b.emit("Y", lambda y, d=dst_, s=src_: y.dma_start(out=d[:, :], in_=s), writes=[nm_], dma="q")
            b.emit("V", lambda v: v.memset(MAXA[:, :], -1e30), writes=["MAXA"])
            b.emit("V", lambda v: v.memset(SUMXN[:, :], 0.0), writes=["SUMXN"])
            b.emit("V", lambda v: v.memset(VT[:, :], 1.0), writes=["VTones"])
        b.emit("V", lambda v: v.memset(KD[64:128, :], 0.0), writes=["KDzero"])
        b.emit("V", lambda v: v.memset(KBALL[0:64, :], 0.0), writes=["KBzero", "ovXN"])
        b.emit("V", lambda v: v.memset(CONS[:, :], 0.0), writes=["CONS"])

        def p1_A1(c):
            bu = c % 2
            r0 = 128 * c
            xch = XCH[:, 1024 * bu:1024 * (bu + 1)]
            rXCH = f"XCH:{bu}"
            b.emit("Y", lambda y, xch=xch, r0=r0: y.dma_start(out=xch, in_=x_in[r0:r0 + 128, :]),
                   writes=[rXCH], dma="q")

        def p1_A2(c):
            bu = c % 2
            r0 = 128 * c
            xch = XCH[:, 1024 * bu:1024 * (bu + 1)]
            xnb = SCR[:, 1024 * bu:1024 * (bu + 1)]
            xnt0 = 2048 + 1024 * bu
            st = 16 * bu
            cso = 512 * bu
            ro = 1024 * bu
            rXCH, rXNB, rXNT = f"XCH:{bu}", f"XNB:{bu}", f"XNT:{bu}"
            rST, rCS, rRSB = f"STT:{bu}", f"CS:{bu}", f"RSB:{bu}"

            for j, tab in enumerate((cosq_in, sinq_in, cosk_in, sink_in)):
                b.emit("Y", lambda y, j=j, tab=tab: y.dma_start(
                    out=CS[:, cso + 128 * j:cso + 128 * (j + 1)], in_=tab[:, r0:r0 + 128]),
                    writes=[rCS], dma="q")

            b.emit("V", lambda v: v.bn_stats(out=STT[:, st:st + 6], in_=xch[:, 0:512]),
                   reads=[rXCH], writes=[rST])
            b.emit("V", lambda v: v.bn_stats(out=STT[:, st + 6:st + 12], in_=xch[:, 512:1024]),
                   reads=[rXCH], writes=[rST])
            b.emit("V", lambda v: v.bn_aggr(out=STT[:, st + 12:st + 14], in_=STT[:, st:st + 12]),
                   reads=[rST], writes=[rST])
            b.emit("V", lambda v: v.tensor_scalar_add(out=STT[:, st + 14:st + 15],
                                                      in0=STT[:, st + 13:st + 14], scalar1=EPS),
                   reads=[rST], writes=[rST])
            b.emit("S", lambda s: s.activation(out=STT[:, st + 15:st + 16],
                                               in_=STT[:, st + 14:st + 15], func=AF.Ln),
                   reads=[rST], writes=[rST])
            b.emit("S", lambda s: s.activation(out=STT[:, st + 14:st + 15],
                                               in_=STT[:, st + 15:st + 16], func=AF.Exp, scale=-0.5),
                   reads=[rST], writes=[rST])
            b.emit("V", lambda v: v.tensor_scalar(out=xnb, in0=xch,
                                                  scalar1=STT[:, st + 12:st + 13],
                                                  scalar2=STT[:, st + 14:st + 15],
                                                  op0=ALU.subtract, op1=ALU.mult),
                   reads=[rXCH, rST, "ovS"], writes=[rXNB])

            for f in range(8):
                bank, off = 4 + (f // 4), 128 * (f % 4)
                b.emit("T", lambda t, f=f, bank=bank, off=off: t.matmul(
                    psb(bank, off, 128), xnb[:, 128 * f:128 * (f + 1)], IDB[:, :],
                    start=True, stop=True),
                    reads=[rXNB, "IDB", "ovS"], writes=[f"pb{bank}"])
            for g2 in range(2):
                bank = 4 + g2
                b.emit("V", lambda v, g2=g2, bank=bank: v.tensor_copy(
                    out=SCR[:, xnt0 + 512 * g2:xnt0 + 512 * (g2 + 1)], in_=psb(bank, 0, 512)),
                    reads=[f"pb{bank}", "ovS"], writes=[rXNT])

        def p1_B(c):
            bu = c % 2
            r0 = 128 * c
            xch = XCH[:, 1024 * bu:1024 * (bu + 1)]
            xnb = SCR[:, 1024 * bu:1024 * (bu + 1)]
            xnt0 = 2048 + 1024 * bu
            st = 16 * bu
            cso = 512 * bu
            ro = 1024 * bu
            rXCH, rXNB, rXNT = f"XCH:{bu}", f"XNB:{bu}", f"XNT:{bu}"
            rST, rCS, rRSB = f"STT:{bu}", f"CS:{bu}", f"RSB:{bu}"

            for m in range(4):
                bank = m
                for k in range(8):
                    if m < 2:
                        st_ap = WQB.ap()[:, 256 * k + 128 * m:256 * k + 128 * m + 128]
                        wnm = "WQB"
                    elif m == 2:
                        st_ap = WKB.ap()[:, 64 * k:64 * (k + 1)]
                        wnm = "WKB"
                    else:
                        st_ap = WVB.ap()[:, 64 * k:64 * (k + 1)]
                        wnm = "WVB"
                    p1_ = 128 if m < 2 else 64
                    b.emit("T", lambda t, k=k, bank=bank, st_ap=st_ap, p1_=p1_: t.matmul(
                        psb(bank, 0, 128, 0, p1_), st_ap,
                        SCR[:, xnt0 + 128 * k:xnt0 + 128 * (k + 1)],
                        start=(k == 0), stop=(k == 7)),
                        reads=[rXNT, wnm], writes=[f"pb{bank}"], accum=("p1", c, m))
            for m in range(2):
                b.emit("S", lambda s, m=m: s.activation(
                    out=RSB[:, ro + 128 * m:ro + 128 * (m + 1)], in_=psb(m, 0, 128), func=AF.Copy),
                    reads=[f"pb{m}"], writes=[rRSB])
            b.emit("S", lambda s: s.activation(
                out=RSB[0:64, ro + 256:ro + 384], in_=psb(2, 0, 128, 0, 64), func=AF.Copy),
                reads=["pb2"], writes=[rRSB])
            b.emit("S", lambda s: s.activation(out=RSB[0:64, ro + 768:ro + 896],
                                               in_=psb(3, 0, 128, 0, 64), func=AF.Copy),
                   reads=["pb3"], writes=[rRSB])

            for m in range(2):
                b.emit("T", lambda t, m=m: t.matmul(
                    psb(m, 128, 128), RP[:, :], RSB[:, ro + 128 * m:ro + 128 * (m + 1)],
                    start=True, stop=True),
                    reads=[rRSB, "RPc"], writes=[f"pb{m}"])
            b.emit("T", lambda t: t.matmul(
                psb(2, 128, 128, 0, 64), RP[0:64, 0:64], RSB[0:64, ro + 256:ro + 384],
                start=True, stop=True),
                reads=[rRSB, "RPc"], writes=["pb2"])
            for m in range(2):
                b.emit("S", lambda s, m=m: s.activation(
                    out=RSB[:, ro + 384 + 128 * m:ro + 384 + 128 * (m + 1)],
                    in_=psb(m, 128, 128), func=AF.Copy),
                    reads=[f"pb{m}"], writes=[rRSB])
            b.emit("S", lambda s: s.activation(
                out=RSB[0:64, ro + 640:ro + 768],
                in_=psb(2, 128, 128, 0, 64), func=AF.Copy),
                reads=["pb2"], writes=[rRSB])

            qho = 128 * (c % 2)
            for m, (dst, dnm, tco, tso, pl) in enumerate((
                    (QCH[0][c // 2][:, qho:qho + 128], f"QP0:{c}", 0, 128, 128),
                    (QCH[1][c // 2][:, qho:qho + 128], f"QP1:{c}", 0, 128, 128),
                    (KCH[c][0:64, :], f"KD:{c}", 256, 384, 64))):
                b.emit("V", lambda v, m=m, tco=tco, pl=pl: v.tensor_tensor(
                    out=TMP1[0:pl, :], in0=RSB[0:pl, ro + 128 * m:ro + 128 * (m + 1)],
                    in1=CS[0:pl, cso + tco:cso + tco + 128], op=ALU.mult),
                    reads=[rRSB, rCS], writes=["TMP1"])
                b.emit("V", lambda v, m=m, tso=tso, pl=pl: v.tensor_tensor(
                    out=TMP2[0:pl, :], in0=RSB[0:pl, ro + 384 + 128 * m:ro + 384 + 128 * (m + 1)],
                    in1=CS[0:pl, cso + tso:cso + tso + 128], op=ALU.mult),
                    reads=[rRSB, rCS], writes=["TMP2"])
                b.emit("V", lambda v, dst=dst, pl=pl: v.tensor_tensor(
                    out=dst, in0=TMP1[0:pl, :], in1=TMP2[0:pl, :], op=ALU.add),
                    reads=["TMP1", "TMP2", "ovA" if m < 2 else "ovK"], writes=[dnm])
            b.emit("Y", lambda y, c=c: y.dma_start(out=KBH[c][64:128, :], in_=KCH[c][0:64, :]),
                   reads=[f"KD:{c}", "KBzero", "ovXN"], writes=[f"KB:{c}"], dma="q")

            b.emit("T", lambda t: t.matmul(psb(3, 128, 64, 0, 128), RSB[0:64, ro + 768:ro + 896],
                                           IDB[0:64, 0:64], start=True, stop=True),
                   reads=[rRSB, "IDB"], writes=["pb3"])
            b.emit("S", lambda s, c=c: s.activation(out=VT[:, 65 * c:65 * c + 64],
                                                    in_=psb(3, 128, 64), func=AF.Copy),
                   reads=["pb3"], writes=[f"VT:{c}"])

            for m, snm in enumerate((f"QP0:{c}", f"QP1:{c}", f"KD:{c}")):
                src = (QCH[0][c // 2][:, qho:qho + 128] if m == 0 else
                       QCH[1][c // 2][:, qho:qho + 128] if m == 1 else
                       KCH[c][0:64, :])
                pl = 128 if m < 2 else 64
                b.emit("S", lambda s, m=m, src=src, pl=pl: s.activation(
                    out=SCB[0:pl, 256 * m:256 * m + 128], in_=src, func=AF.Square),
                    reads=[snm], writes=["SCB"])
            for m in range(3):
                pl = 128 if m < 2 else 64
                b.emit("T", lambda t, m=m, pl=pl: t.matmul(
                    psb(m, 256, 128, 0, 1), ONEB[0:pl, 0:1], SCB[0:pl, 256 * m:256 * m + 128],
                    start=True, stop=True),
                    reads=["SCB", "ONEB"], writes=[f"pb{m}"])
            for m in range(3):
                b.emit("V", lambda v, m=m: v.reduce_max(
                    out=MScr[0:1, m:m + 1], in_=psb(m, 256, 128, 0, 1), axis=AX.X),
                    reads=[f"pb{m}"], writes=["MScr"])
            b.emit("V", lambda v: v.tensor_max(out=MAXA[0:1, 0:3], in0=MAXA[0:1, 0:3],
                                               in1=MScr[0:1, 0:3]),
                   reads=["MScr", "MAXA"], writes=["MAXA"])

            for f in range(8):
                b.emit("T", lambda t, f=f: t.matmul(
                    psb(3, 256 + f, 1), xnb[:, 128 * f:128 * (f + 1)], ONEB[:, 0:1],
                    start=True, stop=True),
                    reads=[rXNB, "ONEB"], writes=["pb3"])
            b.emit("V", lambda v: v.tensor_add(out=SUMXN[:, :], in0=SUMXN[:, :], in1=psb(3, 256, 8)),
                   reads=["pb3", "SUMXN"], writes=["SUMXN"])

        def prelude_tile(tt):
            bu = tt % 2
            xch = XCH[:, 1024 * bu:1024 * (bu + 1)]
            st = 16 * bu
            rXCH, rST = f"XCH:{bu}", f"STT:{bu}"
            b.emit("Y", lambda y, tt=tt, xch=xch: y.dma_start(
                out=xch, in_=xo_in[128 * tt:128 * (tt + 1), :]),
                writes=[rXCH], dma="q")
            b.emit("V", lambda v, xch=xch, st=st: v.bn_stats(out=STT[:, st:st + 6], in_=xch[:, 0:512]),
                   reads=[rXCH], writes=[rST])
            b.emit("V", lambda v, xch=xch, st=st: v.bn_stats(out=STT[:, st + 6:st + 12], in_=xch[:, 512:1024]),
                   reads=[rXCH], writes=[rST])
            b.emit("V", lambda v, st=st: v.bn_aggr(out=STT[:, st + 12:st + 14], in_=STT[:, st:st + 12]),
                   reads=[rST], writes=[rST])
            b.emit("V", lambda v, st=st: v.tensor_scalar_add(out=STT[:, st + 14:st + 15],
                                                             in0=STT[:, st + 13:st + 14], scalar1=EPS),
                   reads=[rST], writes=[rST])
            b.emit("S", lambda s, st=st: s.activation(out=STT[:, st + 15:st + 16],
                                                      in_=STT[:, st + 14:st + 15], func=AF.Ln),
                   reads=[rST], writes=[rST])
            b.emit("S", lambda s, st=st: s.activation(out=STT[:, st + 14:st + 15],
                                                      in_=STT[:, st + 15:st + 16], func=AF.Exp, scale=-0.5),
                   reads=[rST], writes=[rST])
            b.emit("V", lambda v, tt=tt, xch=xch, st=st: v.tensor_scalar(
                out=XN[:, 1024 * tt:1024 * (tt + 1)], in0=xch,
                scalar1=STT[:, st + 12:st + 13], scalar2=STT[:, st + 14:st + 15],
                op0=ALU.subtract, op1=ALU.mult),
                reads=[rXCH, rST], writes=[f"XN:{tt}"] + (["ovXN"] if tt == 0 else []))

        gctr = [0]
        ydr_all = []

        def neg_bias(p):
            g0 = 8 + 4 * p
            b.emit("S", lambda s, p=p, g0=g0: s.activation(out=GFT[0:1, g0:g0 + 1],
                                                           in_=MAXA[0:1, p:p + 1], func=AF.Ln),
                   reads=["MAXA"], writes=["GFTb"])
            b.emit("S", lambda s, g0=g0: s.activation(out=GFT[0:1, g0 + 1:g0 + 2],
                                                      in_=MAXA[0:1, 2:3], func=AF.Ln),
                   reads=["MAXA", "GFTb"], writes=["GFTb"])
            b.emit("V", lambda v, g0=g0: v.tensor_add(out=GFT[0:1, g0 + 2:g0 + 3],
                                                      in0=GFT[0:1, g0:g0 + 1], in1=GFT[0:1, g0 + 1:g0 + 2]),
                   reads=["GFTb"], writes=["GFTb"])
            b.emit("S", lambda s, g0=g0: s.activation(out=GFT[0:1, g0 + 3:g0 + 4], in_=GFT[0:1, g0 + 2:g0 + 3],
                                                      func=AF.Exp, scale=0.5, bias=CONS[0:1, 0:1]),
                   reads=["GFTb", "CONS"], writes=["GFTb"])
            b.emit("V", lambda v, p=p, g0=g0: v.tensor_scalar_mul(out=NEGBB[0:1, p:p + 1],
                                                                  in0=GFT[0:1, g0 + 3:g0 + 4], scalar1=-1.0),
                   reads=["GFTb"], writes=["NEGBB"])
            b.emit("T", lambda t, p=p: t.matmul(psb(4, 400 + p, 1), ONEB[0:1, :],
                                                NEGBB[0:1, p:p + 1], start=True, stop=True),
                   reads=["NEGBB", "ONEB"], writes=["pb4"])
            b.emit("V", lambda v, p=p: v.tensor_copy(out=NEGB[:, p:p + 1], in_=psb(4, 400 + p, 1)),
                   reads=["pb4"], writes=["NEGB"])

        def qk_qb(p, qb):
            inst = 2 * qb + p
            QPn = "QP0" if p == 0 else "QP1"
            qch = QCH[p][qb]
            nkt = 2 * (qb + 1)
            if P2M < 2:
                return
            for g in range(nkt // 2):
                gpar = gctr[0] % 2
                gctr[0] += 1
                base = 0 if gpar == 0 else 2
                pta0 = 0 if gpar == 0 else 1024
                rPT = f"PTA:{gpar}"
                slots = []
                for gi in range(2):
                    kt = 2 * g + gi
                    for h in (0, 1):
                        si = 2 * gi + h
                        bank = base + si // 2
                        off = 256 * (si % 2)
                        pos = pta0 + 256 * si
                        slots.append((kt, h, bank, off, pos))
                qk_l, av_l = [], []
                for (kt, h, bank, off, pos) in slots:
                    kst = KCH[kt] if h == 0 else KBH[kt]
                    knm = f"KD:{kt}" if h == 0 else f"KB:{kt}"
                    qk_l.append(lambda kst=kst, bank=bank, off=off, qch=qch, knm=knm, h=h: b.emit(
                        "T", lambda t, kst=kst, bank=bank, off=off, qch=qch: t.matmul(
                            psb(bank, off, QB, 0, 128),
                            kst[:, :], qch[:, :],
                            start=True, stop=True),
                        reads=[knm, "KDzero" if h == 0 else "KBzero",
                               f"{QPn}:{2 * qb}", f"{QPn}:{2 * qb + 1}", "ovA", "ovK", "ovXN"],
                        writes=[f"pb{bank}"]))

                def mk_exp(base=base, pta0=pta0, p=p, rPT=rPT, slots=slots, qb=qb):
                    b.emit("S", lambda s: s.activation(
                        out=PTA[:, pta0:pta0 + 1024], in_=PS[:, 512 * base:512 * base + 1024],
                        func=AF.Exp, bias=NEGB[:, p:p + 1]),
                        reads=["NEGB", f"pb{base}", f"pb{base + 1}"], writes=[rPT])
                    if P2M >= 3:
                        for (kt, h, bank, off, pos) in slots:
                            if kt >= 2 * qb:
                                moff = 128 if kt == 2 * qb else 0
                                b.emit("V", lambda v, pos=pos, moff=moff: v.tensor_tensor(
                                    out=PTA[:, pos:pos + 256], in0=PTA[:, pos:pos + 256],
                                    in1=MSK[:, moff:moff + 256], op=ALU.mult),
                                    reads=[rPT, "MSK"], writes=[rPT])

                if P2M >= 4:
                    for (kt, h, bank, off, pos) in slots:
                        for j in range(2):
                            yb = 4 + 2 * h + j
                            av_l.append(lambda pos=pos, kt=kt, j=j, yb=yb, rPT=rPT,
                                        st=(kt == 0), lst=(kt == nkt - 1), inst=inst, h=h: b.emit(
                                "T", lambda t, pos=pos, yb=yb, kt=kt, st=st, lst=lst: t.matmul(
                                    psb(yb, 0, 65), PTA[:, pos + 128 * j:pos + 128 * j + 128],
                                    VT[:, 65 * kt:65 * (kt + 1)], start=st, stop=lst),
                                reads=[rPT, f"VT:{kt}", "VTones"], writes=[f"pb{yb}"],
                                accum=("av", inst, h, j)))
                tasks.append((qk_l, mk_exp, av_l, None))
            if P2M < 5:
                return

            def mk_norm(p=p, qb=qb, inst=inst):
                for h in range(2):
                    for j in range(2):
                        yb = 4 + 2 * h + j
                        sl = (inst % 2) * 4 + 2 * h + j
                        b.emit("V", lambda v, yb=yb, sl=sl: v.reciprocal(
                            out=RD[:, sl:sl + 1], in_=psb(yb, 64, 1)),
                            reads=[f"pb{yb}"], writes=[f"RD:{sl}"])
                        b.emit("V", lambda v, yb=yb, sl=sl: v.tensor_scalar_mul(
                            out=YST[:, 64 * sl:64 * (sl + 1)], in0=psb(yb, 0, 64),
                            scalar1=RD[:, sl:sl + 1]),
                            reads=[f"pb{yb}", f"RD:{sl}"], writes=[f"YST:{sl}"])
                        lh = 2 * p + h
                        r0y = QB * qb + 128 * j
                        b.emit("Y", lambda y, r0y=r0y, lh=lh, sl=sl: y.dma_start(
                            out=ydram[r0y:r0y + 128, 64 * lh:64 * (lh + 1)],
                            in_=YST[:, 64 * sl:64 * (sl + 1)]),
                            reads=[f"YST:{sl}"], writes=[f"ydr:{r0y}:{lh}"], dma="q")
                        ydr_all.append(f"ydr:{r0y}:{lh}")
            tasks[-1] = tasks[-1][:3] + (mk_norm,)

        if UPTO >= 1:
            n1 = min(NCH, int(os.environ.get("K_P1N", "999")))
            if n1 >= 1:
                p1_A1(0)
            for c in range(n1):
                if c + 1 < n1:
                    p1_A1(c + 1)
                if c >= 1:
                    p1_B(c - 1)
                p1_A2(c)
            if n1 >= 1:
                p1_B(n1 - 1)
        if UPTO >= 2:
            neg_bias(0)
            neg_bias(1)
            if UPTO >= 3 and not int(os.environ.get("K_NOWL", "0")):
                wload(WBIG, cw1_in, 8, 1024, "WBIG")
                wload(CN2, cw2_in, 8, 512, "CN2")
                wload(CN3, cw3_in, 4, 256, "CN3")
                wload(CN4, cw4_in, 2, 1, "CN4")
                wload(GC1, gw1_in, 8, 512, "GC1")
                wload(GC2, gw2_in, 4, 256, "GC2")
                wload(GC3, gw3_in, 2, 1, "GC3")
            if UPTO >= 3 and not NOPRE:
                for tt in range(4, NT):
                    prelude_tile(tt)
            tasks = []
            for qb in range(min(NQB, P2N)):
                qk_qb(0, qb)
                qk_qb(1, qb)
            prev = None
            for tk in tasks:
                for f in tk[0]:
                    f()
                tk[1]()
                if prev is not None:
                    for f in prev[2]:
                        f()
                    if prev[3] is not None:
                        prev[3]()
                prev = tk
            if prev is not None:
                for f in prev[2]:
                    f()
                if prev[3] is not None:
                    prev[3]()

        def early_out():
            b.emit("V", lambda v: v.memset(XN[:, :], 0.25), writes=[f"XN:{t}" for t in range(NT)])
            for tt in range(NT):
                b.emit("Y", lambda y, tt=tt: y.dma_start(
                    out=out_d[128 * tt:128 * (tt + 1), :], in_=XN[:, 1024 * tt:1024 * (tt + 1)]),
                    reads=[f"XN:{tt}"], writes=["outw"], dma="q")
            b.emit("P", lambda p: p.engine_nop(), reads=["outw"])
            block = es.enter_context(nc.Block())
            b.finalize(nc, es, block)

        if UPTO < 3:
            early_out()
            return nc

        if not NOCC:
            op_cc = b.emit("P", lambda p: p.collective_compute(
                "AllGather", ALU.bypass,
                replica_groups=[list(range(8))],
                ins=[ydram_h.ap().opt()], outs=[ygat_h.ap().opt()]),
                reads=([ydr_all[-1]] if ydr_all else ["ydram"]), writes=["ygat"])
            op_cc.need_inc = True

        for tt in (range(0) if NOPRE else range(4)):
            prelude_tile(tt)

        if NOFFN:
            early_out()
            return nc
        b.emit("V", lambda v: v.tensor_scalar_mul(out=XBARB[:, 0:8], in0=SUMXN[:, :], scalar1=1.0 / S),
               reads=["SUMXN"], writes=["XBARB"])
        for m in range(4):
            for k in range(8):
                b.emit("T", lambda t, m=m, k=k: t.matmul(
                    psb(0, m, 1), GC1.ap()[:, 512 * k + 128 * m:512 * k + 128 * m + 128],
                    XBARB[:, k:k + 1], start=(k == 0), stop=(k == 7)),
                    reads=["GC1", "XBARB"], writes=["pb0"], accum=("gc1", m))
        for m in range(4):
            b.emit("S", lambda s, m=m: s.activation(out=GFH1[:, m:m + 1], in_=psb(0, m, 1),
                                                    func=AF.Gelu, bias=GB1[:, m:m + 1]),
                   reads=["pb0", "GB1"], writes=["GFH1"])
        for m in range(2):
            for k in range(4):
                b.emit("T", lambda t, m=m, k=k: t.matmul(
                    psb(1, m, 1), GC2.ap()[:, 256 * k + 128 * m:256 * k + 128 * m + 128],
                    GFH1[:, k:k + 1], start=(k == 0), stop=(k == 3)),
                    reads=["GC2", "GFH1"], writes=["pb1"], accum=("gc2", m))
        for m in range(2):
            b.emit("S", lambda s, m=m: s.activation(out=GFH2[:, m:m + 1], in_=psb(1, m, 1),
                                                    func=AF.Gelu, bias=GB2[:, m:m + 1]),
                   reads=["pb1", "GB2"], writes=["GFH2"])
        for k in range(2):
            b.emit("T", lambda t, k=k: t.matmul(
                psb(2, 0, 1, 0, 1), GC3.ap()[:, k:k + 1], GFH2[:, k:k + 1],
                start=(k == 0), stop=(k == 1)),
                reads=["GC3", "GFH2"], writes=["pb2"], accum=("gc3", 0))
        b.emit("V", lambda v: v.tensor_scalar_mul(out=GB3[0:1, 1:2], in0=GB3[0:1, 0:1], scalar1=0.5),
               reads=["GB3"], writes=["GB3"])
        b.emit("V", lambda v: v.tensor_scalar_mul(out=CB4[0:1, 1:2], in0=CB4[0:1, 0:1], scalar1=0.5),
               reads=["CB4"], writes=["CB4"])
        b.emit("S", lambda s: s.activation(out=GFT[0:1, 0:1], in_=psb(2, 0, 1, 0, 1),
                                           func=AF.Tanh, scale=0.5, bias=GB3[0:1, 1:2]),
               reads=["pb2", "GB3"], writes=["GFTg"])
        b.emit("V", lambda v: v.tensor_scalar(out=GFT[0:1, 1:2], in0=GFT[0:1, 0:1],
                                              scalar1=0.5, scalar2=0.5, op0=ALU.mult, op1=ALU.add),
               reads=["GFTg"], writes=["GFTg"])
        b.emit("V", lambda v: v.tensor_mul(out=GFT[0:1, 2:3], in0=GFT[0:1, 1:2], in1=ART[0:1, 0:1]),
               reads=["GFTg", "ART"], writes=["GFTg"])
        b.emit("V", lambda v: v.tensor_scalar_mul(out=GFT[0:1, 3:4], in0=GFT[0:1, 2:3], scalar1=1.4),
               reads=["GFTg"], writes=["GFTg"])
        b.emit("V", lambda v: v.tensor_sub(out=GFT[0:1, 4:5], in0=GFT[0:1, 3:4], in1=ART[0:1, 0:1]),
               reads=["GFTg", "ART"], writes=["GFTg"])
        b.emit("V", lambda v: v.tensor_scalar_add(out=GFT[0:1, 5:6], in0=GFT[0:1, 4:5], scalar1=0.1),
               reads=["GFTg"], writes=["GFTg"])
        b.emit("V", lambda v: v.tensor_scalar_mul(out=GFT[0:1, 6:7], in0=ART[0:1, 0:1], scalar1=0.3),
               reads=["ART"], writes=["GFTg"])
        b.emit("V", lambda v: v.tensor_add(out=GFT[0:1, 7:8], in0=GFT[0:1, 5:6], in1=GFT[0:1, 6:7]),
               reads=["GFTg"], writes=["GFTg"])

        def ffn_cycle(cyc):
            for tc in (1, 0):
                t0 = TCH * tc
                for ttl in range(8):
                    tt = 8 * tc + ttl
                    bu = ttl % 2
                    cdst = SCR[:, 1024 * bu:1024 * (bu + 1)]
                    first = (cyc == 0 and tc == 1 and ttl == 0)
                    b.emit("V", lambda v, tt=tt, cdst=cdst: v.tensor_copy(
                        out=cdst, in_=XN[:, 1024 * tt:1024 * (tt + 1)]),
                        reads=[f"XN:{tt}"] + ([] if first else ["ovS"]),
                        writes=[f"XNB:{bu}"] + (["ovS"] if first else []))
                    for f in range(8):
                        bank, off = 6 + (f // 4), 128 * (f % 4)
                        b.emit("T", lambda t, f=f, bank=bank, off=off, cdst=cdst: t.matmul(
                            psb(bank, off, 128), cdst[:, 128 * f:128 * (f + 1)], IDB[:, :],
                            start=True, stop=True),
                            reads=[f"XNB:{bu}", "IDB", "ovS"], writes=[f"pb{bank}"])
                    for f in range(8):
                        bank, off = 6 + (f // 4), 128 * (f % 4)
                        fe = (first and f == 0)
                        b.emit("V", lambda v, f=f, bank=bank, off=off, ttl=ttl: v.tensor_copy(
                            out=BIGA[:, 1024 * f + 128 * ttl:1024 * f + 128 * (ttl + 1)],
                            in_=psb(bank, off, 128)),
                            reads=[f"pb{bank}"] + ([] if fe else ["ovA"]),
                            writes=[f"OUTBT:{f}"] + (["ovA"] if fe else []))
                gi = 0
                for m in range(8):
                    for n in range(2):
                        bank = gi % 6
                        gi += 1
                        for k in range(8):
                            b.emit("T", lambda t, m=m, n=n, k=k, bank=bank: t.matmul(
                                psb(bank, 0, 512), WBIG.ap()[:, 1024 * k + 128 * m:1024 * k + 128 * m + 128],
                                BIGA[:, 1024 * k + 512 * n:1024 * k + 512 * n + 512],
                                start=(k == 0), stop=(k == 7)),
                                reads=[f"OUTBT:{k}", "WBIG", "ovA"], writes=[f"pb{bank}"],
                                accum=("h1", cyc, tc, m, n))
                        b.emit("S", lambda s, m=m, n=n, bank=bank: s.activation(
                            out=BIGA[:, 8192 + 1024 * m + 512 * n:8192 + 1024 * m + 512 * n + 512],
                            in_=psb(bank, 0, 512), func=AF.Gelu, bias=CB1[:, m:m + 1]),
                            reads=[f"pb{bank}", "CB1", "ovA"], writes=[f"H1B:{m}"])
                for m in range(4):
                    for n in range(2):
                        bank = gi % 6
                        gi += 1
                        for k in range(8):
                            b.emit("T", lambda t, m=m, n=n, k=k, bank=bank: t.matmul(
                                psb(bank, 0, 512), CN2.ap()[:, 512 * k + 128 * m:512 * k + 128 * m + 128],
                                BIGA[:, 8192 + 1024 * k + 512 * n:8192 + 1024 * k + 512 * n + 512],
                                start=(k == 0), stop=(k == 7)),
                                reads=[f"H1B:{k}", "CN2", "ovA"], writes=[f"pb{bank}"],
                                accum=("h2", cyc, tc, m, n))
                        wk = ["ovK"] if (cyc == 0 and tc == 0 and m == 0 and n == 0) else []
                        b.emit("S", lambda s, m=m, n=n, bank=bank: s.activation(
                            out=KD[:, 1024 * m + 512 * n:1024 * m + 512 * n + 512],
                            in_=psb(bank, 0, 512), func=AF.Gelu, bias=CB2[:, m:m + 1]),
                            reads=[f"pb{bank}", "CB2"] + ([] if wk else ["ovK"]),
                            writes=[f"H2B:{m}"] + wk)
                for m in range(2):
                    for n in range(2):
                        bank = gi % 6
                        gi += 1
                        for k in range(4):
                            b.emit("T", lambda t, m=m, n=n, k=k, bank=bank: t.matmul(
                                psb(bank, 0, 512), CN3.ap()[:, 256 * k + 128 * m:256 * k + 128 * m + 128],
                                KD[:, 1024 * k + 512 * n:1024 * k + 512 * n + 512],
                                start=(k == 0), stop=(k == 3)),
                                reads=[f"H2B:{k}", "CN3", "ovK"], writes=[f"pb{bank}"],
                                accum=("h3", cyc, tc, m, n))
                        b.emit("S", lambda s, m=m, n=n, bank=bank: s.activation(
                            out=KD[:, 4096 + 1024 * m + 512 * n:4096 + 1024 * m + 512 * n + 512],
                            in_=psb(bank, 0, 512), func=AF.Gelu, bias=CB3[:, m:m + 1]),
                            reads=[f"pb{bank}", "CB3", "ovK"], writes=[f"H3B:{m}"])
                for n in range(2):
                    for k in range(2):
                        b.emit("T", lambda t, n=n, k=k: t.matmul(
                            psb(6 + n, 0, 512, 0, 1),
                            CN4.ap()[:, k:k + 1],
                            KD[:, 4096 + 1024 * k + 512 * n:4096 + 1024 * k + 512 * n + 512],
                            start=(k == 0), stop=(k == 1)),
                            reads=[f"H3B:{k}", "CN4", "ovK"], writes=[f"pb{6 + n}"],
                            accum=("z", cyc, tc, n))
                for n in range(2):
                    b.emit("S", lambda s, n=n: s.activation(
                        out=CTLB[0:1, 512 * n:512 * (n + 1)], in_=psb(6 + n, 0, 512, 0, 1),
                        func=AF.Tanh, scale=0.5, bias=CB4[0:1, 1:2]),
                        reads=[f"pb{6 + n}", "CB4"], writes=["CTLB"])
                b.emit("V", lambda v: v.tensor_scalar(
                    out=CTLB[0:1, 0:1024], in0=CTLB[0:1, 0:1024],
                    scalar1=GFT[0:1, 6:7], scalar2=GFT[0:1, 7:8],
                    op0=ALU.mult, op1=ALU.add),
                    reads=["CTLB", "GFTg"], writes=["CTLB"])
                for ttl in range(8):
                    tt = 8 * tc + ttl
                    cb = ttl % 2
                    b.emit("T", lambda t, ttl=ttl: t.matmul(
                        psb(6, 480, 1), CTLB[0:1, 128 * ttl:128 * (ttl + 1)],
                        IDF[0:1, 0:1], start=True, stop=True),
                        reads=["CTLB", "IDF"], writes=["pb6"])
                    b.emit("V", lambda v, cb=cb: v.tensor_copy(out=CTL[:, 2 * cb:2 * cb + 1],
                                                               in_=psb(6, 480, 1)),
                           reads=["pb6"], writes=[f"CTL:{cb}"])
                    b.emit("V", lambda v, cb=cb: v.tensor_scalar_mul(out=CTL[:, 2 * cb + 1:2 * cb + 2],
                                                                     in0=CTL[:, 2 * cb:2 * cb + 1],
                                                                     scalar1=-1.0),
                           reads=[f"CTL:{cb}"], writes=[f"CTL:{cb}"])
                    x0 = 1024 * tt
                    q = lambda i: XN[:, x0 + 256 * i:x0 + 256 * (i + 1)]
                    rX = f"XN:{tt}"
                    b.emit("V", lambda v, tt=tt, cb=cb: v.scalar_tensor_tensor(
                        out=TMPQ[:, 0:256], in0=XN[:, 1024 * tt + 512:1024 * tt + 768],
                        scalar=CTL[:, 2 * cb:2 * cb + 1], in1=XN[:, 1024 * tt:1024 * tt + 256],
                        op0=ALU.mult, op1=ALU.add),
                        reads=[rX, f"CTL:{cb}"], writes=["TMPQ"])
                    b.emit("V", lambda v, tt=tt, cb=cb: v.scalar_tensor_tensor(
                        out=TMPQ[:, 256:512], in0=XN[:, 1024 * tt + 768:1024 * tt + 1024],
                        scalar=CTL[:, 2 * cb + 1:2 * cb + 2], in1=XN[:, 1024 * tt + 256:1024 * tt + 512],
                        op0=ALU.mult, op1=ALU.add),
                        reads=[rX, f"CTL:{cb}"], writes=["TMPQ"])
                    b.emit("V", lambda v, tt=tt, cb=cb: v.scalar_tensor_tensor(
                        out=XN[:, 1024 * tt + 512:1024 * tt + 768], in0=XN[:, 1024 * tt:1024 * tt + 256],
                        scalar=CTL[:, 2 * cb + 1:2 * cb + 2], in1=XN[:, 1024 * tt + 512:1024 * tt + 768],
                        op0=ALU.mult, op1=ALU.add),
                        reads=[rX, f"CTL:{cb}"], writes=[rX])
                    b.emit("V", lambda v, tt=tt, cb=cb: v.scalar_tensor_tensor(
                        out=XN[:, 1024 * tt + 768:1024 * tt + 1024], in0=XN[:, 1024 * tt + 256:1024 * tt + 512],
                        scalar=CTL[:, 2 * cb:2 * cb + 1], in1=XN[:, 1024 * tt + 768:1024 * tt + 1024],
                        op0=ALU.mult, op1=ALU.add),
                        reads=[rX, f"CTL:{cb}"], writes=[rX])
                    b.emit("V", lambda v, tt=tt: v.tensor_copy(
                        out=XN[:, 1024 * tt:1024 * tt + 256], in_=TMPQ[:, 0:256]),
                        reads=["TMPQ"], writes=[rX])
                    b.emit("V", lambda v, tt=tt: v.tensor_copy(
                        out=XN[:, 1024 * tt + 256:1024 * tt + 512], in_=TMPQ[:, 256:512]),
                        reads=["TMPQ"], writes=[rX])

        for cyc in range(3):
            ffn_cycle(cyc)

        wload(WBIG, wo_in, 8, 1024, "WBIG")

        if UPTO < 4:
            early_out()
            return nc
        for tt in range(NT):
            for r in range(4):
                wA = ["ovA"] if (tt == 0 and r == 0) else []
                b.emit("P", lambda p, tt=tt, r=r: p.indirect_dma_start(
                    out=BIGA[:, 1024 * tt + 256 * r:1024 * tt + 256 * (r + 1)],
                    out_offset=None, in_=ygat,
                    in_offset=IndirectOffsetOnAxis(ap=YIX[:, 4 * tt + r:4 * tt + r + 1], axis=0)),
                    reads=["ygat", "YIX"] + ([] if wA else ["ovA"]),
                    writes=[f"YALL:{tt}"] + wA, dma="yin")
        for tt in range(NT):
            for f in range(8):
                bank, off = 6 + (f // 4), 128 * (f % 4)
                wS = ["ovS"] if (tt == 0 and f == 0) else []
                b.emit("T", lambda t, tt=tt, f=f, bank=bank, off=off: t.matmul(
                    psb(bank, off, 128), BIGA[:, 1024 * tt + 128 * f:1024 * tt + 128 * (f + 1)],
                    IDB[:, :], start=True, stop=True),
                    reads=[f"YALL:{tt}", "IDB"], writes=[f"pb{bank}"])
                b.emit("V", lambda v, f=f, bank=bank, off=off: v.tensor_copy(
                    out=SCR[:, 128 * f:128 * (f + 1)], in_=psb(bank, off, 128)),
                    reads=[f"pb{bank}"] + ([] if wS else ["ovS"]), writes=["YTT"] + wS)
            for n in range(2):
                bank = (2 * tt + n) % 4
                for k in range(8):
                    b.emit("T", lambda t, n=n, k=k, bank=bank: t.matmul(
                        psb(bank, 0, 512), SCR[:, 128 * k:128 * (k + 1)],
                        WBIG.ap()[:, 1024 * k + 512 * n:1024 * k + 512 * n + 512],
                        start=(k == 0), stop=(k == 7)),
                        reads=["YTT", "WBIG"], writes=[f"pb{bank}"], accum=("wo", tt, n))
                b.emit("V", lambda v, tt=tt, n=n, bank=bank: v.tensor_add(
                    out=XN[:, 1024 * tt + 512 * n:1024 * tt + 512 * n + 512],
                    in0=XN[:, 1024 * tt + 512 * n:1024 * tt + 512 * n + 512],
                    in1=psb(bank, 0, 512)),
                    reads=[f"pb{bank}", f"XN:{tt}"], writes=[f"XN:{tt}"])
            bu = tt % 2
            xch = XCH[:, 1024 * bu:1024 * (bu + 1)]
            b.emit("Y", lambda y, tt=tt, xch=xch: y.dma_start(
                out=xch, in_=xo_in[128 * tt:128 * (tt + 1), :]),
                writes=[f"XCH:{bu}"], dma="q")
            b.emit("V", lambda v, tt=tt, xch=xch: v.tensor_add(
                out=XN[:, 1024 * tt:1024 * (tt + 1)],
                in0=XN[:, 1024 * tt:1024 * (tt + 1)], in1=xch),
                reads=[f"XCH:{bu}", f"XN:{tt}"], writes=[f"XN:{tt}"])
            b.emit("Y", lambda y, tt=tt: y.dma_start(
                out=out_d[128 * tt:128 * (tt + 1), :], in_=XN[:, 1024 * tt:1024 * (tt + 1)]),
                reads=[f"XN:{tt}"], writes=[f"outw:{tt}"], dma="q")

        b.emit("P", lambda p: p.engine_nop(), reads=[f"outw:{NT - 1}"])

        block = es.enter_context(nc.Block())
        b.finalize(nc, es, block)
    return nc


_CACHE = {}


def _tables():
    if "t" in _CACHE:
        return _CACHE["t"]
    half = 32
    inv = 1.0 / (10000.0 ** (np.arange(half, dtype=np.float64) / half))
    t = np.arange(S, dtype=np.float64)
    fr = np.outer(inv, t)
    emb = np.concatenate([fr, fr], axis=0)
    cos = np.cos(emb)
    sin = np.sin(emb)
    cos2 = np.concatenate([cos, cos], axis=0)
    sin2 = np.concatenate([sin, sin], axis=0)
    bf = lambda a: a.astype(ml_dtypes.bfloat16)
    cosq = bf(cos2 / 8.0)
    sinq = bf(sin2 / 8.0)
    cosk = bf(cos2)
    sink = bf(sin2)
    rp = np.zeros((128, 128), dtype=np.float32)
    for base in (0, 64):
        for m in range(32):
            rp[base + m + 32, base + m] = -1.0
            rp[base + m, base + m + 32] = 1.0
    idb = np.eye(128, dtype=ml_dtypes.bfloat16)
    idf = np.eye(128, dtype=np.float32)
    oneb = np.ones((128, 128), dtype=ml_dtypes.bfloat16)
    msk = np.zeros((128, 384), dtype=np.float32)
    for pp in range(128):
        for j in range(384):
            if pp <= j - 128:
                msk[pp, j] = 1.0
    _CACHE["t"] = (cosq, sinq, cosk, sink, bf(rp), idb, idf, oneb, bf(msk))
    return _CACHE["t"]


def kernel(**inputs):
    x = np.asarray(inputs["L0_attn_x"], dtype=np.float32)
    Wq = np.ascontiguousarray(np.asarray(inputs["L0_attn_Wq"], dtype=np.float32))
    Wk = np.ascontiguousarray(np.asarray(inputs["L0_attn_Wk"], dtype=np.float32))
    Wv = np.ascontiguousarray(np.asarray(inputs["L0_attn_Wv"], dtype=np.float32))
    Wo = np.ascontiguousarray(np.asarray(inputs["L0_attn_Wo"], dtype=np.float32))
    cosq, sinq, cosk, sink, rp, idb, idf, oneb, msk = _tables()

    if "nc" not in _CACHE:
        _CACHE["nc"] = build_program()
    nc = _CACHE["nc"]

    shared = {
        "wo": Wo,
        "cw1": np.ascontiguousarray(np.asarray(inputs["L0_ffn_cn_w1"], dtype=np.float32)),
        "cw2": np.ascontiguousarray(np.asarray(inputs["L0_ffn_cn_w2"], dtype=np.float32)),
        "cw3": np.ascontiguousarray(np.asarray(inputs["L0_ffn_cn_w3"], dtype=np.float32)),
        "cw4": np.ascontiguousarray(np.asarray(inputs["L0_ffn_cn_w4"], dtype=np.float32)),
        "gw1": np.ascontiguousarray(np.asarray(inputs["L0_ffn_gc_w1"], dtype=np.float32)),
        "gw2": np.ascontiguousarray(np.asarray(inputs["L0_ffn_gc_w2"], dtype=np.float32)),
        "gw3": np.ascontiguousarray(np.asarray(inputs["L0_ffn_gc_w3"], dtype=np.float32)),
        "cb1": np.ascontiguousarray(np.asarray(inputs["L0_ffn_cn_b1"], dtype=np.float32)),
        "cb2": np.ascontiguousarray(np.asarray(inputs["L0_ffn_cn_b2"], dtype=np.float32)),
        "cb3": np.ascontiguousarray(np.asarray(inputs["L0_ffn_cn_b3"], dtype=np.float32)),
        "cb4": np.ascontiguousarray(np.asarray(inputs["L0_ffn_cn_b4"], dtype=np.float32)),
        "gb1": np.ascontiguousarray(np.asarray(inputs["L0_ffn_gc_b1"], dtype=np.float32)),
        "gb2": np.ascontiguousarray(np.asarray(inputs["L0_ffn_gc_b2"], dtype=np.float32)),
        "gb3": np.ascontiguousarray(np.asarray(inputs["L0_ffn_gc_b3"], dtype=np.float32)),
        "ar": np.ascontiguousarray(np.asarray(inputs["L0_ffn_adaptive_range"], dtype=np.float32)),
        "cosq": cosq, "sinq": sinq, "cosk": cosk, "sink": sink,
        "rp": rp, "idb": idb, "idf": idf, "oneb": oneb, "msk": msk,
    }
    in_maps = []
    for c in range(NCORE := 8):
        bb, kh = divmod(c, 4)
        m = dict(shared)
        m["x"] = np.ascontiguousarray(x[bb])
        m["xown"] = np.ascontiguousarray(x[bb, TOK * kh:TOK * (kh + 1)])
        m["wq"] = np.ascontiguousarray(Wq[:, 256 * kh:256 * (kh + 1)])
        m["wk"] = np.ascontiguousarray(Wk[:, 64 * kh:64 * (kh + 1)])
        m["wv"] = np.ascontiguousarray(Wv[:, 64 * kh:64 * (kh + 1)])
        yidx = np.empty((128, 64), dtype=np.int32)
        for tt in range(NT):
            for r in range(4):
                yidx[:, 4 * tt + r] = S * (4 * bb + r) + TOK * kh + 128 * tt + np.arange(128)
        m["yidx"] = yidx
        in_maps.append(m)

    res = run_bass_kernel_spmd(nc, in_maps, core_ids=list(range(8)))
    out = np.empty((2, S, D), dtype=np.float32)
    for c in range(8):
        bb, kh = divmod(c, 4)
        out[bb, TOK * kh:TOK * (kh + 1)] = res.results[c]["out"]
    return out


# version 11 (final)
# speedup vs baseline: 1.6242x; 1.0080x over previous
import contextlib
import math
import os
import numpy as np
import ml_dtypes
import concourse.bass as bass
import concourse.mybir as mybir
from concourse.bass_utils import run_bass_kernel_spmd
from concourse.bass import IndirectOffsetOnAxis
F32 = mybir.dt.float32
BF16 = mybir.dt.bfloat16
AF = mybir.ActivationFunctionType
ALU = mybir.AluOpType
AX = mybir.AxisListType

D = 1024
S = 8192
NCH = S // 128
QB = 256
NQB = S // QB
TOK = S // 4
NT = TOK // 128
TCH = 1024
NTCH = TOK // TCH
EPS = 1e-5
LN2H = 0.5 * math.log(2.0)
UPTO = int(os.environ.get("K_UPTO", "4"))
P2N = int(os.environ.get("K_P2N", "999"))
P2M = int(os.environ.get("K_P2M", "5"))
P2QK = int(os.environ.get("K_P2QK", "9999"))
P2E = int(os.environ.get("K_P2E", "1"))
NOCC = int(os.environ.get("K_NOCC", "0"))
NOFFN = int(os.environ.get("K_NOFFN", "0"))
NOPRE = int(os.environ.get("K_NOPRE", "0"))


class Op:
    __slots__ = ("eng", "fn", "deps", "idx", "need_inc", "val", "dma", "accum", "burst")

    def __init__(self, eng, fn, dma, accum):
        self.eng, self.fn, self.dma, self.accum = eng, fn, dma, accum
        self.deps = []
        self.idx = -1
        self.need_inc = False
        self.val = 0
        self.burst = 0


class Bld:
    def __init__(self):
        self.ops = {e: [] for e in "TVSPY"}
        self.res = {}
        self.groups = {}

    def emit(self, eng, fn, reads=(), writes=(), dma=None, accum=None):
        op = Op(eng, fn, dma, accum)
        deps = {}
        for r in reads:
            ent = self.res.get(r)
            if ent and ent[0] is not None:
                deps[id(ent[0])] = ent[0]
        for w in writes:
            ent = self.res.get(w)
            if ent:
                if ent[0] is not None:
                    deps[id(ent[0])] = ent[0]
                for rd in ent[1]:
                    deps[id(rd)] = rd
        ds = []
        for d in deps.values():
            if d is op:
                continue
            if d.eng == "T" and eng == "T" and d.dma is None and dma is None:
                continue
            if accum is not None and d.accum == accum:
                continue
            if dma not in (None, "q") and d.eng == eng and d.dma == dma:
                continue
            ds.append(d)
        op.deps = ds
        for r in reads:
            self.res.setdefault(r, [None, []])[1].append(op)
        for w in writes:
            self.res[w] = [op, []]
        op.idx = len(self.ops[eng])
        self.ops[eng].append(op)
        if dma is not None and dma != "q" and eng == "P":
            self.groups.setdefault(dma, []).append(op)
        return op

    def finalize(self, nc, es, block):
        sems = {}
        for e in "TVSPY":
            sems[e] = es.enter_context(nc.semaphore(f"sem_{e}"))
        for g in self.groups:
            sems["g:" + g] = es.enter_context(nc.semaphore(f"sem_g_{g}"))
        for e in "TVSPY":
            for op in self.ops[e]:
                for d in op.deps:
                    d.need_inc = True
        for ops_ in self.groups.values():
            for o in ops_:
                o.need_inc = True
        for e in "TVSPY":
            for op in self.ops[e]:
                if op.dma is not None:
                    op.need_inc = True
        gtotal = {g: 16 * len(ops_) for g, ops_ in self.groups.items()}
        for e in "TVSPY":
            c = 0
            for op in self.ops[e]:
                if op.eng == "P" and op.dma not in (None, "q"):
                    op.val = 0
                    continue
                if op.need_inc:
                    c += 16 if (op.dma is not None) else 1
                op.val = c

        burst_end = {"Y": {}, "P": {}}
        for e in ("Y", "P"):
            bid = 0
            for op in self.ops[e]:
                if op.deps:
                    bid += 1
                op.burst = bid
                burst_end[e][bid] = op.val

        def sem_of(d):
            if d.eng == "P" and d.dma not in (None, "q"):
                return ("g:" + d.dma, gtotal[d.dma])
            if d.eng in ("Y", "P"):
                return (d.eng, burst_end[d.eng][d.burst])
            return (d.eng, d.val)

        waited = {"Y": set(), "P": set()}
        for e in "TVSPY":
            for op in self.ops[e]:
                for d in op.deps:
                    s, v = sem_of(d)
                    if s in waited:
                        waited[s].add(v)

        def run_engine(e, eng_obj):
            water = {}
            for op in self.ops[e]:
                waits = {}
                for d in op.deps:
                    s, v = sem_of(d)
                    if waits.get(s, -1) < v:
                        waits[s] = v
                for s, v in sorted(waits.items()):
                    if water.get(s, -1) >= v:
                        continue
                    eng_obj.wait_ge(sems[s], v)
                    water[s] = v
                ins = op.fn(eng_obj)
                if op.need_inc:
                    if op.eng == "P" and op.dma not in (None, "q"):
                        ins.then_inc(sems["g:" + op.dma], 16)
                    elif op.dma is not None:
                        ins.then_inc(sems[op.eng], 16)
                    else:
                        ins.then_inc(sems[op.eng], 1)
                if e in ("Y", "P") and op.need_inc and op.dma in (None, "q") and op.val in waited[e]:
                    eng_obj.wait_ge(sems[e], op.val)
                    water[e] = max(water.get(e, -1), op.val)

        @block.tensor
        def _(t):
            run_engine("T", t)

        @block.vector
        def _(v):
            run_engine("V", v)

        @block.scalar
        def _(s):
            run_engine("S", s)

        @block.gpsimd
        def _(p):
            run_engine("P", p)

        @block.sync
        def _(y):
            run_engine("Y", y)


def build_program():
    nc = bass.Bass("TRN2", target_bir_lowering=False)

    x_in = nc.dram_tensor("x", [S, D], F32, kind="ExternalInput").ap()
    xo_in = nc.dram_tensor("xown", [TOK, D], F32, kind="ExternalInput").ap()
    wq_in = nc.dram_tensor("wq", [D, 256], F32, kind="ExternalInput").ap()
    wk_in = nc.dram_tensor("wk", [D, 64], F32, kind="ExternalInput").ap()
    wv_in = nc.dram_tensor("wv", [D, 64], F32, kind="ExternalInput").ap()
    wo_in = nc.dram_tensor("wo", [D, D], F32, kind="ExternalInput").ap()
    cw1_in = nc.dram_tensor("cw1", [D, D], F32, kind="ExternalInput").ap()
    cw2_in = nc.dram_tensor("cw2", [D, 512], F32, kind="ExternalInput").ap()
    cw3_in = nc.dram_tensor("cw3", [512, 256], F32, kind="ExternalInput").ap()
    cw4_in = nc.dram_tensor("cw4", [256, 1], F32, kind="ExternalInput").ap()
    gw1_in = nc.dram_tensor("gw1", [D, 512], F32, kind="ExternalInput").ap()
    gw2_in = nc.dram_tensor("gw2", [512, 256], F32, kind="ExternalInput").ap()
    gw3_in = nc.dram_tensor("gw3", [256, 1], F32, kind="ExternalInput").ap()
    cb1_in = nc.dram_tensor("cb1", [D], F32, kind="ExternalInput").ap()
    cb2_in = nc.dram_tensor("cb2", [512], F32, kind="ExternalInput").ap()
    cb3_in = nc.dram_tensor("cb3", [256], F32, kind="ExternalInput").ap()
    cb4_in = nc.dram_tensor("cb4", [1], F32, kind="ExternalInput").ap()
    gb1_in = nc.dram_tensor("gb1", [512], F32, kind="ExternalInput").ap()
    gb2_in = nc.dram_tensor("gb2", [256], F32, kind="ExternalInput").ap()
    gb3_in = nc.dram_tensor("gb3", [1], F32, kind="ExternalInput").ap()
    ar_in = nc.dram_tensor("ar", [1], F32, kind="ExternalInput").ap()
    cosq_in = nc.dram_tensor("cosq", [128, S], BF16, kind="ExternalInput").ap()
    sinq_in = nc.dram_tensor("sinq", [128, S], BF16, kind="ExternalInput").ap()
    cosk_in = nc.dram_tensor("cosk", [128, S], BF16, kind="ExternalInput").ap()
    sink_in = nc.dram_tensor("sink", [128, S], BF16, kind="ExternalInput").ap()
    rp_in = nc.dram_tensor("rp", [128, 128], BF16, kind="ExternalInput").ap()
    idb_in = nc.dram_tensor("idb", [128, 128], BF16, kind="ExternalInput").ap()
    idf_in = nc.dram_tensor("idf", [128, 128], F32, kind="ExternalInput").ap()
    oneb_in = nc.dram_tensor("oneb", [128, 128], BF16, kind="ExternalInput").ap()
    msk_in = nc.dram_tensor("msk", [128, 384], BF16, kind="ExternalInput").ap()
    out_d = nc.dram_tensor("out", [TOK, D], F32, kind="ExternalOutput").ap()
    yidx_in = nc.dram_tensor("yidx", [128, 64], mybir.dt.int32, kind="ExternalInput").ap()
    ydram_h = nc.dram_tensor("ydram", [S, 256], BF16)
    ygat_h = nc.dram_tensor("ygat", [8 * S, 256], BF16)
    ydram = ydram_h.ap()
    ygat = ygat_h.ap()

    es = contextlib.ExitStack()
    SB = lambda n, sh, dt: es.enter_context(nc.sbuf_tensor(n, sh, dt))
    b = Bld()

    with es:
        XN = SB("XN", [128, NT * 1024], F32)
        BIGA = SB("BIGA", [128, 16384], BF16)
        KD = SB("KD", [128, 8192], BF16)
        VT = SB("VT", [128, 65 * NCH], BF16)
        XCH = SB("XCH", [128, 2048], F32)
        SCR = SB("SCR", [128, 4096], BF16)
        PTA = SB("PTA", [128, 2048], BF16)
        CS = SB("CS", [128, 1024], BF16)
        RSB = SB("RSB", [128, 2048], BF16)
        SCB = SB("SCB", [128, 768], BF16)
        STT = SB("STT", [128, 32], F32)
        SUMXN = SB("SUMXN", [128, 8], F32)
        MAXA = SB("MAXA", [1, 4], F32)
        MScr = SB("MScr", [1, 8], F32)
        NEGB = SB("NEGB", [128, 2], F32)
        NEGBB = SB("NEGBB", [1, 2], BF16)
        GFT = SB("GFT", [1, 16], F32)
        XBARB = SB("XBARB", [128, 8], BF16)
        GFH1 = SB("GFH1", [128, 4], BF16)
        GFH2 = SB("GFH2", [128, 2], BF16)
        CTL = SB("CTL", [128, 4], F32)
        CTLB = SB("CTLB", [1, 1024], F32)
        TMP1 = SB("TMP1", [128, 128], F32)
        TMP2 = SB("TMP2", [128, 128], F32)
        TMPQ = SB("TMPQ", [128, 512], F32)
        YST = SB("YST", [128, 512], BF16)
        RD = SB("RD", [128, 8], F32)
        IDB = SB("IDB", [128, 128], BF16)
        IDF = SB("IDF", [128, 128], F32)
        ONEB = SB("ONEB", [128, 128], BF16)
        MSK = SB("MSK", [128, 384], BF16)
        RP = SB("RPs", [128, 128], BF16)
        WQB = SB("WQB", [128, 2048], BF16)
        WKB = SB("WKB", [128, 512], BF16)
        WVB = SB("WVB", [128, 512], BF16)
        WBIG = SB("WBIG", [128, 8192], BF16)
        CN2 = SB("CN2", [128, 4096], BF16)
        CN3 = SB("CN3", [128, 1024], BF16)
        CN4 = SB("CN4", [128, 2], BF16)
        GC1 = SB("GC1", [128, 4096], BF16)
        GC2 = SB("GC2", [128, 1024], BF16)
        GC3 = SB("GC3", [128, 2], BF16)
        CB1 = SB("CB1", [128, 8], F32)
        CB2 = SB("CB2", [128, 4], F32)
        CB3 = SB("CB3", [128, 2], F32)
        CB4 = SB("CB4", [1, 2], F32)
        GB1 = SB("GB1", [128, 4], F32)
        GB2 = SB("GB2", [128, 2], F32)
        GB3 = SB("GB3", [1, 2], F32)
        ART = SB("ART", [1, 1], F32)
        CONS = SB("CONS", [1, 4], F32)
        YIX = SB("YIX", [128, 64], mybir.dt.int32)
        PS = es.enter_context(nc.psum_tensor("PS", [128, 4096], F32))
        biga_addr = nc.lookup_mloc(BIGA).addr
        kd_addr = nc.lookup_mloc(KD).addr
        QCH = [[nc.alloc_sbuf_tensor_at(f"QCH{p_}_{c_}", (128, 256), BF16,
                                        offset=biga_addr + 16384 * p_ + 512 * c_)
                for c_ in range(NCH // 2)] for p_ in range(2)]
        KCH = [nc.alloc_sbuf_tensor_at(f"KCH_{c_}", (128, 128), BF16,
                                       offset=kd_addr + 256 * c_)
               for c_ in range(NCH)]
        xn_addr = nc.lookup_mloc(XN).addr
        KBH = [nc.alloc_sbuf_tensor_at(f"KBH_{c_}", (128, 128), BF16,
                                       offset=xn_addr + 256 * c_)
               for c_ in range(NCH)]
        KBALL = nc.alloc_sbuf_tensor_at("KBALL", (128, 8192), BF16, offset=xn_addr)

        def psb(bank, off, w, p0=0, p1=128):
            return PS[p0:p1, 512 * bank + off:512 * bank + off + w]

        stg_par = [0]

        def wload(dst, srcap, kt, ncols, name):
            for k in range(kt):
                bu = stg_par[0] % 2
                stg_par[0] += 1
                stg = XCH[:, 1024 * bu:1024 * bu + ncols]
                b.emit("Y", lambda y, k=k, stg=stg, srcap=srcap: y.dma_start(
                    out=stg, in_=srcap[128 * k:128 * (k + 1), :]),
                    writes=[f"XCH:{bu}"], dma="q")
                b.emit("P", lambda p, k=k, stg=stg, dst=dst, ncols=ncols: p.tensor_copy(
                    out=dst[:, ncols * k:ncols * (k + 1)], in_=stg),
                    reads=[f"XCH:{bu}"], writes=[name])

        if not int(os.environ.get("K_NOQKV", "0")):
            b.emit("Y", lambda y: y.dma_start(out=YIX[:, :], in_=yidx_in), writes=["YIX"], dma="q")
            wload(WQB, wq_in, 8, 256, "WQB")
            wload(WKB, wk_in, 8, 64, "WKB")
            wload(WVB, wv_in, 8, 64, "WVB")

        def bload(dst, src, m, name):
            for i in range(m):
                b.emit("Y", lambda y, d=dst, s=src, i=i:
                       y.dma_start(out=d[:, i:i + 1],
                                   in_=s[128 * i:128 * (i + 1)].rearrange("(p o) -> p o", o=1)),
                       writes=[name], dma="q")

        if not int(os.environ.get("K_NOBL", "0")):
            bload(CB1, cb1_in, 8, "CB1")
            bload(CB2, cb2_in, 4, "CB2")
            bload(CB3, cb3_in, 2, "CB3")
            bload(GB1, gb1_in, 4, "GB1")
            bload(GB2, gb2_in, 2, "GB2")
            for dst_, src_, nm_ in ((CB4, cb4_in, "CB4"), (GB3, gb3_in, "GB3"), (ART, ar_in, "ART")):
                b.emit("Y", lambda y, d=dst_, s=src_: y.dma_start(out=d[0:1, 0:1], in_=s.rearrange("(x o) -> x o", x=1)),
                       writes=[nm_], dma="q")
        if not int(os.environ.get("K_NOTB", "0")):
            for dst_, src_, nm_ in ((IDB, idb_in, "IDB"), (IDF, idf_in, "IDF"), (ONEB, oneb_in, "ONEB"),
                                    (MSK, msk_in, "MSK"), (RP, rp_in, "RPc")):
                b.emit("Y", lambda y, d=dst_, s=src_: y.dma_start(out=d[:, :], in_=s), writes=[nm_], dma="q")
            b.emit("V", lambda v: v.memset(MAXA[:, :], -1e30), writes=["MAXA"])
            b.emit("V", lambda v: v.memset(SUMXN[:, :], 0.0), writes=["SUMXN"])
            b.emit("V", lambda v: v.memset(VT[:, :], 1.0), writes=["VTones"])
        b.emit("V", lambda v: v.memset(KD[64:128, :], 0.0), writes=["KDzero"])
        b.emit("V", lambda v: v.memset(KBALL[0:64, :], 0.0), writes=["KBzero", "ovXN"])
        b.emit("V", lambda v: v.memset(CONS[:, :], 0.0), writes=["CONS"])

        def p1_A1(c):
            bu = c % 2
            r0 = 128 * c
            xch = XCH[:, 1024 * bu:1024 * (bu + 1)]
            rXCH = f"XCH:{bu}"
            b.emit("Y", lambda y, xch=xch, r0=r0: y.dma_start(out=xch, in_=x_in[r0:r0 + 128, :]),
                   writes=[rXCH], dma="q")

        def p1_A2(c):
            bu = c % 2
            r0 = 128 * c
            xch = XCH[:, 1024 * bu:1024 * (bu + 1)]
            xnb = SCR[:, 1024 * bu:1024 * (bu + 1)]
            xnt0 = 2048 + 1024 * bu
            st = 16 * bu
            cso = 512 * bu
            ro = 1024 * bu
            rXCH, rXNB, rXNT = f"XCH:{bu}", f"XNB:{bu}", f"XNT:{bu}"
            rST, rCS, rRSB = f"STT:{bu}", f"CS:{bu}", f"RSB:{bu}"

            for j, tab in enumerate((cosq_in, sinq_in, cosk_in, sink_in)):
                b.emit("Y", lambda y, j=j, tab=tab: y.dma_start(
                    out=CS[:, cso + 128 * j:cso + 128 * (j + 1)], in_=tab[:, r0:r0 + 128]),
                    writes=[rCS], dma="q")

            b.emit("V", lambda v: v.bn_stats(out=STT[:, st:st + 6], in_=xch[:, 0:512]),
                   reads=[rXCH], writes=[rST])
            b.emit("V", lambda v: v.bn_stats(out=STT[:, st + 6:st + 12], in_=xch[:, 512:1024]),
                   reads=[rXCH], writes=[rST])
            b.emit("V", lambda v: v.bn_aggr(out=STT[:, st + 12:st + 14], in_=STT[:, st:st + 12]),
                   reads=[rST], writes=[rST])
            b.emit("V", lambda v: v.tensor_scalar_add(out=STT[:, st + 14:st + 15],
                                                      in0=STT[:, st + 13:st + 14], scalar1=EPS),
                   reads=[rST], writes=[rST])
            b.emit("S", lambda s: s.activation(out=STT[:, st + 15:st + 16],
                                               in_=STT[:, st + 14:st + 15], func=AF.Ln),
                   reads=[rST], writes=[rST])
            b.emit("S", lambda s: s.activation(out=STT[:, st + 14:st + 15],
                                               in_=STT[:, st + 15:st + 16], func=AF.Exp, scale=-0.5),
                   reads=[rST], writes=[rST])
            b.emit("V", lambda v: v.tensor_scalar(out=xnb, in0=xch,
                                                  scalar1=STT[:, st + 12:st + 13],
                                                  scalar2=STT[:, st + 14:st + 15],
                                                  op0=ALU.subtract, op1=ALU.mult),
                   reads=[rXCH, rST, "ovS"], writes=[rXNB])

            for f in range(8):
                bank, off = 4 + (f // 4), 128 * (f % 4)
                b.emit("T", lambda t, f=f, bank=bank, off=off: t.matmul(
                    psb(bank, off, 128), xnb[:, 128 * f:128 * (f + 1)], IDB[:, :],
                    start=True, stop=True),
                    reads=[rXNB, "IDB", "ovS"], writes=[f"pb{bank}"])
            for g2 in range(2):
                bank = 4 + g2
                b.emit("S", lambda s, g2=g2, bank=bank: s.activation(
                    out=SCR[:, xnt0 + 512 * g2:xnt0 + 512 * (g2 + 1)], in_=psb(bank, 0, 512),
                    func=AF.Copy),
                    reads=[f"pb{bank}", "ovS"], writes=[rXNT])

        def p1_B(c):
            bu = c % 2
            r0 = 128 * c
            xch = XCH[:, 1024 * bu:1024 * (bu + 1)]
            xnb = SCR[:, 1024 * bu:1024 * (bu + 1)]
            xnt0 = 2048 + 1024 * bu
            st = 16 * bu
            cso = 512 * bu
            ro = 1024 * bu
            rXCH, rXNB, rXNT = f"XCH:{bu}", f"XNB:{bu}", f"XNT:{bu}"
            rST, rCS, rRSB = f"STT:{bu}", f"CS:{bu}", f"RSB:{bu}"

            for m in range(4):
                bank = m
                for k in range(8):
                    if m < 2:
                        st_ap = WQB.ap()[:, 256 * k + 128 * m:256 * k + 128 * m + 128]
                        wnm = "WQB"
                    elif m == 2:
                        st_ap = WKB.ap()[:, 64 * k:64 * (k + 1)]
                        wnm = "WKB"
                    else:
                        st_ap = WVB.ap()[:, 64 * k:64 * (k + 1)]
                        wnm = "WVB"
                    p1_ = 128 if m < 2 else 64
                    b.emit("T", lambda t, k=k, bank=bank, st_ap=st_ap, p1_=p1_: t.matmul(
                        psb(bank, 0, 128, 0, p1_), st_ap,
                        SCR[:, xnt0 + 128 * k:xnt0 + 128 * (k + 1)],
                        start=(k == 0), stop=(k == 7)),
                        reads=[rXNT, wnm], writes=[f"pb{bank}"], accum=("p1", c, m))
            for m in range(2):
                b.emit("S", lambda s, m=m: s.activation(
                    out=RSB[:, ro + 128 * m:ro + 128 * (m + 1)], in_=psb(m, 0, 128), func=AF.Copy),
                    reads=[f"pb{m}"], writes=[rRSB])
            b.emit("S", lambda s: s.activation(
                out=RSB[0:64, ro + 256:ro + 384], in_=psb(2, 0, 128, 0, 64), func=AF.Copy),
                reads=["pb2"], writes=[rRSB])
            b.emit("S", lambda s: s.activation(out=RSB[0:64, ro + 768:ro + 896],
                                               in_=psb(3, 0, 128, 0, 64), func=AF.Copy),
                   reads=["pb3"], writes=[rRSB])

            for m in range(2):
                b.emit("T", lambda t, m=m: t.matmul(
                    psb(m, 128, 128), RP[:, :], RSB[:, ro + 128 * m:ro + 128 * (m + 1)],
                    start=True, stop=True),
                    reads=[rRSB, "RPc"], writes=[f"pb{m}"])
            b.emit("T", lambda t: t.matmul(
                psb(2, 128, 128, 0, 64), RP[0:64, 0:64], RSB[0:64, ro + 256:ro + 384],
                start=True, stop=True),
                reads=[rRSB, "RPc"], writes=["pb2"])
            for m in range(2):
                b.emit("S", lambda s, m=m: s.activation(
                    out=RSB[:, ro + 384 + 128 * m:ro + 384 + 128 * (m + 1)],
                    in_=psb(m, 128, 128), func=AF.Copy),
                    reads=[f"pb{m}"], writes=[rRSB])
            b.emit("S", lambda s: s.activation(
                out=RSB[0:64, ro + 640:ro + 768],
                in_=psb(2, 128, 128, 0, 64), func=AF.Copy),
                reads=["pb2"], writes=[rRSB])

            qho = 128 * (c % 2)
            for m, (dst, dnm, tco, tso, pl) in enumerate((
                    (QCH[0][c // 2][:, qho:qho + 128], f"QP0:{c}", 0, 128, 128),
                    (QCH[1][c // 2][:, qho:qho + 128], f"QP1:{c}", 0, 128, 128),
                    (KCH[c][0:64, :], f"KD:{c}", 256, 384, 64))):
                b.emit("V", lambda v, m=m, tco=tco, pl=pl: v.tensor_tensor(
                    out=TMP1[0:pl, :], in0=RSB[0:pl, ro + 128 * m:ro + 128 * (m + 1)],
                    in1=CS[0:pl, cso + tco:cso + tco + 128], op=ALU.mult),
                    reads=[rRSB, rCS], writes=["TMP1"])
                b.emit("V", lambda v, m=m, tso=tso, pl=pl: v.tensor_tensor(
                    out=TMP2[0:pl, :], in0=RSB[0:pl, ro + 384 + 128 * m:ro + 384 + 128 * (m + 1)],
                    in1=CS[0:pl, cso + tso:cso + tso + 128], op=ALU.mult),
                    reads=[rRSB, rCS], writes=["TMP2"])
                b.emit("V", lambda v, dst=dst, pl=pl: v.tensor_tensor(
                    out=dst, in0=TMP1[0:pl, :], in1=TMP2[0:pl, :], op=ALU.add),
                    reads=["TMP1", "TMP2", "ovA" if m < 2 else "ovK"], writes=[dnm])
            b.emit("Y", lambda y, c=c: y.dma_start(out=KBH[c][64:128, :], in_=KCH[c][0:64, :]),
                   reads=[f"KD:{c}", "KBzero", "ovXN"], writes=[f"KB:{c}"], dma="q")

            b.emit("T", lambda t: t.matmul(psb(3, 128, 64, 0, 128), RSB[0:64, ro + 768:ro + 896],
                                           IDB[0:64, 0:64], start=True, stop=True),
                   reads=[rRSB, "IDB"], writes=["pb3"])
            b.emit("S", lambda s, c=c: s.activation(out=VT[:, 65 * c:65 * c + 64],
                                                    in_=psb(3, 128, 64), func=AF.Copy),
                   reads=["pb3"], writes=[f"VT:{c}"])

            for m, snm in enumerate((f"QP0:{c}", f"QP1:{c}", f"KD:{c}")):
                src = (QCH[0][c // 2][:, qho:qho + 128] if m == 0 else
                       QCH[1][c // 2][:, qho:qho + 128] if m == 1 else
                       KCH[c][0:64, :])
                pl = 128 if m < 2 else 64
                b.emit("S", lambda s, m=m, src=src, pl=pl: s.activation(
                    out=SCB[0:pl, 256 * m:256 * m + 128], in_=src, func=AF.Square),
                    reads=[snm], writes=["SCB"])
            for m in range(3):
                pl = 128 if m < 2 else 64
                b.emit("T", lambda t, m=m, pl=pl: t.matmul(
                    psb(m, 256, 128, 0, 1), ONEB[0:pl, 0:1], SCB[0:pl, 256 * m:256 * m + 128],
                    start=True, stop=True),
                    reads=["SCB", "ONEB"], writes=[f"pb{m}"])
            for m in range(3):
                b.emit("V", lambda v, m=m: v.reduce_max(
                    out=MScr[0:1, m:m + 1], in_=psb(m, 256, 128, 0, 1), axis=AX.X),
                    reads=[f"pb{m}"], writes=["MScr"])
            b.emit("V", lambda v: v.tensor_max(out=MAXA[0:1, 0:3], in0=MAXA[0:1, 0:3],
                                               in1=MScr[0:1, 0:3]),
                   reads=["MScr", "MAXA"], writes=["MAXA"])

            for f in range(8):
                b.emit("T", lambda t, f=f: t.matmul(
                    psb(3, 256 + f, 1), xnb[:, 128 * f:128 * (f + 1)], ONEB[:, 0:1],
                    start=True, stop=True),
                    reads=[rXNB, "ONEB"], writes=["pb3"])
            b.emit("V", lambda v: v.tensor_add(out=SUMXN[:, :], in0=SUMXN[:, :], in1=psb(3, 256, 8)),
                   reads=["pb3", "SUMXN"], writes=["SUMXN"])

        def prelude_tile(tt):
            bu = tt % 2
            xch = XCH[:, 1024 * bu:1024 * (bu + 1)]
            st = 16 * bu
            rXCH, rST = f"XCH:{bu}", f"STT:{bu}"
            b.emit("Y", lambda y, tt=tt, xch=xch: y.dma_start(
                out=xch, in_=xo_in[128 * tt:128 * (tt + 1), :]),
                writes=[rXCH], dma="q")
            b.emit("V", lambda v, xch=xch, st=st: v.bn_stats(out=STT[:, st:st + 6], in_=xch[:, 0:512]),
                   reads=[rXCH], writes=[rST])
            b.emit("V", lambda v, xch=xch, st=st: v.bn_stats(out=STT[:, st + 6:st + 12], in_=xch[:, 512:1024]),
                   reads=[rXCH], writes=[rST])
            b.emit("V", lambda v, st=st: v.bn_aggr(out=STT[:, st + 12:st + 14], in_=STT[:, st:st + 12]),
                   reads=[rST], writes=[rST])
            b.emit("V", lambda v, st=st: v.tensor_scalar_add(out=STT[:, st + 14:st + 15],
                                                             in0=STT[:, st + 13:st + 14], scalar1=EPS),
                   reads=[rST], writes=[rST])
            b.emit("S", lambda s, st=st: s.activation(out=STT[:, st + 15:st + 16],
                                                      in_=STT[:, st + 14:st + 15], func=AF.Ln),
                   reads=[rST], writes=[rST])
            b.emit("S", lambda s, st=st: s.activation(out=STT[:, st + 14:st + 15],
                                                      in_=STT[:, st + 15:st + 16], func=AF.Exp, scale=-0.5),
                   reads=[rST], writes=[rST])
            b.emit("V", lambda v, tt=tt, xch=xch, st=st: v.tensor_scalar(
                out=XN[:, 1024 * tt:1024 * (tt + 1)], in0=xch,
                scalar1=STT[:, st + 12:st + 13], scalar2=STT[:, st + 14:st + 15],
                op0=ALU.subtract, op1=ALU.mult),
                reads=[rXCH, rST], writes=[f"XN:{tt}"] + (["ovXN"] if tt == 0 else []))

        gctr = [0]
        ydr_all = []

        def neg_bias(p):
            g0 = 8 + 4 * p
            b.emit("S", lambda s, p=p, g0=g0: s.activation(out=GFT[0:1, g0:g0 + 1],
                                                           in_=MAXA[0:1, p:p + 1], func=AF.Ln),
                   reads=["MAXA"], writes=["GFTb"])
            b.emit("S", lambda s, g0=g0: s.activation(out=GFT[0:1, g0 + 1:g0 + 2],
                                                      in_=MAXA[0:1, 2:3], func=AF.Ln),
                   reads=["MAXA", "GFTb"], writes=["GFTb"])
            b.emit("V", lambda v, g0=g0: v.tensor_add(out=GFT[0:1, g0 + 2:g0 + 3],
                                                      in0=GFT[0:1, g0:g0 + 1], in1=GFT[0:1, g0 + 1:g0 + 2]),
                   reads=["GFTb"], writes=["GFTb"])
            b.emit("S", lambda s, g0=g0: s.activation(out=GFT[0:1, g0 + 3:g0 + 4], in_=GFT[0:1, g0 + 2:g0 + 3],
                                                      func=AF.Exp, scale=0.5, bias=CONS[0:1, 0:1]),
                   reads=["GFTb", "CONS"], writes=["GFTb"])
            b.emit("V", lambda v, p=p, g0=g0: v.tensor_scalar_mul(out=NEGBB[0:1, p:p + 1],
                                                                  in0=GFT[0:1, g0 + 3:g0 + 4], scalar1=-1.0),
                   reads=["GFTb"], writes=["NEGBB"])
            b.emit("T", lambda t, p=p: t.matmul(psb(4, 400 + p, 1), ONEB[0:1, :],
                                                NEGBB[0:1, p:p + 1], start=True, stop=True),
                   reads=["NEGBB", "ONEB"], writes=["pb4"])
            b.emit("V", lambda v, p=p: v.tensor_copy(out=NEGB[:, p:p + 1], in_=psb(4, 400 + p, 1)),
                   reads=["pb4"], writes=["NEGB"])

        def qk_qb(p, qb):
            inst = 2 * qb + p
            QPn = "QP0" if p == 0 else "QP1"
            qch = QCH[p][qb]
            nkt = 2 * (qb + 1)
            if P2M < 2:
                return
            for g in range(nkt // 2):
                gpar = gctr[0] % 2
                gctr[0] += 1
                base = 0 if gpar == 0 else 2
                pta0 = 0 if gpar == 0 else 1024
                rPT = f"PTA:{gpar}"
                slots = []
                for gi in range(2):
                    kt = 2 * g + gi
                    for h in (0, 1):
                        si = 2 * gi + h
                        bank = base + si // 2
                        off = 256 * (si % 2)
                        pos = pta0 + 256 * si
                        slots.append((kt, h, bank, off, pos))
                qk_l, av_l = [], []
                for (kt, h, bank, off, pos) in slots:
                    kst = KCH[kt] if h == 0 else KBH[kt]
                    knm = f"KD:{kt}" if h == 0 else f"KB:{kt}"
                    qk_l.append(lambda kst=kst, bank=bank, off=off, qch=qch, knm=knm, h=h: b.emit(
                        "T", lambda t, kst=kst, bank=bank, off=off, qch=qch: t.matmul(
                            psb(bank, off, QB, 0, 128),
                            kst[:, :], qch[:, :],
                            start=True, stop=True),
                        reads=[knm, "KDzero" if h == 0 else "KBzero",
                               f"{QPn}:{2 * qb}", f"{QPn}:{2 * qb + 1}", "ovA", "ovK", "ovXN"],
                        writes=[f"pb{bank}"]))

                def mk_exp(base=base, pta0=pta0, p=p, rPT=rPT, slots=slots, qb=qb):
                    b.emit("S", lambda s: s.activation(
                        out=PTA[:, pta0:pta0 + 1024], in_=PS[:, 512 * base:512 * base + 1024],
                        func=AF.Exp, bias=NEGB[:, p:p + 1]),
                        reads=["NEGB", f"pb{base}", f"pb{base + 1}"], writes=[rPT])
                    if P2M >= 3:
                        for (kt, h, bank, off, pos) in slots:
                            if kt >= 2 * qb:
                                moff = 128 if kt == 2 * qb else 0
                                b.emit("V", lambda v, pos=pos, moff=moff: v.tensor_tensor(
                                    out=PTA[:, pos:pos + 256], in0=PTA[:, pos:pos + 256],
                                    in1=MSK[:, moff:moff + 256], op=ALU.mult),
                                    reads=[rPT, "MSK"], writes=[rPT])

                if P2M >= 4:
                    for (kt, h, bank, off, pos) in slots:
                        for j in range(2):
                            yb = 4 + 2 * h + j
                            av_l.append(lambda pos=pos, kt=kt, j=j, yb=yb, rPT=rPT,
                                        st=(kt == 0), lst=(kt == nkt - 1), inst=inst, h=h: b.emit(
                                "T", lambda t, pos=pos, yb=yb, kt=kt, st=st, lst=lst: t.matmul(
                                    psb(yb, 0, 65), PTA[:, pos + 128 * j:pos + 128 * j + 128],
                                    VT[:, 65 * kt:65 * (kt + 1)], start=st, stop=lst),
                                reads=[rPT, f"VT:{kt}", "VTones"], writes=[f"pb{yb}"],
                                accum=("av", inst, h, j)))
                tasks.append((qk_l, mk_exp, av_l, None))
            if P2M < 5:
                return

            def mk_norm(p=p, qb=qb, inst=inst):
                for h in range(2):
                    for j in range(2):
                        yb = 4 + 2 * h + j
                        sl = (inst % 2) * 4 + 2 * h + j
                        b.emit("V", lambda v, yb=yb, sl=sl: v.reciprocal(
                            out=RD[:, sl:sl + 1], in_=psb(yb, 64, 1)),
                            reads=[f"pb{yb}"], writes=[f"RD:{sl}"])
                        b.emit("V", lambda v, yb=yb, sl=sl: v.tensor_scalar_mul(
                            out=YST[:, 64 * sl:64 * (sl + 1)], in0=psb(yb, 0, 64),
                            scalar1=RD[:, sl:sl + 1]),
                            reads=[f"pb{yb}", f"RD:{sl}"], writes=[f"YST:{sl}"])
                        lh = 2 * p + h
                        r0y = QB * qb + 128 * j
                        b.emit("Y", lambda y, r0y=r0y, lh=lh, sl=sl: y.dma_start(
                            out=ydram[r0y:r0y + 128, 64 * lh:64 * (lh + 1)],
                            in_=YST[:, 64 * sl:64 * (sl + 1)]),
                            reads=[f"YST:{sl}"], writes=[f"ydr:{r0y}:{lh}"], dma="q")
                        ydr_all.append(f"ydr:{r0y}:{lh}")
            tasks[-1] = tasks[-1][:3] + (mk_norm,)

        if UPTO >= 1:
            n1 = min(NCH, int(os.environ.get("K_P1N", "999")))
            if n1 >= 1:
                p1_A1(0)
            for c in range(n1):
                if c + 1 < n1:
                    p1_A1(c + 1)
                if c >= 1:
                    p1_B(c - 1)
                p1_A2(c)
            if n1 >= 1:
                p1_B(n1 - 1)
        if UPTO >= 2:
            neg_bias(0)
            neg_bias(1)
            if UPTO >= 3 and not int(os.environ.get("K_NOWL", "0")):
                wload(WBIG, cw1_in, 8, 1024, "WBIG")
                wload(CN2, cw2_in, 8, 512, "CN2")
                wload(CN3, cw3_in, 4, 256, "CN3")
                wload(CN4, cw4_in, 2, 1, "CN4")
                wload(GC1, gw1_in, 8, 512, "GC1")
                wload(GC2, gw2_in, 4, 256, "GC2")
                wload(GC3, gw3_in, 2, 1, "GC3")
            if UPTO >= 3 and not NOPRE:
                for tt in range(4, NT):
                    prelude_tile(tt)
            tasks = []
            for qb in range(min(NQB, P2N)):
                qk_qb(0, qb)
                qk_qb(1, qb)
            prev = None
            for tk in tasks:
                for f in tk[0]:
                    f()
                tk[1]()
                if prev is not None:
                    for f in prev[2]:
                        f()
                    if prev[3] is not None:
                        prev[3]()
                prev = tk
            if prev is not None:
                for f in prev[2]:
                    f()
                if prev[3] is not None:
                    prev[3]()

        def early_out():
            b.emit("V", lambda v: v.memset(XN[:, :], 0.25), writes=[f"XN:{t}" for t in range(NT)])
            for tt in range(NT):
                b.emit("Y", lambda y, tt=tt: y.dma_start(
                    out=out_d[128 * tt:128 * (tt + 1), :], in_=XN[:, 1024 * tt:1024 * (tt + 1)]),
                    reads=[f"XN:{tt}"], writes=["outw"], dma="q")
            b.emit("P", lambda p: p.engine_nop(), reads=["outw"])
            block = es.enter_context(nc.Block())
            b.finalize(nc, es, block)

        if UPTO < 3:
            early_out()
            return nc

        if not NOCC:
            op_cc = b.emit("P", lambda p: p.collective_compute(
                "AllGather", ALU.bypass,
                replica_groups=[list(range(8))],
                ins=[ydram_h.ap().opt()], outs=[ygat_h.ap().opt()]),
                reads=([ydr_all[-1]] if ydr_all else ["ydram"]), writes=["ygat"])
            op_cc.need_inc = True

        for tt in (range(0) if NOPRE else range(4)):
            prelude_tile(tt)

        if NOFFN:
            early_out()
            return nc
        b.emit("V", lambda v: v.tensor_scalar_mul(out=XBARB[:, 0:8], in0=SUMXN[:, :], scalar1=1.0 / S),
               reads=["SUMXN"], writes=["XBARB"])
        for m in range(4):
            for k in range(8):
                b.emit("T", lambda t, m=m, k=k: t.matmul(
                    psb(0, m, 1), GC1.ap()[:, 512 * k + 128 * m:512 * k + 128 * m + 128],
                    XBARB[:, k:k + 1], start=(k == 0), stop=(k == 7)),
                    reads=["GC1", "XBARB"], writes=["pb0"], accum=("gc1", m))
        for m in range(4):
            b.emit("S", lambda s, m=m: s.activation(out=GFH1[:, m:m + 1], in_=psb(0, m, 1),
                                                    func=AF.Gelu, bias=GB1[:, m:m + 1]),
                   reads=["pb0", "GB1"], writes=["GFH1"])
        for m in range(2):
            for k in range(4):
                b.emit("T", lambda t, m=m, k=k: t.matmul(
                    psb(1, m, 1), GC2.ap()[:, 256 * k + 128 * m:256 * k + 128 * m + 128],
                    GFH1[:, k:k + 1], start=(k == 0), stop=(k == 3)),
                    reads=["GC2", "GFH1"], writes=["pb1"], accum=("gc2", m))
        for m in range(2):
            b.emit("S", lambda s, m=m: s.activation(out=GFH2[:, m:m + 1], in_=psb(1, m, 1),
                                                    func=AF.Gelu, bias=GB2[:, m:m + 1]),
                   reads=["pb1", "GB2"], writes=["GFH2"])
        for k in range(2):
            b.emit("T", lambda t, k=k: t.matmul(
                psb(2, 0, 1, 0, 1), GC3.ap()[:, k:k + 1], GFH2[:, k:k + 1],
                start=(k == 0), stop=(k == 1)),
                reads=["GC3", "GFH2"], writes=["pb2"], accum=("gc3", 0))
        b.emit("V", lambda v: v.tensor_scalar_mul(out=GB3[0:1, 1:2], in0=GB3[0:1, 0:1], scalar1=0.5),
               reads=["GB3"], writes=["GB3"])
        b.emit("V", lambda v: v.tensor_scalar_mul(out=CB4[0:1, 1:2], in0=CB4[0:1, 0:1], scalar1=0.5),
               reads=["CB4"], writes=["CB4"])
        b.emit("S", lambda s: s.activation(out=GFT[0:1, 0:1], in_=psb(2, 0, 1, 0, 1),
                                           func=AF.Tanh, scale=0.5, bias=GB3[0:1, 1:2]),
               reads=["pb2", "GB3"], writes=["GFTg"])
        b.emit("V", lambda v: v.tensor_scalar(out=GFT[0:1, 1:2], in0=GFT[0:1, 0:1],
                                              scalar1=0.5, scalar2=0.5, op0=ALU.mult, op1=ALU.add),
               reads=["GFTg"], writes=["GFTg"])
        b.emit("V", lambda v: v.tensor_mul(out=GFT[0:1, 2:3], in0=GFT[0:1, 1:2], in1=ART[0:1, 0:1]),
               reads=["GFTg", "ART"], writes=["GFTg"])
        b.emit("V", lambda v: v.tensor_scalar_mul(out=GFT[0:1, 3:4], in0=GFT[0:1, 2:3], scalar1=1.4),
               reads=["GFTg"], writes=["GFTg"])
        b.emit("V", lambda v: v.tensor_sub(out=GFT[0:1, 4:5], in0=GFT[0:1, 3:4], in1=ART[0:1, 0:1]),
               reads=["GFTg", "ART"], writes=["GFTg"])
        b.emit("V", lambda v: v.tensor_scalar_add(out=GFT[0:1, 5:6], in0=GFT[0:1, 4:5], scalar1=0.1),
               reads=["GFTg"], writes=["GFTg"])
        b.emit("V", lambda v: v.tensor_scalar_mul(out=GFT[0:1, 6:7], in0=ART[0:1, 0:1], scalar1=0.3),
               reads=["ART"], writes=["GFTg"])
        b.emit("V", lambda v: v.tensor_add(out=GFT[0:1, 7:8], in0=GFT[0:1, 5:6], in1=GFT[0:1, 6:7]),
               reads=["GFTg"], writes=["GFTg"])

        def ffn_cycle(cyc):
            for tc in (1, 0):
                t0 = TCH * tc
                for ttl in range(8):
                    tt = 8 * tc + ttl
                    bu = ttl % 2
                    cdst = SCR[:, 1024 * bu:1024 * (bu + 1)]
                    first = (cyc == 0 and tc == 1 and ttl == 0)
                    b.emit("V", lambda v, tt=tt, cdst=cdst: v.tensor_copy(
                        out=cdst, in_=XN[:, 1024 * tt:1024 * (tt + 1)]),
                        reads=[f"XN:{tt}"] + ([] if first else ["ovS"]),
                        writes=[f"XNB:{bu}"] + (["ovS"] if first else []))
                    for f in range(8):
                        bank, off = 6 + (f // 4), 128 * (f % 4)
                        b.emit("T", lambda t, f=f, bank=bank, off=off, cdst=cdst: t.matmul(
                            psb(bank, off, 128), cdst[:, 128 * f:128 * (f + 1)], IDB[:, :],
                            start=True, stop=True),
                            reads=[f"XNB:{bu}", "IDB", "ovS"], writes=[f"pb{bank}"])
                    for f in range(8):
                        bank, off = 6 + (f // 4), 128 * (f % 4)
                        fe = (first and f == 0)
                        b.emit("S", lambda s, f=f, bank=bank, off=off, ttl=ttl: s.activation(
                            out=BIGA[:, 1024 * f + 128 * ttl:1024 * f + 128 * (ttl + 1)],
                            in_=psb(bank, off, 128), func=AF.Copy),
                            reads=[f"pb{bank}"] + ([] if fe else ["ovA"]),
                            writes=[f"OUTBT:{f}"] + (["ovA"] if fe else []))
                gi = 0
                for m in range(8):
                    for n in range(2):
                        bank = gi % 6
                        gi += 1
                        for k in range(8):
                            b.emit("T", lambda t, m=m, n=n, k=k, bank=bank: t.matmul(
                                psb(bank, 0, 512), WBIG.ap()[:, 1024 * k + 128 * m:1024 * k + 128 * m + 128],
                                BIGA[:, 1024 * k + 512 * n:1024 * k + 512 * n + 512],
                                start=(k == 0), stop=(k == 7)),
                                reads=[f"OUTBT:{k}", "WBIG", "ovA"], writes=[f"pb{bank}"],
                                accum=("h1", cyc, tc, m, n))
                        b.emit("S", lambda s, m=m, n=n, bank=bank: s.activation(
                            out=BIGA[:, 8192 + 1024 * m + 512 * n:8192 + 1024 * m + 512 * n + 512],
                            in_=psb(bank, 0, 512), func=AF.Gelu, bias=CB1[:, m:m + 1]),
                            reads=[f"pb{bank}", "CB1", "ovA"], writes=[f"H1B:{m}"])
                for m in range(4):
                    for n in range(2):
                        bank = gi % 6
                        gi += 1
                        for k in range(8):
                            b.emit("T", lambda t, m=m, n=n, k=k, bank=bank: t.matmul(
                                psb(bank, 0, 512), CN2.ap()[:, 512 * k + 128 * m:512 * k + 128 * m + 128],
                                BIGA[:, 8192 + 1024 * k + 512 * n:8192 + 1024 * k + 512 * n + 512],
                                start=(k == 0), stop=(k == 7)),
                                reads=[f"H1B:{k}", "CN2", "ovA"], writes=[f"pb{bank}"],
                                accum=("h2", cyc, tc, m, n))
                        wk = ["ovK"] if (cyc == 0 and tc == 0 and m == 0 and n == 0) else []
                        b.emit("S", lambda s, m=m, n=n, bank=bank: s.activation(
                            out=KD[:, 1024 * m + 512 * n:1024 * m + 512 * n + 512],
                            in_=psb(bank, 0, 512), func=AF.Gelu, bias=CB2[:, m:m + 1]),
                            reads=[f"pb{bank}", "CB2"] + ([] if wk else ["ovK"]),
                            writes=[f"H2B:{m}"] + wk)
                for m in range(2):
                    for n in range(2):
                        bank = gi % 6
                        gi += 1
                        for k in range(4):
                            b.emit("T", lambda t, m=m, n=n, k=k, bank=bank: t.matmul(
                                psb(bank, 0, 512), CN3.ap()[:, 256 * k + 128 * m:256 * k + 128 * m + 128],
                                KD[:, 1024 * k + 512 * n:1024 * k + 512 * n + 512],
                                start=(k == 0), stop=(k == 3)),
                                reads=[f"H2B:{k}", "CN3", "ovK"], writes=[f"pb{bank}"],
                                accum=("h3", cyc, tc, m, n))
                        b.emit("S", lambda s, m=m, n=n, bank=bank: s.activation(
                            out=KD[:, 4096 + 1024 * m + 512 * n:4096 + 1024 * m + 512 * n + 512],
                            in_=psb(bank, 0, 512), func=AF.Gelu, bias=CB3[:, m:m + 1]),
                            reads=[f"pb{bank}", "CB3", "ovK"], writes=[f"H3B:{m}"])
                for n in range(2):
                    for k in range(2):
                        b.emit("T", lambda t, n=n, k=k: t.matmul(
                            psb(6 + n, 0, 512, 0, 1),
                            CN4.ap()[:, k:k + 1],
                            KD[:, 4096 + 1024 * k + 512 * n:4096 + 1024 * k + 512 * n + 512],
                            start=(k == 0), stop=(k == 1)),
                            reads=[f"H3B:{k}", "CN4", "ovK"], writes=[f"pb{6 + n}"],
                            accum=("z", cyc, tc, n))
                for n in range(2):
                    b.emit("S", lambda s, n=n: s.activation(
                        out=CTLB[0:1, 512 * n:512 * (n + 1)], in_=psb(6 + n, 0, 512, 0, 1),
                        func=AF.Tanh, scale=0.5, bias=CB4[0:1, 1:2]),
                        reads=[f"pb{6 + n}", "CB4"], writes=["CTLB"])
                b.emit("V", lambda v: v.tensor_scalar(
                    out=CTLB[0:1, 0:1024], in0=CTLB[0:1, 0:1024],
                    scalar1=GFT[0:1, 6:7], scalar2=GFT[0:1, 7:8],
                    op0=ALU.mult, op1=ALU.add),
                    reads=["CTLB", "GFTg"], writes=["CTLB"])
                for ttl in range(8):
                    tt = 8 * tc + ttl
                    cb = ttl % 2
                    b.emit("T", lambda t, ttl=ttl: t.matmul(
                        psb(6, 480, 1), CTLB[0:1, 128 * ttl:128 * (ttl + 1)],
                        IDF[0:1, 0:1], start=True, stop=True),
                        reads=["CTLB", "IDF"], writes=["pb6"])
                    b.emit("V", lambda v, cb=cb: v.tensor_copy(out=CTL[:, 2 * cb:2 * cb + 1],
                                                               in_=psb(6, 480, 1)),
                           reads=["pb6"], writes=[f"CTL:{cb}"])
                    b.emit("V", lambda v, cb=cb: v.tensor_scalar_mul(out=CTL[:, 2 * cb + 1:2 * cb + 2],
                                                                     in0=CTL[:, 2 * cb:2 * cb + 1],
                                                                     scalar1=-1.0),
                           reads=[f"CTL:{cb}"], writes=[f"CTL:{cb}"])
                    x0 = 1024 * tt
                    q = lambda i: XN[:, x0 + 256 * i:x0 + 256 * (i + 1)]
                    rX = f"XN:{tt}"
                    b.emit("V", lambda v, tt=tt, cb=cb: v.scalar_tensor_tensor(
                        out=TMPQ[:, 0:256], in0=XN[:, 1024 * tt + 512:1024 * tt + 768],
                        scalar=CTL[:, 2 * cb:2 * cb + 1], in1=XN[:, 1024 * tt:1024 * tt + 256],
                        op0=ALU.mult, op1=ALU.add),
                        reads=[rX, f"CTL:{cb}"], writes=["TMPQ"])
                    b.emit("V", lambda v, tt=tt, cb=cb: v.scalar_tensor_tensor(
                        out=TMPQ[:, 256:512], in0=XN[:, 1024 * tt + 768:1024 * tt + 1024],
                        scalar=CTL[:, 2 * cb + 1:2 * cb + 2], in1=XN[:, 1024 * tt + 256:1024 * tt + 512],
                        op0=ALU.mult, op1=ALU.add),
                        reads=[rX, f"CTL:{cb}"], writes=["TMPQ"])
                    b.emit("V", lambda v, tt=tt, cb=cb: v.scalar_tensor_tensor(
                        out=XN[:, 1024 * tt + 512:1024 * tt + 768], in0=XN[:, 1024 * tt:1024 * tt + 256],
                        scalar=CTL[:, 2 * cb + 1:2 * cb + 2], in1=XN[:, 1024 * tt + 512:1024 * tt + 768],
                        op0=ALU.mult, op1=ALU.add),
                        reads=[rX, f"CTL:{cb}"], writes=[rX])
                    b.emit("V", lambda v, tt=tt, cb=cb: v.scalar_tensor_tensor(
                        out=XN[:, 1024 * tt + 768:1024 * tt + 1024], in0=XN[:, 1024 * tt + 256:1024 * tt + 512],
                        scalar=CTL[:, 2 * cb:2 * cb + 1], in1=XN[:, 1024 * tt + 768:1024 * tt + 1024],
                        op0=ALU.mult, op1=ALU.add),
                        reads=[rX, f"CTL:{cb}"], writes=[rX])
                    b.emit("V", lambda v, tt=tt: v.tensor_copy(
                        out=XN[:, 1024 * tt:1024 * tt + 256], in_=TMPQ[:, 0:256]),
                        reads=["TMPQ"], writes=[rX])
                    b.emit("V", lambda v, tt=tt: v.tensor_copy(
                        out=XN[:, 1024 * tt + 256:1024 * tt + 512], in_=TMPQ[:, 256:512]),
                        reads=["TMPQ"], writes=[rX])

        for cyc in range(3):
            ffn_cycle(cyc)

        wload(WBIG, wo_in, 8, 1024, "WBIG")

        if UPTO < 4:
            early_out()
            return nc
        for tt in range(NT):
            for r in range(4):
                wA = ["ovA"] if (tt == 0 and r == 0) else []
                b.emit("P", lambda p, tt=tt, r=r: p.indirect_dma_start(
                    out=BIGA[:, 1024 * tt + 256 * r:1024 * tt + 256 * (r + 1)],
                    out_offset=None, in_=ygat,
                    in_offset=IndirectOffsetOnAxis(ap=YIX[:, 4 * tt + r:4 * tt + r + 1], axis=0)),
                    reads=["ygat", "YIX"] + ([] if wA else ["ovA"]),
                    writes=[f"YALL:{tt}"] + wA, dma="yin")
        for tt in range(NT):
            for f in range(8):
                bank, off = 6 + (f // 4), 128 * (f % 4)
                wS = ["ovS"] if (tt == 0 and f == 0) else []
                b.emit("T", lambda t, tt=tt, f=f, bank=bank, off=off: t.matmul(
                    psb(bank, off, 128), BIGA[:, 1024 * tt + 128 * f:1024 * tt + 128 * (f + 1)],
                    IDB[:, :], start=True, stop=True),
                    reads=[f"YALL:{tt}", "IDB"], writes=[f"pb{bank}"])
                b.emit("V", lambda v, f=f, bank=bank, off=off: v.tensor_copy(
                    out=SCR[:, 128 * f:128 * (f + 1)], in_=psb(bank, off, 128)),
                    reads=[f"pb{bank}"] + ([] if wS else ["ovS"]), writes=["YTT"] + wS)
            for n in range(2):
                bank = (2 * tt + n) % 4
                for k in range(8):
                    b.emit("T", lambda t, n=n, k=k, bank=bank: t.matmul(
                        psb(bank, 0, 512), SCR[:, 128 * k:128 * (k + 1)],
                        WBIG.ap()[:, 1024 * k + 512 * n:1024 * k + 512 * n + 512],
                        start=(k == 0), stop=(k == 7)),
                        reads=["YTT", "WBIG"], writes=[f"pb{bank}"], accum=("wo", tt, n))
                b.emit("V", lambda v, tt=tt, n=n, bank=bank: v.tensor_add(
                    out=XN[:, 1024 * tt + 512 * n:1024 * tt + 512 * n + 512],
                    in0=XN[:, 1024 * tt + 512 * n:1024 * tt + 512 * n + 512],
                    in1=psb(bank, 0, 512)),
                    reads=[f"pb{bank}", f"XN:{tt}"], writes=[f"XN:{tt}"])
            bu = tt % 2
            xch = XCH[:, 1024 * bu:1024 * (bu + 1)]
            b.emit("Y", lambda y, tt=tt, xch=xch: y.dma_start(
                out=xch, in_=xo_in[128 * tt:128 * (tt + 1), :]),
                writes=[f"XCH:{bu}"], dma="q")
            b.emit("V", lambda v, tt=tt, xch=xch: v.tensor_add(
                out=XN[:, 1024 * tt:1024 * (tt + 1)],
                in0=XN[:, 1024 * tt:1024 * (tt + 1)], in1=xch),
                reads=[f"XCH:{bu}", f"XN:{tt}"], writes=[f"XN:{tt}"])
            b.emit("Y", lambda y, tt=tt: y.dma_start(
                out=out_d[128 * tt:128 * (tt + 1), :], in_=XN[:, 1024 * tt:1024 * (tt + 1)]),
                reads=[f"XN:{tt}"], writes=[f"outw:{tt}"], dma="q")

        b.emit("P", lambda p: p.engine_nop(), reads=[f"outw:{NT - 1}"])

        block = es.enter_context(nc.Block())
        b.finalize(nc, es, block)
    return nc


_CACHE = {}


def _tables():
    if "t" in _CACHE:
        return _CACHE["t"]
    half = 32
    inv = 1.0 / (10000.0 ** (np.arange(half, dtype=np.float64) / half))
    t = np.arange(S, dtype=np.float64)
    fr = np.outer(inv, t)
    emb = np.concatenate([fr, fr], axis=0)
    cos = np.cos(emb)
    sin = np.sin(emb)
    cos2 = np.concatenate([cos, cos], axis=0)
    sin2 = np.concatenate([sin, sin], axis=0)
    bf = lambda a: a.astype(ml_dtypes.bfloat16)
    cosq = bf(cos2 / 8.0)
    sinq = bf(sin2 / 8.0)
    cosk = bf(cos2)
    sink = bf(sin2)
    rp = np.zeros((128, 128), dtype=np.float32)
    for base in (0, 64):
        for m in range(32):
            rp[base + m + 32, base + m] = -1.0
            rp[base + m, base + m + 32] = 1.0
    idb = np.eye(128, dtype=ml_dtypes.bfloat16)
    idf = np.eye(128, dtype=np.float32)
    oneb = np.ones((128, 128), dtype=ml_dtypes.bfloat16)
    msk = np.zeros((128, 384), dtype=np.float32)
    for pp in range(128):
        for j in range(384):
            if pp <= j - 128:
                msk[pp, j] = 1.0
    _CACHE["t"] = (cosq, sinq, cosk, sink, bf(rp), idb, idf, oneb, bf(msk))
    return _CACHE["t"]


def kernel(**inputs):
    x = np.asarray(inputs["L0_attn_x"], dtype=np.float32)
    Wq = np.ascontiguousarray(np.asarray(inputs["L0_attn_Wq"], dtype=np.float32))
    Wk = np.ascontiguousarray(np.asarray(inputs["L0_attn_Wk"], dtype=np.float32))
    Wv = np.ascontiguousarray(np.asarray(inputs["L0_attn_Wv"], dtype=np.float32))
    Wo = np.ascontiguousarray(np.asarray(inputs["L0_attn_Wo"], dtype=np.float32))
    cosq, sinq, cosk, sink, rp, idb, idf, oneb, msk = _tables()

    if "nc" not in _CACHE:
        _CACHE["nc"] = build_program()
    nc = _CACHE["nc"]

    shared = {
        "wo": Wo,
        "cw1": np.ascontiguousarray(np.asarray(inputs["L0_ffn_cn_w1"], dtype=np.float32)),
        "cw2": np.ascontiguousarray(np.asarray(inputs["L0_ffn_cn_w2"], dtype=np.float32)),
        "cw3": np.ascontiguousarray(np.asarray(inputs["L0_ffn_cn_w3"], dtype=np.float32)),
        "cw4": np.ascontiguousarray(np.asarray(inputs["L0_ffn_cn_w4"], dtype=np.float32)),
        "gw1": np.ascontiguousarray(np.asarray(inputs["L0_ffn_gc_w1"], dtype=np.float32)),
        "gw2": np.ascontiguousarray(np.asarray(inputs["L0_ffn_gc_w2"], dtype=np.float32)),
        "gw3": np.ascontiguousarray(np.asarray(inputs["L0_ffn_gc_w3"], dtype=np.float32)),
        "cb1": np.ascontiguousarray(np.asarray(inputs["L0_ffn_cn_b1"], dtype=np.float32)),
        "cb2": np.ascontiguousarray(np.asarray(inputs["L0_ffn_cn_b2"], dtype=np.float32)),
        "cb3": np.ascontiguousarray(np.asarray(inputs["L0_ffn_cn_b3"], dtype=np.float32)),
        "cb4": np.ascontiguousarray(np.asarray(inputs["L0_ffn_cn_b4"], dtype=np.float32)),
        "gb1": np.ascontiguousarray(np.asarray(inputs["L0_ffn_gc_b1"], dtype=np.float32)),
        "gb2": np.ascontiguousarray(np.asarray(inputs["L0_ffn_gc_b2"], dtype=np.float32)),
        "gb3": np.ascontiguousarray(np.asarray(inputs["L0_ffn_gc_b3"], dtype=np.float32)),
        "ar": np.ascontiguousarray(np.asarray(inputs["L0_ffn_adaptive_range"], dtype=np.float32)),
        "cosq": cosq, "sinq": sinq, "cosk": cosk, "sink": sink,
        "rp": rp, "idb": idb, "idf": idf, "oneb": oneb, "msk": msk,
    }
    in_maps = []
    for c in range(NCORE := 8):
        bb, kh = divmod(c, 4)
        m = dict(shared)
        m["x"] = np.ascontiguousarray(x[bb])
        m["xown"] = np.ascontiguousarray(x[bb, TOK * kh:TOK * (kh + 1)])
        m["wq"] = np.ascontiguousarray(Wq[:, 256 * kh:256 * (kh + 1)])
        m["wk"] = np.ascontiguousarray(Wk[:, 64 * kh:64 * (kh + 1)])
        m["wv"] = np.ascontiguousarray(Wv[:, 64 * kh:64 * (kh + 1)])
        yidx = np.empty((128, 64), dtype=np.int32)
        for tt in range(NT):
            for r in range(4):
                yidx[:, 4 * tt + r] = S * (4 * bb + r) + TOK * kh + 128 * tt + np.arange(128)
        m["yidx"] = yidx
        in_maps.append(m)

    res = run_bass_kernel_spmd(nc, in_maps, core_ids=list(range(8)))
    out = np.empty((2, S, D), dtype=np.float32)
    for c in range(8):
        bb, kh = divmod(c, 4)
        out[bb, TOK * kh:TOK * (kh + 1)] = res.results[c]["out"]
    return out
